# version 03
# baseline (speedup 1.0000x reference)
import contextlib
import numpy as np
import ml_dtypes

import concourse.bass as bass
import concourse.mybir as mybir
from concourse.bass_utils import run_bass_kernel_spmd

F32 = mybir.dt.float32
BF16 = mybir.dt.bfloat16
AF = mybir.ActivationFunctionType
ALU = mybir.AluOpType
AX = mybir.AxisListType

BF = ml_dtypes.bfloat16
T_FULL = 4096
D = 1024
NT = T_FULL // 128
HD = 64
EPS_TOP = 1e-5

ENGINES = ("tensor", "vector", "scalar", "gpsimd", "sync")
DEBUG = False
DEBUG_ZERO = True
DEBUG_A = True
DEBUG_B = True


NUM_EPOCH_SEMS = 6


class Epoch:
    __slots__ = ("engine", "idx", "count", "open", "cum_before", "cum_after")

    def __init__(self, engine, idx):
        self.engine = engine
        self.idx = idx
        self.count = 0
        self.open = True


class Op:
    __slots__ = ("engine", "fn", "dma", "deps", "marked", "inc_no", "tag", "epoch",
                 "raw", "fence_eps")

    def __init__(self, engine, fn, dma, tag, raw=False):
        self.engine = engine
        self.fn = fn
        self.dma = dma
        self.deps = set()
        self.marked = False
        self.inc_no = None
        self.tag = tag
        self.epoch = None
        self.raw = raw
        self.fence_eps = None


class Prog:

    def __init__(self, nc):
        self.nc = nc
        self.per_eng = {e: [] for e in ENGINES}
        self.last_w = {}
        self.readers = {}
        self.cur_epoch = {}
        self.epochs = {e: [] for e in ENGINES}

    def fence(self):
        lasts = [ops[-1] for ops in self.per_eng.values() if ops]
        fence_eps = {}
        for e in ENGINES:
            for ep in self.epochs[e]:
                ep.open = False
                fence_eps[(e, ep.idx % NUM_EPOCH_SEMS)] = ep
        self.cur_epoch = {}
        for e in ENGINES:
            op = Op(e, (lambda eng: eng.nop()), False, "fence")
            op.deps = {d for d in lasts
                       if not (d.engine == e == "tensor" and not d.dma)}
            op.fence_eps = list(fence_eps.values())
            self.per_eng[e].append(op)
        self.last_w = {}
        self.readers = {}

    def _dma_epoch(self, engine):
        ep = self.cur_epoch.get(engine)
        if ep is None or not ep.open:
            ep = Epoch(engine, len(self.epochs[engine]))
            self.epochs[engine].append(ep)
            self.cur_epoch[engine] = ep
        return ep

    def add(self, engine, fn, reads=(), writes=(), dma=False, tag="", raw=False):
        op = Op(engine, fn, dma, tag, raw)
        deps = set()
        for r in reads:
            w = self.last_w.get(r)
            if w is not None:
                deps.add(w)
        for r in writes:
            w = self.last_w.get(r)
            if w is not None:
                deps.add(w)
            for rd in self.readers.get(r, ()):
                deps.add(rd)
        for r in reads:
            self.readers.setdefault(r, []).append(op)
        for r in writes:
            self.last_w[r] = op
            self.readers[r] = []
        op.deps = {d for d in deps
                   if d is not op and not (d.engine == engine == "tensor"
                                           and not d.dma and not dma)}
        for d in op.deps:
            if d.dma and d.epoch.open:
                d.epoch.open = False
        if dma:
            ep = self._dma_epoch(engine)
            ep.count += 1
            op.epoch = ep
        self.per_eng[engine].append(op)
        return op

    def emit(self, es):
        nc = self.nc
        for ops in self.per_eng.values():
            for op in ops:
                for d in op.deps:
                    if not d.dma:
                        d.marked = True
        sem_cum = {}
        for e in ENGINES:
            for ep in self.epochs[e]:
                ep.open = False
                slot = ep.idx % NUM_EPOCH_SEMS
                prev = sem_cum.get((e, slot), 0)
                ep.cum_before = prev
                ep.cum_after = prev + ep.count
                sem_cum[(e, slot)] = ep.cum_after
        sem_c, sem_d = {}, {}
        cnt_c = {e: 0 for e in ENGINES}
        for e in ENGINES:
            for op in self.per_eng[e]:
                if not op.dma and op.marked:
                    cnt_c[e] += 1
                    op.inc_no = cnt_c[e]
        for e in ENGINES:
            if cnt_c[e]:
                sem_c[e] = es.enter_context(nc.semaphore(f"c_{e}"))
            nslots = min(NUM_EPOCH_SEMS, len(self.epochs[e]))
            if nslots:
                sem_d[e] = [es.enter_context(nc.semaphore(f"d_{e}{i}"))
                            for i in range(nslots)]

        def emit_engine(eng_obj, ename):
            waited = {}
            seen_epoch = {}
            for op in self.per_eng[ename]:
                if op.fence_eps is not None:
                    for ep in op.fence_eps:
                        slot = ep.idx % NUM_EPOCH_SEMS
                        key = ("d", ep.engine, slot)
                        val = 16 * ep.cum_after
                        if waited.get(key, 0) < val:
                            waited[key] = val
                            eng_obj.wait_ge(sem_d[ep.engine][slot], val)
                for d in sorted(op.deps, key=lambda o: (o.engine, o.inc_no or 0, id(o))):
                    if d.dma:
                        ep = d.epoch
                        slot = ep.idx % NUM_EPOCH_SEMS
                        key = ("d", d.engine, slot)
                        val = 16 * ep.cum_after
                        sem = sem_d[d.engine][slot]
                    else:
                        key, val, sem = ("c", d.engine), d.inc_no, sem_c[d.engine]
                    if waited.get(key, 0) < val:
                        waited[key] = val
                        eng_obj.wait_ge(sem, val)
                if op.dma:
                    ep = op.epoch
                    slot = ep.idx % NUM_EPOCH_SEMS
                    if seen_epoch.get(slot) != ep.idx:
                        seen_epoch[slot] = ep.idx
                        if ep.cum_before > 0:
                            key = ("d", ename, slot)
                            val = 16 * ep.cum_before
                            if waited.get(key, 0) < val:
                                waited[key] = val
                                eng_obj.wait_ge(sem_d[ename][slot], val)
                if op.raw:
                    sem = sem_d[ename][op.epoch.idx % NUM_EPOCH_SEMS]
                    op.fn(eng_obj, lambda inst, sem=sem: inst.then_inc(sem, 16))
                else:
                    inst = op.fn(eng_obj)
                    if op.dma:
                        inst.then_inc(sem_d[ename][op.epoch.idx % NUM_EPOCH_SEMS], 16)
                    elif op.marked:
                        inst.then_inc(sem_c[ename], 1)
            if ename == "sync":
                for e in ENGINES:
                    for slot in range(min(NUM_EPOCH_SEMS, len(self.epochs[e]))):
                        tot = sem_cum.get((e, slot), 0)
                        if tot:
                            eng_obj.wait_ge(sem_d[e][slot], 16 * tot)

        block = es.enter_context(nc.Block())
        block.tensor(lambda t: emit_engine(t, "tensor"))
        block.vector(lambda v: emit_engine(v, "vector"))
        block.scalar(lambda s: emit_engine(s, "scalar"))
        block.gpsimd(lambda g: emit_engine(g, "gpsimd"))
        block.sync(lambda s: emit_engine(s, "sync"))


def R(*names):
    return tuple(names)


class _FakeHandle:
    def __init__(self, ap):
        self._ap = ap

    def ap(self):
        return self._ap


class NCProxy:

    def __init__(self, nc, pfx, redirect):
        self._nc = nc
        self._pfx = pfx
        self._rd = redirect

    def dram_tensor(self, name, shape, dtype, kind="Internal", **kw):
        if name in self._rd:
            return _FakeHandle(self._rd[name])
        return self._nc.dram_tensor(self._pfx + name, shape, dtype, kind="Internal"
                                    if kind == "ExternalOutput" and name not in self._rd
                                    else ("ExternalInput" if kind == "ExternalInput" else "Internal"),
                                    **kw)

    def sbuf_tensor(self, name, shape, dtype, **kw):
        return self._nc.sbuf_tensor(self._pfx + name, shape, dtype, **kw)

    def psum_tensor(self, name, shape, dtype, **kw):
        return self._nc.psum_tensor(self._pfx + name, shape, dtype, **kw)

    def __getattr__(self, a):
        return getattr(self._nc, a)



def host_constants():
    c = {}
    c["c_ident"] = np.eye(128, dtype=np.float32)
    c["c_onesb"] = np.ones((1, 512), dtype=BF)
    c["c_onesf"] = np.ones((1, 128), dtype=np.float32)
    masks = np.zeros((4, 128, 512), dtype=np.float32)
    for j, off in enumerate((0, 128, 256, 384)):
        kl = np.arange(128)[:, None]
        ql = np.arange(512)[None, :]
        masks[j] = np.where(ql >= kl + off, 0.0, -1e30)
    c["c_masks"] = masks.astype(BF)
    return c


CONSTS = host_constants()



def emit_ln_tok(p, yt, ytn, sqd, small, eps_col, out, outname, smn="small"):
    p.add("scalar", lambda sc: sc.activation(sqd[:], yt[:], AF.Square,
                                             accum_out=small[:, 4:5]),
          reads=R(ytn), writes=R(smn))
    p.add("vector", lambda v: v.reduce_sum(small[:, 5:6], small[:, 0:2], axis=AX.X),
          reads=R(smn), writes=R(smn))
    p.add("vector", lambda v: v.tensor_scalar_mul(small[:, 6:7], small[:, 5:6], 1.0 / D),
          reads=R(smn), writes=R(smn))
    p.add("vector", lambda v: v.tensor_mul(small[:, 7:8], small[:, 6:7], small[:, 6:7]),
          reads=R(smn), writes=R(smn))
    p.add("vector", lambda v: v.scalar_tensor_tensor(small[:, 7:8], small[:, 4:5], 1.0 / D,
                                                     small[:, 7:8], ALU.mult, ALU.subtract),
          reads=R(smn), writes=R(smn))
    p.add("scalar", lambda sc: sc.activation(small[:, 8:9], small[:, 7:8], AF.Sqrt,
                                             bias=eps_col),
          reads=R(smn), writes=R(smn))
    p.add("vector", lambda v: v.reciprocal(small[:, 9:10], small[:, 8:9]),
          reads=R(smn), writes=R(smn))
    p.add("vector", lambda v: v.scalar_tensor_tensor(small[:, 10:11], small[:, 6:7], -1.0,
                                                     small[:, 9:10], ALU.mult, ALU.mult),
          reads=R(smn), writes=R(smn))
    p.add("scalar", lambda sc: sc.activation(out[:], yt[:], AF.Identity,
                                             bias=small[:, 10:11], scale=small[:, 9:10]),
          reads=R(ytn, smn), writes=R(outname))



class AttnCfg:
    def __init__(self, name, n_heads=16, kv_heads=16, dq=1024, dkv=1024,
                 has_bq=False, has_bkv=False, has_bo=False,
                 qk_prep=None, pre_norm=None, post_ln=False, epilogue="top",
                 gains=False, nf=0, prep_eps=1e-6, fold_gb=False):
        self.name = name
        self.n_heads, self.kv_heads = n_heads, kv_heads
        self.dq, self.dkv = dq, dkv
        self.has_bq, self.has_bkv, self.has_bo = has_bq, has_bkv, has_bo
        self.qk_prep = qk_prep
        self.pre_norm = pre_norm
        self.post_ln = post_ln
        self.epilogue = epilogue
        self.gains = gains
        self.nf = nf
        self.prep_eps = prep_eps
        self.fold_gb = fold_gb


def build_attn_launch(cfg: AttnCfg, FUSE=None):
    standalone = FUSE is None
    nc = (bass.Bass("TRN2", target_bir_lowering=False) if standalone
          else NCProxy(FUSE["nc"], FUSE["pfx"], FUSE["rd"]))
    es = contextlib.ExitStack()
    H, KVH, dq, dkv = cfg.n_heads, cfg.kv_heads, cfg.dq, cfg.dkv
    NPAIR = H // 2
    KVPAIR = max(KVH // 2, 1)

    xin = nc.dram_tensor("xin", [T_FULL, D], F32, kind="ExternalInput").ap()
    if cfg.epilogue == "top":
        xout = nc.dram_tensor("xout", [T_FULL, D], F32, kind="ExternalOutput").ap()

    ident_d = nc.dram_tensor("c_ident", [128, 128], F32, kind="ExternalInput").ap()
    onesb_d = nc.dram_tensor("c_onesb", [1, 512], BF16, kind="ExternalInput").ap()
    onesf_d = nc.dram_tensor("c_onesf", [1, 128], F32, kind="ExternalInput").ap()
    masks_d = nc.dram_tensor("c_masks", [4, 128, 512], BF16, kind="ExternalInput").ap()

    wq_d = nc.dram_tensor("wq", [D, dq], F32, kind="ExternalInput").ap()
    wk_d = nc.dram_tensor("wk", [D, dkv], F32, kind="ExternalInput").ap()
    wv_d = nc.dram_tensor("wv", [D, dkv], F32, kind="ExternalInput").ap()
    wo_d = nc.dram_tensor("wo", [D, D], F32, kind="ExternalInput").ap()
    bias_d = {}
    if cfg.has_bq:
        bias_d["bq"] = nc.dram_tensor("bq", [1, dq], F32, kind="ExternalInput").ap()
    if cfg.has_bkv:
        bias_d["bk"] = nc.dram_tensor("bk", [1, dkv], F32, kind="ExternalInput").ap()
        bias_d["bv"] = nc.dram_tensor("bv", [1, dkv], F32, kind="ExternalInput").ap()
    if cfg.has_bo:
        bias_d["bo"] = nc.dram_tensor("bo", [1, D], F32, kind="ExternalInput").ap()
    if cfg.post_ln:
        pg_d = nc.dram_tensor("post_gamma", [1, D], F32, kind="ExternalInput").ap()
        pb_d = nc.dram_tensor("post_beta", [1, D], F32, kind="ExternalInput").ap()
    if cfg.gains:
        qg_d = nc.dram_tensor("q_gain", [1, cfg.n_heads], F32, kind="ExternalInput").ap()
        kg_d = nc.dram_tensor("k_gain", [1, cfg.kv_heads], F32, kind="ExternalInput").ap()
    if cfg.qk_prep is not None and cfg.nf:
        ropecs_d = nc.dram_tensor("rope_cs", [T_FULL, 2 * cfg.nf], F32, kind="ExternalInput").ap()
    if cfg.fold_gb:
        gam_d = nc.dram_tensor("ln_gam_col", [128, 8], F32, kind="ExternalInput").ap()
    if cfg.epilogue == "inner":
        xmid_d = nc.dram_tensor("xmid", [T_FULL, D], F32, kind="ExternalOutput").ap()

    skind = "ExternalOutput" if (DEBUG and DEBUG_ZERO) else "Internal"
    nbf = nc.dram_tensor("nbf", [T_FULL, D], BF16, kind=skind).ap()
    qd2 = nc.dram_tensor("qd2", [NPAIR, T_FULL, 128], BF16, kind=skind).ap()
    kd2n = KVPAIR if KVH == H else KVH
    kd2 = nc.dram_tensor("kd2", [kd2n, T_FULL, 128], BF16, kind=skind).ap()
    vd = nc.dram_tensor("vd", [KVH, T_FULL, HD], BF16, kind=skind).ap()
    if DEBUG:
        dbg_ot = nc.dram_tensor("dbg_ot", [8, 128, T_FULL], BF16, kind="ExternalOutput").ap()
        dbg_mx = nc.dram_tensor("dbg_mx", [1, T_FULL], BF16, kind="ExternalOutput").ap()
        dbg_rm = nc.dram_tensor("dbg_rm", [128, 32], F32, kind="ExternalOutput").ap()
        dbg_yt = nc.dram_tensor("dbg_yt", [T_FULL, D], F32, kind="ExternalOutput").ap()
        dbg_sm = nc.dram_tensor("dbg_sm", [NT, 128, 16], F32, kind="ExternalOutput").ap()

    big = es.enter_context(nc.sbuf_tensor("big", [128, 8 * T_FULL], BF16))
    wq_sb = es.enter_context(nc.sbuf_tensor("wq_sb", [128, 8 * dq], BF16))
    wk_sb = es.enter_context(nc.sbuf_tensor("wk_sb", [128, 8 * dkv], BF16))
    wv_sb = es.enter_context(nc.sbuf_tensor("wv_sb", [128, 8 * dkv], BF16))
    xa = es.enter_context(nc.sbuf_tensor("xa", [128, D], BF16))
    xb = es.enter_context(nc.sbuf_tensor("xb", [128, D], BF16))
    qt_a = es.enter_context(nc.sbuf_tensor("qt_a", [128, dq], BF16))
    qt_b = es.enter_context(nc.sbuf_tensor("qt_b", [128, dq], BF16))
    kt_a = es.enter_context(nc.sbuf_tensor("kt_a", [128, dkv], BF16))
    kt_b = es.enter_context(nc.sbuf_tensor("kt_b", [128, dkv], BF16))
    vt_a = es.enter_context(nc.sbuf_tensor("vt_a", [128, dkv], BF16))
    vt_b = es.enter_context(nc.sbuf_tensor("vt_b", [128, dkv], BF16))
    QT = es.enter_context(nc.sbuf_tensor("QT", [128, T_FULL], BF16))
    KT = es.enter_context(nc.sbuf_tensor("KT", [128, T_FULL], BF16))
    VB = es.enter_context(nc.sbuf_tensor("VB", [128, NT * 65], BF16))
    PT_a = es.enter_context(nc.sbuf_tensor("PT_a", [128, 512], BF16))
    PT_b = es.enter_context(nc.sbuf_tensor("PT_b", [128, 512], BF16))
    PT_c = es.enter_context(nc.sbuf_tensor("PT_c", [128, 512], BF16))
    cmx = es.enter_context(nc.sbuf_tensor("cmx", [128, 2 * max(T_FULL // 512, 1)], F32))
    rm = es.enter_context(nc.sbuf_tensor("rm", [128, 2 * NT], F32))
    maxneg = es.enter_context(nc.sbuf_tensor("maxneg", [1, T_FULL], BF16))
    rs_row = es.enter_context(nc.sbuf_tensor("rs_row", [1, 512], F32))
    otmp = es.enter_context(nc.sbuf_tensor("otmp", [64, 512], F32))
    ident = es.enter_context(nc.sbuf_tensor("ident", [128, 128], F32))
    onesb = es.enter_context(nc.sbuf_tensor("onesb", [1, 512], BF16))
    onesf = es.enter_context(nc.sbuf_tensor("onesf", [1, 128], F32))
    masks = es.enter_context(nc.sbuf_tensor("masks", [128, 4 * 512], BF16))
    bias_sb = {}
    for bn, bd in bias_d.items():
        bias_sb[bn] = es.enter_context(nc.sbuf_tensor(bn + "_sb", [1, bd.shape[1]], BF16))
    if cfg.fold_gb:
        for bn, n in (("bq", dq), ("bk", dkv), ("bv", dkv)):
            assert bn not in bias_sb
            bias_sb[bn] = es.enter_context(nc.sbuf_tensor(bn + "_sb", [1, n], BF16))
    xf_a = es.enter_context(nc.sbuf_tensor("xf_a", [128, D], F32))
    xf_b = es.enter_context(nc.sbuf_tensor("xf_b", [128, D], F32))
    yt_a = es.enter_context(nc.sbuf_tensor("yt_a", [128, D], F32))
    yt_b = es.enter_context(nc.sbuf_tensor("yt_b", [128, D], F32))
    sqd = es.enter_context(nc.sbuf_tensor("sqd", [128, D], F32))
    small = es.enter_context(nc.sbuf_tensor("small", [128, 16], F32))
    small2 = es.enter_context(nc.sbuf_tensor("small2", [128, 16], F32))
    if cfg.post_ln:
        gb_bc = es.enter_context(nc.sbuf_tensor("gb_bc", [128, 2 * D], BF16))
        prow = es.enter_context(nc.sbuf_tensor("prow", [1, 2 * D], BF16))
    if cfg.gains:
        gains_sb = es.enter_context(nc.sbuf_tensor("gains_sb", [1, cfg.n_heads + cfg.kv_heads], F32))
        gain_bc = es.enter_context(nc.sbuf_tensor("gain_bc", [128, cfg.n_heads + cfg.kv_heads], F32))
    if cfg.qk_prep is not None:
        rope_a = es.enter_context(nc.sbuf_tensor("rope_a", [128, 2 * max(cfg.nf, 1)], F32))
        rtmp = es.enter_context(nc.sbuf_tensor("rtmp", [128, 2048], F32))
        prep_sc = es.enter_context(nc.sbuf_tensor("prep_sc", [128, 128], F32))
    if cfg.fold_gb:
        gam_sb = es.enter_context(nc.sbuf_tensor("gam_sb", [128, 8], F32))

    ps = [es.enter_context(nc.psum_tensor(f"ps{i}", [128, 512], F32)) for i in range(8)]

    p = Prog(nc) if standalone else FUSE["p"]

    p.add("vector", lambda v: v.memset(small[:, 11:12], EPS_TOP), writes=R("small"))
    p.add("vector", lambda v: v.memset(small2[:, 11:12], EPS_TOP), writes=R("small2"))

    p.add("gpsimd", lambda g: g.dma_start(out=ident[:], in_=ident_d), writes=R("ident"), dma=True)
    p.add("gpsimd", lambda g: g.dma_start(out=onesb[:], in_=onesb_d), writes=R("onesb"), dma=True)
    p.add("gpsimd", lambda g: g.dma_start(out=onesf[:], in_=onesf_d), writes=R("onesf"), dma=True)
    p.add("gpsimd", lambda g: g.dma_start(
        out=masks[:].rearrange("p (j n) -> p j n", n=512),
        in_=masks_d.rearrange("j p n -> p j n")),
        writes=R("masks"), dma=True)
    for wname, wsb, wd, dout in (("wq_sb", wq_sb, wq_d, dq), ("wk_sb", wk_sb, wk_d, dkv),
                                 ("wv_sb", wv_sb, wv_d, dkv)):
        p.add("gpsimd", lambda g, wsb=wsb, wd=wd, dout=dout: g.dma_start(
            out=wsb[:].rearrange("p (c o) -> p c o", o=dout),
            in_=wd.rearrange("(c p) o -> p c o", p=128)),
            writes=R(wname), dma=True)
    for bn in bias_d:
        p.add("gpsimd", lambda g, bn=bn: g.dma_start(out=bias_sb[bn][:], in_=bias_d[bn]),
              writes=R(bn), dma=True)
    if cfg.gains:
        p.add("gpsimd", lambda g: g.dma_start(out=gains_sb[0:1, 0:H], in_=qg_d),
              writes=R("gains"), dma=True)
        p.add("gpsimd", lambda g: g.dma_start(out=gains_sb[0:1, H:H + KVH], in_=kg_d),
              writes=R("gains"), dma=True)
        p.add("tensor", lambda t: t.matmul(ps[7][0:128, 0:H + KVH], onesf[0:1, :],
                                           gains_sb[:], start=True, stop=True),
              reads=R("onesf", "gains"), writes=R("ps7"))
        p.add("vector", lambda v: v.tensor_copy(gain_bc[:], ps[7][0:128, 0:H + KVH]),
              reads=R("ps7"), writes=R("gain_bc"))
    if cfg.fold_gb:
        p.add("gpsimd", lambda g: g.dma_start(out=gam_sb[:], in_=gam_d),
              writes=R("gam"), dma=True)
        beta_d = nc.dram_tensor("ln_beta_col", [128, 8], BF16, kind="ExternalInput").ap()
        beta_sb = es.enter_context(nc.sbuf_tensor("beta_sb", [128, 8], BF16))
        p.add("gpsimd", lambda g: g.dma_start(out=beta_sb[:], in_=beta_d),
              writes=R("beta"), dma=True)
        for wname, wsb, dout, bn in (("wq_sb", wq_sb, dq, "bq"), ("wk_sb", wk_sb, dkv, "bk"),
                                     ("wv_sb", wv_sb, dkv, "bv")):
            nchk = (dout + 511) // 512
            for half in range(nchk):
                n = min(512, dout - half * 512)

                def mm_brow(tt, wsb=wsb, dout=dout, half=half, n=n):
                    for c in range(8):
                        inst = tt.matmul(ps[6][0:1, 0:n], beta_sb[:, c:c + 1],
                                         wsb[:, c * dout + half * 512: c * dout + half * 512 + n],
                                         start=(c == 0), stop=(c == 7))
                    return inst
                p.add("tensor", mm_brow, reads=R("beta", wname), writes=R("ps6"))
                p.add("scalar", lambda sc, bn=bn, half=half, n=n: sc.activation(
                    bias_sb[bn][0:1, half * 512:half * 512 + n], ps[6][0:1, 0:n], AF.Copy,
                    bias=0.0, scale=1.0),
                    reads=R("ps6"), writes=R(bn))
        for wname, wsb, dout in (("wq_sb", wq_sb, dq), ("wk_sb", wk_sb, dkv),
                                 ("wv_sb", wv_sb, dkv)):
            for c in range(8):
                p.add("vector", lambda v, wsb=wsb, dout=dout, c=c: v.tensor_scalar(
                    wsb[:, c * dout:(c + 1) * dout], wsb[:, c * dout:(c + 1) * dout],
                    gam_sb[:, c:c + 1], None, ALU.mult),
                    reads=R(wname, "gam"), writes=R(wname))

    if cfg.pre_norm is not None:
        p.add("vector", lambda v: v.memset(small[:, 12:13], float(cfg.prep_eps)),
              writes=R("small"))
    for t in range(NT):
        par = t % 2
        buf, bname = (xa, "xa") if par == 0 else (xb, "xb")
        if cfg.pre_norm is None:
            p.add("gpsimd", lambda g, buf=buf, t=t: g.dma_start(
                out=buf[:], in_=xin[t * 128:(t + 1) * 128, :]),
                writes=R(bname), dma=True)
        else:
            xf, xfn = (xf_a, "xf_a") if par == 0 else (xf_b, "xf_b")
            p.add("gpsimd", lambda g, xf=xf, t=t: g.dma_start(
                out=xf[:], in_=xin[t * 128:(t + 1) * 128, :]),
                writes=R(xfn), dma=True)
            p.add("scalar", lambda sc, xf=xf: sc.activation(
                sqd[:], xf[:], AF.Square, accum_out=small[:, 4:5]),
                reads=R(xfn), writes=R("sqd", "small"))
            if cfg.pre_norm == "rms":
                p.add("vector", lambda v: v.tensor_scalar_mul(small[:, 5:6], small[:, 4:5], 1.0 / D),
                      reads=R("small"), writes=R("small"))
                p.add("scalar", lambda sc: sc.activation(small[:, 6:7], small[:, 5:6],
                                                         AF.Sqrt, bias=small[:, 12:13]),
                      reads=R("small"), writes=R("small"))
                p.add("vector", lambda v: v.reciprocal(small[:, 7:8], small[:, 6:7]),
                      reads=R("small"), writes=R("small"))
                p.add("scalar", lambda sc, xf=xf, buf=buf: sc.activation(
                    buf[:], xf[:], AF.Identity, bias=0.0, scale=small[:, 7:8]),
                    reads=R(xfn, "small"), writes=R(bname))
            else:
                p.add("scalar", lambda sc, xf=xf: sc.activation(
                    sqd[:], xf[:], AF.Identity, accum_out=small[:, 0:1]),
                    reads=R(xfn), writes=R("sqd", "small"))
                p.add("vector", lambda v: v.memset(small[:, 1:2], 0.0),
                      writes=R("small"))
                emit_ln_tok(p, xf, xfn, sqd, small, small[:, 12:13], xf, xfn)
                p.add("vector", lambda v, xf=xf, buf=buf: v.tensor_copy(buf[:], xf[:]),
                      reads=R(xfn), writes=R(bname))
        p.add("sync", lambda s, buf=buf, t=t: s.dma_start(
            out=nbf[t * 128:(t + 1) * 128, :], in_=buf[:]),
            reads=R(bname), writes=R(f"dram_nbf{t}"), dma=True)

    for c in range(8):
        p.add("sync", lambda s, c=c: s.dma_start(
            out=big[:, c * T_FULL:(c + 1) * T_FULL],
            in_=nbf[:, c * 128:(c + 1) * 128], transpose=True),
            reads=tuple(f"dram_nbf{t}" for t in range(NT)), writes=R("big"), dma=True)

    def emit_proj(t, wsb, wname, dout, outtile, outname, bias, scale, evac_eng, bank_base):
        nch = (dout + 511) // 512
        for half in range(nch):
            n = min(512, dout - half * 512)
            bank = bank_base
            pst = ps[bank]
            has_b = bias in bias_sb

            def mm_proj(tt, wsb=wsb, half=half, t=t, pst=pst, dout=dout, n=n, has_b=has_b):
                for c in range(8):
                    inst = tt.matmul(
                        pst[:, 0:n],
                        big[:, c * T_FULL + t * 128: c * T_FULL + (t + 1) * 128],
                        wsb[:, c * dout + half * 512: c * dout + half * 512 + n],
                        start=(c == 0), stop=(c == 7 and not has_b))
                return inst
            p.add("tensor", mm_proj, reads=R("big", wname), writes=R(f"ps{bank}"))
            if has_b:
                p.add("tensor", lambda tt, pst=pst, bias=bias, half=half, n=n: tt.matmul(
                    pst[:, 0:n], onesb[0:1, 0:128],
                    bias_sb[bias][0:1, half * 512:half * 512 + n], start=False, stop=True),
                    reads=R("onesb", bias), writes=R(f"ps{bank}"))
            if evac_eng == "scalar":
                p.add("scalar", lambda sc, outtile=outtile, half=half, pst=pst, n=n, scale=scale:
                      sc.activation(outtile[:, half * 512:half * 512 + n], pst[:, 0:n],
                                    AF.Copy, bias=0.0, scale=float(scale)),
                      reads=R(f"ps{bank}"), writes=R(outname))
            else:
                p.add("vector", lambda v, outtile=outtile, half=half, pst=pst, n=n:
                      v.tensor_copy(outtile[:, half * 512:half * 512 + n], pst[:, 0:n]),
                      reads=R(f"ps{bank}"), writes=R(outname))

    def rope_views(tile, nh, kind):
        if kind == "neox":
            base = tile[:, 0:nh * 64].rearrange("p (h w) -> p h w", w=64)
            e, o = base[:, :, 0:32], base[:, :, 32:64]
            w = 32
            cb = rope_a[:, 0:32].unsqueeze(1).broadcast_to((128, nh, 32))
            sb = rope_a[:, 32:64].unsqueeze(1).broadcast_to((128, nh, 32))
        else:
            base = tile[:, 0:nh * 64].rearrange("p (h i two) -> p h i two", h=nh, two=2)
            e, o = base[:, :, 0:16, 0:1], base[:, :, 0:16, 1:2]
            w = 16
            cb = rope_a[:, 0:16].unsqueeze(1).unsqueeze(3).broadcast_to((128, nh, 16, 1))
            sb = rope_a[:, 16:32].unsqueeze(1).unsqueeze(3).broadcast_to((128, nh, 16, 1))
        return e, o, w, cb, sb

    def tview(region, nh, w, kind):
        v = region[:, 0:nh * w].rearrange("p (h w) -> p h w", w=w)
        if kind == "gptj":
            v = v.unsqueeze(3)
        return v

    def emit_rope(tile, tname, nh, kind):
        e, o, w, cb, sb = rope_views(tile, nh, kind)
        t1 = tview(rtmp[:, 0:512], nh, w, kind)
        t2 = tview(rtmp[:, 512:1024], nh, w, kind)
        t3 = tview(rtmp[:, 1024:1536], nh, w, kind)
        t4 = tview(rtmp[:, 1536:2048], nh, w, kind)
        p.add("vector", lambda v: v.tensor_tensor(t1, e, cb, ALU.mult),
              reads=R(tname, "rope_a"), writes=R("rt1"))
        p.add("vector", lambda v: v.tensor_tensor(t2, o, sb, ALU.mult),
              reads=R(tname, "rope_a"), writes=R("rt2"))
        p.add("vector", lambda v: v.tensor_tensor(t3, e, sb, ALU.mult),
              reads=R(tname, "rope_a"), writes=R("rt3"))
        p.add("vector", lambda v: v.tensor_tensor(t4, o, cb, ALU.mult),
              reads=R(tname, "rope_a"), writes=R("rt4"))
        if kind == "neox":
            p.add("vector", lambda v: v.tensor_tensor(e, t1, t2, ALU.add),
                  reads=R("rt1", "rt2"), writes=R(tname))
            p.add("vector", lambda v: v.tensor_tensor(o, t4, t3, ALU.subtract),
                  reads=R(tname, "rt3", "rt4"), writes=R(tname))
        else:
            p.add("vector", lambda v: v.tensor_tensor(e, t1, t2, ALU.subtract),
                  reads=R("rt1", "rt2"), writes=R(tname))
            p.add("vector", lambda v: v.tensor_tensor(o, t4, t3, ALU.add),
                  reads=R(tname, "rt3", "rt4"), writes=R(tname))

    def emit_rms_gain(tile, tname, nh, gain_off, extra_scale):
        W = nh * 64
        p.add("scalar", lambda sc: sc.activation(sqd[:, 0:W], tile[:, 0:W], AF.Square),
              reads=R(tname), writes=R("sqd"))
        p.add("vector", lambda v: v.reduce_sum(
            prep_sc[:, 0:nh], sqd[:, 0:W].rearrange("p (h w) -> p h w", w=64), axis=AX.X),
            reads=R("sqd"), writes=R("prep"))
        p.add("vector", lambda v: v.tensor_scalar_mul(prep_sc[:, 16:16 + nh],
                                                      prep_sc[:, 0:nh], 1.0 / 64),
              reads=R("prep"), writes=R("prep"))
        p.add("scalar", lambda sc: sc.activation(prep_sc[:, 32:32 + nh], prep_sc[:, 16:16 + nh],
                                                 AF.Sqrt, bias=small[:, 12:13]),
              reads=R("prep", "small"), writes=R("prep"))
        p.add("vector", lambda v: v.reciprocal(prep_sc[:, 48:48 + nh], prep_sc[:, 32:32 + nh]),
              reads=R("prep"), writes=R("prep"))
        p.add("vector", lambda v: v.tensor_mul(prep_sc[:, 64:64 + nh], prep_sc[:, 48:48 + nh],
                                               gain_bc[:, gain_off:gain_off + nh]),
              reads=R("prep", "gain_bc"), writes=R("prep"))
        if extra_scale != 1.0:
            p.add("vector", lambda v: v.tensor_scalar_mul(prep_sc[:, 64:64 + nh],
                                                          prep_sc[:, 64:64 + nh], extra_scale),
                  reads=R("prep"), writes=R("prep"))
        p.add("vector", lambda v: v.tensor_tensor(
            tile[:, 0:W].rearrange("p (h w) -> p h w", w=64),
            tile[:, 0:W].rearrange("p (h w) -> p h w", w=64),
            prep_sc[:, 64:64 + nh].unsqueeze(2).broadcast_to((128, nh, 64)), ALU.mult),
            reads=R(tname, "prep"), writes=R(tname))

    q_evac_scale = 1.0 if cfg.qk_prep == "l0ffn" else 0.125
    for t in range(NT):
        par = t % 2
        qtile, qn = (qt_a, "qt_a") if par == 0 else (qt_b, "qt_b")
        ktile, kn = (kt_a, "kt_a") if par == 0 else (kt_b, "kt_b")
        vtile, vn = (vt_a, "vt_a") if par == 0 else (vt_b, "vt_b")
        emit_proj(t, wq_sb, "wq_sb", dq, qtile, qn, "bq", q_evac_scale, "scalar", par * 2)
        emit_proj(t, wk_sb, "wk_sb", dkv, ktile, kn, "bk", 1.0, "vector", par * 2 + 1)
        emit_proj(t, wv_sb, "wv_sb", dkv, vtile, vn, "bv", 1.0, "vector", par * 2 + 1)
        if cfg.qk_prep is not None:
            p.add("gpsimd", lambda g, t=t: g.dma_start(
                out=rope_a[:, 0:2 * cfg.nf], in_=ropecs_d[t * 128:(t + 1) * 128, :]),
                writes=R("rope_a"), dma=True)
            if cfg.qk_prep == "l0ffn":
                emit_rms_gain(qtile, qn, H, 0, 0.125)
                emit_rms_gain(ktile, kn, KVH, H, 1.0)
                emit_rope(qtile, qn, H, "neox")
                emit_rope(ktile, kn, KVH, "neox")
            else:
                emit_rope(qtile, qn, H, "gptj")
                emit_rope(ktile, kn, H, "gptj")
        p.add("sync", lambda s, qtile=qtile, t=t: s.dma_start(
            out=qd2.rearrange("h t w -> t h w")[t * 128:(t + 1) * 128],
            in_=qtile[:].rearrange("p (h w) -> p h w", w=128)),
            reads=R(qn), writes=R(f"dram_q{t}"), dma=True)
        if KVH == H:
            p.add("sync", lambda s, ktile=ktile, t=t: s.dma_start(
                out=kd2.rearrange("h t w -> t h w")[t * 128:(t + 1) * 128],
                in_=ktile[:].rearrange("p (h w) -> p h w", w=128)),
                reads=R(kn), writes=R(f"dram_k{t}"), dma=True)
        else:
            for two in range(2):
                p.add("sync", lambda s, ktile=ktile, t=t, two=two: s.dma_start(
                    out=kd2.rearrange("h t (two w) -> t h two w", two=2)
                    [t * 128:(t + 1) * 128][:, :, two:two + 1, :],
                    in_=ktile[:].rearrange("p (h w) -> p h w", w=64).unsqueeze(2)),
                    reads=R(kn), writes=R(f"dram_k{t}"), dma=True)
        p.add("sync", lambda s, vtile=vtile, t=t: s.dma_start(
            out=vd.rearrange("h t w -> t h w")[t * 128:(t + 1) * 128],
            in_=vtile[:].rearrange("p (h w) -> p h w", w=HD)),
            reads=R(vn), writes=R(f"dram_v{t}"), dma=True)

    allq = tuple(f"dram_q{t}" for t in range(NT))
    allk = tuple(f"dram_k{t}" for t in range(NT))
    allv = tuple(f"dram_v{t}" for t in range(NT))

    p.add("vector", lambda v: v.memset(VB[:], 1.0), writes=R("VB"))

    prev_pair = prev_kvpair = prev_kvh = -1
    for h in range(H):
        pair = h // 2
        sub = (h % 2) * 64
        kvh = h * KVH // H
        if KVH == H:
            kvpair, kvsub = kvh // 2, (kvh % 2) * 64
        else:
            kvpair, kvsub = kvh, sub
        if pair != prev_pair:
            p.add("sync", lambda s, pair=pair: s.dma_start(
                out=QT[:], in_=qd2[pair], transpose=True),
                reads=allq, writes=R("QT"), dma=True)
            prev_pair = pair
        if kvpair != prev_kvpair:
            p.add("sync", lambda s, kvpair=kvpair: s.dma_start(
                out=KT[:], in_=kd2[kvpair], transpose=True),
                reads=allk, writes=R("KT"), dma=True)
            prev_kvpair = kvpair
        if kvh != prev_kvh:
            p.add("sync", lambda s, kvh=kvh: s.dma_start(
                out=VB[:].rearrange("p (c w) -> p c w", w=65)[:, :, 0:HD],
                in_=vd[kvh].rearrange("(c p) w -> p c w", p=128)),
                reads=allv, writes=R("VB"), dma=True)
            prev_kvh = kvh

        hp = h % 2
        NKGMAX = max(T_FULL // 512, 1)
        for qt in range(NT):
            nkg = (qt * 128 + 128 + 511) // 512
            for kg in range(nkg):
                n = min(512, (qt + 1) * 128 - kg * 512)
                bank = kg % 2
                p.add("tensor", lambda tt, sub=sub, qt=qt, kg=kg, n=n, bank=bank, kvsub=kvsub:
                      tt.matmul(ps[bank][:, 0:n],
                                QT[sub:sub + 64, qt * 128:(qt + 1) * 128],
                                KT[kvsub:kvsub + 64, kg * 512:kg * 512 + n],
                                start=True, stop=True),
                      reads=R("QT", "KT"), writes=R(f"ps{bank}"))
                p.add("vector", lambda v, kg=kg, n=n, bank=bank, hp=hp: v.reduce_max(
                    cmx[:, hp * NKGMAX + kg:hp * NKGMAX + kg + 1], ps[bank][:, 0:n], axis=AX.X),
                    reads=R(f"ps{bank}"), writes=R(f"cmx{hp}_{kg}"))
            p.add("vector", lambda v, qt=qt, nkg=nkg, hp=hp: v.reduce_max(
                rm[:, hp * NT + qt:hp * NT + qt + 1], cmx[:, hp * NKGMAX:hp * NKGMAX + nkg],
                axis=AX.X),
                reads=tuple(f"cmx{hp}_{kg}" for kg in range(nkg)), writes=R(f"rm{hp}"))
        for qt in range(NT):
            p.add("tensor", lambda tt, qt=qt, hp=hp: tt.matmul(
                ps[7][0:1, 0:128], rm[:, hp * NT + qt:hp * NT + qt + 1], ident[:],
                start=True, stop=True),
                reads=R(f"rm{hp}", "ident"), writes=R("ps7"))
            p.add("scalar", lambda sc, qt=qt: sc.activation(
                maxneg[0:1, qt * 128:(qt + 1) * 128], ps[7][0:1, 0:128],
                AF.Copy, bias=0.0, scale=-1.0),
                reads=R("ps7"), writes=R("maxneg"))

        if DEBUG and DEBUG_A and h == 0:
            p.add("sync", lambda s: s.dma_start(out=dbg_mx, in_=maxneg[:]),
                  reads=R("maxneg"), writes=R("dbg1"), dma=True)
            p.add("sync", lambda s: s.dma_start(out=dbg_rm, in_=rm[:]),
                  reads=R("rm"), writes=R("dbg2"), dma=True)

        for qg in range(T_FULL // 512):
            nkc = (qg + 1) * 4
            for kc in range(nkc):
                sbank = 4 + (kc % 3)
                pst = ps[sbank]
                ptb, ptn = ((PT_a, "PT_a"), (PT_b, "PT_b"), (PT_c, "PT_c"))[kc % 3]

                def mmB(tt, pst=pst, kc=kc, qg=qg, kvsub=kvsub, sub=sub):
                    tt.matmul(pst[:, 0:512],
                              KT[kvsub:kvsub + 64, kc * 128:(kc + 1) * 128],
                              QT[sub:sub + 64, qg * 512:(qg + 1) * 512],
                              start=True, stop=False)
                    return tt.matmul(pst[:, 0:512], onesb[0:1, 0:128],
                                     maxneg[0:1, qg * 512:(qg + 1) * 512],
                                     start=False, stop=True)
                p.add("tensor", mmB, reads=R("KT", "QT", "onesb", "maxneg"),
                      writes=R(f"ps{sbank}"))
                off = kc * 128 - qg * 512
                if off >= 0:
                    j = off // 128
                    p.add("vector", lambda v, pst=pst, j=j, sbank=sbank: v.tensor_tensor(
                        pst[:, 0:512], pst[:, 0:512], masks[:, j * 512:(j + 1) * 512], ALU.add),
                        reads=R(f"ps{sbank}", "masks"), writes=R(f"ps{sbank}"))
                p.add("scalar", lambda sc, pst=pst, ptb=ptb: sc.activation(
                    ptb[:], pst[:, 0:512], AF.Exp),
                    reads=R(f"ps{sbank}"), writes=R(ptn))
                p.add("tensor", lambda tt, kc=kc, ptb=ptb, nkc=nkc: tt.matmul(
                    ps[2][0:65, 0:512], VB[:, kc * 65:(kc + 1) * 65], ptb[:],
                    start=(kc == 0), stop=(kc == nkc - 1)),
                    reads=R("VB", ptn), writes=R("ps2"))
            p.add("vector", lambda v: v.reciprocal(rs_row[:], ps[2][64:65, 0:512]),
                  reads=R("ps2"), writes=R("rs_row"))
            p.add("tensor", lambda tt: tt.matmul(ps[3][0:64, 0:512], onesf[0:1, 0:64],
                                                 rs_row[:], start=True, stop=True),
                  reads=R("onesf", "rs_row"), writes=R("ps3"))
            p.add("scalar", lambda sc: sc.copy(otmp[:], ps[2][0:64, 0:512]),
                  reads=R("ps2"), writes=R("otmp"))
            p.add("vector", lambda v, pair=pair, sub=sub, qg=qg: v.tensor_tensor(
                big[sub:sub + 64, pair * T_FULL + qg * 512: pair * T_FULL + (qg + 1) * 512],
                otmp[:], ps[3][0:64, 0:512], ALU.mult),
                reads=R("otmp", "ps3"), writes=R("big"))

    if DEBUG and DEBUG_B:
        for c in range(8):
            p.add("sync", lambda s, c=c: s.dma_start(
                out=dbg_ot[c], in_=big[:, c * T_FULL:(c + 1) * T_FULL]),
                reads=R("big"), writes=R(f"dbgot{c}"), dma=True)

    p.add("gpsimd", lambda g: g.dma_start(
        out=wq_sb[:].rearrange("p (c o) -> p c o", o=D)[:, :, :],
        in_=wo_d.rearrange("(c p) o -> p c o", p=128)),
        writes=R("wq_sb"), dma=True)

    if cfg.post_ln:
        p.add("gpsimd", lambda g: g.dma_start(out=prow[0:1, 0:D], in_=pg_d),
              writes=R("prow"), dma=True)
        p.add("gpsimd", lambda g: g.dma_start(out=prow[0:1, D:2 * D], in_=pb_d),
              writes=R("prow"), dma=True)
        for c in range(4):
            p.add("tensor", lambda tt, c=c: tt.matmul(
                ps[7][:, 0:512], onesb[0:1, 0:128], prow[0:1, c * 512:(c + 1) * 512],
                start=True, stop=True), reads=R("onesb", "prow"), writes=R("ps7"))
            p.add("vector", lambda v, c=c: v.tensor_copy(
                gb_bc[:, c * 512:(c + 1) * 512], ps[7][:, 0:512]),
                reads=R("ps7"), writes=R("gb_bc"))

    for t in range(NT):
        par = t % 2
        xf, xfn = (xf_a, "xf_a") if par == 0 else (xf_b, "xf_b")
        yt, ytn = (yt_a, "yt_a") if par == 0 else (yt_b, "yt_b")
        sml, smln = (small, "small") if par == 0 else (small2, "small2")
        p.add("gpsimd", lambda g, xf=xf, t=t: g.dma_start(
            out=xf[:], in_=xin[t * 128:(t + 1) * 128, :]),
            writes=R(xfn), dma=True)
        for half in range(2):
            bank = 4 + half
            pst = ps[bank]
            has_bo = "bo" in bias_sb

            def mmY(tt, t=t, half=half, pst=pst, has_bo=has_bo):
                for c in range(8):
                    inst = tt.matmul(
                        pst[:, 0:512],
                        big[:, c * T_FULL + t * 128: c * T_FULL + (t + 1) * 128],
                        wq_sb[:, c * D + half * 512: c * D + half * 512 + 512],
                        start=(c == 0), stop=(c == 7 and not has_bo))
                return inst
            p.add("tensor", mmY, reads=R("big", "wq_sb"), writes=R(f"ps{bank}"))
            if has_bo:
                p.add("tensor", lambda tt, pst=pst, half=half: tt.matmul(
                    pst[:, 0:512], onesb[0:1, 0:128],
                    bias_sb["bo"][0:1, half * 512:half * 512 + 512],
                    start=False, stop=True), reads=R("onesb", "bo"), writes=R(f"ps{bank}"))
        if cfg.post_ln:
            for half in range(2):
                pst = ps[4 + half]
                p.add("vector", lambda v, yt=yt, half=half, pst=pst, sml=sml: v.tensor_scalar(
                    yt[:, half * 512:half * 512 + 512], pst[:, 0:512], 1.0, 0.0,
                    ALU.mult, ALU.add, accum_out=sml[:, half:half + 1]),
                    reads=R(f"ps{4 + half}"), writes=R(ytn, smln))
            emit_ln_tok(p, yt, ytn, sqd, sml, sml[:, 11:12], yt, ytn, smn=smln)
            p.add("vector", lambda v, yt=yt: v.tensor_tensor(
                yt[:], yt[:], gb_bc[:, 0:D], ALU.mult),
                reads=R(ytn, "gb_bc"), writes=R(ytn))
            p.add("vector", lambda v, yt=yt: v.tensor_tensor(
                yt[:], yt[:], gb_bc[:, D:2 * D], ALU.add),
                reads=R(ytn, "gb_bc"), writes=R(ytn))
            p.add("vector", lambda v, yt=yt, xf=xf, sml=sml: v.scalar_tensor_tensor(
                yt[:], yt[:], 1.0, xf[:], ALU.mult, ALU.add, accum_out=sml[:, 0:1]),
                reads=R(ytn, xfn), writes=R(ytn, smln))
            p.add("vector", lambda v, sml=sml: v.memset(sml[:, 1:2], 0.0), writes=R(smln))
        else:
            for half in range(2):
                pst = ps[4 + half]
                p.add("vector", lambda v, yt=yt, xf=xf, half=half, pst=pst, sml=sml:
                      v.scalar_tensor_tensor(
                          yt[:, half * 512:half * 512 + 512], pst[:, 0:512], 1.0,
                          xf[:, half * 512:half * 512 + 512],
                          ALU.mult, ALU.add, accum_out=sml[:, half:half + 1]),
                      reads=R(f"ps{4 + half}", xfn), writes=R(ytn, smln))
        if cfg.epilogue == "inner":
            p.add("sync", lambda s, yt=yt, t=t: s.dma_start(
                out=xmid_d[t * 128:(t + 1) * 128, :], in_=yt[:]),
                reads=R(ytn), writes=R(f"dram_out{t}"), dma=True)
            continue
        emit_ln_tok(p, yt, ytn, sqd, sml, sml[:, 11:12], yt, ytn, smn=smln)
        p.add("sync", lambda s, yt=yt, t=t: s.dma_start(
            out=xout[t * 128:(t + 1) * 128, :], in_=yt[:]),
            reads=R(ytn), writes=R(f"dram_out{t}"), dma=True)

    if standalone:
        p.emit(es)
        es.close()
        return nc
    p.fence()
    es.close()
    return None



def _rope_neox_tables():
    inv = 1.0 / 10000.0 ** (np.arange(0, HD, 2, dtype=np.float32) / HD)
    fr = np.outer(np.arange(T_FULL, dtype=np.float32), inv)
    return np.concatenate([np.cos(fr), np.sin(fr)], axis=1).astype(np.float32)


def _rope_gptj_tables():
    inv = 1.0 / 10000.0 ** (np.arange(0, 32, 2, dtype=np.float32) / 32.0)
    fr = np.outer(np.arange(T_FULL, dtype=np.float32), inv)
    return np.concatenate([np.cos(fr), np.sin(fr)], axis=1).astype(np.float32)


L3_ATTN_CFG = AttnCfg("L3_attn", has_bq=True, has_bkv=True, has_bo=True)


def prep_l3_attn(inp):
    io = dict(CONSTS)
    io["wq"] = np.ascontiguousarray(inp["L3_attn_wq"])
    io["wk"] = np.ascontiguousarray(inp["L3_attn_wk"])
    io["wv"] = np.ascontiguousarray(inp["L3_attn_wv"])
    io["wo"] = np.ascontiguousarray(inp["L3_attn_wo"])
    for bn, key in (("bq", "L3_attn_bq"), ("bk", "L3_attn_bk"),
                    ("bv", "L3_attn_bv"), ("bo", "L3_attn_bo")):
        io[bn] = np.ascontiguousarray(inp[key]).reshape(1, -1)
    return io


L2_ATTN_CFG = AttnCfg("L2_attn", has_bq=True, has_bkv=True, has_bo=True, post_ln=True)


def prep_l2_attn(inp):
    io = dict(CONSTS)
    io["wq"] = np.ascontiguousarray(inp["L2_attn_Wq"])
    io["wk"] = np.ascontiguousarray(inp["L2_attn_Wk"])
    io["wv"] = np.ascontiguousarray(inp["L2_attn_Wv"])
    io["wo"] = np.ascontiguousarray(inp["L2_attn_Wo"])
    for bn, key in (("bq", "L2_attn_bq"), ("bk", "L2_attn_bk"),
                    ("bv", "L2_attn_bv"), ("bo", "L2_attn_bo")):
        io[bn] = np.ascontiguousarray(inp[key]).reshape(1, -1)
    io["post_gamma"] = np.ascontiguousarray(inp["L2_attn_gamma"]).reshape(1, D)
    io["post_beta"] = np.ascontiguousarray(inp["L2_attn_beta"]).reshape(1, D)
    return io


L1_ATTN_CFG = AttnCfg("L1_attn", qk_prep="l1attn", nf=16)


def prep_l1_attn(inp):
    io = dict(CONSTS)
    wqkvT = np.ascontiguousarray(np.asarray(inp["L1_attn_w_qkv"]).T)
    wq = np.empty((D, D), np.float32)
    wv = np.empty((D, D), np.float32)
    wk = np.empty((D, D), np.float32)
    for h in range(16):
        mp, j = h // 4, h % 4
        wq[:, h * 64:(h + 1) * 64] = wqkvT[:, mp * 768 + j * 64: mp * 768 + (j + 1) * 64]
        wv[:, h * 64:(h + 1) * 64] = wqkvT[:, mp * 768 + 256 + j * 64: mp * 768 + 256 + (j + 1) * 64]
        wk[:, h * 64:(h + 1) * 64] = wqkvT[:, mp * 768 + 512 + j * 64: mp * 768 + 512 + (j + 1) * 64]
    io["wq"], io["wk"], io["wv"] = wq, wk, wv
    io["wo"] = np.ascontiguousarray(np.asarray(inp["L1_attn_w_out"]).T)
    io["rope_cs"] = _rope_gptj_tables()
    return io


L0_FFN_ATTN_CFG = AttnCfg("L0_ffn_attn", kv_heads=4, dkv=256, qk_prep="l0ffn",
                          pre_norm="rms", epilogue="inner", gains=True, nf=32,
                          prep_eps=1e-6)


def prep_l0_ffn_attn(inp):
    io = dict(CONSTS)
    io["wq"] = np.ascontiguousarray(np.asarray(inp["L0_ffn_wq"]).T)
    io["wk"] = np.ascontiguousarray(np.asarray(inp["L0_ffn_wk"]).T)
    io["wv"] = np.ascontiguousarray(np.asarray(inp["L0_ffn_wv"]).T)
    io["wo"] = np.ascontiguousarray(np.asarray(inp["L0_ffn_wo"]).T)
    io["q_gain"] = np.ascontiguousarray(inp["L0_ffn_q_gain"]).reshape(1, 16)
    io["k_gain"] = np.ascontiguousarray(inp["L0_ffn_k_gain"]).reshape(1, 4)
    io["rope_cs"] = _rope_neox_tables()
    return io


L1_FFN_ATTN_CFG = AttnCfg("L1_ffn_attn", has_bo=True, pre_norm="ln", epilogue="inner",
                          prep_eps=1e-5, fold_gb=True)


def prep_l1_ffn_attn(inp):
    io = dict(CONSTS)
    io["wq"] = np.ascontiguousarray(inp["L1_ffn_wq"])
    io["wk"] = np.ascontiguousarray(inp["L1_ffn_wk"])
    io["wv"] = np.ascontiguousarray(inp["L1_ffn_wv"])
    io["wo"] = np.ascontiguousarray(inp["L1_ffn_wo"])
    io["bo"] = np.ascontiguousarray(inp["L1_ffn_bo"]).reshape(1, D)
    io["ln_gam_col"] = np.ascontiguousarray(
        np.asarray(inp["L1_ffn_ln1_scale"]).reshape(8, 128).T.astype(np.float32))
    io["ln_beta_col"] = np.ascontiguousarray(
        np.asarray(inp["L1_ffn_ln1_shift"]).reshape(8, 128).T).astype(BF)
    return io


def prep_l0_attn(inp):
    io = dict(CONSTS)
    io["wq"] = np.ascontiguousarray(inp["L0_attn_Wq"])
    io["wk"] = np.ascontiguousarray(inp["L0_attn_Wk"])
    io["wv"] = np.ascontiguousarray(inp["L0_attn_Wv"])
    io["wo"] = np.ascontiguousarray(inp["L0_attn_Wo"])
    io["bo"] = np.ascontiguousarray(inp["L0_attn_bo"]).reshape(1, D)
    return io


L0_ATTN_CFG = AttnCfg("L0_attn", has_bo=True)



def _run8(nc, common, per_core=None, extra=None):
    in_maps = []
    for c in range(8):
        m = dict(common)
        if per_core is not None:
            m.update(per_core[c])
        in_maps.append(m)
    res = run_bass_kernel_spmd(nc, in_maps, core_ids=list(range(8)))
    return res.results


def kernel(**inputs):
    x = np.asarray(inputs["L0_attn_x"])
    shards = [np.ascontiguousarray(x[c]) for c in range(8)]

    nc = bass.Bass("TRN2", target_bir_lowering=False)
    es = contextlib.ExitStack()
    p = Prog(nc)
    s_in = nc.dram_tensor("stream_in", [T_FULL, D], F32, kind="ExternalInput").ap()
    stream = {0: s_in}
    for i in range(1, 7):
        stream[i] = nc.dram_tensor(f"s{i}", [T_FULL, D], F32, kind="Internal").ap()
    s7 = nc.dram_tensor("s7_out", [T_FULL, D], F32, kind="ExternalOutput").ap()
    m0 = nc.dram_tensor("m0", [T_FULL, D], F32, kind="Internal").ap()
    m1 = nc.dram_tensor("m1", [T_FULL, D], F32, kind="Internal").ap()
    selw = nc.dram_tensor("selw_out", [T_FULL, 32], F32, kind="ExternalOutput").ap()
    lpos = nc.dram_tensor("lpos_out", [T_FULL, 32], F32, kind="ExternalOutput").ap()
    cnt = nc.dram_tensor("cnt_out", [1, 32], F32, kind="ExternalOutput").ap()

    def FU(pfx, rd):
        return dict(nc=nc, p=p, pfx=pfx, rd=rd)

    build_attn_launch(L0_ATTN_CFG, FU("l0a_", {"xin": s_in, "xout": stream[1]}))
    build_attn_launch(L0_FFN_ATTN_CFG, FU("l0fa_", {"xin": stream[1], "xmid": m0}))
    build_l0ffn_bank_launch(FU("l0fb_", {"xin": stream[1], "xmid": m0, "xout": stream[2]}))
    build_attn_launch(L1_ATTN_CFG, FU("l1a_", {"xin": stream[2], "xout": stream[3]}))
    build_attn_launch(L1_FFN_ATTN_CFG, FU("l1fa_", {"xin": stream[3], "xmid": m1}))
    build_l1ffn_moe_launch(FU("l1fb_", {"xin": stream[3], "xmid": m1, "xout": stream[4]}))
    build_attn_launch(L2_ATTN_CFG, FU("l2a_", {"xin": stream[4], "xout": stream[5]}))
    build_l2ffn_launch(FU("l2f_", {"xin": stream[5], "xout": stream[6]}))
    build_attn_launch(L3_ATTN_CFG, FU("l3a_", {"xin": stream[6], "xout": s7}))
    build_l3ffn_a_launch(FU("l3fa_", {"xin": s7, "selw": selw, "lpos": lpos, "cnt": cnt}))
    p.emit(es)
    es.close()

    common = {}
    for pfx, prep in (("l0a_", prep_l0_attn(inputs)),
                      ("l0fa_", prep_l0_ffn_attn(inputs)),
                      ("l0fb_", prep_l0_ffn_bank(inputs)),
                      ("l1a_", prep_l1_attn(inputs)),
                      ("l1fa_", prep_l1_ffn_attn(inputs)),
                      ("l1fb_", prep_l1_ffn_moe(inputs)),
                      ("l2a_", prep_l2_attn(inputs)),
                      ("l2f_", prep_l2_ffn(inputs)),
                      ("l3a_", prep_l3_attn(inputs)),
                      ("l3fa_", prep_l3_ffn_a(inputs))):
        for k, v in prep.items():
            common[pfx + k] = v
    r1 = _run8(nc, common, [{"stream_in": shards[c]} for c in range(8)])

    cnts = np.stack([r1[c]["cnt_out"][0] for c in range(8)], axis=0)
    pcb = []
    for c in range(8):
        pm = np.zeros((8, 1), np.float32)
        pm[:c] = 1.0
        pcb.append({"xin": np.ascontiguousarray(r1[c]["s7_out"]),
                    "selw": np.ascontiguousarray(r1[c]["selw_out"]),
                    "lpos": np.ascontiguousarray(r1[c]["lpos_out"]),
                    "gathered": cnts, "premask": pm})
    r = _run8(build_l3ffn_b_launch(), prep_l3_ffn_b(inputs), pcb)
    return np.stack([r[c]["xout"] for c in range(8)], axis=0)



def _emit_mirror_nT(p, nc, es, xin, nbf, big, xa, xb, xf_a, xf_b, sqd, small,
                    pre_norm=None, prep_eps=1e-5, src_name="xin"):
    if pre_norm is not None:
        p.add("vector", lambda v: v.memset(small[:, 12:13], float(prep_eps)),
              writes=R("small"))
    for t in range(NT):
        par = t % 2
        buf, bname = (xa, "xa") if par == 0 else (xb, "xb")
        if pre_norm is None:
            p.add("gpsimd", lambda g, buf=buf, t=t: g.dma_start(
                out=buf[:], in_=xin[t * 128:(t + 1) * 128, :]),
                writes=R(bname), dma=True)
        else:
            xf, xfn = (xf_a, "xf_a") if par == 0 else (xf_b, "xf_b")
            p.add("gpsimd", lambda g, xf=xf, t=t: g.dma_start(
                out=xf[:], in_=xin[t * 128:(t + 1) * 128, :]),
                writes=R(xfn), dma=True)
            p.add("scalar", lambda sc, xf=xf: sc.activation(
                sqd[:], xf[:], AF.Square, accum_out=small[:, 4:5]),
                reads=R(xfn), writes=R("sqd", "small"))
            if pre_norm == "rms":
                p.add("vector", lambda v: v.tensor_scalar_mul(small[:, 5:6], small[:, 4:5], 1.0 / D),
                      reads=R("small"), writes=R("small"))
                p.add("scalar", lambda sc: sc.activation(small[:, 6:7], small[:, 5:6],
                                                         AF.Sqrt, bias=small[:, 12:13]),
                      reads=R("small"), writes=R("small"))
                p.add("vector", lambda v: v.reciprocal(small[:, 7:8], small[:, 6:7]),
                      reads=R("small"), writes=R("small"))
                p.add("scalar", lambda sc, xf=xf, buf=buf: sc.activation(
                    buf[:], xf[:], AF.Identity, bias=0.0, scale=small[:, 7:8]),
                    reads=R(xfn, "small"), writes=R(bname))
            else:
                p.add("scalar", lambda sc, xf=xf: sc.activation(
                    sqd[:], xf[:], AF.Identity, accum_out=small[:, 0:1]),
                    reads=R(xfn), writes=R("sqd", "small"))
                p.add("vector", lambda v: v.memset(small[:, 1:2], 0.0),
                      writes=R("small"))
                emit_ln_tok(p, xf, xfn, sqd, small, small[:, 12:13], xf, xfn)
                p.add("vector", lambda v, xf=xf, buf=buf: v.tensor_copy(buf[:], xf[:]),
                      reads=R(xfn), writes=R(bname))
        p.add("sync", lambda s, buf=buf, t=t: s.dma_start(
            out=nbf[t * 128:(t + 1) * 128, :], in_=buf[:]),
            reads=R(bname), writes=R(f"dram_nbf{t}"), dma=True)
    if big is None:
        return
    for c in range(8):
        p.add("sync", lambda s, c=c: s.dma_start(
            out=big[:, c * T_FULL:(c + 1) * T_FULL],
            in_=nbf[:, c * 128:(c + 1) * 128], transpose=True),
            reads=tuple(f"dram_nbf{t}" for t in range(NT)), writes=R("big"), dma=True)


def _emit_top2_gate(p, gsc, wgt, t, mode, rs_bc=None):
    G = "gsc"
    p.add("vector", lambda v: v.reduce_max(gsc[:, 33:34], gsc[:, 0:8], axis=AX.X),
          reads=R(G), writes=R(G))
    p.add("vector", lambda v: v.tensor_single_scalar(gsc[:, 8:16], gsc[:, 0:8],
                                                     gsc[:, 33:34], ALU.is_ge),
          reads=R(G), writes=R(G))
    p.add("vector", lambda v: v.scalar_tensor_tensor(gsc[:, 16:24], gsc[:, 8:16], -1e30,
                                                     gsc[:, 0:8], ALU.mult, ALU.add),
          reads=R(G), writes=R(G))
    p.add("vector", lambda v: v.reduce_max(gsc[:, 34:35], gsc[:, 16:24], axis=AX.X),
          reads=R(G), writes=R(G))
    p.add("vector", lambda v: v.tensor_single_scalar(gsc[:, 24:32], gsc[:, 16:24],
                                                     gsc[:, 34:35], ALU.is_ge),
          reads=R(G), writes=R(G))
    if mode == "softmax":
        p.add("vector", lambda v: v.tensor_sub(gsc[:, 35:36], gsc[:, 33:34], gsc[:, 34:35]),
              reads=R(G), writes=R(G))
        p.add("scalar", lambda sc: sc.activation(gsc[:, 36:37], gsc[:, 35:36], AF.Sigmoid),
              reads=R(G), writes=R(G))
        p.add("vector", lambda v: v.tensor_scalar(gsc[:, 37:38], gsc[:, 36:37], -1.0, 1.0,
                                                  ALU.mult, ALU.add),
              reads=R(G), writes=R(G))
        p.add("vector", lambda v, t=t: v.tensor_scalar(wgt[:, t * 8:(t + 1) * 8],
                                                       gsc[:, 8:16], gsc[:, 36:37], None,
                                                       ALU.mult),
              reads=R(G), writes=R("wgt"))
        p.add("vector", lambda v, t=t: v.scalar_tensor_tensor(
            wgt[:, t * 8:(t + 1) * 8], gsc[:, 24:32], gsc[:, 37:38],
            wgt[:, t * 8:(t + 1) * 8], ALU.mult, ALU.add),
            reads=R(G, "wgt"), writes=R("wgt"))
    else:
        p.add("vector", lambda v: v.tensor_add(gsc[:, 35:36], gsc[:, 33:34], gsc[:, 34:35]),
              reads=R(G), writes=R(G))
        p.add("vector", lambda v: v.reciprocal(gsc[:, 36:37], gsc[:, 35:36]),
              reads=R(G), writes=R(G))
        p.add("vector", lambda v: v.tensor_add(gsc[:, 24:32], gsc[:, 8:16], gsc[:, 24:32]),
              reads=R(G), writes=R(G))
        p.add("vector", lambda v: v.tensor_mul(gsc[:, 24:32], gsc[:, 24:32], gsc[:, 0:8]),
              reads=R(G), writes=R(G))
        p.add("vector", lambda v, t=t: v.tensor_scalar(wgt[:, t * 8:(t + 1) * 8],
                                                       gsc[:, 24:32], gsc[:, 36:37], None,
                                                       ALU.mult),
              reads=R(G), writes=R("wgt"))


def _emit_final_epilogue(p, nc, t, acc_d, xin, xmid, xf, xfn, x1f, x1fn, yt, ytn,
                         sqd, small, xout, acc_reads=()):
    p.add("gpsimd", lambda g, yt=yt, t=t: g.dma_start(
        out=yt[:], in_=acc_d[t * 128:(t + 1) * 128, :]),
        reads=tuple(acc_reads), writes=R(ytn), dma=True)
    p.add("gpsimd", lambda g, xf=xf, t=t: g.dma_start(
        out=xf[:], in_=xin[t * 128:(t + 1) * 128, :]),
        writes=R(xfn), dma=True)
    if xmid is not None:
        p.add("gpsimd", lambda g, x1f=x1f, t=t: g.dma_start(
            out=x1f[:], in_=xmid[t * 128:(t + 1) * 128, :]),
            writes=R(x1fn), dma=True)
        p.add("vector", lambda v, yt=yt, x1f=x1f: v.tensor_add(yt[:], yt[:], x1f[:]),
              reads=R(ytn, x1fn), writes=R(ytn))
    p.add("vector", lambda v, yt=yt, xf=xf: v.scalar_tensor_tensor(
        yt[:], yt[:], 1.0, xf[:], ALU.mult, ALU.add, accum_out=small[:, 0:1]),
        reads=R(ytn, xfn), writes=R(ytn, "small"))
    p.add("vector", lambda v: v.memset(small[:, 1:2], 0.0), writes=R("small"))
    emit_ln_tok(p, yt, ytn, sqd, small, small[:, 11:12], yt, ytn)
    p.add("sync", lambda s, yt=yt, t=t: s.dma_start(
        out=xout[t * 128:(t + 1) * 128, :], in_=yt[:]),
        reads=R(ytn), writes=R(f"dram_out{t}"), dma=True)


def _load_nTg(p, nbf, tg, buf, bufname):
    for c in range(8):
        p.add("sync", lambda s, tg=tg, c=c, buf=buf: s.dma_start(
            out=buf[:, c * 512:(c + 1) * 512],
            in_=nbf[tg * 512:(tg + 1) * 512, c * 128:(c + 1) * 128], transpose=True),
            reads=tuple(f"dram_nbf{t}" for t in range(tg * 4, tg * 4 + 4)),
            writes=R(bufname), dma=True)


def build_l2ffn_launch(FUSE=None):
    standalone = FUSE is None
    nc = (bass.Bass("TRN2", target_bir_lowering=False) if standalone
          else NCProxy(FUSE["nc"], FUSE["pfx"], FUSE["rd"]))
    es = contextlib.ExitStack()
    E, HH = 8, 512

    xin = nc.dram_tensor("xin", [T_FULL, D], F32, kind="ExternalInput").ap()
    xout = nc.dram_tensor("xout", [T_FULL, D], F32, kind="ExternalOutput").ap()
    onesb_d = nc.dram_tensor("c_onesb", [1, 512], BF16, kind="ExternalInput").ap()
    onesf_d = nc.dram_tensor("c_onesf", [1, 128], F32, kind="ExternalInput").ap()
    gw_d = nc.dram_tensor("gate_W", [D, E], F32, kind="ExternalInput").ap()
    w1_d = nc.dram_tensor("W1", [E, D, HH], F32, kind="ExternalInput").ap()
    b1_d = nc.dram_tensor("b1", [1, E * HH], F32, kind="ExternalInput").ap()
    w2_d = nc.dram_tensor("W2", [E, HH, D], F32, kind="ExternalInput").ap()
    b2_d = nc.dram_tensor("b2", [1, E * D], F32, kind="ExternalInput").ap()
    ws1_d = nc.dram_tensor("Ws1", [D, HH], F32, kind="ExternalInput").ap()
    bs1_d = nc.dram_tensor("bs1", [1, HH], F32, kind="ExternalInput").ap()
    ws2_d = nc.dram_tensor("Ws2", [HH, D], F32, kind="ExternalInput").ap()
    bs2_d = nc.dram_tensor("bs2", [1, D], F32, kind="ExternalInput").ap()
    rsc_d = nc.dram_tensor("route_scale", [1, 1], F32, kind="ExternalInput").ap()

    nbf = nc.dram_tensor("nbf", [T_FULL, D], BF16, kind="Internal").ap()
    acc_d = nc.dram_tensor("acc", [T_FULL, D], F32, kind="Internal").ap()

    big = es.enter_context(nc.sbuf_tensor("big", [128, 8 * T_FULL], BF16))
    hsb = es.enter_context(nc.sbuf_tensor("hsb", [128, 4 * T_FULL], BF16))
    w1a = es.enter_context(nc.sbuf_tensor("w1a", [128, 8 * HH], BF16))
    w1b = es.enter_context(nc.sbuf_tensor("w1b", [128, 8 * HH], BF16))
    w2a = es.enter_context(nc.sbuf_tensor("w2a", [128, 4 * D], BF16))
    w2b = es.enter_context(nc.sbuf_tensor("w2b", [128, 4 * D], BF16))
    gw_sb = es.enter_context(nc.sbuf_tensor("gw_sb", [128, 8 * E], BF16))
    b1row = es.enter_context(nc.sbuf_tensor("b1row", [1, E * HH], BF16))
    b2row = es.enter_context(nc.sbuf_tensor("b2row", [1, E * D], BF16))
    bs1row = es.enter_context(nc.sbuf_tensor("bs1row", [1, HH], BF16))
    bs2row = es.enter_context(nc.sbuf_tensor("bs2row", [1, D], BF16))
    onesb = es.enter_context(nc.sbuf_tensor("onesb", [1, 512], BF16))
    onesf = es.enter_context(nc.sbuf_tensor("onesf", [1, 128], F32))
    rsc_sb = es.enter_context(nc.sbuf_tensor("rsc_sb", [1, 1], F32))
    rs_bc = es.enter_context(nc.sbuf_tensor("rs_bc", [128, 1], F32))
    gsc = es.enter_context(nc.sbuf_tensor("gsc", [128, 40], F32))
    wgt = es.enter_context(nc.sbuf_tensor("wgt", [128, NT * 8], F32))
    xa = es.enter_context(nc.sbuf_tensor("xa", [128, D], BF16))
    xb = es.enter_context(nc.sbuf_tensor("xb", [128, D], BF16))
    xf_a = es.enter_context(nc.sbuf_tensor("xf_a", [128, D], F32))
    xf_b = es.enter_context(nc.sbuf_tensor("xf_b", [128, D], F32))
    yt_a = es.enter_context(nc.sbuf_tensor("yt_a", [128, D], F32))
    yt_b = es.enter_context(nc.sbuf_tensor("yt_b", [128, D], F32))
    sqd = es.enter_context(nc.sbuf_tensor("sqd", [128, D], F32))
    small = es.enter_context(nc.sbuf_tensor("small", [128, 16], F32))
    stmp = es.enter_context(nc.sbuf_tensor("stmp", [128, 4 * 512], F32))

    ps = [es.enter_context(nc.psum_tensor(f"ps{i}", [128, 512], F32)) for i in range(8)]
    p = Prog(nc) if standalone else FUSE["p"]

    p.add("vector", lambda v: v.memset(small[:, 11:12], EPS_TOP), writes=R("small"))
    p.add("gpsimd", lambda g: g.dma_start(out=onesb[:], in_=onesb_d), writes=R("onesb"), dma=True)
    p.add("gpsimd", lambda g: g.dma_start(out=onesf[:], in_=onesf_d), writes=R("onesf"), dma=True)
    p.add("gpsimd", lambda g: g.dma_start(
        out=gw_sb[:].rearrange("p (c e) -> p c e", e=E),
        in_=gw_d.rearrange("(c p) e -> p c e", p=128)), writes=R("gw"), dma=True)
    for nm, sb, dd in (("b1row", b1row, b1_d), ("b2row", b2row, b2_d),
                       ("bs1row", bs1row, bs1_d), ("bs2row", bs2row, bs2_d),
                       ("rsc", rsc_sb, rsc_d)):
        p.add("gpsimd", lambda g, sb=sb, dd=dd: g.dma_start(out=sb[:], in_=dd),
              writes=R(nm), dma=True)

    _emit_mirror_nT(p, nc, es, xin, nbf, big, xa, xb, xf_a, xf_b, sqd, small)

    p.add("tensor", lambda t: t.matmul(ps[6][0:128, 0:1], onesf[0:1, :], rsc_sb[:],
                                       start=True, stop=True),
          reads=R("onesf", "rsc"), writes=R("ps6"))
    p.add("vector", lambda v: v.tensor_copy(rs_bc[:], ps[6][0:128, 0:1]),
          reads=R("ps6"), writes=R("rs_bc"))

    for t in range(NT):
        bank = 6 + (t % 2)
        pst = ps[bank]

        def mm_gate(tt, t=t, pst=pst):
            for c in range(8):
                inst = tt.matmul(pst[:, 0:E],
                                 big[:, c * T_FULL + t * 128: c * T_FULL + (t + 1) * 128],
                                 gw_sb[:, c * E:(c + 1) * E],
                                 start=(c == 0), stop=(c == 7))
            return inst
        p.add("tensor", mm_gate, reads=R("big", "gw"), writes=R(f"ps{bank}"))
        p.add("scalar", lambda sc, pst=pst: sc.activation(gsc[:, 0:8], pst[:, 0:E], AF.Sigmoid),
              reads=R(f"ps{bank}"), writes=R("gsc"))
        p.add("vector", lambda v: v.tensor_scalar(gsc[:, 0:8], gsc[:, 0:8], rs_bc[:, 0:1],
                                                  None, ALU.mult),
              reads=R("gsc", "rs_bc"), writes=R("gsc"))
        _emit_top2_gate(p, gsc, wgt, t, "norm")

    def expert_pass(tag, w1sb, w1n, w2sb, w2n, brow1, brow1_off, brow2, brow2_off,
                    act, wcol, first):
        for tg in range(T_FULL // 512):
            for mo in range(4):
                bank = tg % 2
                pst = ps[bank]

                def mm_h(tt, tg=tg, mo=mo, pst=pst, w1sb=w1sb):
                    for c in range(8):
                        tt.matmul(pst[:, 0:512],
                                  w1sb[:, c * HH + mo * 128: c * HH + mo * 128 + 128],
                                  big[:, c * T_FULL + tg * 512: c * T_FULL + (tg + 1) * 512],
                                  start=(c == 0), stop=False)
                    return tt.matmul(pst[:, 0:512],
                                     brow1[0:1, brow1_off + mo * 128: brow1_off + mo * 128 + 128],
                                     onesb[0:1, 0:512], start=False, stop=True)
                p.add("tensor", mm_h, reads=R("big", w1n, "b1row", "bs1row", "onesb"),
                      writes=R(f"ps{bank}"))
                p.add("scalar", lambda sc, tg=tg, mo=mo, pst=pst: sc.activation(
                    hsb[:, mo * T_FULL + tg * 512: mo * T_FULL + (tg + 1) * 512],
                    pst[:, 0:512], act),
                    reads=R(f"ps{bank}"), writes=R("hsb"))
        for t in range(NT):
            for half in range(2):
                bank = 2 + (t % 2) * 2 + half
                pst = ps[bank]

                def mm_y(tt, t=t, half=half, pst=pst, w2sb=w2sb):
                    for mo in range(4):
                        tt.matmul(pst[:, 0:512],
                                  hsb[:, mo * T_FULL + t * 128: mo * T_FULL + (t + 1) * 128],
                                  w2sb[:, mo * D + half * 512: mo * D + half * 512 + 512],
                                  start=(mo == 0), stop=False)
                    return tt.matmul(pst[:, 0:512], onesb[0:1, 0:128],
                                     brow2[0:1, brow2_off + half * 512: brow2_off + half * 512 + 512],
                                     start=False, stop=True)
                p.add("tensor", mm_y, reads=R("hsb", w2n, "b2row", "bs2row", "onesb"),
                      writes=R(f"ps{bank}"))
                sl = (t * 2 + half) % 4
                if wcol is None:
                    p.add("vector", lambda v, pst=pst, sl=sl: v.tensor_copy(
                        stmp[:, sl * 512:(sl + 1) * 512], pst[:, 0:512]),
                        reads=R(f"ps{bank}"), writes=R(f"stmp{sl}"))
                else:
                    p.add("vector", lambda v, pst=pst, t=t, wcol=wcol, sl=sl: v.tensor_scalar(
                        stmp[:, sl * 512:(sl + 1) * 512], pst[:, 0:512],
                        wgt[:, t * 8 + wcol: t * 8 + wcol + 1], None, ALU.mult),
                        reads=R(f"ps{bank}", "wgt"), writes=R(f"stmp{sl}"))
                p.add("gpsimd", lambda g, t=t, half=half, first=first, sl=sl: g.dma_start(
                    out=acc_d[t * 128:(t + 1) * 128, half * 512:half * 512 + 512],
                    in_=stmp[:, sl * 512:(sl + 1) * 512],
                    accum_op=(ALU.bypass if first else ALU.add)),
                    reads=R(f"stmp{sl}"), writes=R(f"acc{t}_{half}"), dma=True)

    p.add("gpsimd", lambda g: g.dma_start(
        out=w1a[:].rearrange("p (c o) -> p c o", o=HH),
        in_=ws1_d.rearrange("(c p) o -> p c o", p=128)), writes=R("w1a"), dma=True)
    p.add("gpsimd", lambda g: g.dma_start(
        out=w2a[:].rearrange("p (c o) -> p c o", o=D),
        in_=ws2_d.rearrange("(c p) o -> p c o", p=128)), writes=R("w2a"), dma=True)
    expert_pass("sh", w1a, "w1a", w2a, "w2a", bs1row, 0, bs2row, 0, AF.Gelu, None, True)
    for e in range(E):
        w1sb, w1n = (w1a, "w1a") if e % 2 == 0 else (w1b, "w1b")
        w2sb, w2n = (w2a, "w2a") if e % 2 == 0 else (w2b, "w2b")
        p.add("gpsimd", lambda g, w1sb=w1sb, e=e: g.dma_start(
            out=w1sb[:].rearrange("p (c o) -> p c o", o=HH),
            in_=w1_d[e].rearrange("(c p) o -> p c o", p=128)), writes=R(w1n), dma=True)
        p.add("gpsimd", lambda g, w2sb=w2sb, e=e: g.dma_start(
            out=w2sb[:].rearrange("p (c o) -> p c o", o=D),
            in_=w2_d[e].rearrange("(c p) o -> p c o", p=128)), writes=R(w2n), dma=True)
        expert_pass(f"e{e}", w1sb, w1n, w2sb, w2n, b1row, e * HH, b2row, e * D,
                    AF.Gelu, e, False)

    for t in range(NT):
        par = t % 2
        xf, xfn = (xf_a, "xf_a") if par == 0 else (xf_b, "xf_b")
        yt, ytn = (yt_a, "yt_a") if par == 0 else (yt_b, "yt_b")
        _emit_final_epilogue(p, nc, t, acc_d, xin, None, xf, xfn, None, None,
                             yt, ytn, sqd, small, xout,
                             acc_reads=[f"acc{t}_{h}" for h in range(2)])

    if standalone:
        p.emit(es)
        es.close()
        return nc
    p.fence()
    es.close()
    return None


def prep_l2_ffn(inp):
    io = {}
    io["c_onesb"] = CONSTS["c_onesb"]
    io["c_onesf"] = CONSTS["c_onesf"]
    io["gate_W"] = np.ascontiguousarray(inp["L2_ffn_gate_W"])
    io["W1"] = np.ascontiguousarray(inp["L2_ffn_W1"])
    io["b1"] = np.ascontiguousarray(inp["L2_ffn_b1"]).reshape(1, -1)
    io["W2"] = np.ascontiguousarray(inp["L2_ffn_W2"])
    io["b2"] = np.ascontiguousarray(inp["L2_ffn_b2"]).reshape(1, -1)
    io["Ws1"] = np.ascontiguousarray(inp["L2_ffn_Ws1"])
    io["bs1"] = np.ascontiguousarray(inp["L2_ffn_bs1"]).reshape(1, -1)
    io["Ws2"] = np.ascontiguousarray(inp["L2_ffn_Ws2"])
    io["bs2"] = np.ascontiguousarray(inp["L2_ffn_bs2"]).reshape(1, -1)
    io["route_scale"] = np.asarray(inp["L2_ffn_route_scale"], np.float32).reshape(1, 1)
    return io


def build_l1ffn_moe_launch(FUSE=None):
    standalone = FUSE is None
    nc = (bass.Bass("TRN2", target_bir_lowering=False) if standalone
          else NCProxy(FUSE["nc"], FUSE["pfx"], FUSE["rd"]))
    es = contextlib.ExitStack()
    E, HH = 8, 2048

    xin = nc.dram_tensor("xin", [T_FULL, D], F32, kind="ExternalInput").ap()
    xmid = nc.dram_tensor("xmid", [T_FULL, D], F32, kind="ExternalInput").ap()
    xout = nc.dram_tensor("xout", [T_FULL, D], F32, kind="ExternalOutput").ap()
    onesb_d = nc.dram_tensor("c_onesb", [1, 512], BF16, kind="ExternalInput").ap()
    gw_d = nc.dram_tensor("gate_w", [D, E], F32, kind="ExternalInput").ap()
    w1_d = nc.dram_tensor("w1", [E, D, HH], F32, kind="ExternalInput").ap()
    w2_d = nc.dram_tensor("w2", [E, D, HH], F32, kind="ExternalInput").ap()
    w3_d = nc.dram_tensor("w3", [E, HH, D], F32, kind="ExternalInput").ap()
    gam_d = nc.dram_tensor("ln_gam_col", [128, 8], F32, kind="ExternalInput").ap()
    beta_d = nc.dram_tensor("ln_beta_col", [128, 8], BF16, kind="ExternalInput").ap()

    nbf = nc.dram_tensor("nbf", [T_FULL, D], BF16, kind="Internal").ap()
    acc_d = nc.dram_tensor("acc", [T_FULL, D], F32, kind="Internal").ap()

    nTg_a = es.enter_context(nc.sbuf_tensor("nTg_a", [128, 8 * 512], BF16))
    nTg_b = es.enter_context(nc.sbuf_tensor("nTg_b", [128, 8 * 512], BF16))
    hsb = es.enter_context(nc.sbuf_tensor("hsb", [128, 16 * 512], BF16))
    w1sb = es.enter_context(nc.sbuf_tensor("w1sb", [128, 8 * HH], BF16))
    w2sb = es.enter_context(nc.sbuf_tensor("w2sb", [128, 8 * HH], BF16))
    w3sb = es.enter_context(nc.sbuf_tensor("w3sb", [128, 16 * D], BF16))
    gw_sb = es.enter_context(nc.sbuf_tensor("gw_sb", [128, 8 * E], BF16))
    gam_sb = es.enter_context(nc.sbuf_tensor("gam_sb", [128, 8], F32))
    beta_sb = es.enter_context(nc.sbuf_tensor("beta_sb", [128, 8], BF16))
    brow1 = es.enter_context(nc.sbuf_tensor("brow1", [1, HH], BF16))
    brow2 = es.enter_context(nc.sbuf_tensor("brow2", [1, HH], BF16))
    browg = es.enter_context(nc.sbuf_tensor("browg", [1, E], BF16))
    onesb = es.enter_context(nc.sbuf_tensor("onesb", [1, 512], BF16))
    gsc = es.enter_context(nc.sbuf_tensor("gsc", [128, 40], F32))
    wgt = es.enter_context(nc.sbuf_tensor("wgt", [128, NT * 8], F32))
    xa = es.enter_context(nc.sbuf_tensor("xa", [128, D], BF16))
    xb = es.enter_context(nc.sbuf_tensor("xb", [128, D], BF16))
    xf_a = es.enter_context(nc.sbuf_tensor("xf_a", [128, D], F32))
    xf_b = es.enter_context(nc.sbuf_tensor("xf_b", [128, D], F32))
    x1f_a = es.enter_context(nc.sbuf_tensor("x1f_a", [128, D], F32))
    x1f_b = es.enter_context(nc.sbuf_tensor("x1f_b", [128, D], F32))
    yt_a = es.enter_context(nc.sbuf_tensor("yt_a", [128, D], F32))
    yt_b = es.enter_context(nc.sbuf_tensor("yt_b", [128, D], F32))
    sqd = es.enter_context(nc.sbuf_tensor("sqd", [128, D], F32))
    small = es.enter_context(nc.sbuf_tensor("small", [128, 16], F32))
    stmp = es.enter_context(nc.sbuf_tensor("stmp", [128, 4 * 512], F32))
    sil = es.enter_context(nc.sbuf_tensor("sil", [128, 512], F32))

    ps = [es.enter_context(nc.psum_tensor(f"ps{i}", [128, 512], F32)) for i in range(8)]
    p = Prog(nc) if standalone else FUSE["p"]

    p.add("vector", lambda v: v.memset(small[:, 11:12], EPS_TOP), writes=R("small"))
    p.add("gpsimd", lambda g: g.dma_start(out=onesb[:], in_=onesb_d), writes=R("onesb"), dma=True)
    p.add("gpsimd", lambda g: g.dma_start(out=gam_sb[:], in_=gam_d), writes=R("gam"), dma=True)
    p.add("gpsimd", lambda g: g.dma_start(out=beta_sb[:], in_=beta_d), writes=R("beta"), dma=True)
    p.add("gpsimd", lambda g: g.dma_start(
        out=gw_sb[:].rearrange("p (c e) -> p c e", e=E),
        in_=gw_d.rearrange("(c p) e -> p c e", p=128)), writes=R("gw"), dma=True)

    _emit_mirror_nT(p, nc, es, xmid, nbf, None, xa, xb, xf_a, xf_b, sqd, small,
                    pre_norm="ln", prep_eps=1e-5)

    def load_nTg(tg, buf, bufname):
        _load_nTg(p, nbf, tg, buf, bufname)

    def fold_weight(wsb, wname, dout, brow_sb, brow_name):
        nch = (dout + 511) // 512
        for half in range(nch):
            n = min(512, dout - half * 512)

            def mm_brow(tt, wsb=wsb, dout=dout, half=half, n=n):
                for c in range(8):
                    inst = tt.matmul(ps[6][0:1, 0:n], beta_sb[:, c:c + 1],
                                     wsb[:, c * dout + half * 512: c * dout + half * 512 + n],
                                     start=(c == 0), stop=(c == 7))
                return inst
            p.add("tensor", mm_brow, reads=R("beta", wname), writes=R("ps6"))
            p.add("scalar", lambda sc, half=half, n=n, brow_sb=brow_sb: sc.activation(
                brow_sb[0:1, half * 512:half * 512 + n], ps[6][0:1, 0:n], AF.Copy,
                bias=0.0, scale=1.0),
                reads=R("ps6"), writes=R(brow_name))
        for c in range(8):
            p.add("vector", lambda v, wsb=wsb, dout=dout, c=c: v.tensor_scalar(
                wsb[:, c * dout:(c + 1) * dout], wsb[:, c * dout:(c + 1) * dout],
                gam_sb[:, c:c + 1], None, ALU.mult),
                reads=R(wname, "gam"), writes=R(wname))

    fold_weight(gw_sb, "gw", E, browg, "browg")

    for tg in range(T_FULL // 512):
        buf, bufname = (nTg_a, "nTg_a") if tg % 2 == 0 else (nTg_b, "nTg_b")
        load_nTg(tg, buf, bufname)
        for ti in range(4):
            t = tg * 4 + ti
            bank = 6 + (t % 2)
            pst = ps[bank]

            def mm_gate(tt, ti=ti, pst=pst, buf=buf):
                for c in range(8):
                    tt.matmul(pst[:, 0:E],
                              buf[:, c * 512 + ti * 128: c * 512 + ti * 128 + 128],
                              gw_sb[:, c * E:(c + 1) * E],
                              start=(c == 0), stop=False)
                return tt.matmul(pst[:, 0:E], onesb[0:1, 0:128], browg[0:1, 0:E],
                                 start=False, stop=True)
            p.add("tensor", mm_gate, reads=R(bufname, "gw", "browg", "onesb"),
                  writes=R(f"ps{bank}"))
            p.add("vector", lambda v, pst=pst: v.tensor_copy(gsc[:, 0:8], pst[:, 0:E]),
                  reads=R(f"ps{bank}"), writes=R("gsc"))
            _emit_top2_gate(p, gsc, wgt, t, "softmax")

    for e in range(E):
        p.add("gpsimd", lambda g, e=e: g.dma_start(
            out=w1sb[:].rearrange("p (c o) -> p c o", o=HH),
            in_=w1_d[e].rearrange("(c p) o -> p c o", p=128)), writes=R("w1"), dma=True)
        p.add("gpsimd", lambda g, e=e: g.dma_start(
            out=w2sb[:].rearrange("p (c o) -> p c o", o=HH),
            in_=w2_d[e].rearrange("(c p) o -> p c o", p=128)), writes=R("w2"), dma=True)
        p.add("gpsimd", lambda g, e=e: g.dma_start(
            out=w3sb[:].rearrange("p (c o) -> p c o", o=D),
            in_=w3_d[e].rearrange("(c p) o -> p c o", p=128)), writes=R("w3"), dma=True)
        fold_weight(w1sb, "w1", HH, brow1, "brow1")
        fold_weight(w2sb, "w2", HH, brow2, "brow2")
        for tg in range(T_FULL // 512):
            buf, bufname = (nTg_a, "nTg_a") if tg % 2 == 0 else (nTg_b, "nTg_b")
            load_nTg(tg, buf, bufname)
            for hc in range(16):
                pa = ps[tg % 2]
                pb = ps[2 + tg % 2]

                def mm_h(tt, hc=hc, pa=pa, pb=pb, buf=buf):
                    for c in range(8):
                        tt.matmul(pa[:, 0:512],
                                  w1sb[:, c * HH + hc * 128: c * HH + hc * 128 + 128],
                                  buf[:, c * 512:(c + 1) * 512],
                                  start=(c == 0), stop=False)
                    tt.matmul(pa[:, 0:512],
                              brow1[0:1, hc * 128:hc * 128 + 128],
                              onesb[0:1, 0:512], start=False, stop=True)
                    for c in range(8):
                        tt.matmul(pb[:, 0:512],
                                  w2sb[:, c * HH + hc * 128: c * HH + hc * 128 + 128],
                                  buf[:, c * 512:(c + 1) * 512],
                                  start=(c == 0), stop=False)
                    return tt.matmul(pb[:, 0:512],
                                     brow2[0:1, hc * 128:hc * 128 + 128],
                                     onesb[0:1, 0:512], start=False, stop=True)
                p.add("tensor", mm_h, reads=R(bufname, "w1", "w2", "brow1", "brow2", "onesb"),
                      writes=R(f"ps{tg % 2}", f"ps{2 + tg % 2}"))
                p.add("scalar", lambda sc, pa=pa: sc.activation(sil[:, 0:512], pa[:, 0:512],
                                                                AF.Silu),
                      reads=R(f"ps{tg % 2}"), writes=R("sil"))
                p.add("vector", lambda v, pb=pb, hc=hc: v.tensor_tensor(
                    hsb[:, hc * 512:(hc + 1) * 512], sil[:, 0:512], pb[:, 0:512], ALU.mult),
                    reads=R("sil", f"ps{2 + tg % 2}"), writes=R("hsb"))
            for ti in range(4):
                t = tg * 4 + ti
                for half in range(2):
                    bank = 4 + (ti % 2) * 2 + half
                    pst = ps[bank]

                    def mm_y(tt, t=t, ti=ti, half=half, pst=pst):
                        for hc in range(16):
                            inst = tt.matmul(
                                pst[:, 0:512],
                                hsb[:, hc * 512 + ti * 128: hc * 512 + ti * 128 + 128],
                                w3sb[:, hc * D + half * 512: hc * D + half * 512 + 512],
                                start=(hc == 0), stop=(hc == 15))
                        return inst
                    p.add("tensor", mm_y, reads=R("hsb", "w3"), writes=R(f"ps{bank}"))
                    sl = (t * 2 + half) % 4
                    p.add("vector", lambda v, pst=pst, t=t, e=e, sl=sl: v.tensor_scalar(
                        stmp[:, sl * 512:(sl + 1) * 512], pst[:, 0:512],
                        wgt[:, t * 8 + e: t * 8 + e + 1], None, ALU.mult),
                        reads=R(f"ps{bank}", "wgt"), writes=R(f"stmp{sl}"))
                    p.add("gpsimd", lambda g, t=t, half=half, e=e, sl=sl: g.dma_start(
                        out=acc_d[t * 128:(t + 1) * 128, half * 512:half * 512 + 512],
                        in_=stmp[:, sl * 512:(sl + 1) * 512],
                        accum_op=(ALU.bypass if e == 0 else ALU.add)),
                        reads=R(f"stmp{sl}"), writes=R(f"acc{t}_{half}"), dma=True)

    for t in range(NT):
        par = t % 2
        xf, xfn = (xf_a, "xf_a") if par == 0 else (xf_b, "xf_b")
        x1f, x1fn = (x1f_a, "x1f_a") if par == 0 else (x1f_b, "x1f_b")
        yt, ytn = (yt_a, "yt_a") if par == 0 else (yt_b, "yt_b")
        _emit_final_epilogue(p, nc, t, acc_d, xin, xmid, xf, xfn, x1f, x1fn,
                             yt, ytn, sqd, small, xout,
                             acc_reads=[f"acc{t}_{h}" for h in range(2)])

    if standalone:
        p.emit(es)
        es.close()
        return nc
    p.fence()
    es.close()
    return None


def prep_l1_ffn_moe(inp):
    io = {}
    io["c_onesb"] = CONSTS["c_onesb"]
    io["gate_w"] = np.ascontiguousarray(inp["L1_ffn_gate_w"])
    io["w1"] = np.ascontiguousarray(inp["L1_ffn_w1"])
    io["w2"] = np.ascontiguousarray(inp["L1_ffn_w2"])
    io["w3"] = np.ascontiguousarray(inp["L1_ffn_w3"])
    io["ln_gam_col"] = np.ascontiguousarray(
        np.asarray(inp["L1_ffn_ln2_scale"]).reshape(8, 128).T.astype(np.float32))
    io["ln_beta_col"] = np.ascontiguousarray(
        np.asarray(inp["L1_ffn_ln2_shift"]).reshape(8, 128).T).astype(BF)
    return io


def build_l3ffn_a_launch(FUSE=None):
    standalone = FUSE is None
    nc = (bass.Bass("TRN2", target_bir_lowering=False) if standalone
          else NCProxy(FUSE["nc"], FUSE["pfx"], FUSE["rd"]))
    es = contextlib.ExitStack()
    EE = 32

    xin = nc.dram_tensor("xin", [T_FULL, D], F32, kind="ExternalInput").ap()
    onesb_d = nc.dram_tensor("c_onesb", [1, 512], BF16, kind="ExternalInput").ap()
    onesf_d = nc.dram_tensor("c_onesf", [1, 128], F32, kind="ExternalInput").ap()
    sut_d = nc.dram_tensor("c_strictut", [128, 128], BF16, kind="ExternalInput").ap()
    ones128_d = nc.dram_tensor("c_ones128", [128, 128], BF16, kind="ExternalInput").ap()
    rwT_d = nc.dram_tensor("rwT", [D, EE], F32, kind="ExternalInput").ap()
    ebias_d = nc.dram_tensor("ebias", [1, EE], F32, kind="ExternalInput").ap()

    selw_d = nc.dram_tensor("selw", [T_FULL, EE], F32, kind="ExternalOutput").ap()
    lpos_d = nc.dram_tensor("lpos", [T_FULL, EE], F32, kind="ExternalOutput").ap()
    cnt_d = nc.dram_tensor("cnt", [1, EE], F32, kind="ExternalOutput").ap()

    nbf = nc.dram_tensor("nbf", [T_FULL, D], BF16, kind="Internal").ap()

    big = es.enter_context(nc.sbuf_tensor("big", [128, 8 * T_FULL], BF16))
    rw_sb = es.enter_context(nc.sbuf_tensor("rw_sb", [128, 8 * EE], BF16))
    onesb = es.enter_context(nc.sbuf_tensor("onesb", [1, 512], BF16))
    onesf = es.enter_context(nc.sbuf_tensor("onesf", [1, 128], F32))
    sut = es.enter_context(nc.sbuf_tensor("sut", [128, 128], BF16))
    ones128 = es.enter_context(nc.sbuf_tensor("ones128", [128, 128], BF16))
    ebias_sb = es.enter_context(nc.sbuf_tensor("ebias_sb", [1, EE], F32))
    eb_bc = es.enter_context(nc.sbuf_tensor("eb_bc", [128, EE], F32))
    gsc = es.enter_context(nc.sbuf_tensor("gsc", [128, 5 * EE + 8], F32))
    selm = es.enter_context(nc.sbuf_tensor("selm", [128, NT * EE], BF16))
    sw_t = es.enter_context(nc.sbuf_tensor("sw_t", [128, 2 * EE], F32))
    lp_t = es.enter_context(nc.sbuf_tensor("lp_t", [128, 2 * EE], F32))
    cnt_sb = es.enter_context(nc.sbuf_tensor("cnt_sb", [1, EE], F32))
    xa = es.enter_context(nc.sbuf_tensor("xa", [128, D], BF16))
    xb = es.enter_context(nc.sbuf_tensor("xb", [128, D], BF16))
    xf_a = es.enter_context(nc.sbuf_tensor("xf_a", [128, D], F32))
    xf_b = es.enter_context(nc.sbuf_tensor("xf_b", [128, D], F32))
    sqd = es.enter_context(nc.sbuf_tensor("sqd", [128, D], F32))
    small = es.enter_context(nc.sbuf_tensor("small", [128, 16], F32))

    ps = [es.enter_context(nc.psum_tensor(f"ps{i}", [128, 512], F32)) for i in range(8)]
    p = Prog(nc) if standalone else FUSE["p"]

    p.add("gpsimd", lambda g: g.dma_start(out=onesb[:], in_=onesb_d), writes=R("onesb"), dma=True)
    p.add("gpsimd", lambda g: g.dma_start(out=onesf[:], in_=onesf_d), writes=R("onesf"), dma=True)
    p.add("gpsimd", lambda g: g.dma_start(out=sut[:], in_=sut_d), writes=R("sut"), dma=True)
    p.add("gpsimd", lambda g: g.dma_start(out=ones128[:], in_=ones128_d), writes=R("o128"), dma=True)
    p.add("gpsimd", lambda g: g.dma_start(out=ebias_sb[:], in_=ebias_d), writes=R("ebias"), dma=True)
    p.add("gpsimd", lambda g: g.dma_start(
        out=rw_sb[:].rearrange("p (c e) -> p c e", e=EE),
        in_=rwT_d.rearrange("(c p) e -> p c e", p=128)), writes=R("rw"), dma=True)

    _emit_mirror_nT(p, nc, es, xin, nbf, big, xa, xb, xf_a, xf_b, sqd, small)

    p.add("tensor", lambda t: t.matmul(ps[6][0:128, 0:EE], onesf[0:1, :], ebias_sb[:],
                                        start=True, stop=True),
          reads=R("onesf", "ebias"), writes=R("ps6"))
    p.add("vector", lambda v: v.tensor_copy(eb_bc[:], ps[6][0:128, 0:EE]),
          reads=R("ps6"), writes=R("eb_bc"))

    for t in range(NT):
        bank = 6 + (t % 2)
        pst = ps[bank]

        def mm_r(tt, t=t, pst=pst):
            for c in range(8):
                inst = tt.matmul(pst[:, 0:EE],
                                 big[:, c * T_FULL + t * 128: c * T_FULL + (t + 1) * 128],
                                 rw_sb[:, c * EE:(c + 1) * EE],
                                 start=(c == 0), stop=(c == 7))
            return inst
        p.add("tensor", mm_r, reads=R("big", "rw"), writes=R(f"ps{bank}"))
        p.add("scalar", lambda sc, pst=pst: sc.activation(gsc[:, 0:EE], pst[:, 0:EE], AF.Sigmoid),
              reads=R(f"ps{bank}"), writes=R("gsc"))
        p.add("vector", lambda v: v.tensor_add(gsc[:, EE:2 * EE], gsc[:, 0:EE], eb_bc[:]),
              reads=R("gsc", "eb_bc"), writes=R("gsc"))
        p.add("vector", lambda v: v.memset(gsc[:, 2 * EE:3 * EE], 0.0),
              writes=R("gsc"))
        for k in range(8):
            p.add("vector", lambda v: v.reduce_max(gsc[:, 5 * EE:5 * EE + 1],
                                                   gsc[:, EE:2 * EE], axis=AX.X),
                  reads=R("gsc"), writes=R("gsc"))
            p.add("vector", lambda v: v.tensor_single_scalar(
                gsc[:, 3 * EE:4 * EE], gsc[:, EE:2 * EE], gsc[:, 5 * EE:5 * EE + 1], ALU.is_ge),
                reads=R("gsc"), writes=R("gsc"))
            p.add("vector", lambda v: v.tensor_add(gsc[:, 2 * EE:3 * EE],
                                                   gsc[:, 2 * EE:3 * EE], gsc[:, 3 * EE:4 * EE]),
                  reads=R("gsc"), writes=R("gsc"))
            p.add("vector", lambda v: v.scalar_tensor_tensor(
                gsc[:, EE:2 * EE], gsc[:, 3 * EE:4 * EE], -1e30,
                gsc[:, EE:2 * EE], ALU.mult, ALU.add),
                reads=R("gsc"), writes=R("gsc"))
        p.add("vector", lambda v, t=t: v.tensor_copy(selm[:, t * EE:(t + 1) * EE],
                                                     gsc[:, 2 * EE:3 * EE]),
              reads=R("gsc"), writes=R(f"selm{t}"))
        p.add("vector", lambda v: v.tensor_mul(gsc[:, 4 * EE:5 * EE], gsc[:, 0:EE],
                                               gsc[:, 2 * EE:3 * EE]),
              reads=R("gsc"), writes=R("gsc"))
        p.add("vector", lambda v: v.reduce_sum(gsc[:, 5 * EE + 1:5 * EE + 2],
                                               gsc[:, 4 * EE:5 * EE], axis=AX.X),
              reads=R("gsc"), writes=R("gsc"))
        p.add("vector", lambda v: v.tensor_scalar_add(gsc[:, 5 * EE + 1:5 * EE + 2],
                                                      gsc[:, 5 * EE + 1:5 * EE + 2], 1e-20),
              reads=R("gsc"), writes=R("gsc"))
        p.add("vector", lambda v: v.reciprocal(gsc[:, 5 * EE + 2:5 * EE + 3],
                                               gsc[:, 5 * EE + 1:5 * EE + 2]),
              reads=R("gsc"), writes=R("gsc"))
        p.add("vector", lambda v: v.tensor_scalar_mul(gsc[:, 5 * EE + 2:5 * EE + 3],
                                                      gsc[:, 5 * EE + 2:5 * EE + 3], 2.5),
              reads=R("gsc"), writes=R("gsc"))
        sl = t % 2
        p.add("vector", lambda v, sl=sl: v.tensor_scalar(
            sw_t[:, sl * EE:(sl + 1) * EE], gsc[:, 4 * EE:5 * EE],
            gsc[:, 5 * EE + 2:5 * EE + 3], None, ALU.mult),
            reads=R("gsc"), writes=R(f"swt{sl}"))
        p.add("sync", lambda s, t=t, sl=sl: s.dma_start(
            out=selw_d[t * 128:(t + 1) * 128, :], in_=sw_t[:, sl * EE:(sl + 1) * EE]),
            reads=R(f"swt{sl}"), writes=R(f"dram_sw{t}"), dma=True)

    for t in range(NT):
        bank = 4 + (t % 2)
        pst = ps[bank]

        def mm_lp(tt, t=t, pst=pst):
            if t == 0:
                return tt.matmul(pst[:, 0:EE], sut[:], selm[:, 0:EE], start=True, stop=True)
            for i in range(t):
                tt.matmul(pst[:, 0:EE], ones128[:], selm[:, i * EE:(i + 1) * EE],
                          start=(i == 0), stop=False)
            return tt.matmul(pst[:, 0:EE], sut[:], selm[:, t * EE:(t + 1) * EE],
                             start=False, stop=True)
        p.add("tensor", mm_lp, reads=tuple(f"selm{i}" for i in range(t + 1)) + R("sut", "o128"),
              writes=R(f"ps{bank}"))
        sl = t % 2
        p.add("vector", lambda v, pst=pst, sl=sl: v.tensor_copy(
            lp_t[:, sl * EE:(sl + 1) * EE], pst[:, 0:EE]),
            reads=R(f"ps{bank}"), writes=R(f"lpt{sl}"))
        p.add("sync", lambda s, t=t, sl=sl: s.dma_start(
            out=lpos_d[t * 128:(t + 1) * 128, :], in_=lp_t[:, sl * EE:(sl + 1) * EE]),
            reads=R(f"lpt{sl}"), writes=R(f"dram_lp{t}"), dma=True)

    def mm_cnt(tt):
        for t in range(NT):
            inst = tt.matmul(ps[3][0:1, 0:EE], onesb[0:1, 0:128].bitcast(BF16),
                             selm[:, t * EE:(t + 1) * EE], start=(t == 0), stop=(t == NT - 1)) \
                if False else tt.matmul(ps[3][0:1, 0:EE], ones128[:, 0:1],
                                        selm[:, t * EE:(t + 1) * EE],
                                        start=(t == 0), stop=(t == NT - 1))
        return inst
    p.add("tensor", mm_cnt, reads=tuple(f"selm{i}" for i in range(NT)) + R("o128"),
          writes=R("ps3"))
    p.add("vector", lambda v: v.tensor_copy(cnt_sb[:], ps[3][0:1, 0:EE]),
          reads=R("ps3"), writes=R("cnt"))
    p.add("sync", lambda s: s.dma_start(out=cnt_d, in_=cnt_sb[:]),
          reads=R("cnt"), writes=R("dram_cnt"), dma=True)

    if standalone:
        p.emit(es)
        es.close()
        return nc
    p.fence()
    es.close()
    return None


def build_l3ffn_b_launch(FUSE=None):
    standalone = FUSE is None
    nc = (bass.Bass("TRN2", target_bir_lowering=False) if standalone
          else NCProxy(FUSE["nc"], FUSE["pfx"], FUSE["rd"]))
    es = contextlib.ExitStack()
    EE, II, SI = 32, 512, 2048
    NTG = T_FULL // 512

    xin = nc.dram_tensor("xin", [T_FULL, D], F32, kind="ExternalInput").ap()
    onesb_d = nc.dram_tensor("c_onesb", [1, 512], BF16, kind="ExternalInput").ap()
    onesf_d = nc.dram_tensor("c_onesf", [1, 128], F32, kind="ExternalInput").ap()
    selw_d = nc.dram_tensor("selw", [T_FULL, EE], F32, kind="ExternalInput").ap()
    lpos_d = nc.dram_tensor("lpos", [T_FULL, EE], F32, kind="ExternalInput").ap()
    gath_d = nc.dram_tensor("gathered", [8, EE], F32, kind="ExternalInput").ap()
    pmask_d = nc.dram_tensor("premask", [8, 1], F32, kind="ExternalInput").ap()
    gup_d = nc.dram_tensor("gup", [EE, D, 2 * II], F32, kind="ExternalInput").ap()
    down_d = nc.dram_tensor("down", [EE, II, D], F32, kind="ExternalInput").ap()
    sg_d = nc.dram_tensor("sgT", [D, SI], F32, kind="ExternalInput").ap()
    su_d = nc.dram_tensor("suT", [D, SI], F32, kind="ExternalInput").ap()
    sd_d = nc.dram_tensor("sdT", [SI, D], F32, kind="ExternalInput").ap()
    xout = nc.dram_tensor("xout", [T_FULL, D], F32, kind="ExternalOutput").ap()

    nbf = nc.dram_tensor("nbf", [T_FULL, D], BF16, kind="Internal").ap()
    acc_d = nc.dram_tensor("acc", [T_FULL, D], F32, kind="Internal").ap()

    nTg_a = es.enter_context(nc.sbuf_tensor("nTg_a", [128, 8 * 512], BF16))
    nTg_b = es.enter_context(nc.sbuf_tensor("nTg_b", [128, 8 * 512], BF16))
    hsb = es.enter_context(nc.sbuf_tensor("hsb", [128, 16 * 512], BF16))
    pool = es.enter_context(nc.sbuf_tensor("pool", [128, 49152], BF16))
    onesb = es.enter_context(nc.sbuf_tensor("onesb", [1, 512], BF16))
    onesf = es.enter_context(nc.sbuf_tensor("onesf", [1, 128], F32))
    gath_sb = es.enter_context(nc.sbuf_tensor("gath_sb", [8, EE], F32))
    pmask_sb = es.enter_context(nc.sbuf_tensor("pmask_sb", [8, 1], F32))
    offs_bc = es.enter_context(nc.sbuf_tensor("offs_bc", [128, EE], F32))
    wgt = es.enter_context(nc.sbuf_tensor("wgt", [128, NT * EE], F32))
    wtmp = es.enter_context(nc.sbuf_tensor("wtmp", [128, 2 * EE], F32))
    xa = es.enter_context(nc.sbuf_tensor("xa", [128, D], BF16))
    xb = es.enter_context(nc.sbuf_tensor("xb", [128, D], BF16))
    xf_a = es.enter_context(nc.sbuf_tensor("xf_a", [128, D], F32))
    xf_b = es.enter_context(nc.sbuf_tensor("xf_b", [128, D], F32))
    yt_a = es.enter_context(nc.sbuf_tensor("yt_a", [128, D], F32))
    yt_b = es.enter_context(nc.sbuf_tensor("yt_b", [128, D], F32))
    sqd = es.enter_context(nc.sbuf_tensor("sqd", [128, D], F32))
    small = es.enter_context(nc.sbuf_tensor("small", [128, 16], F32))
    stmp = es.enter_context(nc.sbuf_tensor("stmp", [128, 4 * 512], F32))
    sil = es.enter_context(nc.sbuf_tensor("sil", [128, 512], F32))
    utmp = es.enter_context(nc.sbuf_tensor("utmp", [128, 512], F32))
    gtmp = es.enter_context(nc.sbuf_tensor("gtmp", [128, 512], F32))

    ps = [es.enter_context(nc.psum_tensor(f"ps{i}", [128, 512], F32)) for i in range(8)]
    p = Prog(nc) if standalone else FUSE["p"]

    p.add("vector", lambda v: v.memset(small[:, 11:12], EPS_TOP), writes=R("small"))
    p.add("gpsimd", lambda g: g.dma_start(out=onesb[:], in_=onesb_d), writes=R("onesb"), dma=True)
    p.add("gpsimd", lambda g: g.dma_start(out=onesf[:], in_=onesf_d), writes=R("onesf"), dma=True)
    p.add("gpsimd", lambda g: g.dma_start(out=gath_sb[:], in_=gath_d), writes=R("gath"), dma=True)
    p.add("gpsimd", lambda g: g.dma_start(out=pmask_sb[:], in_=pmask_d), writes=R("pmask"), dma=True)

    _emit_mirror_nT(p, nc, es, xin, nbf, None, xa, xb, xf_a, xf_b, sqd, small)

    p.add("tensor", lambda t: t.matmul(ps[6][0:1, 0:EE], pmask_sb[:], gath_sb[:],
                                       start=True, stop=True),
          reads=R("pmask", "gath"), writes=R("ps6"))
    p.add("scalar", lambda sc: sc.copy(wtmp[0:1, 0:EE], ps[6][0:1, 0:EE]),
          reads=R("ps6"), writes=R("wtmp0"))
    p.add("tensor", lambda t: t.matmul(ps[6][0:128, 0:EE], onesf[0:1, :],
                                       wtmp[0:1, 0:EE], start=True, stop=True),
          reads=R("onesf", "wtmp0"), writes=R("ps6"))
    p.add("vector", lambda v: v.tensor_copy(offs_bc[:], ps[6][0:128, 0:EE]),
          reads=R("ps6"), writes=R("offs"))

    for t in range(NT):
        sl = t % 2
        p.add("gpsimd", lambda g, t=t, sl=sl: g.dma_start(
            out=wtmp[:, sl * EE:(sl + 1) * EE], in_=lpos_d[t * 128:(t + 1) * 128, :]),
            writes=R(f"wtmp{sl}"), dma=True)
        p.add("vector", lambda v, sl=sl: v.tensor_add(
            wtmp[:, sl * EE:(sl + 1) * EE], wtmp[:, sl * EE:(sl + 1) * EE], offs_bc[:]),
            reads=R(f"wtmp{sl}", "offs"), writes=R(f"wtmp{sl}"))
        p.add("vector", lambda v, sl=sl: v.tensor_single_scalar(
            wtmp[:, sl * EE:(sl + 1) * EE], wtmp[:, sl * EE:(sl + 1) * EE], 2048.0, ALU.is_lt),
            reads=R(f"wtmp{sl}"), writes=R(f"wtmp{sl}"))
        p.add("gpsimd", lambda g, t=t: g.dma_start(
            out=wgt[:, t * EE:(t + 1) * EE], in_=selw_d[t * 128:(t + 1) * 128, :]),
            writes=R("wgt"), dma=True)
        p.add("vector", lambda v, t=t, sl=sl: v.tensor_mul(
            wgt[:, t * EE:(t + 1) * EE], wgt[:, t * EE:(t + 1) * EE],
            wtmp[:, sl * EE:(sl + 1) * EE]),
            reads=R("wgt", f"wtmp{sl}"), writes=R("wgt"))

    p.add("gpsimd", lambda g: g.dma_start(
        out=pool[:, 0:16384].rearrange("p (c o) -> p c o", o=SI),
        in_=sg_d.rearrange("(c p) o -> p c o", p=128)), writes=R("pool"), dma=True)
    p.add("gpsimd", lambda g: g.dma_start(
        out=pool[:, 16384:32768].rearrange("p (c o) -> p c o", o=SI),
        in_=su_d.rearrange("(c p) o -> p c o", p=128)), writes=R("pool"), dma=True)
    p.add("gpsimd", lambda g: g.dma_start(
        out=pool[:, 32768:49152].rearrange("p (c o) -> p c o", o=D),
        in_=sd_d.rearrange("(c p) o -> p c o", p=128)), writes=R("pool"), dma=True)

    for tg in range(NTG):
        buf, bufname = (nTg_a, "nTg_a") if tg % 2 == 0 else (nTg_b, "nTg_b")
        _load_nTg(p, nbf, tg, buf, bufname)
        for hc in range(16):
            pa = ps[tg % 2]
            pb = ps[2 + tg % 2]

            def mm_gu(tt, hc=hc, pa=pa, pb=pb, buf=buf):
                for c in range(8):
                    tt.matmul(pa[:, 0:512],
                              pool[:, c * SI + hc * 128: c * SI + hc * 128 + 128],
                              buf[:, c * 512:(c + 1) * 512],
                              start=(c == 0), stop=(c == 7))
                for c in range(8):
                    inst = tt.matmul(pb[:, 0:512],
                                     pool[:, 16384 + c * SI + hc * 128: 16384 + c * SI + hc * 128 + 128],
                                     buf[:, c * 512:(c + 1) * 512],
                                     start=(c == 0), stop=(c == 7))
                return inst
            p.add("tensor", mm_gu, reads=R(bufname, "pool"),
                  writes=R(f"ps{tg % 2}", f"ps{2 + tg % 2}"))
            p.add("scalar", lambda sc, pa=pa: sc.activation(sil[:, 0:512], pa[:, 0:512], AF.Silu),
                  reads=R(f"ps{tg % 2}"), writes=R("sil"))
            p.add("vector", lambda v, pb=pb, hc=hc: v.tensor_tensor(
                hsb[:, hc * 512:(hc + 1) * 512], sil[:, 0:512], pb[:, 0:512], ALU.mult),
                reads=R("sil", f"ps{2 + tg % 2}"), writes=R("hsb"))
        for ti in range(4):
            t = tg * 4 + ti
            for half in range(2):
                bank = 4 + (ti % 2) * 2 + half
                pst = ps[bank]

                def mm_ys(tt, ti=ti, half=half, pst=pst):
                    for hc in range(16):
                        inst = tt.matmul(
                            pst[:, 0:512],
                            hsb[:, hc * 512 + ti * 128: hc * 512 + ti * 128 + 128],
                            pool[:, 32768 + hc * D + half * 512: 32768 + hc * D + half * 512 + 512],
                            start=(hc == 0), stop=(hc == 15))
                    return inst
                p.add("tensor", mm_ys, reads=R("hsb", "pool"), writes=R(f"ps{bank}"))
                sl = (t * 2 + half) % 4
                p.add("vector", lambda v, pst=pst, sl=sl: v.tensor_copy(
                    stmp[:, sl * 512:(sl + 1) * 512], pst[:, 0:512]),
                    reads=R(f"ps{bank}"), writes=R(f"stmp{sl}"))
                p.add("gpsimd", lambda g, t=t, half=half, sl=sl: g.dma_start(
                    out=acc_d[t * 128:(t + 1) * 128, half * 512:half * 512 + 512],
                    in_=stmp[:, sl * 512:(sl + 1) * 512]),
                    reads=R(f"stmp{sl}"), writes=R(f"acc{t}_{half}"), dma=True)

    for e in range(EE):
        p.add("gpsimd", lambda g, e=e: g.dma_start(
            out=pool[:, 0:8192].rearrange("p (c o) -> p c o", o=1024),
            in_=gup_d[e].rearrange("(c p) o -> p c o", p=128)), writes=R("pool"), dma=True)
        p.add("gpsimd", lambda g, e=e: g.dma_start(
            out=pool[:, 8192:12288].rearrange("p (c o) -> p c o", o=1024),
            in_=down_d[e].rearrange("(c p) o -> p c o", p=128)), writes=R("pool"), dma=True)
        for tg in range(NTG):
            buf, bufname = (nTg_a, "nTg_a") if tg % 2 == 0 else (nTg_b, "nTg_b")
            _load_nTg(p, nbf, tg, buf, bufname)
            for j in range(4):
                pa = ps[tg % 2]
                pb = ps[2 + tg % 2]

                def mm_gu_e(tt, j=j, pa=pa, pb=pb, buf=buf):
                    for c in range(8):
                        tt.matmul(pa[:, 0:512],
                                  pool[:, c * 1024 + j * 128: c * 1024 + j * 128 + 128],
                                  buf[:, c * 512:(c + 1) * 512],
                                  start=(c == 0), stop=(c == 7))
                    for c in range(8):
                        inst = tt.matmul(pb[:, 0:512],
                                         pool[:, c * 1024 + 512 + j * 128: c * 1024 + 512 + j * 128 + 128],
                                         buf[:, c * 512:(c + 1) * 512],
                                         start=(c == 0), stop=(c == 7))
                    return inst
                p.add("tensor", mm_gu_e, reads=R(bufname, "pool"),
                      writes=R(f"ps{tg % 2}", f"ps{2 + tg % 2}"))
                p.add("vector", lambda v, pa=pa: v.tensor_scalar_min(gtmp[:, 0:512],
                                                                     pa[:, 0:512], 7.0),
                      reads=R(f"ps{tg % 2}"), writes=R("gtmp"))
                p.add("scalar", lambda sc: sc.activation(sil[:, 0:512], gtmp[:, 0:512], AF.Silu),
                      reads=R("gtmp"), writes=R("sil"))
                p.add("vector", lambda v, pb=pb: v.tensor_scalar_min(utmp[:, 0:512],
                                                                     pb[:, 0:512], 7.0),
                      reads=R(f"ps{2 + tg % 2}"), writes=R("utmp"))
                p.add("vector", lambda v: v.tensor_scalar_max(utmp[:, 0:512],
                                                              utmp[:, 0:512], -7.0),
                      reads=R("utmp"), writes=R("utmp"))
                p.add("vector", lambda v, j=j: v.tensor_tensor(
                    hsb[:, j * 512:(j + 1) * 512], sil[:, 0:512], utmp[:, 0:512], ALU.mult),
                    reads=R("sil", "utmp"), writes=R("hsb"))
            for ti in range(4):
                t = tg * 4 + ti
                for half in range(2):
                    bank = 4 + (ti % 2) * 2 + half
                    pst = ps[bank]

                    def mm_ye(tt, ti=ti, half=half, pst=pst):
                        for j in range(4):
                            inst = tt.matmul(
                                pst[:, 0:512],
                                hsb[:, j * 512 + ti * 128: j * 512 + ti * 128 + 128],
                                pool[:, 8192 + j * D + half * 512: 8192 + j * D + half * 512 + 512],
                                start=(j == 0), stop=(j == 3))
                        return inst
                    p.add("tensor", mm_ye, reads=R("hsb", "pool"), writes=R(f"ps{bank}"))
                    sl = (t * 2 + half) % 4
                    p.add("vector", lambda v, pst=pst, t=t, e=e, sl=sl: v.tensor_scalar(
                        stmp[:, sl * 512:(sl + 1) * 512], pst[:, 0:512],
                        wgt[:, t * EE + e: t * EE + e + 1], None, ALU.mult),
                        reads=R(f"ps{bank}", "wgt"), writes=R(f"stmp{sl}"))
                    p.add("gpsimd", lambda g, t=t, half=half, sl=sl: g.dma_start(
                        out=acc_d[t * 128:(t + 1) * 128, half * 512:half * 512 + 512],
                        in_=stmp[:, sl * 512:(sl + 1) * 512], accum_op=ALU.add),
                        reads=R(f"stmp{sl}"), writes=R(f"acc{t}_{half}"), dma=True)

    for t in range(NT):
        par = t % 2
        xf, xfn = (xf_a, "xf_a") if par == 0 else (xf_b, "xf_b")
        yt, ytn = (yt_a, "yt_a") if par == 0 else (yt_b, "yt_b")
        _emit_final_epilogue(p, nc, t, acc_d, xin, None, xf, xfn, None, None,
                             yt, ytn, sqd, small, xout,
                             acc_reads=[f"acc{t}_{h}" for h in range(2)])

    if standalone:
        p.emit(es)
        es.close()
        return nc
    p.fence()
    es.close()
    return None


def _strictut():
    m = np.triu(np.ones((128, 128), np.float32), 1)
    return m.astype(BF)


def prep_l3_ffn_a(inp):
    io = {}
    io["c_onesb"] = CONSTS["c_onesb"]
    io["c_onesf"] = CONSTS["c_onesf"]
    io["c_strictut"] = _strictut()
    io["c_ones128"] = np.ones((128, 128), BF)
    io["rwT"] = np.ascontiguousarray(np.asarray(inp["L3_ffn_router_weight"]).T)
    io["ebias"] = np.ascontiguousarray(inp["L3_ffn_e_score_correction_bias"]).reshape(1, -1)
    return io


def prep_l3_ffn_b(inp):
    io = {}
    io["c_onesb"] = CONSTS["c_onesb"]
    io["c_onesf"] = CONSTS["c_onesf"]
    gup = np.asarray(inp["L3_ffn_gate_up_proj"])
    io["gup"] = np.ascontiguousarray(np.transpose(gup, (0, 2, 1)))
    dn = np.asarray(inp["L3_ffn_down_proj"])
    io["down"] = np.ascontiguousarray(np.transpose(dn, (0, 2, 1)))
    io["sgT"] = np.ascontiguousarray(np.asarray(inp["L3_ffn_shared_gate"]).T)
    io["suT"] = np.ascontiguousarray(np.asarray(inp["L3_ffn_shared_up"]).T)
    io["sdT"] = np.ascontiguousarray(np.asarray(inp["L3_ffn_shared_down"]).T)
    return io


def build_l0ffn_bank_launch(FUSE=None):
    standalone = FUSE is None
    nc = (bass.Bass("TRN2", target_bir_lowering=False) if standalone
          else NCProxy(FUSE["nc"], FUSE["pfx"], FUSE["rd"]))
    es = contextlib.ExitStack()

    xin = nc.dram_tensor("xin", [T_FULL, D], F32, kind="ExternalInput").ap()
    xmid = nc.dram_tensor("xmid", [T_FULL, D], F32, kind="ExternalInput").ap()
    xout = nc.dram_tensor("xout", [T_FULL, D], F32, kind="ExternalOutput").ap()
    onesb_d = nc.dram_tensor("c_onesb", [1, 512], BF16, kind="ExternalInput").ap()
    onesf_d = nc.dram_tensor("c_onesf", [1, 128], F32, kind="ExternalInput").ap()
    iota_d = nc.dram_tensor("c_iota8", [1, 8], F32, kind="ExternalInput").ap()
    fclog_d = nc.dram_tensor("fc_logits", [1, 8], F32, kind="ExternalInput").ap()
    pjlog_d = nc.dram_tensor("pj_logits", [1, 8], F32, kind="ExternalInput").ap()
    fcbd1_d = nc.dram_tensor("fc_bd1", [8, 8, 128, 128], F32, kind="ExternalInput").ap()
    fcbd2_d = nc.dram_tensor("fc_bd2", [8, 32, 128, 128], F32, kind="ExternalInput").ap()
    fcmix_d = nc.dram_tensor("fc_mixp", [8, D, 4096], F32, kind="ExternalInput").ap()
    pjbd1_d = nc.dram_tensor("pj_bd1", [8, 32, 128, 128], F32, kind="ExternalInput").ap()
    pjbd2_d = nc.dram_tensor("pj_bd2", [8, 8, 128, 128], F32, kind="ExternalInput").ap()
    pjmix_d = nc.dram_tensor("pj_mixp", [8, 4096, D], F32, kind="ExternalInput").ap()

    nbf = nc.dram_tensor("nbf", [T_FULL, D], BF16, kind="Internal").ap()
    hacc = nc.dram_tensor("hacc", [4096, T_FULL], F32, kind="Internal").ap()
    hact = nc.dram_tensor("hact", [4096, T_FULL], BF16, kind="Internal").ap()
    acc_d = nc.dram_tensor("acc", [T_FULL, D], F32, kind="Internal").ap()

    mixA = es.enter_context(nc.sbuf_tensor("mixA", [128, 32768], BF16))
    bd1sb = es.enter_context(nc.sbuf_tensor("bd1sb", [128, 32 * 128], BF16))
    bd2sb = es.enter_context(nc.sbuf_tensor("bd2sb", [128, 32 * 128], BF16))
    nTg_a = es.enter_context(nc.sbuf_tensor("nTg_a", [128, 8 * 512], BF16))
    nTg_b = es.enter_context(nc.sbuf_tensor("nTg_b", [128, 8 * 512], BF16))
    h1sb = es.enter_context(nc.sbuf_tensor("h1sb", [128, 32 * 512], BF16))
    mix_sb = es.enter_context(nc.sbuf_tensor("mix_sb", [128, 2 * 512], BF16))
    scr_a = es.enter_context(nc.sbuf_tensor("scr_a", [128, 4096], F32))
    scrbf = es.enter_context(nc.sbuf_tensor("scrbf", [128, 4096], BF16))
    onesb = es.enter_context(nc.sbuf_tensor("onesb", [1, 512], BF16))
    onesf = es.enter_context(nc.sbuf_tensor("onesf", [1, 128], F32))
    iota8 = es.enter_context(nc.sbuf_tensor("iota8", [1, 8], F32))
    lg = es.enter_context(nc.sbuf_tensor("lg", [1, 64], F32))
    idx_i = es.enter_context(nc.sbuf_tensor("idx_i", [1, 4], mybir.dt.int32))
    c_bc = es.enter_context(nc.sbuf_tensor("c_bc", [128, 4], F32))
    xa = es.enter_context(nc.sbuf_tensor("xa", [128, D], BF16))
    xb = es.enter_context(nc.sbuf_tensor("xb", [128, D], BF16))
    xf_a = es.enter_context(nc.sbuf_tensor("xf_a", [128, D], F32))
    xf_b = es.enter_context(nc.sbuf_tensor("xf_b", [128, D], F32))
    x1f_a = es.enter_context(nc.sbuf_tensor("x1f_a", [128, D], F32))
    x1f_b = es.enter_context(nc.sbuf_tensor("x1f_b", [128, D], F32))
    yt_a = es.enter_context(nc.sbuf_tensor("yt_a", [128, D], F32))
    yt_b = es.enter_context(nc.sbuf_tensor("yt_b", [128, D], F32))
    sqd = es.enter_context(nc.sbuf_tensor("sqd", [128, D], F32))
    small = es.enter_context(nc.sbuf_tensor("small", [128, 16], F32))
    stmp = es.enter_context(nc.sbuf_tensor("stmp", [128, 4 * 512], F32))

    ps = [es.enter_context(nc.psum_tensor(f"ps{i}", [128, 512], F32)) for i in range(8)]
    p = Prog(nc) if standalone else FUSE["p"]

    p.add("vector", lambda v: v.memset(small[:, 11:12], EPS_TOP), writes=R("small"))
    p.add("gpsimd", lambda g: g.dma_start(out=onesb[:], in_=onesb_d), writes=R("onesb"), dma=True)
    p.add("gpsimd", lambda g: g.dma_start(out=onesf[:], in_=onesf_d), writes=R("onesf"), dma=True)
    p.add("gpsimd", lambda g: g.dma_start(out=iota8[:], in_=iota_d), writes=R("iota"), dma=True)
    p.add("gpsimd", lambda g: g.dma_start(out=lg[0:1, 0:8], in_=fclog_d), writes=R("lg"), dma=True)
    p.add("gpsimd", lambda g: g.dma_start(out=lg[0:1, 8:16], in_=pjlog_d), writes=R("lg"), dma=True)

    for s_ in range(2):
        base = s_ * 8
        m1c, m2c = 16 + 2 * s_, 17 + 2 * s_
        k1 = 20 + 16 * s_
        k2 = 28 + 16 * s_
        dcol = 52 + 3 * s_
        p.add("vector", lambda v, base=base, m1c=m1c: v.reduce_max(
            lg[0:1, m1c:m1c + 1], lg[0:1, base:base + 8], axis=AX.X),
            reads=R("lg"), writes=R("lg"))
        p.add("vector", lambda v, base=base, m1c=m1c, k1=k1: v.tensor_single_scalar(
            lg[0:1, k1:k1 + 8], lg[0:1, base:base + 8], lg[0:1, m1c:m1c + 1], ALU.is_ge),
            reads=R("lg"), writes=R("lg"))
        p.add("vector", lambda v, base=base, k1=k1, k2=k2: v.scalar_tensor_tensor(
            lg[0:1, k2:k2 + 8], lg[0:1, k1:k1 + 8], -1e30,
            lg[0:1, base:base + 8], ALU.mult, ALU.add),
            reads=R("lg"), writes=R("lg"))
        p.add("vector", lambda v, m2c=m2c, k2=k2: v.reduce_max(
            lg[0:1, m2c:m2c + 1], lg[0:1, k2:k2 + 8], axis=AX.X),
            reads=R("lg"), writes=R("lg"))
        p.add("vector", lambda v, m2c=m2c, k2=k2: v.tensor_single_scalar(
            lg[0:1, k2:k2 + 8], lg[0:1, k2:k2 + 8], lg[0:1, m2c:m2c + 1], ALU.is_ge),
            reads=R("lg"), writes=R("lg"))
        p.add("vector", lambda v, m1c=m1c, m2c=m2c, dcol=dcol: v.tensor_sub(
            lg[0:1, dcol:dcol + 1], lg[0:1, m1c:m1c + 1], lg[0:1, m2c:m2c + 1]),
            reads=R("lg"), writes=R("lg"))
        p.add("scalar", lambda sc, dcol=dcol: sc.activation(
            lg[0:1, dcol + 1:dcol + 2], lg[0:1, dcol:dcol + 1], AF.Sigmoid),
            reads=R("lg"), writes=R("lg"))
        p.add("vector", lambda v, dcol=dcol: v.tensor_scalar(
            lg[0:1, dcol + 2:dcol + 3], lg[0:1, dcol + 1:dcol + 2],
            -1.0, 1.0, ALU.mult, ALU.add),
            reads=R("lg"), writes=R("lg"))
        for m_ in range(2):
            mk = k1 if m_ == 0 else k2
            icol = 58 + 2 * s_ + m_
            p.add("vector", lambda v, mk=mk: v.tensor_mul(
                lg[0:1, mk:mk + 8], lg[0:1, mk:mk + 8], iota8[0:1, :]),
                reads=R("lg", "iota"), writes=R("lg"))
            p.add("vector", lambda v, mk=mk, icol=icol: v.reduce_sum(
                lg[0:1, icol:icol + 1], lg[0:1, mk:mk + 8], axis=AX.X),
                reads=R("lg"), writes=R("lg"))
            p.add("vector", lambda v, s_=s_, m_=m_, icol=icol: v.tensor_copy(
                idx_i[0:1, 2 * s_ + m_:2 * s_ + m_ + 1], lg[0:1, icol:icol + 1]),
                reads=R("lg"), writes=R("idx"))
    for j, col in enumerate((53, 54, 56, 57)):
        p.add("tensor", lambda t, j=j, col=col: t.matmul(
            ps[6][0:128, j:j + 1], onesf[0:1, :], lg[0:1, col:col + 1],
            start=True, stop=True),
            reads=R("onesf", "lg"), writes=R("ps6"))
        p.add("vector", lambda v, j=j: v.tensor_copy(c_bc[:, j:j + 1], ps[6][0:128, j:j + 1]),
              reads=R("ps6"), writes=R("cbc"))

    _emit_mirror_nT(p, nc, es, xmid, nbf, None, xa, xb, xf_a, xf_b, sqd, small,
                    pre_norm="rms", prep_eps=1e-6)

    def branch_load(idx_col, out_ap, dram_by_bank, name, reads_extra=()):
        def fn(g, inc):
            reg = g.alloc_register()
            g.reg_load(reg, idx_i[0:1, idx_col:idx_col + 1])
            for b in range(8):
                with g.If_eq(reg, b):
                    inc(g.dma_start(out=out_ap, in_=dram_by_bank(b)))
            g.end_ifs()
            g.free_register(reg)
        p.add("gpsimd", fn, reads=R("idx") + tuple(reads_extra), writes=R(name),
              dma=True, raw=True)

    def scale_bd2(name, sb, width, ccol):
        p.add("vector", lambda v, sb=sb, width=width, ccol=ccol: v.tensor_scalar(
            sb[:, 0:width], sb[:, 0:width], c_bc[:, ccol:ccol + 1], None, ALU.mult),
            reads=R(name, "cbc"), writes=R(name))

    NTG = T_FULL // 512

    for m_ in range(2):
        idxc = 0 + m_
        branch_load(idxc, bd1sb[:, 0:8 * 128].rearrange("p (j w) -> p j w", w=128),
                    lambda b: fcbd1_d[b].rearrange("j p w -> p j w"), "bd1")
        branch_load(idxc, bd2sb[:].rearrange("p (j w) -> p j w", w=128),
                    lambda b: fcbd2_d[b].rearrange("j p w -> p j w"), "bd2")
        branch_load(idxc, mixA[:].rearrange("p (c o) -> p c o", o=4096),
                    lambda b: fcmix_d[b].rearrange("(c p) o -> p c o", p=128), "mixA")
        scale_bd2("bd2", bd2sb, 32 * 128, 0 + m_)
        for tg in range(NTG):
            buf, bufname = (nTg_a, "nTg_a") if tg % 2 == 0 else (nTg_b, "nTg_b")
            _load_nTg(p, nbf, tg, buf, bufname)
            for ch in range(8):
                pa = ps[tg % 2]
                p.add("tensor", lambda tt, ch=ch, pa=pa, buf=buf: tt.matmul(
                    pa[:, 0:512], bd1sb[:, ch * 128:(ch + 1) * 128],
                    buf[:, ch * 512:(ch + 1) * 512], start=True, stop=True),
                    reads=R("bd1", bufname), writes=R(f"ps{tg % 2}"))
                p.add("vector", lambda v, ch=ch, pa=pa: v.tensor_copy(
                    h1sb[:, ch * 512:(ch + 1) * 512], pa[:, 0:512]),
                    reads=R(f"ps{tg % 2}"), writes=R("h1"))
            for och in range(32):
                pm = ps[2 + och % 2]

                def mm_mix(tt, och=och, pm=pm):
                    for ch in range(8):
                        inst = tt.matmul(
                            pm[:, 0:512],
                            mixA[:, ch * 4096 + och * 128: ch * 4096 + och * 128 + 128],
                            h1sb[:, ch * 512:(ch + 1) * 512],
                            start=(ch == 0), stop=(ch == 7))
                    return inst
                p.add("tensor", mm_mix, reads=R("mixA", "h1"), writes=R(f"ps{2 + och % 2}"))
                msl = och % 2
                p.add("vector", lambda v, pm=pm, msl=msl: v.tensor_copy(
                    mix_sb[:, msl * 512:(msl + 1) * 512], pm[:, 0:512]),
                    reads=R(f"ps{2 + och % 2}"), writes=R(f"mxsb{msl}"))
                pb2 = ps[4 + och % 2]
                p.add("tensor", lambda tt, och=och, msl=msl, pb2=pb2: tt.matmul(
                    pb2[:, 0:512], bd2sb[:, och * 128:(och + 1) * 128],
                    mix_sb[:, msl * 512:(msl + 1) * 512], start=True, stop=True),
                    reads=R("bd2", f"mxsb{msl}"), writes=R(f"ps{4 + och % 2}"))
                sl = och % 4
                p.add("vector", lambda v, pb2=pb2, sl=sl: v.tensor_copy(
                    stmp[:, sl * 512:(sl + 1) * 512], pb2[:, 0:512]),
                    reads=R(f"ps{4 + och % 2}"), writes=R(f"stmp{sl}"))
                p.add("gpsimd", lambda g, och=och, tg=tg, sl=sl, m_=m_: g.dma_start(
                    out=hacc[och * 128:(och + 1) * 128, tg * 512:(tg + 1) * 512],
                    in_=stmp[:, sl * 512:(sl + 1) * 512],
                    accum_op=(ALU.bypass if m_ == 0 else ALU.add)),
                    reads=R(f"stmp{sl}"), writes=R(f"hacc{och}_{tg}"), dma=True)

    for och in range(32):
        scr, scrn = scr_a, "scr_a"
        p.add("gpsimd", lambda g, och=och, scr=scr: g.dma_start(
            out=scr[:], in_=hacc[och * 128:(och + 1) * 128, :]),
            reads=tuple(f"hacc{och}_{tg}" for tg in range(NTG)), writes=R(scrn), dma=True)
        p.add("scalar", lambda sc, scr=scr: sc.activation(scr[:], scr[:], AF.Relu),
              reads=R(scrn), writes=R(scrn))
        p.add("scalar", lambda sc, scr=scr: sc.activation(scrbf[:], scr[:], AF.Square),
              reads=R(scrn), writes=R("scrbf"))
        p.add("sync", lambda s, och=och: s.dma_start(
            out=hact[och * 128:(och + 1) * 128, :], in_=scrbf[:]),
            reads=R("scrbf"), writes=R(f"hact{och}"), dma=True)

    allhact = tuple(f"hact{o}" for o in range(32))
    for m_ in range(2):
        idxc = 2 + m_
        branch_load(idxc, bd1sb[:].rearrange("p (j w) -> p j w", w=128),
                    lambda b: pjbd1_d[b].rearrange("j p w -> p j w"), "bd1")
        branch_load(idxc, bd2sb[:, 0:8 * 128].rearrange("p (j w) -> p j w", w=128),
                    lambda b: pjbd2_d[b].rearrange("j p w -> p j w"), "bd2")
        branch_load(idxc, mixA[:].rearrange("p (c o) -> p c o", o=D),
                    lambda b: pjmix_d[b].rearrange("(c p) o -> p c o", p=128), "mixA")
        scale_bd2("bd2", bd2sb, 8 * 128, 2 + m_)
        for tg in range(NTG):
            for ch in range(32):
                pa = ps[tg % 2]
                sl2 = ch % 2
                p.add("gpsimd", lambda g, ch=ch, tg=tg, sl2=sl2: g.dma_start(
                    out=nTg_a[:, sl2 * 512:(sl2 + 1) * 512] if False else
                    mix_sb[:, sl2 * 512:(sl2 + 1) * 512],
                    in_=hact[ch * 128:(ch + 1) * 128, tg * 512:(tg + 1) * 512]),
                    reads=R(f"hact{ch}",), writes=R(f"mxsb{sl2}"), dma=True)
                p.add("tensor", lambda tt, ch=ch, pa=pa, sl2=sl2: tt.matmul(
                    pa[:, 0:512], bd1sb[:, ch * 128:(ch + 1) * 128],
                    mix_sb[:, sl2 * 512:(sl2 + 1) * 512], start=True, stop=True),
                    reads=R("bd1", f"mxsb{sl2}"), writes=R(f"ps{tg % 2}"))
                p.add("vector", lambda v, ch=ch, pa=pa: v.tensor_copy(
                    h1sb[:, ch * 512:(ch + 1) * 512], pa[:, 0:512]),
                    reads=R(f"ps{tg % 2}"), writes=R("h1"))
            for och in range(8):
                pm = ps[2 + och % 2]

                def mm_mixp(tt, och=och, pm=pm):
                    for ch in range(32):
                        inst = tt.matmul(
                            pm[:, 0:512],
                            mixA[:, ch * D + och * 128: ch * D + och * 128 + 128],
                            h1sb[:, ch * 512:(ch + 1) * 512],
                            start=(ch == 0), stop=(ch == 31))
                    return inst
                p.add("tensor", mm_mixp, reads=R("mixA", "h1"), writes=R(f"ps{2 + och % 2}"))
                msl = och % 2
                p.add("vector", lambda v, pm=pm, msl=msl: v.tensor_copy(
                    mix_sb[:, msl * 512:(msl + 1) * 512], pm[:, 0:512]),
                    reads=R(f"ps{2 + och % 2}"), writes=R(f"mxsb{msl}"))
                for ti in range(4):
                    t = tg * 4 + ti
                    pb2 = ps[4 + ti % 2]
                    p.add("tensor", lambda tt, och=och, msl=msl, ti=ti, pb2=pb2: tt.matmul(
                        pb2[:, 0:128], mix_sb[:, msl * 512 + ti * 128: msl * 512 + ti * 128 + 128],
                        bd2sb[:, och * 128:(och + 1) * 128], start=True, stop=True),
                        reads=R("bd2", f"mxsb{msl}"), writes=R(f"ps{4 + ti % 2}"))
                    sl = (och * 4 + ti) % 4
                    p.add("vector", lambda v, pb2=pb2, sl=sl: v.tensor_copy(
                        stmp[:, sl * 512:sl * 512 + 128], pb2[:, 0:128]),
                        reads=R(f"ps{4 + ti % 2}"), writes=R(f"stmp{sl}"))
                    p.add("gpsimd", lambda g, t=t, och=och, sl=sl, m_=m_: g.dma_start(
                        out=acc_d[t * 128:(t + 1) * 128, och * 128:(och + 1) * 128],
                        in_=stmp[:, sl * 512:sl * 512 + 128],
                        accum_op=(ALU.bypass if m_ == 0 else ALU.add)),
                        reads=R(f"stmp{sl}"), writes=R(f"acc{t}_{och}"), dma=True)

    for t in range(NT):
        par = t % 2
        xf, xfn = (xf_a, "xf_a") if par == 0 else (xf_b, "xf_b")
        x1f, x1fn = (x1f_a, "x1f_a") if par == 0 else (x1f_b, "x1f_b")
        yt, ytn = (yt_a, "yt_a") if par == 0 else (yt_b, "yt_b")
        _emit_final_epilogue(p, nc, t, acc_d, xin, xmid, xf, xfn, x1f, x1fn,
                             yt, ytn, sqd, small, xout,
                             acc_reads=[f"acc{t}_{o}" for o in range(8)])

    if standalone:
        p.emit(es)
        es.close()
        return nc
    p.fence()
    es.close()
    return None


def _bd_pairs(w, nb):
    out = np.zeros((nb // 2, 128, 128), np.float32)
    for j in range(nb // 2):
        out[j, 0:64, 0:64] = w[2 * j]
        out[j, 64:128, 64:128] = w[2 * j + 1]
    return out


def prep_l0_ffn_bank(inp):
    io = {}
    io["c_onesb"] = CONSTS["c_onesb"]
    io["c_onesf"] = CONSTS["c_onesf"]
    io["c_iota8"] = np.arange(8, dtype=np.float32).reshape(1, 8)
    io["fc_logits"] = np.ascontiguousarray(inp["L0_ffn_fc_logits"]).reshape(1, 8)
    io["pj_logits"] = np.ascontiguousarray(inp["L0_ffn_pj_logits"]).reshape(1, 8)
    fc_s1 = np.asarray(inp["L0_ffn_fc_s1"])
    fc_s2 = np.asarray(inp["L0_ffn_fc_s2"])
    fc_mix = np.asarray(inp["L0_ffn_fc_mix"])
    pj_s1 = np.asarray(inp["L0_ffn_pj_s1"])
    pj_s2 = np.asarray(inp["L0_ffn_pj_s2"])
    pj_mix = np.asarray(inp["L0_ffn_pj_mix"])
    io["fc_bd1"] = np.stack([_bd_pairs(fc_s1[b].reshape(16, 64, 64), 16) for b in range(8)])
    io["fc_bd2"] = np.stack([_bd_pairs(fc_s2[b].reshape(64, 64, 64), 64) for b in range(8)])
    io["pj_bd1"] = np.stack([_bd_pairs(pj_s1[b].reshape(64, 64, 64), 64) for b in range(8)])
    io["pj_bd2"] = np.stack([_bd_pairs(pj_s2[b].reshape(16, 64, 64), 16) for b in range(8)])
    g16, c64 = np.meshgrid(np.arange(16), np.arange(64), indexing="ij")
    fcm = np.empty((8, 1024, 4096), np.float32)
    for b in range(8):
        wmT = fc_mix[b].T
        idx = (c64 * 16 + (g16 + b) % 16).reshape(-1)
        fcm[b] = wmT[idx]
    io["fc_mixp"] = fcm
    g64, c64b = np.meshgrid(np.arange(64), np.arange(64), indexing="ij")
    pjm = np.empty((8, 4096, 1024), np.float32)
    for b in range(8):
        wmT = pj_mix[b].T
        idx = (c64b * 64 + (g64 + b) % 64).reshape(-1)
        pjm[b] = wmT[idx]
    io["pj_mixp"] = pjm
    return io


# version 04
# speedup vs baseline: 1.1083x; 1.1083x over previous
import contextlib
import numpy as np
import ml_dtypes

import concourse.bass as bass
import concourse.mybir as mybir
from concourse.bass_utils import run_bass_kernel_spmd

F32 = mybir.dt.float32
BF16 = mybir.dt.bfloat16
AF = mybir.ActivationFunctionType
ALU = mybir.AluOpType
AX = mybir.AxisListType

BF = ml_dtypes.bfloat16
T_FULL = 4096
D = 1024
NT = T_FULL // 128
HD = 64
EPS_TOP = 1e-5

ENGINES = ("tensor", "vector", "scalar", "gpsimd", "sync")
DEBUG = False
DEBUG_ZERO = True
DEBUG_A = True
DEBUG_B = True


NUM_EPOCH_SEMS = 6


class Epoch:
    __slots__ = ("engine", "idx", "count", "open", "cum_before", "cum_after")

    def __init__(self, engine, idx):
        self.engine = engine
        self.idx = idx
        self.count = 0
        self.open = True


class Op:
    __slots__ = ("engine", "fn", "dma", "deps", "marked", "inc_no", "tag", "epoch",
                 "raw", "fence_eps")

    def __init__(self, engine, fn, dma, tag, raw=False):
        self.engine = engine
        self.fn = fn
        self.dma = dma
        self.deps = set()
        self.marked = False
        self.inc_no = None
        self.tag = tag
        self.epoch = None
        self.raw = raw
        self.fence_eps = None


class Prog:

    def __init__(self, nc):
        self.nc = nc
        self.per_eng = {e: [] for e in ENGINES}
        self.last_w = {}
        self.readers = {}
        self.cur_epoch = {}
        self.epochs = {e: [] for e in ENGINES}

    def fence(self):
        lasts = [ops[-1] for ops in self.per_eng.values() if ops]
        fence_eps = {}
        for e in ENGINES:
            for ep in self.epochs[e]:
                ep.open = False
                fence_eps[(e, ep.idx % NUM_EPOCH_SEMS)] = ep
        self.cur_epoch = {}
        for e in ENGINES:
            op = Op(e, (lambda eng: eng.nop()), False, "fence")
            op.deps = {d for d in lasts
                       if not (d.engine == e == "tensor" and not d.dma)}
            op.fence_eps = list(fence_eps.values())
            self.per_eng[e].append(op)
        self.last_w = {}
        self.readers = {}

    def _dma_epoch(self, engine):
        ep = self.cur_epoch.get(engine)
        if ep is None or not ep.open:
            ep = Epoch(engine, len(self.epochs[engine]))
            self.epochs[engine].append(ep)
            self.cur_epoch[engine] = ep
        return ep

    def add(self, engine, fn, reads=(), writes=(), dma=False, tag="", raw=False):
        op = Op(engine, fn, dma, tag, raw)
        deps = set()
        for r in reads:
            w = self.last_w.get(r)
            if w is not None:
                deps.add(w)
        for r in writes:
            w = self.last_w.get(r)
            if w is not None:
                deps.add(w)
            for rd in self.readers.get(r, ()):
                deps.add(rd)
        for r in reads:
            self.readers.setdefault(r, []).append(op)
        for r in writes:
            self.last_w[r] = op
            self.readers[r] = []
        op.deps = {d for d in deps
                   if d is not op and not (d.engine == engine == "tensor"
                                           and not d.dma and not dma)}
        for d in op.deps:
            if d.dma and d.epoch.open:
                d.epoch.open = False
        if dma:
            ep = self._dma_epoch(engine)
            ep.count += 1
            op.epoch = ep
        self.per_eng[engine].append(op)
        return op

    def emit(self, es):
        nc = self.nc
        for ops in self.per_eng.values():
            for op in ops:
                for d in op.deps:
                    if not d.dma:
                        d.marked = True
        sem_cum = {}
        for e in ENGINES:
            for ep in self.epochs[e]:
                ep.open = False
                slot = ep.idx % NUM_EPOCH_SEMS
                prev = sem_cum.get((e, slot), 0)
                ep.cum_before = prev
                ep.cum_after = prev + ep.count
                sem_cum[(e, slot)] = ep.cum_after
        sem_c, sem_d = {}, {}
        cnt_c = {e: 0 for e in ENGINES}
        for e in ENGINES:
            for op in self.per_eng[e]:
                if not op.dma and op.marked:
                    cnt_c[e] += 1
                    op.inc_no = cnt_c[e]
        for e in ENGINES:
            if cnt_c[e]:
                sem_c[e] = es.enter_context(nc.semaphore(f"c_{e}"))
            nslots = min(NUM_EPOCH_SEMS, len(self.epochs[e]))
            if nslots:
                sem_d[e] = [es.enter_context(nc.semaphore(f"d_{e}{i}"))
                            for i in range(nslots)]

        def emit_engine(eng_obj, ename):
            waited = {}
            seen_epoch = {}
            for op in self.per_eng[ename]:
                if op.fence_eps is not None:
                    for ep in op.fence_eps:
                        slot = ep.idx % NUM_EPOCH_SEMS
                        key = ("d", ep.engine, slot)
                        val = 16 * ep.cum_after
                        if waited.get(key, 0) < val:
                            waited[key] = val
                            eng_obj.wait_ge(sem_d[ep.engine][slot], val)
                for d in sorted(op.deps, key=lambda o: (o.engine, o.inc_no or 0, id(o))):
                    if d.dma:
                        ep = d.epoch
                        slot = ep.idx % NUM_EPOCH_SEMS
                        key = ("d", d.engine, slot)
                        val = 16 * ep.cum_after
                        sem = sem_d[d.engine][slot]
                    else:
                        key, val, sem = ("c", d.engine), d.inc_no, sem_c[d.engine]
                    if waited.get(key, 0) < val:
                        waited[key] = val
                        eng_obj.wait_ge(sem, val)
                if op.dma:
                    ep = op.epoch
                    slot = ep.idx % NUM_EPOCH_SEMS
                    if seen_epoch.get(slot) != ep.idx:
                        seen_epoch[slot] = ep.idx
                        if ep.cum_before > 0:
                            key = ("d", ename, slot)
                            val = 16 * ep.cum_before
                            if waited.get(key, 0) < val:
                                waited[key] = val
                                eng_obj.wait_ge(sem_d[ename][slot], val)
                if op.raw:
                    sem = sem_d[ename][op.epoch.idx % NUM_EPOCH_SEMS]
                    op.fn(eng_obj, lambda inst, sem=sem: inst.then_inc(sem, 16))
                else:
                    inst = op.fn(eng_obj)
                    if op.dma:
                        inst.then_inc(sem_d[ename][op.epoch.idx % NUM_EPOCH_SEMS], 16)
                    elif op.marked:
                        inst.then_inc(sem_c[ename], 1)
            if ename == "sync":
                for e in ENGINES:
                    for slot in range(min(NUM_EPOCH_SEMS, len(self.epochs[e]))):
                        tot = sem_cum.get((e, slot), 0)
                        if tot:
                            eng_obj.wait_ge(sem_d[e][slot], 16 * tot)

        block = es.enter_context(nc.Block())
        block.tensor(lambda t: emit_engine(t, "tensor"))
        block.vector(lambda v: emit_engine(v, "vector"))
        block.scalar(lambda s: emit_engine(s, "scalar"))
        block.gpsimd(lambda g: emit_engine(g, "gpsimd"))
        block.sync(lambda s: emit_engine(s, "sync"))


def R(*names):
    return tuple(names)


class _FakeHandle:
    def __init__(self, ap):
        self._ap = ap

    def ap(self):
        return self._ap


class NCProxy:

    def __init__(self, nc, pfx, redirect):
        self._nc = nc
        self._pfx = pfx
        self._rd = redirect

    def dram_tensor(self, name, shape, dtype, kind="Internal", **kw):
        if name in self._rd:
            return _FakeHandle(self._rd[name])
        return self._nc.dram_tensor(self._pfx + name, shape, dtype, kind="Internal"
                                    if kind == "ExternalOutput" and name not in self._rd
                                    else ("ExternalInput" if kind == "ExternalInput" else "Internal"),
                                    **kw)

    def sbuf_tensor(self, name, shape, dtype, **kw):
        return self._nc.sbuf_tensor(self._pfx + name, shape, dtype, **kw)

    def psum_tensor(self, name, shape, dtype, **kw):
        return self._nc.psum_tensor(self._pfx + name, shape, dtype, **kw)

    def __getattr__(self, a):
        return getattr(self._nc, a)



def host_constants():
    c = {}
    c["c_ident"] = np.eye(128, dtype=np.float32)
    c["c_onesb"] = np.ones((1, 512), dtype=BF)
    c["c_onesf"] = np.ones((1, 128), dtype=np.float32)
    masks = np.zeros((4, 128, 512), dtype=np.float32)
    for j, off in enumerate((0, 128, 256, 384)):
        kl = np.arange(128)[:, None]
        ql = np.arange(512)[None, :]
        masks[j] = np.where(ql >= kl + off, 0.0, -1e30)
    c["c_masks"] = masks.astype(BF)
    return c


CONSTS = host_constants()



def emit_ln_tok(p, yt, ytn, sqd, small, eps_col, out, outname, smn="small"):
    p.add("scalar", lambda sc: sc.activation(sqd[:], yt[:], AF.Square,
                                             accum_out=small[:, 4:5]),
          reads=R(ytn), writes=R(smn))
    p.add("vector", lambda v: v.reduce_sum(small[:, 5:6], small[:, 0:2], axis=AX.X),
          reads=R(smn), writes=R(smn))
    p.add("vector", lambda v: v.tensor_scalar_mul(small[:, 6:7], small[:, 5:6], 1.0 / D),
          reads=R(smn), writes=R(smn))
    p.add("vector", lambda v: v.tensor_mul(small[:, 7:8], small[:, 6:7], small[:, 6:7]),
          reads=R(smn), writes=R(smn))
    p.add("vector", lambda v: v.scalar_tensor_tensor(small[:, 7:8], small[:, 4:5], 1.0 / D,
                                                     small[:, 7:8], ALU.mult, ALU.subtract),
          reads=R(smn), writes=R(smn))
    p.add("scalar", lambda sc: sc.activation(small[:, 8:9], small[:, 7:8], AF.Sqrt,
                                             bias=eps_col),
          reads=R(smn), writes=R(smn))
    p.add("vector", lambda v: v.reciprocal(small[:, 9:10], small[:, 8:9]),
          reads=R(smn), writes=R(smn))
    p.add("vector", lambda v: v.scalar_tensor_tensor(small[:, 10:11], small[:, 6:7], -1.0,
                                                     small[:, 9:10], ALU.mult, ALU.mult),
          reads=R(smn), writes=R(smn))
    p.add("scalar", lambda sc: sc.activation(out[:], yt[:], AF.Identity,
                                             bias=small[:, 10:11], scale=small[:, 9:10]),
          reads=R(ytn, smn), writes=R(outname))



class AttnCfg:
    def __init__(self, name, n_heads=16, kv_heads=16, dq=1024, dkv=1024,
                 has_bq=False, has_bkv=False, has_bo=False,
                 qk_prep=None, pre_norm=None, post_ln=False, epilogue="top",
                 gains=False, nf=0, prep_eps=1e-6, fold_gb=False):
        self.name = name
        self.n_heads, self.kv_heads = n_heads, kv_heads
        self.dq, self.dkv = dq, dkv
        self.has_bq, self.has_bkv, self.has_bo = has_bq, has_bkv, has_bo
        self.qk_prep = qk_prep
        self.pre_norm = pre_norm
        self.post_ln = post_ln
        self.epilogue = epilogue
        self.gains = gains
        self.nf = nf
        self.prep_eps = prep_eps
        self.fold_gb = fold_gb


def build_attn_launch(cfg: AttnCfg, FUSE=None):
    standalone = FUSE is None
    nc = (bass.Bass("TRN2", target_bir_lowering=False) if standalone
          else NCProxy(FUSE["nc"], FUSE["pfx"], FUSE["rd"]))
    es = contextlib.ExitStack()
    H, KVH, dq, dkv = cfg.n_heads, cfg.kv_heads, cfg.dq, cfg.dkv
    NPAIR = H // 2
    KVPAIR = max(KVH // 2, 1)

    xin = nc.dram_tensor("xin", [T_FULL, D], F32, kind="ExternalInput").ap()
    if cfg.epilogue == "top":
        xout = nc.dram_tensor("xout", [T_FULL, D], F32, kind="ExternalOutput").ap()

    ident_d = nc.dram_tensor("c_ident", [128, 128], F32, kind="ExternalInput").ap()
    onesb_d = nc.dram_tensor("c_onesb", [1, 512], BF16, kind="ExternalInput").ap()
    onesf_d = nc.dram_tensor("c_onesf", [1, 128], F32, kind="ExternalInput").ap()
    masks_d = nc.dram_tensor("c_masks", [4, 128, 512], BF16, kind="ExternalInput").ap()

    wq_d = nc.dram_tensor("wq", [D, dq], F32, kind="ExternalInput").ap()
    wk_d = nc.dram_tensor("wk", [D, dkv], F32, kind="ExternalInput").ap()
    wv_d = nc.dram_tensor("wv", [D, dkv], F32, kind="ExternalInput").ap()
    wo_d = nc.dram_tensor("wo", [D, D], F32, kind="ExternalInput").ap()
    bias_d = {}
    if cfg.has_bq:
        bias_d["bq"] = nc.dram_tensor("bq", [1, dq], F32, kind="ExternalInput").ap()
    if cfg.has_bkv:
        bias_d["bk"] = nc.dram_tensor("bk", [1, dkv], F32, kind="ExternalInput").ap()
        bias_d["bv"] = nc.dram_tensor("bv", [1, dkv], F32, kind="ExternalInput").ap()
    if cfg.has_bo:
        bias_d["bo"] = nc.dram_tensor("bo", [1, D], F32, kind="ExternalInput").ap()
    if cfg.post_ln:
        pg_d = nc.dram_tensor("post_gamma", [1, D], F32, kind="ExternalInput").ap()
        pb_d = nc.dram_tensor("post_beta", [1, D], F32, kind="ExternalInput").ap()
    if cfg.gains:
        qg_d = nc.dram_tensor("q_gain", [1, cfg.n_heads], F32, kind="ExternalInput").ap()
        kg_d = nc.dram_tensor("k_gain", [1, cfg.kv_heads], F32, kind="ExternalInput").ap()
    if cfg.qk_prep is not None and cfg.nf:
        ropecs_d = nc.dram_tensor("rope_cs", [T_FULL, 2 * cfg.nf], F32, kind="ExternalInput").ap()
    if cfg.fold_gb:
        gam_d = nc.dram_tensor("ln_gam_col", [128, 8], F32, kind="ExternalInput").ap()
    if cfg.epilogue == "inner":
        xmid_d = nc.dram_tensor("xmid", [T_FULL, D], F32, kind="ExternalOutput").ap()

    skind = "ExternalOutput" if (DEBUG and DEBUG_ZERO) else "Internal"
    nbf = nc.dram_tensor("nbf", [T_FULL, D], BF16, kind=skind).ap()
    qd2 = nc.dram_tensor("qd2", [H, T_FULL, 128], BF16, kind=skind).ap()
    kd2 = nc.dram_tensor("kd2", [KVH, T_FULL, 128], BF16, kind=skind).ap()
    vd = nc.dram_tensor("vd", [KVH, T_FULL, HD], BF16, kind=skind).ap()
    if DEBUG:
        dbg_ot = nc.dram_tensor("dbg_ot", [8, 128, T_FULL], BF16, kind="ExternalOutput").ap()
        dbg_mx = nc.dram_tensor("dbg_mx", [1, T_FULL], BF16, kind="ExternalOutput").ap()
        dbg_rm = nc.dram_tensor("dbg_rm", [128, 32], F32, kind="ExternalOutput").ap()
        dbg_yt = nc.dram_tensor("dbg_yt", [T_FULL, D], F32, kind="ExternalOutput").ap()
        dbg_sm = nc.dram_tensor("dbg_sm", [NT, 128, 16], F32, kind="ExternalOutput").ap()

    big = es.enter_context(nc.sbuf_tensor("big", [128, 8 * T_FULL], BF16))
    wq_sb = es.enter_context(nc.sbuf_tensor("wq_sb", [128, 8 * dq], BF16))
    wk_sb = es.enter_context(nc.sbuf_tensor("wk_sb", [128, 8 * dkv], BF16))
    wv_sb = es.enter_context(nc.sbuf_tensor("wv_sb", [128, 8 * dkv], BF16))
    xa = es.enter_context(nc.sbuf_tensor("xa", [128, D], BF16))
    xb = es.enter_context(nc.sbuf_tensor("xb", [128, D], BF16))
    qt_a = es.enter_context(nc.sbuf_tensor("qt_a", [128, dq], BF16))
    qt_b = es.enter_context(nc.sbuf_tensor("qt_b", [128, dq], BF16))
    kt_a = es.enter_context(nc.sbuf_tensor("kt_a", [128, dkv], BF16))
    kt_b = es.enter_context(nc.sbuf_tensor("kt_b", [128, dkv], BF16))
    vt_a = es.enter_context(nc.sbuf_tensor("vt_a", [128, dkv], BF16))
    vt_b = es.enter_context(nc.sbuf_tensor("vt_b", [128, dkv], BF16))
    QT = es.enter_context(nc.sbuf_tensor("QT", [128, T_FULL], BF16))
    KT = es.enter_context(nc.sbuf_tensor("KT", [128, T_FULL], BF16))
    VB = es.enter_context(nc.sbuf_tensor("VB", [128, NT * 65], BF16))
    if KVH < H:
        VBg = es.enter_context(nc.sbuf_tensor("VBg", [128, NT * 65], BF16))
    PT_a = es.enter_context(nc.sbuf_tensor("PT_a", [128, 512], BF16))
    PT_b = es.enter_context(nc.sbuf_tensor("PT_b", [128, 512], BF16))
    PT_c = es.enter_context(nc.sbuf_tensor("PT_c", [128, 512], BF16))
    cmx = es.enter_context(nc.sbuf_tensor("cmx", [128, 2 * max(T_FULL // 512, 1)], F32))
    rm = es.enter_context(nc.sbuf_tensor("rm", [128, 2 * NT], F32))
    rs_row = es.enter_context(nc.sbuf_tensor("rs_row", [1, 512], F32))
    otmp = es.enter_context(nc.sbuf_tensor("otmp", [64, 512], F32))
    ident = es.enter_context(nc.sbuf_tensor("ident", [128, 128], F32))
    onesb = es.enter_context(nc.sbuf_tensor("onesb", [1, 512], BF16))
    onesf = es.enter_context(nc.sbuf_tensor("onesf", [1, 128], F32))
    masks = es.enter_context(nc.sbuf_tensor("masks", [128, 4 * 512], BF16))
    bias_sb = {}
    for bn, bd in bias_d.items():
        bias_sb[bn] = es.enter_context(nc.sbuf_tensor(bn + "_sb", [1, bd.shape[1]], BF16))
    if cfg.fold_gb:
        for bn, n in (("bq", dq), ("bk", dkv), ("bv", dkv)):
            assert bn not in bias_sb
            bias_sb[bn] = es.enter_context(nc.sbuf_tensor(bn + "_sb", [1, n], BF16))
    xf_a = es.enter_context(nc.sbuf_tensor("xf_a", [128, D], F32))
    xf_b = es.enter_context(nc.sbuf_tensor("xf_b", [128, D], F32))
    yt_a = es.enter_context(nc.sbuf_tensor("yt_a", [128, D], F32))
    yt_b = es.enter_context(nc.sbuf_tensor("yt_b", [128, D], F32))
    sqd = es.enter_context(nc.sbuf_tensor("sqd", [128, D], F32))
    small = es.enter_context(nc.sbuf_tensor("small", [128, 16], F32))
    small2 = es.enter_context(nc.sbuf_tensor("small2", [128, 16], F32))
    if cfg.post_ln:
        gb_bc = es.enter_context(nc.sbuf_tensor("gb_bc", [128, 2 * D], BF16))
        prow = es.enter_context(nc.sbuf_tensor("prow", [1, 2 * D], BF16))
    if cfg.gains:
        gains_sb = es.enter_context(nc.sbuf_tensor("gains_sb", [1, cfg.n_heads + cfg.kv_heads], F32))
        gain_bc = es.enter_context(nc.sbuf_tensor("gain_bc", [128, cfg.n_heads + cfg.kv_heads], F32))
    if cfg.qk_prep is not None:
        rope_a = es.enter_context(nc.sbuf_tensor("rope_a", [128, 2 * max(cfg.nf, 1)], F32))
        rtmp = es.enter_context(nc.sbuf_tensor("rtmp", [128, 2048], F32))
        prep_sc = es.enter_context(nc.sbuf_tensor("prep_sc", [128, 128], F32))
    if cfg.fold_gb:
        gam_sb = es.enter_context(nc.sbuf_tensor("gam_sb", [128, 8], F32))

    ps = [es.enter_context(nc.psum_tensor(f"ps{i}", [128, 512], F32)) for i in range(8)]

    p = Prog(nc) if standalone else FUSE["p"]

    p.add("vector", lambda v: v.memset(small[:, 11:12], EPS_TOP), writes=R("small"))
    p.add("vector", lambda v: v.memset(small2[:, 11:12], EPS_TOP), writes=R("small2"))

    p.add("gpsimd", lambda g: g.dma_start(out=ident[:], in_=ident_d), writes=R("ident"), dma=True)
    p.add("gpsimd", lambda g: g.dma_start(out=onesb[:], in_=onesb_d), writes=R("onesb"), dma=True)
    p.add("gpsimd", lambda g: g.dma_start(out=onesf[:], in_=onesf_d), writes=R("onesf"), dma=True)
    p.add("gpsimd", lambda g: g.dma_start(
        out=masks[:].rearrange("p (j n) -> p j n", n=512),
        in_=masks_d.rearrange("j p n -> p j n")),
        writes=R("masks"), dma=True)
    for wname, wsb, wd, dout in (("wq_sb", wq_sb, wq_d, dq), ("wk_sb", wk_sb, wk_d, dkv),
                                 ("wv_sb", wv_sb, wv_d, dkv)):
        p.add("gpsimd", lambda g, wsb=wsb, wd=wd, dout=dout: g.dma_start(
            out=wsb[:].rearrange("p (c o) -> p c o", o=dout),
            in_=wd.rearrange("(c p) o -> p c o", p=128)),
            writes=R(wname), dma=True)
    for bn in bias_d:
        p.add("gpsimd", lambda g, bn=bn: g.dma_start(out=bias_sb[bn][:], in_=bias_d[bn]),
              writes=R(bn), dma=True)
    if cfg.gains:
        p.add("gpsimd", lambda g: g.dma_start(out=gains_sb[0:1, 0:H], in_=qg_d),
              writes=R("gains"), dma=True)
        p.add("gpsimd", lambda g: g.dma_start(out=gains_sb[0:1, H:H + KVH], in_=kg_d),
              writes=R("gains"), dma=True)
        p.add("tensor", lambda t: t.matmul(ps[7][0:128, 0:H + KVH], onesf[0:1, :],
                                           gains_sb[:], start=True, stop=True),
              reads=R("onesf", "gains"), writes=R("ps7"))
        p.add("vector", lambda v: v.tensor_copy(gain_bc[:], ps[7][0:128, 0:H + KVH]),
              reads=R("ps7"), writes=R("gain_bc"))
    if cfg.fold_gb:
        p.add("gpsimd", lambda g: g.dma_start(out=gam_sb[:], in_=gam_d),
              writes=R("gam"), dma=True)
        beta_d = nc.dram_tensor("ln_beta_col", [128, 8], BF16, kind="ExternalInput").ap()
        beta_sb = es.enter_context(nc.sbuf_tensor("beta_sb", [128, 8], BF16))
        p.add("gpsimd", lambda g: g.dma_start(out=beta_sb[:], in_=beta_d),
              writes=R("beta"), dma=True)
        for wname, wsb, dout, bn in (("wq_sb", wq_sb, dq, "bq"), ("wk_sb", wk_sb, dkv, "bk"),
                                     ("wv_sb", wv_sb, dkv, "bv")):
            nchk = (dout + 511) // 512
            for half in range(nchk):
                n = min(512, dout - half * 512)

                def mm_brow(tt, wsb=wsb, dout=dout, half=half, n=n):
                    for c in range(8):
                        inst = tt.matmul(ps[6][0:1, 0:n], beta_sb[:, c:c + 1],
                                         wsb[:, c * dout + half * 512: c * dout + half * 512 + n],
                                         start=(c == 0), stop=(c == 7))
                    return inst
                p.add("tensor", mm_brow, reads=R("beta", wname), writes=R("ps6"))
                p.add("scalar", lambda sc, bn=bn, half=half, n=n: sc.activation(
                    bias_sb[bn][0:1, half * 512:half * 512 + n], ps[6][0:1, 0:n], AF.Copy,
                    bias=0.0, scale=1.0),
                    reads=R("ps6"), writes=R(bn))
        for wname, wsb, dout in (("wq_sb", wq_sb, dq), ("wk_sb", wk_sb, dkv),
                                 ("wv_sb", wv_sb, dkv)):
            for c in range(8):
                p.add("vector", lambda v, wsb=wsb, dout=dout, c=c: v.tensor_scalar(
                    wsb[:, c * dout:(c + 1) * dout], wsb[:, c * dout:(c + 1) * dout],
                    gam_sb[:, c:c + 1], None, ALU.mult),
                    reads=R(wname, "gam"), writes=R(wname))

    if cfg.pre_norm is not None:
        p.add("vector", lambda v: v.memset(small[:, 12:13], float(cfg.prep_eps)),
              writes=R("small"))
    for t in range(NT):
        par = t % 2
        buf, bname = (xa, "xa") if par == 0 else (xb, "xb")
        if cfg.pre_norm is None:
            p.add("gpsimd", lambda g, buf=buf, t=t: g.dma_start(
                out=buf[:], in_=xin[t * 128:(t + 1) * 128, :]),
                writes=R(bname), dma=True)
        else:
            xf, xfn = (xf_a, "xf_a") if par == 0 else (xf_b, "xf_b")
            p.add("gpsimd", lambda g, xf=xf, t=t: g.dma_start(
                out=xf[:], in_=xin[t * 128:(t + 1) * 128, :]),
                writes=R(xfn), dma=True)
            p.add("scalar", lambda sc, xf=xf: sc.activation(
                sqd[:], xf[:], AF.Square, accum_out=small[:, 4:5]),
                reads=R(xfn), writes=R("sqd", "small"))
            if cfg.pre_norm == "rms":
                p.add("vector", lambda v: v.tensor_scalar_mul(small[:, 5:6], small[:, 4:5], 1.0 / D),
                      reads=R("small"), writes=R("small"))
                p.add("scalar", lambda sc: sc.activation(small[:, 6:7], small[:, 5:6],
                                                         AF.Sqrt, bias=small[:, 12:13]),
                      reads=R("small"), writes=R("small"))
                p.add("vector", lambda v: v.reciprocal(small[:, 7:8], small[:, 6:7]),
                      reads=R("small"), writes=R("small"))
                p.add("scalar", lambda sc, xf=xf, buf=buf: sc.activation(
                    buf[:], xf[:], AF.Identity, bias=0.0, scale=small[:, 7:8]),
                    reads=R(xfn, "small"), writes=R(bname))
            else:
                p.add("scalar", lambda sc, xf=xf: sc.activation(
                    sqd[:], xf[:], AF.Identity, accum_out=small[:, 0:1]),
                    reads=R(xfn), writes=R("sqd", "small"))
                p.add("vector", lambda v: v.memset(small[:, 1:2], 0.0),
                      writes=R("small"))
                emit_ln_tok(p, xf, xfn, sqd, small, small[:, 12:13], xf, xfn)
                p.add("vector", lambda v, xf=xf, buf=buf: v.tensor_copy(buf[:], xf[:]),
                      reads=R(xfn), writes=R(bname))
        p.add("sync", lambda s, buf=buf, t=t: s.dma_start(
            out=nbf[t * 128:(t + 1) * 128, :], in_=buf[:]),
            reads=R(bname), writes=R(f"dram_nbf{t}"), dma=True)

    for c in range(8):
        p.add("sync", lambda s, c=c: s.dma_start(
            out=big[:, c * T_FULL:(c + 1) * T_FULL],
            in_=nbf[:, c * 128:(c + 1) * 128], transpose=True),
            reads=tuple(f"dram_nbf{t}" for t in range(NT)), writes=R("big"), dma=True)

    def emit_proj(t, wsb, wname, dout, outtile, outname, bias, scale, evac_eng, bank_base):
        nch = (dout + 511) // 512
        for half in range(nch):
            n = min(512, dout - half * 512)
            bank = bank_base
            pst = ps[bank]
            has_b = bias in bias_sb

            def mm_proj(tt, wsb=wsb, half=half, t=t, pst=pst, dout=dout, n=n, has_b=has_b):
                for c in range(8):
                    inst = tt.matmul(
                        pst[:, 0:n],
                        big[:, c * T_FULL + t * 128: c * T_FULL + (t + 1) * 128],
                        wsb[:, c * dout + half * 512: c * dout + half * 512 + n],
                        start=(c == 0), stop=(c == 7 and not has_b))
                return inst
            p.add("tensor", mm_proj, reads=R("big", wname), writes=R(f"ps{bank}"))
            if has_b:
                p.add("tensor", lambda tt, pst=pst, bias=bias, half=half, n=n: tt.matmul(
                    pst[:, 0:n], onesb[0:1, 0:128],
                    bias_sb[bias][0:1, half * 512:half * 512 + n], start=False, stop=True),
                    reads=R("onesb", bias), writes=R(f"ps{bank}"))
            if evac_eng == "scalar":
                p.add("scalar", lambda sc, outtile=outtile, half=half, pst=pst, n=n, scale=scale:
                      sc.activation(outtile[:, half * 512:half * 512 + n], pst[:, 0:n],
                                    AF.Copy, bias=0.0, scale=float(scale)),
                      reads=R(f"ps{bank}"), writes=R(outname))
            else:
                p.add("vector", lambda v, outtile=outtile, half=half, pst=pst, n=n:
                      v.tensor_copy(outtile[:, half * 512:half * 512 + n], pst[:, 0:n]),
                      reads=R(f"ps{bank}"), writes=R(outname))

    def rope_views(tile, nh, kind):
        if kind == "neox":
            base = tile[:, 0:nh * 64].rearrange("p (h w) -> p h w", w=64)
            e, o = base[:, :, 0:32], base[:, :, 32:64]
            w = 32
            cb = rope_a[:, 0:32].unsqueeze(1).broadcast_to((128, nh, 32))
            sb = rope_a[:, 32:64].unsqueeze(1).broadcast_to((128, nh, 32))
        else:
            base = tile[:, 0:nh * 64].rearrange("p (h i two) -> p h i two", h=nh, two=2)
            e, o = base[:, :, 0:16, 0:1], base[:, :, 0:16, 1:2]
            w = 16
            cb = rope_a[:, 0:16].unsqueeze(1).unsqueeze(3).broadcast_to((128, nh, 16, 1))
            sb = rope_a[:, 16:32].unsqueeze(1).unsqueeze(3).broadcast_to((128, nh, 16, 1))
        return e, o, w, cb, sb

    def tview(region, nh, w, kind):
        v = region[:, 0:nh * w].rearrange("p (h w) -> p h w", w=w)
        if kind == "gptj":
            v = v.unsqueeze(3)
        return v

    def emit_rope(tile, tname, nh, kind):
        e, o, w, cb, sb = rope_views(tile, nh, kind)
        t1 = tview(rtmp[:, 0:512], nh, w, kind)
        t2 = tview(rtmp[:, 512:1024], nh, w, kind)
        t3 = tview(rtmp[:, 1024:1536], nh, w, kind)
        t4 = tview(rtmp[:, 1536:2048], nh, w, kind)
        p.add("vector", lambda v: v.tensor_tensor(t1, e, cb, ALU.mult),
              reads=R(tname, "rope_a"), writes=R("rt1"))
        p.add("vector", lambda v: v.tensor_tensor(t2, o, sb, ALU.mult),
              reads=R(tname, "rope_a"), writes=R("rt2"))
        p.add("vector", lambda v: v.tensor_tensor(t3, e, sb, ALU.mult),
              reads=R(tname, "rope_a"), writes=R("rt3"))
        p.add("vector", lambda v: v.tensor_tensor(t4, o, cb, ALU.mult),
              reads=R(tname, "rope_a"), writes=R("rt4"))
        if kind == "neox":
            p.add("vector", lambda v: v.tensor_tensor(e, t1, t2, ALU.add),
                  reads=R("rt1", "rt2"), writes=R(tname))
            p.add("vector", lambda v: v.tensor_tensor(o, t4, t3, ALU.subtract),
                  reads=R(tname, "rt3", "rt4"), writes=R(tname))
        else:
            p.add("vector", lambda v: v.tensor_tensor(e, t1, t2, ALU.subtract),
                  reads=R("rt1", "rt2"), writes=R(tname))
            p.add("vector", lambda v: v.tensor_tensor(o, t4, t3, ALU.add),
                  reads=R(tname, "rt3", "rt4"), writes=R(tname))

    def emit_rms_gain(tile, tname, nh, gain_off, extra_scale):
        W = nh * 64
        p.add("scalar", lambda sc: sc.activation(sqd[:, 0:W], tile[:, 0:W], AF.Square),
              reads=R(tname), writes=R("sqd"))
        p.add("vector", lambda v: v.reduce_sum(
            prep_sc[:, 0:nh], sqd[:, 0:W].rearrange("p (h w) -> p h w", w=64), axis=AX.X),
            reads=R("sqd"), writes=R("prep"))
        p.add("vector", lambda v: v.tensor_scalar_mul(prep_sc[:, 16:16 + nh],
                                                      prep_sc[:, 0:nh], 1.0 / 64),
              reads=R("prep"), writes=R("prep"))
        p.add("scalar", lambda sc: sc.activation(prep_sc[:, 32:32 + nh], prep_sc[:, 16:16 + nh],
                                                 AF.Sqrt, bias=small[:, 12:13]),
              reads=R("prep", "small"), writes=R("prep"))
        p.add("vector", lambda v: v.reciprocal(prep_sc[:, 48:48 + nh], prep_sc[:, 32:32 + nh]),
              reads=R("prep"), writes=R("prep"))
        p.add("vector", lambda v: v.tensor_mul(prep_sc[:, 64:64 + nh], prep_sc[:, 48:48 + nh],
                                               gain_bc[:, gain_off:gain_off + nh]),
              reads=R("prep", "gain_bc"), writes=R("prep"))
        if extra_scale != 1.0:
            p.add("vector", lambda v: v.tensor_scalar_mul(prep_sc[:, 64:64 + nh],
                                                          prep_sc[:, 64:64 + nh], extra_scale),
                  reads=R("prep"), writes=R("prep"))
        p.add("vector", lambda v: v.tensor_tensor(
            tile[:, 0:W].rearrange("p (h w) -> p h w", w=64),
            tile[:, 0:W].rearrange("p (h w) -> p h w", w=64),
            prep_sc[:, 64:64 + nh].unsqueeze(2).broadcast_to((128, nh, 64)), ALU.mult),
            reads=R(tname, "prep"), writes=R(tname))

    q_evac_scale = 1.0 if cfg.qk_prep == "l0ffn" else 0.125
    for t in range(NT):
        par = t % 2
        qtile, qn = (qt_a, "qt_a") if par == 0 else (qt_b, "qt_b")
        ktile, kn = (kt_a, "kt_a") if par == 0 else (kt_b, "kt_b")
        vtile, vn = (vt_a, "vt_a") if par == 0 else (vt_b, "vt_b")
        emit_proj(t, wq_sb, "wq_sb", dq, qtile, qn, "bq", q_evac_scale, "scalar", par * 2)
        emit_proj(t, wk_sb, "wk_sb", dkv, ktile, kn, "bk", 1.0, "vector", par * 2 + 1)
        emit_proj(t, wv_sb, "wv_sb", dkv, vtile, vn, "bv", 1.0, "vector", par * 2 + 1)
        if cfg.qk_prep is not None:
            p.add("gpsimd", lambda g, t=t: g.dma_start(
                out=rope_a[:, 0:2 * cfg.nf], in_=ropecs_d[t * 128:(t + 1) * 128, :]),
                writes=R("rope_a"), dma=True)
            if cfg.qk_prep == "l0ffn":
                emit_rms_gain(qtile, qn, H, 0, 0.125)
                emit_rms_gain(ktile, kn, KVH, H, 1.0)
                emit_rope(qtile, qn, H, "neox")
                emit_rope(ktile, kn, KVH, "neox")
            else:
                emit_rope(qtile, qn, H, "gptj")
                emit_rope(ktile, kn, H, "gptj")
        p.add("sync", lambda s, qtile=qtile, t=t: s.dma_start(
            out=qd2.rearrange("h t w -> t h w")[t * 128:(t + 1) * 128][:, :, 0:64],
            in_=qtile[:].rearrange("p (h w) -> p h w", w=64)),
            reads=R(qn), writes=R(f"dram_q{t}"), dma=True)
        p.add("sync", lambda s, ktile=ktile, t=t: s.dma_start(
            out=kd2.rearrange("h t w -> t h w")[t * 128:(t + 1) * 128][:, :, 0:64],
            in_=ktile[:].rearrange("p (h w) -> p h w", w=64)),
            reads=R(kn), writes=R(f"dram_k{t}"), dma=True)
        p.add("sync", lambda s, vtile=vtile, t=t: s.dma_start(
            out=vd.rearrange("h t w -> t h w")[t * 128:(t + 1) * 128],
            in_=vtile[:].rearrange("p (h w) -> p h w", w=HD)),
            reads=R(vn), writes=R(f"dram_v{t}"), dma=True)

    allq = tuple(f"dram_q{t}" for t in range(NT))
    allk = tuple(f"dram_k{t}" for t in range(NT))
    allv = tuple(f"dram_v{t}" for t in range(NT))

    p.add("vector", lambda v: v.memset(VB[:], 1.0), writes=R("VB"))
    NKGMAX = max(T_FULL // 512, 1)
    NQG = T_FULL // 512
    QT2 = wq_sb[:, 0:T_FULL]
    KT2 = wq_sb[:, T_FULL:2 * T_FULL]
    QTs = [(QT[:], "QT"), (QT2, "wq_sb")]
    KTs = [(KT[:], "KT"), (KT2, "wq_sb")]
    if KVH == H:
        VB2 = wk_sb[:, 0:NT * 65]
        VBs = [(VB[:], "VB"), (VB2, "wk_sb")]
        p.add("vector", lambda v: v.memset(VB2, 1.0), writes=R("wk_sb"))
    else:
        VBs = [(VB[:], "VB"), (VBg[:], "VBg")]
        p.add("vector", lambda v: v.memset(VBg[:], 1.0), writes=R("VBg"))

    def kvh_of(h):
        return h * KVH // H

    def qbuf(h):
        return QTs[h % 2]

    def kbuf(h):
        return KTs[kvh_of(h) % len(KTs)]

    def vbuf(h):
        return VBs[kvh_of(h) % len(VBs)]

    def emit_head_loads(h):
        qb, qbn = qbuf(h)
        p.add("sync", lambda s, h=h, qb=qb: s.dma_start(
            out=qb, in_=qd2[h], transpose=True),
            reads=allq, writes=R(qbn), dma=True)
        if h == 0 or kvh_of(h) != kvh_of(h - 1):
            kb, kbn = kbuf(h)
            p.add("sync", lambda s, h=h, kb=kb: s.dma_start(
                out=kb, in_=kd2[kvh_of(h)], transpose=True),
                reads=allk, writes=R(kbn), dma=True)
            p.add("gpsimd", lambda g, kb=kb: g.memset(kb[64:65, :], 1.0),
                  reads=R(kbn), writes=R(kbn))
            vb, vbn = vbuf(h)
            p.add("sync", lambda s, h=h, vb=vb: s.dma_start(
                out=vb.rearrange("p (c w) -> p c w", w=65)[:, :, 0:HD],
                in_=vd[kvh_of(h)].rearrange("(c p) w -> p c w", p=128)),
                reads=allv, writes=R(vbn), dma=True)

    def emit_passA(h, qt_lo, qt_hi):
        hp = h % 2
        qb, qbn = qbuf(h)
        kb, kbn = kbuf(h)
        for qt in range(qt_lo, qt_hi):
            nkg = (qt * 128 + 128 + 511) // 512
            for kg in range(nkg):
                n = min(512, (qt + 1) * 128 - kg * 512)
                bank = kg % 2
                p.add("tensor", lambda tt, qb=qb, kb=kb, qt=qt, kg=kg, n=n, bank=bank:
                      tt.matmul(ps[bank][:, 0:n],
                                qb[0:64, qt * 128:(qt + 1) * 128],
                                kb[0:64, kg * 512:kg * 512 + n],
                                start=True, stop=True),
                      reads=R(qbn, kbn), writes=R(f"ps{bank}"))
                p.add("vector", lambda v, kg=kg, n=n, bank=bank, hp=hp: v.reduce_max(
                    cmx[:, hp * NKGMAX + kg:hp * NKGMAX + kg + 1], ps[bank][:, 0:n], axis=AX.X),
                    reads=R(f"ps{bank}"), writes=R(f"cmx{hp}_{kg}"))
            p.add("vector", lambda v, qt=qt, nkg=nkg, hp=hp: v.reduce_max(
                rm[:, hp * NT + qt:hp * NT + qt + 1], cmx[:, hp * NKGMAX:hp * NKGMAX + nkg],
                axis=AX.X),
                reads=tuple(f"cmx{hp}_{kg}" for kg in range(nkg)), writes=R(f"rm{hp}"))

    def emit_maxrows(h):
        hp = h % 2
        qb, qbn = qbuf(h)
        for qt in range(NT):
            p.add("tensor", lambda tt, qt=qt, hp=hp: tt.matmul(
                ps[7][0:1, 0:128], rm[:, hp * NT + qt:hp * NT + qt + 1], ident[:],
                start=True, stop=True),
                reads=R(f"rm{hp}", "ident"), writes=R("ps7"))
            p.add("scalar", lambda sc, qt=qt, qb=qb: sc.activation(
                qb[64:65, qt * 128:(qt + 1) * 128], ps[7][0:1, 0:128],
                AF.Copy, bias=0.0, scale=-1.0),
                reads=R("ps7"), writes=R(qbn))

    emit_head_loads(0)
    emit_passA(0, 0, NT)
    emit_maxrows(0)
    for h in range(H):
        pair = h // 2
        sub = (h % 2) * 64
        qb, qbn = qbuf(h)
        kb, kbn = kbuf(h)
        vb, vbn = vbuf(h)
        if h + 1 < H:
            emit_head_loads(h + 1)
        for qg in range(NQG):
            nkc = (qg + 1) * 4
            for kc in range(nkc):
                sbank = 4 + (kc % 3)
                pst = ps[sbank]
                ptb, ptn = ((PT_a, "PT_a"), (PT_b, "PT_b"), (PT_c, "PT_c"))[kc % 3]
                p.add("tensor", lambda tt, pst=pst, kc=kc, qg=qg, qb=qb, kb=kb: tt.matmul(
                    pst[:, 0:512],
                    kb[0:65, kc * 128:(kc + 1) * 128],
                    qb[0:65, qg * 512:(qg + 1) * 512],
                    start=True, stop=True),
                    reads=R(kbn, qbn), writes=R(f"ps{sbank}"))
                off = kc * 128 - qg * 512
                if off >= 0:
                    j = off // 128
                    p.add("vector", lambda v, pst=pst, j=j, sbank=sbank: v.tensor_tensor(
                        pst[:, 0:512], pst[:, 0:512], masks[:, j * 512:(j + 1) * 512], ALU.add),
                        reads=R(f"ps{sbank}", "masks"), writes=R(f"ps{sbank}"))
                p.add("scalar", lambda sc, pst=pst, ptb=ptb: sc.activation(
                    ptb[:], pst[:, 0:512], AF.Exp),
                    reads=R(f"ps{sbank}"), writes=R(ptn))
                p.add("tensor", lambda tt, kc=kc, ptb=ptb, nkc=nkc, vb=vb: tt.matmul(
                    ps[2][0:65, 0:512], vb[:, kc * 65:(kc + 1) * 65], ptb[:],
                    start=(kc == 0), stop=(kc == nkc - 1)),
                    reads=R(vbn, ptn), writes=R("ps2"))
            if h + 1 < H:
                emit_passA(h + 1, qg * 4, qg * 4 + 4)
            p.add("vector", lambda v: v.reciprocal(rs_row[:], ps[2][64:65, 0:512]),
                  reads=R("ps2"), writes=R("rs_row"))
            p.add("tensor", lambda tt: tt.matmul(ps[3][0:64, 0:512], onesf[0:1, 0:64],
                                                 rs_row[:], start=True, stop=True),
                  reads=R("onesf", "rs_row"), writes=R("ps3"))
            p.add("scalar", lambda sc: sc.copy(otmp[:], ps[2][0:64, 0:512]),
                  reads=R("ps2"), writes=R("otmp"))
            p.add("vector", lambda v, pair=pair, sub=sub, qg=qg: v.tensor_tensor(
                big[sub:sub + 64, pair * T_FULL + qg * 512: pair * T_FULL + (qg + 1) * 512],
                otmp[:], ps[3][0:64, 0:512], ALU.mult),
                reads=R("otmp", "ps3"), writes=R("big"))
        if h + 1 < H:
            emit_maxrows(h + 1)

    p.add("gpsimd", lambda g: g.dma_start(
        out=wq_sb[:].rearrange("p (c o) -> p c o", o=D)[:, :, :],
        in_=wo_d.rearrange("(c p) o -> p c o", p=128)),
        writes=R("wq_sb"), dma=True)

    if cfg.post_ln:
        p.add("gpsimd", lambda g: g.dma_start(out=prow[0:1, 0:D], in_=pg_d),
              writes=R("prow"), dma=True)
        p.add("gpsimd", lambda g: g.dma_start(out=prow[0:1, D:2 * D], in_=pb_d),
              writes=R("prow"), dma=True)
        for c in range(4):
            p.add("tensor", lambda tt, c=c: tt.matmul(
                ps[7][:, 0:512], onesb[0:1, 0:128], prow[0:1, c * 512:(c + 1) * 512],
                start=True, stop=True), reads=R("onesb", "prow"), writes=R("ps7"))
            p.add("vector", lambda v, c=c: v.tensor_copy(
                gb_bc[:, c * 512:(c + 1) * 512], ps[7][:, 0:512]),
                reads=R("ps7"), writes=R("gb_bc"))

    for t in range(NT):
        par = t % 2
        xf, xfn = (xf_a, "xf_a") if par == 0 else (xf_b, "xf_b")
        yt, ytn = (yt_a, "yt_a") if par == 0 else (yt_b, "yt_b")
        sml, smln = (small, "small") if par == 0 else (small2, "small2")
        p.add("gpsimd", lambda g, xf=xf, t=t: g.dma_start(
            out=xf[:], in_=xin[t * 128:(t + 1) * 128, :]),
            writes=R(xfn), dma=True)
        for half in range(2):
            bank = 4 + half
            pst = ps[bank]
            has_bo = "bo" in bias_sb

            def mmY(tt, t=t, half=half, pst=pst, has_bo=has_bo):
                for c in range(8):
                    inst = tt.matmul(
                        pst[:, 0:512],
                        big[:, c * T_FULL + t * 128: c * T_FULL + (t + 1) * 128],
                        wq_sb[:, c * D + half * 512: c * D + half * 512 + 512],
                        start=(c == 0), stop=(c == 7 and not has_bo))
                return inst
            p.add("tensor", mmY, reads=R("big", "wq_sb"), writes=R(f"ps{bank}"))
            if has_bo:
                p.add("tensor", lambda tt, pst=pst, half=half: tt.matmul(
                    pst[:, 0:512], onesb[0:1, 0:128],
                    bias_sb["bo"][0:1, half * 512:half * 512 + 512],
                    start=False, stop=True), reads=R("onesb", "bo"), writes=R(f"ps{bank}"))
        if cfg.post_ln:
            for half in range(2):
                pst = ps[4 + half]
                p.add("vector", lambda v, yt=yt, half=half, pst=pst, sml=sml: v.tensor_scalar(
                    yt[:, half * 512:half * 512 + 512], pst[:, 0:512], 1.0, 0.0,
                    ALU.mult, ALU.add, accum_out=sml[:, half:half + 1]),
                    reads=R(f"ps{4 + half}"), writes=R(ytn, smln))
            emit_ln_tok(p, yt, ytn, sqd, sml, sml[:, 11:12], yt, ytn, smn=smln)
            p.add("vector", lambda v, yt=yt: v.tensor_tensor(
                yt[:], yt[:], gb_bc[:, 0:D], ALU.mult),
                reads=R(ytn, "gb_bc"), writes=R(ytn))
            p.add("vector", lambda v, yt=yt: v.tensor_tensor(
                yt[:], yt[:], gb_bc[:, D:2 * D], ALU.add),
                reads=R(ytn, "gb_bc"), writes=R(ytn))
            p.add("vector", lambda v, yt=yt, xf=xf, sml=sml: v.scalar_tensor_tensor(
                yt[:], yt[:], 1.0, xf[:], ALU.mult, ALU.add, accum_out=sml[:, 0:1]),
                reads=R(ytn, xfn), writes=R(ytn, smln))
            p.add("vector", lambda v, sml=sml: v.memset(sml[:, 1:2], 0.0), writes=R(smln))
        else:
            for half in range(2):
                pst = ps[4 + half]
                p.add("vector", lambda v, yt=yt, xf=xf, half=half, pst=pst, sml=sml:
                      v.scalar_tensor_tensor(
                          yt[:, half * 512:half * 512 + 512], pst[:, 0:512], 1.0,
                          xf[:, half * 512:half * 512 + 512],
                          ALU.mult, ALU.add, accum_out=sml[:, half:half + 1]),
                      reads=R(f"ps{4 + half}", xfn), writes=R(ytn, smln))
        if cfg.epilogue == "inner":
            p.add("sync", lambda s, yt=yt, t=t: s.dma_start(
                out=xmid_d[t * 128:(t + 1) * 128, :], in_=yt[:]),
                reads=R(ytn), writes=R(f"dram_out{t}"), dma=True)
            continue
        emit_ln_tok(p, yt, ytn, sqd, sml, sml[:, 11:12], yt, ytn, smn=smln)
        p.add("sync", lambda s, yt=yt, t=t: s.dma_start(
            out=xout[t * 128:(t + 1) * 128, :], in_=yt[:]),
            reads=R(ytn), writes=R(f"dram_out{t}"), dma=True)

    if standalone:
        p.emit(es)
        es.close()
        return nc
    p.fence()
    es.close()
    return None



def _rope_neox_tables():
    inv = 1.0 / 10000.0 ** (np.arange(0, HD, 2, dtype=np.float32) / HD)
    fr = np.outer(np.arange(T_FULL, dtype=np.float32), inv)
    return np.concatenate([np.cos(fr), np.sin(fr)], axis=1).astype(np.float32)


def _rope_gptj_tables():
    inv = 1.0 / 10000.0 ** (np.arange(0, 32, 2, dtype=np.float32) / 32.0)
    fr = np.outer(np.arange(T_FULL, dtype=np.float32), inv)
    return np.concatenate([np.cos(fr), np.sin(fr)], axis=1).astype(np.float32)


L3_ATTN_CFG = AttnCfg("L3_attn", has_bq=True, has_bkv=True, has_bo=True)


def prep_l3_attn(inp):
    io = dict(CONSTS)
    io["wq"] = np.ascontiguousarray(inp["L3_attn_wq"])
    io["wk"] = np.ascontiguousarray(inp["L3_attn_wk"])
    io["wv"] = np.ascontiguousarray(inp["L3_attn_wv"])
    io["wo"] = np.ascontiguousarray(inp["L3_attn_wo"])
    for bn, key in (("bq", "L3_attn_bq"), ("bk", "L3_attn_bk"),
                    ("bv", "L3_attn_bv"), ("bo", "L3_attn_bo")):
        io[bn] = np.ascontiguousarray(inp[key]).reshape(1, -1)
    return io


L2_ATTN_CFG = AttnCfg("L2_attn", has_bq=True, has_bkv=True, has_bo=True, post_ln=True)


def prep_l2_attn(inp):
    io = dict(CONSTS)
    io["wq"] = np.ascontiguousarray(inp["L2_attn_Wq"])
    io["wk"] = np.ascontiguousarray(inp["L2_attn_Wk"])
    io["wv"] = np.ascontiguousarray(inp["L2_attn_Wv"])
    io["wo"] = np.ascontiguousarray(inp["L2_attn_Wo"])
    for bn, key in (("bq", "L2_attn_bq"), ("bk", "L2_attn_bk"),
                    ("bv", "L2_attn_bv"), ("bo", "L2_attn_bo")):
        io[bn] = np.ascontiguousarray(inp[key]).reshape(1, -1)
    io["post_gamma"] = np.ascontiguousarray(inp["L2_attn_gamma"]).reshape(1, D)
    io["post_beta"] = np.ascontiguousarray(inp["L2_attn_beta"]).reshape(1, D)
    return io


L1_ATTN_CFG = AttnCfg("L1_attn", qk_prep="l1attn", nf=16)


def prep_l1_attn(inp):
    io = dict(CONSTS)
    wqkvT = np.ascontiguousarray(np.asarray(inp["L1_attn_w_qkv"]).T)
    wq = np.empty((D, D), np.float32)
    wv = np.empty((D, D), np.float32)
    wk = np.empty((D, D), np.float32)
    for h in range(16):
        mp, j = h // 4, h % 4
        wq[:, h * 64:(h + 1) * 64] = wqkvT[:, mp * 768 + j * 64: mp * 768 + (j + 1) * 64]
        wv[:, h * 64:(h + 1) * 64] = wqkvT[:, mp * 768 + 256 + j * 64: mp * 768 + 256 + (j + 1) * 64]
        wk[:, h * 64:(h + 1) * 64] = wqkvT[:, mp * 768 + 512 + j * 64: mp * 768 + 512 + (j + 1) * 64]
    io["wq"], io["wk"], io["wv"] = wq, wk, wv
    io["wo"] = np.ascontiguousarray(np.asarray(inp["L1_attn_w_out"]).T)
    io["rope_cs"] = _rope_gptj_tables()
    return io


L0_FFN_ATTN_CFG = AttnCfg("L0_ffn_attn", kv_heads=4, dkv=256, qk_prep="l0ffn",
                          pre_norm="rms", epilogue="inner", gains=True, nf=32,
                          prep_eps=1e-6)


def prep_l0_ffn_attn(inp):
    io = dict(CONSTS)
    io["wq"] = np.ascontiguousarray(np.asarray(inp["L0_ffn_wq"]).T)
    io["wk"] = np.ascontiguousarray(np.asarray(inp["L0_ffn_wk"]).T)
    io["wv"] = np.ascontiguousarray(np.asarray(inp["L0_ffn_wv"]).T)
    io["wo"] = np.ascontiguousarray(np.asarray(inp["L0_ffn_wo"]).T)
    io["q_gain"] = np.ascontiguousarray(inp["L0_ffn_q_gain"]).reshape(1, 16)
    io["k_gain"] = np.ascontiguousarray(inp["L0_ffn_k_gain"]).reshape(1, 4)
    io["rope_cs"] = _rope_neox_tables()
    return io


L1_FFN_ATTN_CFG = AttnCfg("L1_ffn_attn", has_bo=True, pre_norm="ln", epilogue="inner",
                          prep_eps=1e-5, fold_gb=True)


def prep_l1_ffn_attn(inp):
    io = dict(CONSTS)
    io["wq"] = np.ascontiguousarray(inp["L1_ffn_wq"])
    io["wk"] = np.ascontiguousarray(inp["L1_ffn_wk"])
    io["wv"] = np.ascontiguousarray(inp["L1_ffn_wv"])
    io["wo"] = np.ascontiguousarray(inp["L1_ffn_wo"])
    io["bo"] = np.ascontiguousarray(inp["L1_ffn_bo"]).reshape(1, D)
    io["ln_gam_col"] = np.ascontiguousarray(
        np.asarray(inp["L1_ffn_ln1_scale"]).reshape(8, 128).T.astype(np.float32))
    io["ln_beta_col"] = np.ascontiguousarray(
        np.asarray(inp["L1_ffn_ln1_shift"]).reshape(8, 128).T).astype(BF)
    return io


def prep_l0_attn(inp):
    io = dict(CONSTS)
    io["wq"] = np.ascontiguousarray(inp["L0_attn_Wq"])
    io["wk"] = np.ascontiguousarray(inp["L0_attn_Wk"])
    io["wv"] = np.ascontiguousarray(inp["L0_attn_Wv"])
    io["wo"] = np.ascontiguousarray(inp["L0_attn_Wo"])
    io["bo"] = np.ascontiguousarray(inp["L0_attn_bo"]).reshape(1, D)
    return io


L0_ATTN_CFG = AttnCfg("L0_attn", has_bo=True)



def _run8(nc, common, per_core=None, extra=None):
    in_maps = []
    for c in range(8):
        m = dict(common)
        if per_core is not None:
            m.update(per_core[c])
        in_maps.append(m)
    res = run_bass_kernel_spmd(nc, in_maps, core_ids=list(range(8)))
    return res.results


def kernel(**inputs):
    x = np.asarray(inputs["L0_attn_x"])
    shards = [np.ascontiguousarray(x[c]) for c in range(8)]

    nc = bass.Bass("TRN2", target_bir_lowering=False)
    es = contextlib.ExitStack()
    p = Prog(nc)
    s_in = nc.dram_tensor("stream_in", [T_FULL, D], F32, kind="ExternalInput").ap()
    stream = {0: s_in}
    for i in range(1, 7):
        stream[i] = nc.dram_tensor(f"s{i}", [T_FULL, D], F32, kind="Internal").ap()
    s7 = nc.dram_tensor("s7_out", [T_FULL, D], F32, kind="ExternalOutput").ap()
    m0 = nc.dram_tensor("m0", [T_FULL, D], F32, kind="Internal").ap()
    m1 = nc.dram_tensor("m1", [T_FULL, D], F32, kind="Internal").ap()
    selw = nc.dram_tensor("selw_out", [T_FULL, 32], F32, kind="ExternalOutput").ap()
    lpos = nc.dram_tensor("lpos_out", [T_FULL, 32], F32, kind="ExternalOutput").ap()
    cnt = nc.dram_tensor("cnt_out", [1, 32], F32, kind="ExternalOutput").ap()

    def FU(pfx, rd):
        return dict(nc=nc, p=p, pfx=pfx, rd=rd)

    build_attn_launch(L0_ATTN_CFG, FU("l0a_", {"xin": s_in, "xout": stream[1]}))
    build_attn_launch(L0_FFN_ATTN_CFG, FU("l0fa_", {"xin": stream[1], "xmid": m0}))
    build_l0ffn_bank_launch(FU("l0fb_", {"xin": stream[1], "xmid": m0, "xout": stream[2]}))
    build_attn_launch(L1_ATTN_CFG, FU("l1a_", {"xin": stream[2], "xout": stream[3]}))
    build_attn_launch(L1_FFN_ATTN_CFG, FU("l1fa_", {"xin": stream[3], "xmid": m1}))
    build_l1ffn_moe_launch(FU("l1fb_", {"xin": stream[3], "xmid": m1, "xout": stream[4]}))
    build_attn_launch(L2_ATTN_CFG, FU("l2a_", {"xin": stream[4], "xout": stream[5]}))
    build_l2ffn_launch(FU("l2f_", {"xin": stream[5], "xout": stream[6]}))
    build_attn_launch(L3_ATTN_CFG, FU("l3a_", {"xin": stream[6], "xout": s7}))
    build_l3ffn_a_launch(FU("l3fa_", {"xin": s7, "selw": selw, "lpos": lpos, "cnt": cnt}))
    p.emit(es)
    es.close()

    common = {}
    for pfx, prep in (("l0a_", prep_l0_attn(inputs)),
                      ("l0fa_", prep_l0_ffn_attn(inputs)),
                      ("l0fb_", prep_l0_ffn_bank(inputs)),
                      ("l1a_", prep_l1_attn(inputs)),
                      ("l1fa_", prep_l1_ffn_attn(inputs)),
                      ("l1fb_", prep_l1_ffn_moe(inputs)),
                      ("l2a_", prep_l2_attn(inputs)),
                      ("l2f_", prep_l2_ffn(inputs)),
                      ("l3a_", prep_l3_attn(inputs)),
                      ("l3fa_", prep_l3_ffn_a(inputs))):
        for k, v in prep.items():
            common[pfx + k] = v
    r1 = _run8(nc, common, [{"stream_in": shards[c]} for c in range(8)])

    cnts = np.stack([r1[c]["cnt_out"][0] for c in range(8)], axis=0)
    pcb = []
    for c in range(8):
        pm = np.zeros((8, 1), np.float32)
        pm[:c] = 1.0
        pcb.append({"xin": np.ascontiguousarray(r1[c]["s7_out"]),
                    "selw": np.ascontiguousarray(r1[c]["selw_out"]),
                    "lpos": np.ascontiguousarray(r1[c]["lpos_out"]),
                    "gathered": cnts, "premask": pm})
    r = _run8(build_l3ffn_b_launch(), prep_l3_ffn_b(inputs), pcb)
    return np.stack([r[c]["xout"] for c in range(8)], axis=0)



def _emit_mirror_nT(p, nc, es, xin, nbf, big, xa, xb, xf_a, xf_b, sqd, small,
                    pre_norm=None, prep_eps=1e-5, src_name="xin"):
    if pre_norm is not None:
        p.add("vector", lambda v: v.memset(small[:, 12:13], float(prep_eps)),
              writes=R("small"))
    for t in range(NT):
        par = t % 2
        buf, bname = (xa, "xa") if par == 0 else (xb, "xb")
        if pre_norm is None:
            p.add("gpsimd", lambda g, buf=buf, t=t: g.dma_start(
                out=buf[:], in_=xin[t * 128:(t + 1) * 128, :]),
                writes=R(bname), dma=True)
        else:
            xf, xfn = (xf_a, "xf_a") if par == 0 else (xf_b, "xf_b")
            p.add("gpsimd", lambda g, xf=xf, t=t: g.dma_start(
                out=xf[:], in_=xin[t * 128:(t + 1) * 128, :]),
                writes=R(xfn), dma=True)
            p.add("scalar", lambda sc, xf=xf: sc.activation(
                sqd[:], xf[:], AF.Square, accum_out=small[:, 4:5]),
                reads=R(xfn), writes=R("sqd", "small"))
            if pre_norm == "rms":
                p.add("vector", lambda v: v.tensor_scalar_mul(small[:, 5:6], small[:, 4:5], 1.0 / D),
                      reads=R("small"), writes=R("small"))
                p.add("scalar", lambda sc: sc.activation(small[:, 6:7], small[:, 5:6],
                                                         AF.Sqrt, bias=small[:, 12:13]),
                      reads=R("small"), writes=R("small"))
                p.add("vector", lambda v: v.reciprocal(small[:, 7:8], small[:, 6:7]),
                      reads=R("small"), writes=R("small"))
                p.add("scalar", lambda sc, xf=xf, buf=buf: sc.activation(
                    buf[:], xf[:], AF.Identity, bias=0.0, scale=small[:, 7:8]),
                    reads=R(xfn, "small"), writes=R(bname))
            else:
                p.add("scalar", lambda sc, xf=xf: sc.activation(
                    sqd[:], xf[:], AF.Identity, accum_out=small[:, 0:1]),
                    reads=R(xfn), writes=R("sqd", "small"))
                p.add("vector", lambda v: v.memset(small[:, 1:2], 0.0),
                      writes=R("small"))
                emit_ln_tok(p, xf, xfn, sqd, small, small[:, 12:13], xf, xfn)
                p.add("vector", lambda v, xf=xf, buf=buf: v.tensor_copy(buf[:], xf[:]),
                      reads=R(xfn), writes=R(bname))
        p.add("sync", lambda s, buf=buf, t=t: s.dma_start(
            out=nbf[t * 128:(t + 1) * 128, :], in_=buf[:]),
            reads=R(bname), writes=R(f"dram_nbf{t}"), dma=True)
    if big is None:
        return
    for c in range(8):
        p.add("sync", lambda s, c=c: s.dma_start(
            out=big[:, c * T_FULL:(c + 1) * T_FULL],
            in_=nbf[:, c * 128:(c + 1) * 128], transpose=True),
            reads=tuple(f"dram_nbf{t}" for t in range(NT)), writes=R("big"), dma=True)


def _emit_top2_gate(p, gsc, wgt, t, mode, rs_bc=None):
    G = "gsc"
    p.add("vector", lambda v: v.reduce_max(gsc[:, 33:34], gsc[:, 0:8], axis=AX.X),
          reads=R(G), writes=R(G))
    p.add("vector", lambda v: v.tensor_single_scalar(gsc[:, 8:16], gsc[:, 0:8],
                                                     gsc[:, 33:34], ALU.is_ge),
          reads=R(G), writes=R(G))
    p.add("vector", lambda v: v.scalar_tensor_tensor(gsc[:, 16:24], gsc[:, 8:16], -1e30,
                                                     gsc[:, 0:8], ALU.mult, ALU.add),
          reads=R(G), writes=R(G))
    p.add("vector", lambda v: v.reduce_max(gsc[:, 34:35], gsc[:, 16:24], axis=AX.X),
          reads=R(G), writes=R(G))
    p.add("vector", lambda v: v.tensor_single_scalar(gsc[:, 24:32], gsc[:, 16:24],
                                                     gsc[:, 34:35], ALU.is_ge),
          reads=R(G), writes=R(G))
    if mode == "softmax":
        p.add("vector", lambda v: v.tensor_sub(gsc[:, 35:36], gsc[:, 33:34], gsc[:, 34:35]),
              reads=R(G), writes=R(G))
        p.add("scalar", lambda sc: sc.activation(gsc[:, 36:37], gsc[:, 35:36], AF.Sigmoid),
              reads=R(G), writes=R(G))
        p.add("vector", lambda v: v.tensor_scalar(gsc[:, 37:38], gsc[:, 36:37], -1.0, 1.0,
                                                  ALU.mult, ALU.add),
              reads=R(G), writes=R(G))
        p.add("vector", lambda v, t=t: v.tensor_scalar(wgt[:, t * 8:(t + 1) * 8],
                                                       gsc[:, 8:16], gsc[:, 36:37], None,
                                                       ALU.mult),
              reads=R(G), writes=R("wgt"))
        p.add("vector", lambda v, t=t: v.scalar_tensor_tensor(
            wgt[:, t * 8:(t + 1) * 8], gsc[:, 24:32], gsc[:, 37:38],
            wgt[:, t * 8:(t + 1) * 8], ALU.mult, ALU.add),
            reads=R(G, "wgt"), writes=R("wgt"))
    else:
        p.add("vector", lambda v: v.tensor_add(gsc[:, 35:36], gsc[:, 33:34], gsc[:, 34:35]),
              reads=R(G), writes=R(G))
        p.add("vector", lambda v: v.reciprocal(gsc[:, 36:37], gsc[:, 35:36]),
              reads=R(G), writes=R(G))
        p.add("vector", lambda v: v.tensor_add(gsc[:, 24:32], gsc[:, 8:16], gsc[:, 24:32]),
              reads=R(G), writes=R(G))
        p.add("vector", lambda v: v.tensor_mul(gsc[:, 24:32], gsc[:, 24:32], gsc[:, 0:8]),
              reads=R(G), writes=R(G))
        p.add("vector", lambda v, t=t: v.tensor_scalar(wgt[:, t * 8:(t + 1) * 8],
                                                       gsc[:, 24:32], gsc[:, 36:37], None,
                                                       ALU.mult),
              reads=R(G), writes=R("wgt"))


def _emit_final_epilogue(p, nc, t, acc_d, xin, xmid, xf, xfn, x1f, x1fn, yt, ytn,
                         sqd, small, xout, acc_reads=()):
    p.add("gpsimd", lambda g, yt=yt, t=t: g.dma_start(
        out=yt[:], in_=acc_d[t * 128:(t + 1) * 128, :]),
        reads=tuple(acc_reads), writes=R(ytn), dma=True)
    p.add("gpsimd", lambda g, xf=xf, t=t: g.dma_start(
        out=xf[:], in_=xin[t * 128:(t + 1) * 128, :]),
        writes=R(xfn), dma=True)
    if xmid is not None:
        p.add("gpsimd", lambda g, x1f=x1f, t=t: g.dma_start(
            out=x1f[:], in_=xmid[t * 128:(t + 1) * 128, :]),
            writes=R(x1fn), dma=True)
        p.add("vector", lambda v, yt=yt, x1f=x1f: v.tensor_add(yt[:], yt[:], x1f[:]),
              reads=R(ytn, x1fn), writes=R(ytn))
    p.add("vector", lambda v, yt=yt, xf=xf: v.scalar_tensor_tensor(
        yt[:], yt[:], 1.0, xf[:], ALU.mult, ALU.add, accum_out=small[:, 0:1]),
        reads=R(ytn, xfn), writes=R(ytn, "small"))
    p.add("vector", lambda v: v.memset(small[:, 1:2], 0.0), writes=R("small"))
    emit_ln_tok(p, yt, ytn, sqd, small, small[:, 11:12], yt, ytn)
    p.add("sync", lambda s, yt=yt, t=t: s.dma_start(
        out=xout[t * 128:(t + 1) * 128, :], in_=yt[:]),
        reads=R(ytn), writes=R(f"dram_out{t}"), dma=True)


def _load_nTg(p, nbf, tg, buf, bufname):
    for c in range(8):
        p.add("sync", lambda s, tg=tg, c=c, buf=buf: s.dma_start(
            out=buf[:, c * 512:(c + 1) * 512],
            in_=nbf[tg * 512:(tg + 1) * 512, c * 128:(c + 1) * 128], transpose=True),
            reads=tuple(f"dram_nbf{t}" for t in range(tg * 4, tg * 4 + 4)),
            writes=R(bufname), dma=True)


def build_l2ffn_launch(FUSE=None):
    standalone = FUSE is None
    nc = (bass.Bass("TRN2", target_bir_lowering=False) if standalone
          else NCProxy(FUSE["nc"], FUSE["pfx"], FUSE["rd"]))
    es = contextlib.ExitStack()
    E, HH = 8, 512

    xin = nc.dram_tensor("xin", [T_FULL, D], F32, kind="ExternalInput").ap()
    xout = nc.dram_tensor("xout", [T_FULL, D], F32, kind="ExternalOutput").ap()
    onesb_d = nc.dram_tensor("c_onesb", [1, 512], BF16, kind="ExternalInput").ap()
    onesf_d = nc.dram_tensor("c_onesf", [1, 128], F32, kind="ExternalInput").ap()
    gw_d = nc.dram_tensor("gate_W", [D, E], F32, kind="ExternalInput").ap()
    w1_d = nc.dram_tensor("W1", [E, D, HH], F32, kind="ExternalInput").ap()
    b1_d = nc.dram_tensor("b1", [1, E * HH], F32, kind="ExternalInput").ap()
    w2_d = nc.dram_tensor("W2", [E, HH, D], F32, kind="ExternalInput").ap()
    b2_d = nc.dram_tensor("b2", [1, E * D], F32, kind="ExternalInput").ap()
    ws1_d = nc.dram_tensor("Ws1", [D, HH], F32, kind="ExternalInput").ap()
    bs1_d = nc.dram_tensor("bs1", [1, HH], F32, kind="ExternalInput").ap()
    ws2_d = nc.dram_tensor("Ws2", [HH, D], F32, kind="ExternalInput").ap()
    bs2_d = nc.dram_tensor("bs2", [1, D], F32, kind="ExternalInput").ap()
    rsc_d = nc.dram_tensor("route_scale", [1, 1], F32, kind="ExternalInput").ap()

    nbf = nc.dram_tensor("nbf", [T_FULL, D], BF16, kind="Internal").ap()
    acc_d = nc.dram_tensor("acc", [T_FULL, D], F32, kind="Internal").ap()

    big = es.enter_context(nc.sbuf_tensor("big", [128, 8 * T_FULL], BF16))
    hsb = es.enter_context(nc.sbuf_tensor("hsb", [128, 4 * T_FULL], BF16))
    w1a = es.enter_context(nc.sbuf_tensor("w1a", [128, 8 * HH], BF16))
    w1b = es.enter_context(nc.sbuf_tensor("w1b", [128, 8 * HH], BF16))
    w2a = es.enter_context(nc.sbuf_tensor("w2a", [128, 4 * D], BF16))
    w2b = es.enter_context(nc.sbuf_tensor("w2b", [128, 4 * D], BF16))
    gw_sb = es.enter_context(nc.sbuf_tensor("gw_sb", [128, 8 * E], BF16))
    b1row = es.enter_context(nc.sbuf_tensor("b1row", [1, E * HH], BF16))
    b2row = es.enter_context(nc.sbuf_tensor("b2row", [1, E * D], BF16))
    bs1row = es.enter_context(nc.sbuf_tensor("bs1row", [1, HH], BF16))
    bs2row = es.enter_context(nc.sbuf_tensor("bs2row", [1, D], BF16))
    onesb = es.enter_context(nc.sbuf_tensor("onesb", [1, 512], BF16))
    onesf = es.enter_context(nc.sbuf_tensor("onesf", [1, 128], F32))
    rsc_sb = es.enter_context(nc.sbuf_tensor("rsc_sb", [1, 1], F32))
    rs_bc = es.enter_context(nc.sbuf_tensor("rs_bc", [128, 1], F32))
    gsc = es.enter_context(nc.sbuf_tensor("gsc", [128, 40], F32))
    wgt = es.enter_context(nc.sbuf_tensor("wgt", [128, NT * 8], F32))
    xa = es.enter_context(nc.sbuf_tensor("xa", [128, D], BF16))
    xb = es.enter_context(nc.sbuf_tensor("xb", [128, D], BF16))
    xf_a = es.enter_context(nc.sbuf_tensor("xf_a", [128, D], F32))
    xf_b = es.enter_context(nc.sbuf_tensor("xf_b", [128, D], F32))
    yt_a = es.enter_context(nc.sbuf_tensor("yt_a", [128, D], F32))
    yt_b = es.enter_context(nc.sbuf_tensor("yt_b", [128, D], F32))
    sqd = es.enter_context(nc.sbuf_tensor("sqd", [128, D], F32))
    small = es.enter_context(nc.sbuf_tensor("small", [128, 16], F32))
    stmp = es.enter_context(nc.sbuf_tensor("stmp", [128, 4 * 512], F32))

    ps = [es.enter_context(nc.psum_tensor(f"ps{i}", [128, 512], F32)) for i in range(8)]
    p = Prog(nc) if standalone else FUSE["p"]

    p.add("vector", lambda v: v.memset(small[:, 11:12], EPS_TOP), writes=R("small"))
    p.add("gpsimd", lambda g: g.dma_start(out=onesb[:], in_=onesb_d), writes=R("onesb"), dma=True)
    p.add("gpsimd", lambda g: g.dma_start(out=onesf[:], in_=onesf_d), writes=R("onesf"), dma=True)
    p.add("gpsimd", lambda g: g.dma_start(
        out=gw_sb[:].rearrange("p (c e) -> p c e", e=E),
        in_=gw_d.rearrange("(c p) e -> p c e", p=128)), writes=R("gw"), dma=True)
    for nm, sb, dd in (("b1row", b1row, b1_d), ("b2row", b2row, b2_d),
                       ("bs1row", bs1row, bs1_d), ("bs2row", bs2row, bs2_d),
                       ("rsc", rsc_sb, rsc_d)):
        p.add("gpsimd", lambda g, sb=sb, dd=dd: g.dma_start(out=sb[:], in_=dd),
              writes=R(nm), dma=True)

    _emit_mirror_nT(p, nc, es, xin, nbf, big, xa, xb, xf_a, xf_b, sqd, small)

    p.add("tensor", lambda t: t.matmul(ps[6][0:128, 0:1], onesf[0:1, :], rsc_sb[:],
                                       start=True, stop=True),
          reads=R("onesf", "rsc"), writes=R("ps6"))
    p.add("vector", lambda v: v.tensor_copy(rs_bc[:], ps[6][0:128, 0:1]),
          reads=R("ps6"), writes=R("rs_bc"))

    for t in range(NT):
        bank = 6 + (t % 2)
        pst = ps[bank]

        def mm_gate(tt, t=t, pst=pst):
            for c in range(8):
                inst = tt.matmul(pst[:, 0:E],
                                 big[:, c * T_FULL + t * 128: c * T_FULL + (t + 1) * 128],
                                 gw_sb[:, c * E:(c + 1) * E],
                                 start=(c == 0), stop=(c == 7))
            return inst
        p.add("tensor", mm_gate, reads=R("big", "gw"), writes=R(f"ps{bank}"))
        p.add("scalar", lambda sc, pst=pst: sc.activation(gsc[:, 0:8], pst[:, 0:E], AF.Sigmoid),
              reads=R(f"ps{bank}"), writes=R("gsc"))
        p.add("vector", lambda v: v.tensor_scalar(gsc[:, 0:8], gsc[:, 0:8], rs_bc[:, 0:1],
                                                  None, ALU.mult),
              reads=R("gsc", "rs_bc"), writes=R("gsc"))
        _emit_top2_gate(p, gsc, wgt, t, "norm")

    def expert_pass(tag, w1sb, w1n, w2sb, w2n, brow1, brow1_off, brow2, brow2_off,
                    act, wcol, first):
        for tg in range(T_FULL // 512):
            for mo in range(4):
                bank = tg % 2
                pst = ps[bank]

                def mm_h(tt, tg=tg, mo=mo, pst=pst, w1sb=w1sb):
                    for c in range(8):
                        tt.matmul(pst[:, 0:512],
                                  w1sb[:, c * HH + mo * 128: c * HH + mo * 128 + 128],
                                  big[:, c * T_FULL + tg * 512: c * T_FULL + (tg + 1) * 512],
                                  start=(c == 0), stop=False)
                    return tt.matmul(pst[:, 0:512],
                                     brow1[0:1, brow1_off + mo * 128: brow1_off + mo * 128 + 128],
                                     onesb[0:1, 0:512], start=False, stop=True)
                p.add("tensor", mm_h, reads=R("big", w1n, "b1row", "bs1row", "onesb"),
                      writes=R(f"ps{bank}"))
                p.add("scalar", lambda sc, tg=tg, mo=mo, pst=pst: sc.activation(
                    hsb[:, mo * T_FULL + tg * 512: mo * T_FULL + (tg + 1) * 512],
                    pst[:, 0:512], act),
                    reads=R(f"ps{bank}"), writes=R("hsb"))
        for t in range(NT):
            for half in range(2):
                bank = 2 + (t % 2) * 2 + half
                pst = ps[bank]

                def mm_y(tt, t=t, half=half, pst=pst, w2sb=w2sb):
                    for mo in range(4):
                        tt.matmul(pst[:, 0:512],
                                  hsb[:, mo * T_FULL + t * 128: mo * T_FULL + (t + 1) * 128],
                                  w2sb[:, mo * D + half * 512: mo * D + half * 512 + 512],
                                  start=(mo == 0), stop=False)
                    return tt.matmul(pst[:, 0:512], onesb[0:1, 0:128],
                                     brow2[0:1, brow2_off + half * 512: brow2_off + half * 512 + 512],
                                     start=False, stop=True)
                p.add("tensor", mm_y, reads=R("hsb", w2n, "b2row", "bs2row", "onesb"),
                      writes=R(f"ps{bank}"))
                sl = (t * 2 + half) % 4
                if wcol is None:
                    p.add("vector", lambda v, pst=pst, sl=sl: v.tensor_copy(
                        stmp[:, sl * 512:(sl + 1) * 512], pst[:, 0:512]),
                        reads=R(f"ps{bank}"), writes=R(f"stmp{sl}"))
                else:
                    p.add("vector", lambda v, pst=pst, t=t, wcol=wcol, sl=sl: v.tensor_scalar(
                        stmp[:, sl * 512:(sl + 1) * 512], pst[:, 0:512],
                        wgt[:, t * 8 + wcol: t * 8 + wcol + 1], None, ALU.mult),
                        reads=R(f"ps{bank}", "wgt"), writes=R(f"stmp{sl}"))
                p.add("gpsimd", lambda g, t=t, half=half, first=first, sl=sl: g.dma_start(
                    out=acc_d[t * 128:(t + 1) * 128, half * 512:half * 512 + 512],
                    in_=stmp[:, sl * 512:(sl + 1) * 512],
                    accum_op=(ALU.bypass if first else ALU.add)),
                    reads=R(f"stmp{sl}"), writes=R(f"acc{t}_{half}"), dma=True)

    p.add("gpsimd", lambda g: g.dma_start(
        out=w1a[:].rearrange("p (c o) -> p c o", o=HH),
        in_=ws1_d.rearrange("(c p) o -> p c o", p=128)), writes=R("w1a"), dma=True)
    p.add("gpsimd", lambda g: g.dma_start(
        out=w2a[:].rearrange("p (c o) -> p c o", o=D),
        in_=ws2_d.rearrange("(c p) o -> p c o", p=128)), writes=R("w2a"), dma=True)
    expert_pass("sh", w1a, "w1a", w2a, "w2a", bs1row, 0, bs2row, 0, AF.Gelu, None, True)
    for e in range(E):
        w1sb, w1n = (w1a, "w1a") if e % 2 == 0 else (w1b, "w1b")
        w2sb, w2n = (w2a, "w2a") if e % 2 == 0 else (w2b, "w2b")
        p.add("gpsimd", lambda g, w1sb=w1sb, e=e: g.dma_start(
            out=w1sb[:].rearrange("p (c o) -> p c o", o=HH),
            in_=w1_d[e].rearrange("(c p) o -> p c o", p=128)), writes=R(w1n), dma=True)
        p.add("gpsimd", lambda g, w2sb=w2sb, e=e: g.dma_start(
            out=w2sb[:].rearrange("p (c o) -> p c o", o=D),
            in_=w2_d[e].rearrange("(c p) o -> p c o", p=128)), writes=R(w2n), dma=True)
        expert_pass(f"e{e}", w1sb, w1n, w2sb, w2n, b1row, e * HH, b2row, e * D,
                    AF.Gelu, e, False)

    for t in range(NT):
        par = t % 2
        xf, xfn = (xf_a, "xf_a") if par == 0 else (xf_b, "xf_b")
        yt, ytn = (yt_a, "yt_a") if par == 0 else (yt_b, "yt_b")
        _emit_final_epilogue(p, nc, t, acc_d, xin, None, xf, xfn, None, None,
                             yt, ytn, sqd, small, xout,
                             acc_reads=[f"acc{t}_{h}" for h in range(2)])

    if standalone:
        p.emit(es)
        es.close()
        return nc
    p.fence()
    es.close()
    return None


def prep_l2_ffn(inp):
    io = {}
    io["c_onesb"] = CONSTS["c_onesb"]
    io["c_onesf"] = CONSTS["c_onesf"]
    io["gate_W"] = np.ascontiguousarray(inp["L2_ffn_gate_W"])
    io["W1"] = np.ascontiguousarray(inp["L2_ffn_W1"])
    io["b1"] = np.ascontiguousarray(inp["L2_ffn_b1"]).reshape(1, -1)
    io["W2"] = np.ascontiguousarray(inp["L2_ffn_W2"])
    io["b2"] = np.ascontiguousarray(inp["L2_ffn_b2"]).reshape(1, -1)
    io["Ws1"] = np.ascontiguousarray(inp["L2_ffn_Ws1"])
    io["bs1"] = np.ascontiguousarray(inp["L2_ffn_bs1"]).reshape(1, -1)
    io["Ws2"] = np.ascontiguousarray(inp["L2_ffn_Ws2"])
    io["bs2"] = np.ascontiguousarray(inp["L2_ffn_bs2"]).reshape(1, -1)
    io["route_scale"] = np.asarray(inp["L2_ffn_route_scale"], np.float32).reshape(1, 1)
    return io


def build_l1ffn_moe_launch(FUSE=None):
    standalone = FUSE is None
    nc = (bass.Bass("TRN2", target_bir_lowering=False) if standalone
          else NCProxy(FUSE["nc"], FUSE["pfx"], FUSE["rd"]))
    es = contextlib.ExitStack()
    E, HH = 8, 2048

    xin = nc.dram_tensor("xin", [T_FULL, D], F32, kind="ExternalInput").ap()
    xmid = nc.dram_tensor("xmid", [T_FULL, D], F32, kind="ExternalInput").ap()
    xout = nc.dram_tensor("xout", [T_FULL, D], F32, kind="ExternalOutput").ap()
    onesb_d = nc.dram_tensor("c_onesb", [1, 512], BF16, kind="ExternalInput").ap()
    gw_d = nc.dram_tensor("gate_w", [D, E], F32, kind="ExternalInput").ap()
    w1_d = nc.dram_tensor("w1", [E, D, HH], F32, kind="ExternalInput").ap()
    w2_d = nc.dram_tensor("w2", [E, D, HH], F32, kind="ExternalInput").ap()
    w3_d = nc.dram_tensor("w3", [E, HH, D], F32, kind="ExternalInput").ap()
    gam_d = nc.dram_tensor("ln_gam_col", [128, 8], F32, kind="ExternalInput").ap()
    beta_d = nc.dram_tensor("ln_beta_col", [128, 8], BF16, kind="ExternalInput").ap()

    nbf = nc.dram_tensor("nbf", [T_FULL, D], BF16, kind="Internal").ap()
    acc_d = nc.dram_tensor("acc", [T_FULL, D], F32, kind="Internal").ap()

    nTg_a = es.enter_context(nc.sbuf_tensor("nTg_a", [128, 8 * 512], BF16))
    nTg_b = es.enter_context(nc.sbuf_tensor("nTg_b", [128, 8 * 512], BF16))
    hsb = es.enter_context(nc.sbuf_tensor("hsb", [128, 16 * 512], BF16))
    w1sb = es.enter_context(nc.sbuf_tensor("w1sb", [128, 8 * HH], BF16))
    w2sb = es.enter_context(nc.sbuf_tensor("w2sb", [128, 8 * HH], BF16))
    w3sb = es.enter_context(nc.sbuf_tensor("w3sb", [128, 16 * D], BF16))
    gw_sb = es.enter_context(nc.sbuf_tensor("gw_sb", [128, 8 * E], BF16))
    gam_sb = es.enter_context(nc.sbuf_tensor("gam_sb", [128, 8], F32))
    beta_sb = es.enter_context(nc.sbuf_tensor("beta_sb", [128, 8], BF16))
    brow1 = es.enter_context(nc.sbuf_tensor("brow1", [1, HH], BF16))
    brow2 = es.enter_context(nc.sbuf_tensor("brow2", [1, HH], BF16))
    browg = es.enter_context(nc.sbuf_tensor("browg", [1, E], BF16))
    onesb = es.enter_context(nc.sbuf_tensor("onesb", [1, 512], BF16))
    gsc = es.enter_context(nc.sbuf_tensor("gsc", [128, 40], F32))
    wgt = es.enter_context(nc.sbuf_tensor("wgt", [128, NT * 8], F32))
    xa = es.enter_context(nc.sbuf_tensor("xa", [128, D], BF16))
    xb = es.enter_context(nc.sbuf_tensor("xb", [128, D], BF16))
    xf_a = es.enter_context(nc.sbuf_tensor("xf_a", [128, D], F32))
    xf_b = es.enter_context(nc.sbuf_tensor("xf_b", [128, D], F32))
    x1f_a = es.enter_context(nc.sbuf_tensor("x1f_a", [128, D], F32))
    x1f_b = es.enter_context(nc.sbuf_tensor("x1f_b", [128, D], F32))
    yt_a = es.enter_context(nc.sbuf_tensor("yt_a", [128, D], F32))
    yt_b = es.enter_context(nc.sbuf_tensor("yt_b", [128, D], F32))
    sqd = es.enter_context(nc.sbuf_tensor("sqd", [128, D], F32))
    small = es.enter_context(nc.sbuf_tensor("small", [128, 16], F32))
    stmp = es.enter_context(nc.sbuf_tensor("stmp", [128, 4 * 512], F32))
    sil = es.enter_context(nc.sbuf_tensor("sil", [128, 512], F32))

    ps = [es.enter_context(nc.psum_tensor(f"ps{i}", [128, 512], F32)) for i in range(8)]
    p = Prog(nc) if standalone else FUSE["p"]

    p.add("vector", lambda v: v.memset(small[:, 11:12], EPS_TOP), writes=R("small"))
    p.add("gpsimd", lambda g: g.dma_start(out=onesb[:], in_=onesb_d), writes=R("onesb"), dma=True)
    p.add("gpsimd", lambda g: g.dma_start(out=gam_sb[:], in_=gam_d), writes=R("gam"), dma=True)
    p.add("gpsimd", lambda g: g.dma_start(out=beta_sb[:], in_=beta_d), writes=R("beta"), dma=True)
    p.add("gpsimd", lambda g: g.dma_start(
        out=gw_sb[:].rearrange("p (c e) -> p c e", e=E),
        in_=gw_d.rearrange("(c p) e -> p c e", p=128)), writes=R("gw"), dma=True)

    _emit_mirror_nT(p, nc, es, xmid, nbf, None, xa, xb, xf_a, xf_b, sqd, small,
                    pre_norm="ln", prep_eps=1e-5)

    def load_nTg(tg, buf, bufname):
        _load_nTg(p, nbf, tg, buf, bufname)

    def fold_weight(wsb, wname, dout, brow_sb, brow_name):
        nch = (dout + 511) // 512
        for half in range(nch):
            n = min(512, dout - half * 512)

            def mm_brow(tt, wsb=wsb, dout=dout, half=half, n=n):
                for c in range(8):
                    inst = tt.matmul(ps[6][0:1, 0:n], beta_sb[:, c:c + 1],
                                     wsb[:, c * dout + half * 512: c * dout + half * 512 + n],
                                     start=(c == 0), stop=(c == 7))
                return inst
            p.add("tensor", mm_brow, reads=R("beta", wname), writes=R("ps6"))
            p.add("scalar", lambda sc, half=half, n=n, brow_sb=brow_sb: sc.activation(
                brow_sb[0:1, half * 512:half * 512 + n], ps[6][0:1, 0:n], AF.Copy,
                bias=0.0, scale=1.0),
                reads=R("ps6"), writes=R(brow_name))
        for c in range(8):
            p.add("vector", lambda v, wsb=wsb, dout=dout, c=c: v.tensor_scalar(
                wsb[:, c * dout:(c + 1) * dout], wsb[:, c * dout:(c + 1) * dout],
                gam_sb[:, c:c + 1], None, ALU.mult),
                reads=R(wname, "gam"), writes=R(wname))

    fold_weight(gw_sb, "gw", E, browg, "browg")

    for tg in range(T_FULL // 512):
        buf, bufname = (nTg_a, "nTg_a") if tg % 2 == 0 else (nTg_b, "nTg_b")
        load_nTg(tg, buf, bufname)
        for ti in range(4):
            t = tg * 4 + ti
            bank = 6 + (t % 2)
            pst = ps[bank]

            def mm_gate(tt, ti=ti, pst=pst, buf=buf):
                for c in range(8):
                    tt.matmul(pst[:, 0:E],
                              buf[:, c * 512 + ti * 128: c * 512 + ti * 128 + 128],
                              gw_sb[:, c * E:(c + 1) * E],
                              start=(c == 0), stop=False)
                return tt.matmul(pst[:, 0:E], onesb[0:1, 0:128], browg[0:1, 0:E],
                                 start=False, stop=True)
            p.add("tensor", mm_gate, reads=R(bufname, "gw", "browg", "onesb"),
                  writes=R(f"ps{bank}"))
            p.add("vector", lambda v, pst=pst: v.tensor_copy(gsc[:, 0:8], pst[:, 0:E]),
                  reads=R(f"ps{bank}"), writes=R("gsc"))
            _emit_top2_gate(p, gsc, wgt, t, "softmax")

    for e in range(E):
        p.add("gpsimd", lambda g, e=e: g.dma_start(
            out=w1sb[:].rearrange("p (c o) -> p c o", o=HH),
            in_=w1_d[e].rearrange("(c p) o -> p c o", p=128)), writes=R("w1"), dma=True)
        p.add("gpsimd", lambda g, e=e: g.dma_start(
            out=w2sb[:].rearrange("p (c o) -> p c o", o=HH),
            in_=w2_d[e].rearrange("(c p) o -> p c o", p=128)), writes=R("w2"), dma=True)
        p.add("gpsimd", lambda g, e=e: g.dma_start(
            out=w3sb[:].rearrange("p (c o) -> p c o", o=D),
            in_=w3_d[e].rearrange("(c p) o -> p c o", p=128)), writes=R("w3"), dma=True)
        fold_weight(w1sb, "w1", HH, brow1, "brow1")
        fold_weight(w2sb, "w2", HH, brow2, "brow2")
        for tg in range(T_FULL // 512):
            buf, bufname = (nTg_a, "nTg_a") if tg % 2 == 0 else (nTg_b, "nTg_b")
            load_nTg(tg, buf, bufname)
            for hc in range(16):
                pa = ps[tg % 2]
                pb = ps[2 + tg % 2]

                def mm_h(tt, hc=hc, pa=pa, pb=pb, buf=buf):
                    for c in range(8):
                        tt.matmul(pa[:, 0:512],
                                  w1sb[:, c * HH + hc * 128: c * HH + hc * 128 + 128],
                                  buf[:, c * 512:(c + 1) * 512],
                                  start=(c == 0), stop=False)
                    tt.matmul(pa[:, 0:512],
                              brow1[0:1, hc * 128:hc * 128 + 128],
                              onesb[0:1, 0:512], start=False, stop=True)
                    for c in range(8):
                        tt.matmul(pb[:, 0:512],
                                  w2sb[:, c * HH + hc * 128: c * HH + hc * 128 + 128],
                                  buf[:, c * 512:(c + 1) * 512],
                                  start=(c == 0), stop=False)
                    return tt.matmul(pb[:, 0:512],
                                     brow2[0:1, hc * 128:hc * 128 + 128],
                                     onesb[0:1, 0:512], start=False, stop=True)
                p.add("tensor", mm_h, reads=R(bufname, "w1", "w2", "brow1", "brow2", "onesb"),
                      writes=R(f"ps{tg % 2}", f"ps{2 + tg % 2}"))
                p.add("scalar", lambda sc, pa=pa: sc.activation(sil[:, 0:512], pa[:, 0:512],
                                                                AF.Silu),
                      reads=R(f"ps{tg % 2}"), writes=R("sil"))
                p.add("vector", lambda v, pb=pb, hc=hc: v.tensor_tensor(
                    hsb[:, hc * 512:(hc + 1) * 512], sil[:, 0:512], pb[:, 0:512], ALU.mult),
                    reads=R("sil", f"ps{2 + tg % 2}"), writes=R("hsb"))
            for ti in range(4):
                t = tg * 4 + ti
                for half in range(2):
                    bank = 4 + (ti % 2) * 2 + half
                    pst = ps[bank]

                    def mm_y(tt, t=t, ti=ti, half=half, pst=pst):
                        for hc in range(16):
                            inst = tt.matmul(
                                pst[:, 0:512],
                                hsb[:, hc * 512 + ti * 128: hc * 512 + ti * 128 + 128],
                                w3sb[:, hc * D + half * 512: hc * D + half * 512 + 512],
                                start=(hc == 0), stop=(hc == 15))
                        return inst
                    p.add("tensor", mm_y, reads=R("hsb", "w3"), writes=R(f"ps{bank}"))
                    sl = (t * 2 + half) % 4
                    p.add("vector", lambda v, pst=pst, t=t, e=e, sl=sl: v.tensor_scalar(
                        stmp[:, sl * 512:(sl + 1) * 512], pst[:, 0:512],
                        wgt[:, t * 8 + e: t * 8 + e + 1], None, ALU.mult),
                        reads=R(f"ps{bank}", "wgt"), writes=R(f"stmp{sl}"))
                    p.add("gpsimd", lambda g, t=t, half=half, e=e, sl=sl: g.dma_start(
                        out=acc_d[t * 128:(t + 1) * 128, half * 512:half * 512 + 512],
                        in_=stmp[:, sl * 512:(sl + 1) * 512],
                        accum_op=(ALU.bypass if e == 0 else ALU.add)),
                        reads=R(f"stmp{sl}"), writes=R(f"acc{t}_{half}"), dma=True)

    for t in range(NT):
        par = t % 2
        xf, xfn = (xf_a, "xf_a") if par == 0 else (xf_b, "xf_b")
        x1f, x1fn = (x1f_a, "x1f_a") if par == 0 else (x1f_b, "x1f_b")
        yt, ytn = (yt_a, "yt_a") if par == 0 else (yt_b, "yt_b")
        _emit_final_epilogue(p, nc, t, acc_d, xin, xmid, xf, xfn, x1f, x1fn,
                             yt, ytn, sqd, small, xout,
                             acc_reads=[f"acc{t}_{h}" for h in range(2)])

    if standalone:
        p.emit(es)
        es.close()
        return nc
    p.fence()
    es.close()
    return None


def prep_l1_ffn_moe(inp):
    io = {}
    io["c_onesb"] = CONSTS["c_onesb"]
    io["gate_w"] = np.ascontiguousarray(inp["L1_ffn_gate_w"])
    io["w1"] = np.ascontiguousarray(inp["L1_ffn_w1"])
    io["w2"] = np.ascontiguousarray(inp["L1_ffn_w2"])
    io["w3"] = np.ascontiguousarray(inp["L1_ffn_w3"])
    io["ln_gam_col"] = np.ascontiguousarray(
        np.asarray(inp["L1_ffn_ln2_scale"]).reshape(8, 128).T.astype(np.float32))
    io["ln_beta_col"] = np.ascontiguousarray(
        np.asarray(inp["L1_ffn_ln2_shift"]).reshape(8, 128).T).astype(BF)
    return io


def build_l3ffn_a_launch(FUSE=None):
    standalone = FUSE is None
    nc = (bass.Bass("TRN2", target_bir_lowering=False) if standalone
          else NCProxy(FUSE["nc"], FUSE["pfx"], FUSE["rd"]))
    es = contextlib.ExitStack()
    EE = 32

    xin = nc.dram_tensor("xin", [T_FULL, D], F32, kind="ExternalInput").ap()
    onesb_d = nc.dram_tensor("c_onesb", [1, 512], BF16, kind="ExternalInput").ap()
    onesf_d = nc.dram_tensor("c_onesf", [1, 128], F32, kind="ExternalInput").ap()
    sut_d = nc.dram_tensor("c_strictut", [128, 128], BF16, kind="ExternalInput").ap()
    ones128_d = nc.dram_tensor("c_ones128", [128, 128], BF16, kind="ExternalInput").ap()
    rwT_d = nc.dram_tensor("rwT", [D, EE], F32, kind="ExternalInput").ap()
    ebias_d = nc.dram_tensor("ebias", [1, EE], F32, kind="ExternalInput").ap()

    selw_d = nc.dram_tensor("selw", [T_FULL, EE], F32, kind="ExternalOutput").ap()
    lpos_d = nc.dram_tensor("lpos", [T_FULL, EE], F32, kind="ExternalOutput").ap()
    cnt_d = nc.dram_tensor("cnt", [1, EE], F32, kind="ExternalOutput").ap()

    nbf = nc.dram_tensor("nbf", [T_FULL, D], BF16, kind="Internal").ap()

    big = es.enter_context(nc.sbuf_tensor("big", [128, 8 * T_FULL], BF16))
    rw_sb = es.enter_context(nc.sbuf_tensor("rw_sb", [128, 8 * EE], BF16))
    onesb = es.enter_context(nc.sbuf_tensor("onesb", [1, 512], BF16))
    onesf = es.enter_context(nc.sbuf_tensor("onesf", [1, 128], F32))
    sut = es.enter_context(nc.sbuf_tensor("sut", [128, 128], BF16))
    ones128 = es.enter_context(nc.sbuf_tensor("ones128", [128, 128], BF16))
    ebias_sb = es.enter_context(nc.sbuf_tensor("ebias_sb", [1, EE], F32))
    eb_bc = es.enter_context(nc.sbuf_tensor("eb_bc", [128, EE], F32))
    gsc = es.enter_context(nc.sbuf_tensor("gsc", [128, 5 * EE + 8], F32))
    selm = es.enter_context(nc.sbuf_tensor("selm", [128, NT * EE], BF16))
    sw_t = es.enter_context(nc.sbuf_tensor("sw_t", [128, 2 * EE], F32))
    lp_t = es.enter_context(nc.sbuf_tensor("lp_t", [128, 2 * EE], F32))
    cnt_sb = es.enter_context(nc.sbuf_tensor("cnt_sb", [1, EE], F32))
    xa = es.enter_context(nc.sbuf_tensor("xa", [128, D], BF16))
    xb = es.enter_context(nc.sbuf_tensor("xb", [128, D], BF16))
    xf_a = es.enter_context(nc.sbuf_tensor("xf_a", [128, D], F32))
    xf_b = es.enter_context(nc.sbuf_tensor("xf_b", [128, D], F32))
    sqd = es.enter_context(nc.sbuf_tensor("sqd", [128, D], F32))
    small = es.enter_context(nc.sbuf_tensor("small", [128, 16], F32))

    ps = [es.enter_context(nc.psum_tensor(f"ps{i}", [128, 512], F32)) for i in range(8)]
    p = Prog(nc) if standalone else FUSE["p"]

    p.add("gpsimd", lambda g: g.dma_start(out=onesb[:], in_=onesb_d), writes=R("onesb"), dma=True)
    p.add("gpsimd", lambda g: g.dma_start(out=onesf[:], in_=onesf_d), writes=R("onesf"), dma=True)
    p.add("gpsimd", lambda g: g.dma_start(out=sut[:], in_=sut_d), writes=R("sut"), dma=True)
    p.add("gpsimd", lambda g: g.dma_start(out=ones128[:], in_=ones128_d), writes=R("o128"), dma=True)
    p.add("gpsimd", lambda g: g.dma_start(out=ebias_sb[:], in_=ebias_d), writes=R("ebias"), dma=True)
    p.add("gpsimd", lambda g: g.dma_start(
        out=rw_sb[:].rearrange("p (c e) -> p c e", e=EE),
        in_=rwT_d.rearrange("(c p) e -> p c e", p=128)), writes=R("rw"), dma=True)

    _emit_mirror_nT(p, nc, es, xin, nbf, big, xa, xb, xf_a, xf_b, sqd, small)

    p.add("tensor", lambda t: t.matmul(ps[6][0:128, 0:EE], onesf[0:1, :], ebias_sb[:],
                                        start=True, stop=True),
          reads=R("onesf", "ebias"), writes=R("ps6"))
    p.add("vector", lambda v: v.tensor_copy(eb_bc[:], ps[6][0:128, 0:EE]),
          reads=R("ps6"), writes=R("eb_bc"))

    for t in range(NT):
        bank = 6 + (t % 2)
        pst = ps[bank]

        def mm_r(tt, t=t, pst=pst):
            for c in range(8):
                inst = tt.matmul(pst[:, 0:EE],
                                 big[:, c * T_FULL + t * 128: c * T_FULL + (t + 1) * 128],
                                 rw_sb[:, c * EE:(c + 1) * EE],
                                 start=(c == 0), stop=(c == 7))
            return inst
        p.add("tensor", mm_r, reads=R("big", "rw"), writes=R(f"ps{bank}"))
        p.add("scalar", lambda sc, pst=pst: sc.activation(gsc[:, 0:EE], pst[:, 0:EE], AF.Sigmoid),
              reads=R(f"ps{bank}"), writes=R("gsc"))
        p.add("vector", lambda v: v.tensor_add(gsc[:, EE:2 * EE], gsc[:, 0:EE], eb_bc[:]),
              reads=R("gsc", "eb_bc"), writes=R("gsc"))
        p.add("vector", lambda v: v.memset(gsc[:, 2 * EE:3 * EE], 0.0),
              writes=R("gsc"))
        for k in range(8):
            p.add("vector", lambda v: v.reduce_max(gsc[:, 5 * EE:5 * EE + 1],
                                                   gsc[:, EE:2 * EE], axis=AX.X),
                  reads=R("gsc"), writes=R("gsc"))
            p.add("vector", lambda v: v.tensor_single_scalar(
                gsc[:, 3 * EE:4 * EE], gsc[:, EE:2 * EE], gsc[:, 5 * EE:5 * EE + 1], ALU.is_ge),
                reads=R("gsc"), writes=R("gsc"))
            p.add("vector", lambda v: v.tensor_add(gsc[:, 2 * EE:3 * EE],
                                                   gsc[:, 2 * EE:3 * EE], gsc[:, 3 * EE:4 * EE]),
                  reads=R("gsc"), writes=R("gsc"))
            p.add("vector", lambda v: v.scalar_tensor_tensor(
                gsc[:, EE:2 * EE], gsc[:, 3 * EE:4 * EE], -1e30,
                gsc[:, EE:2 * EE], ALU.mult, ALU.add),
                reads=R("gsc"), writes=R("gsc"))
        p.add("vector", lambda v, t=t: v.tensor_copy(selm[:, t * EE:(t + 1) * EE],
                                                     gsc[:, 2 * EE:3 * EE]),
              reads=R("gsc"), writes=R(f"selm{t}"))
        p.add("vector", lambda v: v.tensor_mul(gsc[:, 4 * EE:5 * EE], gsc[:, 0:EE],
                                               gsc[:, 2 * EE:3 * EE]),
              reads=R("gsc"), writes=R("gsc"))
        p.add("vector", lambda v: v.reduce_sum(gsc[:, 5 * EE + 1:5 * EE + 2],
                                               gsc[:, 4 * EE:5 * EE], axis=AX.X),
              reads=R("gsc"), writes=R("gsc"))
        p.add("vector", lambda v: v.tensor_scalar_add(gsc[:, 5 * EE + 1:5 * EE + 2],
                                                      gsc[:, 5 * EE + 1:5 * EE + 2], 1e-20),
              reads=R("gsc"), writes=R("gsc"))
        p.add("vector", lambda v: v.reciprocal(gsc[:, 5 * EE + 2:5 * EE + 3],
                                               gsc[:, 5 * EE + 1:5 * EE + 2]),
              reads=R("gsc"), writes=R("gsc"))
        p.add("vector", lambda v: v.tensor_scalar_mul(gsc[:, 5 * EE + 2:5 * EE + 3],
                                                      gsc[:, 5 * EE + 2:5 * EE + 3], 2.5),
              reads=R("gsc"), writes=R("gsc"))
        sl = t % 2
        p.add("vector", lambda v, sl=sl: v.tensor_scalar(
            sw_t[:, sl * EE:(sl + 1) * EE], gsc[:, 4 * EE:5 * EE],
            gsc[:, 5 * EE + 2:5 * EE + 3], None, ALU.mult),
            reads=R("gsc"), writes=R(f"swt{sl}"))
        p.add("sync", lambda s, t=t, sl=sl: s.dma_start(
            out=selw_d[t * 128:(t + 1) * 128, :], in_=sw_t[:, sl * EE:(sl + 1) * EE]),
            reads=R(f"swt{sl}"), writes=R(f"dram_sw{t}"), dma=True)

    for t in range(NT):
        bank = 4 + (t % 2)
        pst = ps[bank]

        def mm_lp(tt, t=t, pst=pst):
            if t == 0:
                return tt.matmul(pst[:, 0:EE], sut[:], selm[:, 0:EE], start=True, stop=True)
            for i in range(t):
                tt.matmul(pst[:, 0:EE], ones128[:], selm[:, i * EE:(i + 1) * EE],
                          start=(i == 0), stop=False)
            return tt.matmul(pst[:, 0:EE], sut[:], selm[:, t * EE:(t + 1) * EE],
                             start=False, stop=True)
        p.add("tensor", mm_lp, reads=tuple(f"selm{i}" for i in range(t + 1)) + R("sut", "o128"),
              writes=R(f"ps{bank}"))
        sl = t % 2
        p.add("vector", lambda v, pst=pst, sl=sl: v.tensor_copy(
            lp_t[:, sl * EE:(sl + 1) * EE], pst[:, 0:EE]),
            reads=R(f"ps{bank}"), writes=R(f"lpt{sl}"))
        p.add("sync", lambda s, t=t, sl=sl: s.dma_start(
            out=lpos_d[t * 128:(t + 1) * 128, :], in_=lp_t[:, sl * EE:(sl + 1) * EE]),
            reads=R(f"lpt{sl}"), writes=R(f"dram_lp{t}"), dma=True)

    def mm_cnt(tt):
        for t in range(NT):
            inst = tt.matmul(ps[3][0:1, 0:EE], onesb[0:1, 0:128].bitcast(BF16),
                             selm[:, t * EE:(t + 1) * EE], start=(t == 0), stop=(t == NT - 1)) \
                if False else tt.matmul(ps[3][0:1, 0:EE], ones128[:, 0:1],
                                        selm[:, t * EE:(t + 1) * EE],
                                        start=(t == 0), stop=(t == NT - 1))
        return inst
    p.add("tensor", mm_cnt, reads=tuple(f"selm{i}" for i in range(NT)) + R("o128"),
          writes=R("ps3"))
    p.add("vector", lambda v: v.tensor_copy(cnt_sb[:], ps[3][0:1, 0:EE]),
          reads=R("ps3"), writes=R("cnt"))
    p.add("sync", lambda s: s.dma_start(out=cnt_d, in_=cnt_sb[:]),
          reads=R("cnt"), writes=R("dram_cnt"), dma=True)

    if standalone:
        p.emit(es)
        es.close()
        return nc
    p.fence()
    es.close()
    return None


def build_l3ffn_b_launch(FUSE=None):
    standalone = FUSE is None
    nc = (bass.Bass("TRN2", target_bir_lowering=False) if standalone
          else NCProxy(FUSE["nc"], FUSE["pfx"], FUSE["rd"]))
    es = contextlib.ExitStack()
    EE, II, SI = 32, 512, 2048
    NTG = T_FULL // 512

    xin = nc.dram_tensor("xin", [T_FULL, D], F32, kind="ExternalInput").ap()
    onesb_d = nc.dram_tensor("c_onesb", [1, 512], BF16, kind="ExternalInput").ap()
    onesf_d = nc.dram_tensor("c_onesf", [1, 128], F32, kind="ExternalInput").ap()
    selw_d = nc.dram_tensor("selw", [T_FULL, EE], F32, kind="ExternalInput").ap()
    lpos_d = nc.dram_tensor("lpos", [T_FULL, EE], F32, kind="ExternalInput").ap()
    gath_d = nc.dram_tensor("gathered", [8, EE], F32, kind="ExternalInput").ap()
    pmask_d = nc.dram_tensor("premask", [8, 1], F32, kind="ExternalInput").ap()
    gup_d = nc.dram_tensor("gup", [EE, D, 2 * II], F32, kind="ExternalInput").ap()
    down_d = nc.dram_tensor("down", [EE, II, D], F32, kind="ExternalInput").ap()
    sg_d = nc.dram_tensor("sgT", [D, SI], F32, kind="ExternalInput").ap()
    su_d = nc.dram_tensor("suT", [D, SI], F32, kind="ExternalInput").ap()
    sd_d = nc.dram_tensor("sdT", [SI, D], F32, kind="ExternalInput").ap()
    xout = nc.dram_tensor("xout", [T_FULL, D], F32, kind="ExternalOutput").ap()

    nbf = nc.dram_tensor("nbf", [T_FULL, D], BF16, kind="Internal").ap()
    acc_d = nc.dram_tensor("acc", [T_FULL, D], F32, kind="Internal").ap()

    nTg_a = es.enter_context(nc.sbuf_tensor("nTg_a", [128, 8 * 512], BF16))
    nTg_b = es.enter_context(nc.sbuf_tensor("nTg_b", [128, 8 * 512], BF16))
    hsb = es.enter_context(nc.sbuf_tensor("hsb", [128, 16 * 512], BF16))
    pool = es.enter_context(nc.sbuf_tensor("pool", [128, 49152], BF16))
    onesb = es.enter_context(nc.sbuf_tensor("onesb", [1, 512], BF16))
    onesf = es.enter_context(nc.sbuf_tensor("onesf", [1, 128], F32))
    gath_sb = es.enter_context(nc.sbuf_tensor("gath_sb", [8, EE], F32))
    pmask_sb = es.enter_context(nc.sbuf_tensor("pmask_sb", [8, 1], F32))
    offs_bc = es.enter_context(nc.sbuf_tensor("offs_bc", [128, EE], F32))
    wgt = es.enter_context(nc.sbuf_tensor("wgt", [128, NT * EE], F32))
    wtmp = es.enter_context(nc.sbuf_tensor("wtmp", [128, 2 * EE], F32))
    xa = es.enter_context(nc.sbuf_tensor("xa", [128, D], BF16))
    xb = es.enter_context(nc.sbuf_tensor("xb", [128, D], BF16))
    xf_a = es.enter_context(nc.sbuf_tensor("xf_a", [128, D], F32))
    xf_b = es.enter_context(nc.sbuf_tensor("xf_b", [128, D], F32))
    yt_a = es.enter_context(nc.sbuf_tensor("yt_a", [128, D], F32))
    yt_b = es.enter_context(nc.sbuf_tensor("yt_b", [128, D], F32))
    sqd = es.enter_context(nc.sbuf_tensor("sqd", [128, D], F32))
    small = es.enter_context(nc.sbuf_tensor("small", [128, 16], F32))
    stmp = es.enter_context(nc.sbuf_tensor("stmp", [128, 4 * 512], F32))
    sil = es.enter_context(nc.sbuf_tensor("sil", [128, 512], F32))
    utmp = es.enter_context(nc.sbuf_tensor("utmp", [128, 512], F32))
    gtmp = es.enter_context(nc.sbuf_tensor("gtmp", [128, 512], F32))

    ps = [es.enter_context(nc.psum_tensor(f"ps{i}", [128, 512], F32)) for i in range(8)]
    p = Prog(nc) if standalone else FUSE["p"]

    p.add("vector", lambda v: v.memset(small[:, 11:12], EPS_TOP), writes=R("small"))
    p.add("gpsimd", lambda g: g.dma_start(out=onesb[:], in_=onesb_d), writes=R("onesb"), dma=True)
    p.add("gpsimd", lambda g: g.dma_start(out=onesf[:], in_=onesf_d), writes=R("onesf"), dma=True)
    p.add("gpsimd", lambda g: g.dma_start(out=gath_sb[:], in_=gath_d), writes=R("gath"), dma=True)
    p.add("gpsimd", lambda g: g.dma_start(out=pmask_sb[:], in_=pmask_d), writes=R("pmask"), dma=True)

    _emit_mirror_nT(p, nc, es, xin, nbf, None, xa, xb, xf_a, xf_b, sqd, small)

    p.add("tensor", lambda t: t.matmul(ps[6][0:1, 0:EE], pmask_sb[:], gath_sb[:],
                                       start=True, stop=True),
          reads=R("pmask", "gath"), writes=R("ps6"))
    p.add("scalar", lambda sc: sc.copy(wtmp[0:1, 0:EE], ps[6][0:1, 0:EE]),
          reads=R("ps6"), writes=R("wtmp0"))
    p.add("tensor", lambda t: t.matmul(ps[6][0:128, 0:EE], onesf[0:1, :],
                                       wtmp[0:1, 0:EE], start=True, stop=True),
          reads=R("onesf", "wtmp0"), writes=R("ps6"))
    p.add("vector", lambda v: v.tensor_copy(offs_bc[:], ps[6][0:128, 0:EE]),
          reads=R("ps6"), writes=R("offs"))

    for t in range(NT):
        sl = t % 2
        p.add("gpsimd", lambda g, t=t, sl=sl: g.dma_start(
            out=wtmp[:, sl * EE:(sl + 1) * EE], in_=lpos_d[t * 128:(t + 1) * 128, :]),
            writes=R(f"wtmp{sl}"), dma=True)
        p.add("vector", lambda v, sl=sl: v.tensor_add(
            wtmp[:, sl * EE:(sl + 1) * EE], wtmp[:, sl * EE:(sl + 1) * EE], offs_bc[:]),
            reads=R(f"wtmp{sl}", "offs"), writes=R(f"wtmp{sl}"))
        p.add("vector", lambda v, sl=sl: v.tensor_single_scalar(
            wtmp[:, sl * EE:(sl + 1) * EE], wtmp[:, sl * EE:(sl + 1) * EE], 2048.0, ALU.is_lt),
            reads=R(f"wtmp{sl}"), writes=R(f"wtmp{sl}"))
        p.add("gpsimd", lambda g, t=t: g.dma_start(
            out=wgt[:, t * EE:(t + 1) * EE], in_=selw_d[t * 128:(t + 1) * 128, :]),
            writes=R("wgt"), dma=True)
        p.add("vector", lambda v, t=t, sl=sl: v.tensor_mul(
            wgt[:, t * EE:(t + 1) * EE], wgt[:, t * EE:(t + 1) * EE],
            wtmp[:, sl * EE:(sl + 1) * EE]),
            reads=R("wgt", f"wtmp{sl}"), writes=R("wgt"))

    p.add("gpsimd", lambda g: g.dma_start(
        out=pool[:, 0:16384].rearrange("p (c o) -> p c o", o=SI),
        in_=sg_d.rearrange("(c p) o -> p c o", p=128)), writes=R("pool"), dma=True)
    p.add("gpsimd", lambda g: g.dma_start(
        out=pool[:, 16384:32768].rearrange("p (c o) -> p c o", o=SI),
        in_=su_d.rearrange("(c p) o -> p c o", p=128)), writes=R("pool"), dma=True)
    p.add("gpsimd", lambda g: g.dma_start(
        out=pool[:, 32768:49152].rearrange("p (c o) -> p c o", o=D),
        in_=sd_d.rearrange("(c p) o -> p c o", p=128)), writes=R("pool"), dma=True)

    for tg in range(NTG):
        buf, bufname = (nTg_a, "nTg_a") if tg % 2 == 0 else (nTg_b, "nTg_b")
        _load_nTg(p, nbf, tg, buf, bufname)
        for hc in range(16):
            pa = ps[tg % 2]
            pb = ps[2 + tg % 2]

            def mm_gu(tt, hc=hc, pa=pa, pb=pb, buf=buf):
                for c in range(8):
                    tt.matmul(pa[:, 0:512],
                              pool[:, c * SI + hc * 128: c * SI + hc * 128 + 128],
                              buf[:, c * 512:(c + 1) * 512],
                              start=(c == 0), stop=(c == 7))
                for c in range(8):
                    inst = tt.matmul(pb[:, 0:512],
                                     pool[:, 16384 + c * SI + hc * 128: 16384 + c * SI + hc * 128 + 128],
                                     buf[:, c * 512:(c + 1) * 512],
                                     start=(c == 0), stop=(c == 7))
                return inst
            p.add("tensor", mm_gu, reads=R(bufname, "pool"),
                  writes=R(f"ps{tg % 2}", f"ps{2 + tg % 2}"))
            p.add("scalar", lambda sc, pa=pa: sc.activation(sil[:, 0:512], pa[:, 0:512], AF.Silu),
                  reads=R(f"ps{tg % 2}"), writes=R("sil"))
            p.add("vector", lambda v, pb=pb, hc=hc: v.tensor_tensor(
                hsb[:, hc * 512:(hc + 1) * 512], sil[:, 0:512], pb[:, 0:512], ALU.mult),
                reads=R("sil", f"ps{2 + tg % 2}"), writes=R("hsb"))
        for ti in range(4):
            t = tg * 4 + ti
            for half in range(2):
                bank = 4 + (ti % 2) * 2 + half
                pst = ps[bank]

                def mm_ys(tt, ti=ti, half=half, pst=pst):
                    for hc in range(16):
                        inst = tt.matmul(
                            pst[:, 0:512],
                            hsb[:, hc * 512 + ti * 128: hc * 512 + ti * 128 + 128],
                            pool[:, 32768 + hc * D + half * 512: 32768 + hc * D + half * 512 + 512],
                            start=(hc == 0), stop=(hc == 15))
                    return inst
                p.add("tensor", mm_ys, reads=R("hsb", "pool"), writes=R(f"ps{bank}"))
                sl = (t * 2 + half) % 4
                p.add("vector", lambda v, pst=pst, sl=sl: v.tensor_copy(
                    stmp[:, sl * 512:(sl + 1) * 512], pst[:, 0:512]),
                    reads=R(f"ps{bank}"), writes=R(f"stmp{sl}"))
                p.add("gpsimd", lambda g, t=t, half=half, sl=sl: g.dma_start(
                    out=acc_d[t * 128:(t + 1) * 128, half * 512:half * 512 + 512],
                    in_=stmp[:, sl * 512:(sl + 1) * 512]),
                    reads=R(f"stmp{sl}"), writes=R(f"acc{t}_{half}"), dma=True)

    for e in range(EE):
        p.add("gpsimd", lambda g, e=e: g.dma_start(
            out=pool[:, 0:8192].rearrange("p (c o) -> p c o", o=1024),
            in_=gup_d[e].rearrange("(c p) o -> p c o", p=128)), writes=R("pool"), dma=True)
        p.add("gpsimd", lambda g, e=e: g.dma_start(
            out=pool[:, 8192:12288].rearrange("p (c o) -> p c o", o=1024),
            in_=down_d[e].rearrange("(c p) o -> p c o", p=128)), writes=R("pool"), dma=True)
        for tg in range(NTG):
            buf, bufname = (nTg_a, "nTg_a") if tg % 2 == 0 else (nTg_b, "nTg_b")
            _load_nTg(p, nbf, tg, buf, bufname)
            for j in range(4):
                pa = ps[tg % 2]
                pb = ps[2 + tg % 2]

                def mm_gu_e(tt, j=j, pa=pa, pb=pb, buf=buf):
                    for c in range(8):
                        tt.matmul(pa[:, 0:512],
                                  pool[:, c * 1024 + j * 128: c * 1024 + j * 128 + 128],
                                  buf[:, c * 512:(c + 1) * 512],
                                  start=(c == 0), stop=(c == 7))
                    for c in range(8):
                        inst = tt.matmul(pb[:, 0:512],
                                         pool[:, c * 1024 + 512 + j * 128: c * 1024 + 512 + j * 128 + 128],
                                         buf[:, c * 512:(c + 1) * 512],
                                         start=(c == 0), stop=(c == 7))
                    return inst
                p.add("tensor", mm_gu_e, reads=R(bufname, "pool"),
                      writes=R(f"ps{tg % 2}", f"ps{2 + tg % 2}"))
                p.add("vector", lambda v, pa=pa: v.tensor_scalar_min(gtmp[:, 0:512],
                                                                     pa[:, 0:512], 7.0),
                      reads=R(f"ps{tg % 2}"), writes=R("gtmp"))
                p.add("scalar", lambda sc: sc.activation(sil[:, 0:512], gtmp[:, 0:512], AF.Silu),
                      reads=R("gtmp"), writes=R("sil"))
                p.add("vector", lambda v, pb=pb: v.tensor_scalar_min(utmp[:, 0:512],
                                                                     pb[:, 0:512], 7.0),
                      reads=R(f"ps{2 + tg % 2}"), writes=R("utmp"))
                p.add("vector", lambda v: v.tensor_scalar_max(utmp[:, 0:512],
                                                              utmp[:, 0:512], -7.0),
                      reads=R("utmp"), writes=R("utmp"))
                p.add("vector", lambda v, j=j: v.tensor_tensor(
                    hsb[:, j * 512:(j + 1) * 512], sil[:, 0:512], utmp[:, 0:512], ALU.mult),
                    reads=R("sil", "utmp"), writes=R("hsb"))
            for ti in range(4):
                t = tg * 4 + ti
                for half in range(2):
                    bank = 4 + (ti % 2) * 2 + half
                    pst = ps[bank]

                    def mm_ye(tt, ti=ti, half=half, pst=pst):
                        for j in range(4):
                            inst = tt.matmul(
                                pst[:, 0:512],
                                hsb[:, j * 512 + ti * 128: j * 512 + ti * 128 + 128],
                                pool[:, 8192 + j * D + half * 512: 8192 + j * D + half * 512 + 512],
                                start=(j == 0), stop=(j == 3))
                        return inst
                    p.add("tensor", mm_ye, reads=R("hsb", "pool"), writes=R(f"ps{bank}"))
                    sl = (t * 2 + half) % 4
                    p.add("vector", lambda v, pst=pst, t=t, e=e, sl=sl: v.tensor_scalar(
                        stmp[:, sl * 512:(sl + 1) * 512], pst[:, 0:512],
                        wgt[:, t * EE + e: t * EE + e + 1], None, ALU.mult),
                        reads=R(f"ps{bank}", "wgt"), writes=R(f"stmp{sl}"))
                    p.add("gpsimd", lambda g, t=t, half=half, sl=sl: g.dma_start(
                        out=acc_d[t * 128:(t + 1) * 128, half * 512:half * 512 + 512],
                        in_=stmp[:, sl * 512:(sl + 1) * 512], accum_op=ALU.add),
                        reads=R(f"stmp{sl}"), writes=R(f"acc{t}_{half}"), dma=True)

    for t in range(NT):
        par = t % 2
        xf, xfn = (xf_a, "xf_a") if par == 0 else (xf_b, "xf_b")
        yt, ytn = (yt_a, "yt_a") if par == 0 else (yt_b, "yt_b")
        _emit_final_epilogue(p, nc, t, acc_d, xin, None, xf, xfn, None, None,
                             yt, ytn, sqd, small, xout,
                             acc_reads=[f"acc{t}_{h}" for h in range(2)])

    if standalone:
        p.emit(es)
        es.close()
        return nc
    p.fence()
    es.close()
    return None


def _strictut():
    m = np.triu(np.ones((128, 128), np.float32), 1)
    return m.astype(BF)


def prep_l3_ffn_a(inp):
    io = {}
    io["c_onesb"] = CONSTS["c_onesb"]
    io["c_onesf"] = CONSTS["c_onesf"]
    io["c_strictut"] = _strictut()
    io["c_ones128"] = np.ones((128, 128), BF)
    io["rwT"] = np.ascontiguousarray(np.asarray(inp["L3_ffn_router_weight"]).T)
    io["ebias"] = np.ascontiguousarray(inp["L3_ffn_e_score_correction_bias"]).reshape(1, -1)
    return io


def prep_l3_ffn_b(inp):
    io = {}
    io["c_onesb"] = CONSTS["c_onesb"]
    io["c_onesf"] = CONSTS["c_onesf"]
    gup = np.asarray(inp["L3_ffn_gate_up_proj"])
    io["gup"] = np.ascontiguousarray(np.transpose(gup, (0, 2, 1)))
    dn = np.asarray(inp["L3_ffn_down_proj"])
    io["down"] = np.ascontiguousarray(np.transpose(dn, (0, 2, 1)))
    io["sgT"] = np.ascontiguousarray(np.asarray(inp["L3_ffn_shared_gate"]).T)
    io["suT"] = np.ascontiguousarray(np.asarray(inp["L3_ffn_shared_up"]).T)
    io["sdT"] = np.ascontiguousarray(np.asarray(inp["L3_ffn_shared_down"]).T)
    return io


def build_l0ffn_bank_launch(FUSE=None):
    standalone = FUSE is None
    nc = (bass.Bass("TRN2", target_bir_lowering=False) if standalone
          else NCProxy(FUSE["nc"], FUSE["pfx"], FUSE["rd"]))
    es = contextlib.ExitStack()

    xin = nc.dram_tensor("xin", [T_FULL, D], F32, kind="ExternalInput").ap()
    xmid = nc.dram_tensor("xmid", [T_FULL, D], F32, kind="ExternalInput").ap()
    xout = nc.dram_tensor("xout", [T_FULL, D], F32, kind="ExternalOutput").ap()
    onesb_d = nc.dram_tensor("c_onesb", [1, 512], BF16, kind="ExternalInput").ap()
    onesf_d = nc.dram_tensor("c_onesf", [1, 128], F32, kind="ExternalInput").ap()
    iota_d = nc.dram_tensor("c_iota8", [1, 8], F32, kind="ExternalInput").ap()
    fclog_d = nc.dram_tensor("fc_logits", [1, 8], F32, kind="ExternalInput").ap()
    pjlog_d = nc.dram_tensor("pj_logits", [1, 8], F32, kind="ExternalInput").ap()
    fcbd1_d = nc.dram_tensor("fc_bd1", [8, 8, 128, 128], F32, kind="ExternalInput").ap()
    fcbd2_d = nc.dram_tensor("fc_bd2", [8, 32, 128, 128], F32, kind="ExternalInput").ap()
    fcmix_d = nc.dram_tensor("fc_mixp", [8, D, 4096], F32, kind="ExternalInput").ap()
    pjbd1_d = nc.dram_tensor("pj_bd1", [8, 32, 128, 128], F32, kind="ExternalInput").ap()
    pjbd2_d = nc.dram_tensor("pj_bd2", [8, 8, 128, 128], F32, kind="ExternalInput").ap()
    pjmix_d = nc.dram_tensor("pj_mixp", [8, 4096, D], F32, kind="ExternalInput").ap()

    nbf = nc.dram_tensor("nbf", [T_FULL, D], BF16, kind="Internal").ap()
    hacc = nc.dram_tensor("hacc", [4096, T_FULL], F32, kind="Internal").ap()
    hact = nc.dram_tensor("hact", [4096, T_FULL], BF16, kind="Internal").ap()
    acc_d = nc.dram_tensor("acc", [T_FULL, D], F32, kind="Internal").ap()

    mixA = es.enter_context(nc.sbuf_tensor("mixA", [128, 32768], BF16))
    bd1sb = es.enter_context(nc.sbuf_tensor("bd1sb", [128, 32 * 128], BF16))
    bd2sb = es.enter_context(nc.sbuf_tensor("bd2sb", [128, 32 * 128], BF16))
    nTg_a = es.enter_context(nc.sbuf_tensor("nTg_a", [128, 8 * 512], BF16))
    nTg_b = es.enter_context(nc.sbuf_tensor("nTg_b", [128, 8 * 512], BF16))
    h1sb = es.enter_context(nc.sbuf_tensor("h1sb", [128, 32 * 512], BF16))
    mix_sb = es.enter_context(nc.sbuf_tensor("mix_sb", [128, 2 * 512], BF16))
    scr_a = es.enter_context(nc.sbuf_tensor("scr_a", [128, 4096], F32))
    scrbf = es.enter_context(nc.sbuf_tensor("scrbf", [128, 4096], BF16))
    onesb = es.enter_context(nc.sbuf_tensor("onesb", [1, 512], BF16))
    onesf = es.enter_context(nc.sbuf_tensor("onesf", [1, 128], F32))
    iota8 = es.enter_context(nc.sbuf_tensor("iota8", [1, 8], F32))
    lg = es.enter_context(nc.sbuf_tensor("lg", [1, 64], F32))
    idx_i = es.enter_context(nc.sbuf_tensor("idx_i", [1, 4], mybir.dt.int32))
    c_bc = es.enter_context(nc.sbuf_tensor("c_bc", [128, 4], F32))
    xa = es.enter_context(nc.sbuf_tensor("xa", [128, D], BF16))
    xb = es.enter_context(nc.sbuf_tensor("xb", [128, D], BF16))
    xf_a = es.enter_context(nc.sbuf_tensor("xf_a", [128, D], F32))
    xf_b = es.enter_context(nc.sbuf_tensor("xf_b", [128, D], F32))
    x1f_a = es.enter_context(nc.sbuf_tensor("x1f_a", [128, D], F32))
    x1f_b = es.enter_context(nc.sbuf_tensor("x1f_b", [128, D], F32))
    yt_a = es.enter_context(nc.sbuf_tensor("yt_a", [128, D], F32))
    yt_b = es.enter_context(nc.sbuf_tensor("yt_b", [128, D], F32))
    sqd = es.enter_context(nc.sbuf_tensor("sqd", [128, D], F32))
    small = es.enter_context(nc.sbuf_tensor("small", [128, 16], F32))
    stmp = es.enter_context(nc.sbuf_tensor("stmp", [128, 4 * 512], F32))

    ps = [es.enter_context(nc.psum_tensor(f"ps{i}", [128, 512], F32)) for i in range(8)]
    p = Prog(nc) if standalone else FUSE["p"]

    p.add("vector", lambda v: v.memset(small[:, 11:12], EPS_TOP), writes=R("small"))
    p.add("gpsimd", lambda g: g.dma_start(out=onesb[:], in_=onesb_d), writes=R("onesb"), dma=True)
    p.add("gpsimd", lambda g: g.dma_start(out=onesf[:], in_=onesf_d), writes=R("onesf"), dma=True)
    p.add("gpsimd", lambda g: g.dma_start(out=iota8[:], in_=iota_d), writes=R("iota"), dma=True)
    p.add("gpsimd", lambda g: g.dma_start(out=lg[0:1, 0:8], in_=fclog_d), writes=R("lg"), dma=True)
    p.add("gpsimd", lambda g: g.dma_start(out=lg[0:1, 8:16], in_=pjlog_d), writes=R("lg"), dma=True)

    for s_ in range(2):
        base = s_ * 8
        m1c, m2c = 16 + 2 * s_, 17 + 2 * s_
        k1 = 20 + 16 * s_
        k2 = 28 + 16 * s_
        dcol = 52 + 3 * s_
        p.add("vector", lambda v, base=base, m1c=m1c: v.reduce_max(
            lg[0:1, m1c:m1c + 1], lg[0:1, base:base + 8], axis=AX.X),
            reads=R("lg"), writes=R("lg"))
        p.add("vector", lambda v, base=base, m1c=m1c, k1=k1: v.tensor_single_scalar(
            lg[0:1, k1:k1 + 8], lg[0:1, base:base + 8], lg[0:1, m1c:m1c + 1], ALU.is_ge),
            reads=R("lg"), writes=R("lg"))
        p.add("vector", lambda v, base=base, k1=k1, k2=k2: v.scalar_tensor_tensor(
            lg[0:1, k2:k2 + 8], lg[0:1, k1:k1 + 8], -1e30,
            lg[0:1, base:base + 8], ALU.mult, ALU.add),
            reads=R("lg"), writes=R("lg"))
        p.add("vector", lambda v, m2c=m2c, k2=k2: v.reduce_max(
            lg[0:1, m2c:m2c + 1], lg[0:1, k2:k2 + 8], axis=AX.X),
            reads=R("lg"), writes=R("lg"))
        p.add("vector", lambda v, m2c=m2c, k2=k2: v.tensor_single_scalar(
            lg[0:1, k2:k2 + 8], lg[0:1, k2:k2 + 8], lg[0:1, m2c:m2c + 1], ALU.is_ge),
            reads=R("lg"), writes=R("lg"))
        p.add("vector", lambda v, m1c=m1c, m2c=m2c, dcol=dcol: v.tensor_sub(
            lg[0:1, dcol:dcol + 1], lg[0:1, m1c:m1c + 1], lg[0:1, m2c:m2c + 1]),
            reads=R("lg"), writes=R("lg"))
        p.add("scalar", lambda sc, dcol=dcol: sc.activation(
            lg[0:1, dcol + 1:dcol + 2], lg[0:1, dcol:dcol + 1], AF.Sigmoid),
            reads=R("lg"), writes=R("lg"))
        p.add("vector", lambda v, dcol=dcol: v.tensor_scalar(
            lg[0:1, dcol + 2:dcol + 3], lg[0:1, dcol + 1:dcol + 2],
            -1.0, 1.0, ALU.mult, ALU.add),
            reads=R("lg"), writes=R("lg"))
        for m_ in range(2):
            mk = k1 if m_ == 0 else k2
            icol = 58 + 2 * s_ + m_
            p.add("vector", lambda v, mk=mk: v.tensor_mul(
                lg[0:1, mk:mk + 8], lg[0:1, mk:mk + 8], iota8[0:1, :]),
                reads=R("lg", "iota"), writes=R("lg"))
            p.add("vector", lambda v, mk=mk, icol=icol: v.reduce_sum(
                lg[0:1, icol:icol + 1], lg[0:1, mk:mk + 8], axis=AX.X),
                reads=R("lg"), writes=R("lg"))
            p.add("vector", lambda v, s_=s_, m_=m_, icol=icol: v.tensor_copy(
                idx_i[0:1, 2 * s_ + m_:2 * s_ + m_ + 1], lg[0:1, icol:icol + 1]),
                reads=R("lg"), writes=R("idx"))
    for j, col in enumerate((53, 54, 56, 57)):
        p.add("tensor", lambda t, j=j, col=col: t.matmul(
            ps[6][0:128, j:j + 1], onesf[0:1, :], lg[0:1, col:col + 1],
            start=True, stop=True),
            reads=R("onesf", "lg"), writes=R("ps6"))
        p.add("vector", lambda v, j=j: v.tensor_copy(c_bc[:, j:j + 1], ps[6][0:128, j:j + 1]),
              reads=R("ps6"), writes=R("cbc"))

    _emit_mirror_nT(p, nc, es, xmid, nbf, None, xa, xb, xf_a, xf_b, sqd, small,
                    pre_norm="rms", prep_eps=1e-6)

    def branch_load(idx_col, out_ap, dram_by_bank, name, reads_extra=()):
        def fn(g, inc):
            reg = g.alloc_register()
            g.reg_load(reg, idx_i[0:1, idx_col:idx_col + 1])
            for b in range(8):
                with g.If_eq(reg, b):
                    inc(g.dma_start(out=out_ap, in_=dram_by_bank(b)))
            g.end_ifs()
            g.free_register(reg)
        p.add("gpsimd", fn, reads=R("idx") + tuple(reads_extra), writes=R(name),
              dma=True, raw=True)

    def scale_bd2(name, sb, width, ccol):
        p.add("vector", lambda v, sb=sb, width=width, ccol=ccol: v.tensor_scalar(
            sb[:, 0:width], sb[:, 0:width], c_bc[:, ccol:ccol + 1], None, ALU.mult),
            reads=R(name, "cbc"), writes=R(name))

    NTG = T_FULL // 512

    for m_ in range(2):
        idxc = 0 + m_
        branch_load(idxc, bd1sb[:, 0:8 * 128].rearrange("p (j w) -> p j w", w=128),
                    lambda b: fcbd1_d[b].rearrange("j p w -> p j w"), "bd1")
        branch_load(idxc, bd2sb[:].rearrange("p (j w) -> p j w", w=128),
                    lambda b: fcbd2_d[b].rearrange("j p w -> p j w"), "bd2")
        branch_load(idxc, mixA[:].rearrange("p (c o) -> p c o", o=4096),
                    lambda b: fcmix_d[b].rearrange("(c p) o -> p c o", p=128), "mixA")
        scale_bd2("bd2", bd2sb, 32 * 128, 0 + m_)
        for tg in range(NTG):
            buf, bufname = (nTg_a, "nTg_a") if tg % 2 == 0 else (nTg_b, "nTg_b")
            _load_nTg(p, nbf, tg, buf, bufname)
            for ch in range(8):
                pa = ps[tg % 2]
                p.add("tensor", lambda tt, ch=ch, pa=pa, buf=buf: tt.matmul(
                    pa[:, 0:512], bd1sb[:, ch * 128:(ch + 1) * 128],
                    buf[:, ch * 512:(ch + 1) * 512], start=True, stop=True),
                    reads=R("bd1", bufname), writes=R(f"ps{tg % 2}"))
                p.add("vector", lambda v, ch=ch, pa=pa: v.tensor_copy(
                    h1sb[:, ch * 512:(ch + 1) * 512], pa[:, 0:512]),
                    reads=R(f"ps{tg % 2}"), writes=R("h1"))
            for och in range(32):
                pm = ps[2 + och % 2]

                def mm_mix(tt, och=och, pm=pm):
                    for ch in range(8):
                        inst = tt.matmul(
                            pm[:, 0:512],
                            mixA[:, ch * 4096 + och * 128: ch * 4096 + och * 128 + 128],
                            h1sb[:, ch * 512:(ch + 1) * 512],
                            start=(ch == 0), stop=(ch == 7))
                    return inst
                p.add("tensor", mm_mix, reads=R("mixA", "h1"), writes=R(f"ps{2 + och % 2}"))
                msl = och % 2
                p.add("vector", lambda v, pm=pm, msl=msl: v.tensor_copy(
                    mix_sb[:, msl * 512:(msl + 1) * 512], pm[:, 0:512]),
                    reads=R(f"ps{2 + och % 2}"), writes=R(f"mxsb{msl}"))
                pb2 = ps[4 + och % 2]
                p.add("tensor", lambda tt, och=och, msl=msl, pb2=pb2: tt.matmul(
                    pb2[:, 0:512], bd2sb[:, och * 128:(och + 1) * 128],
                    mix_sb[:, msl * 512:(msl + 1) * 512], start=True, stop=True),
                    reads=R("bd2", f"mxsb{msl}"), writes=R(f"ps{4 + och % 2}"))
                sl = och % 4
                p.add("vector", lambda v, pb2=pb2, sl=sl: v.tensor_copy(
                    stmp[:, sl * 512:(sl + 1) * 512], pb2[:, 0:512]),
                    reads=R(f"ps{4 + och % 2}"), writes=R(f"stmp{sl}"))
                p.add("gpsimd", lambda g, och=och, tg=tg, sl=sl, m_=m_: g.dma_start(
                    out=hacc[och * 128:(och + 1) * 128, tg * 512:(tg + 1) * 512],
                    in_=stmp[:, sl * 512:(sl + 1) * 512],
                    accum_op=(ALU.bypass if m_ == 0 else ALU.add)),
                    reads=R(f"stmp{sl}"), writes=R(f"hacc{och}_{tg}"), dma=True)

    for och in range(32):
        scr, scrn = scr_a, "scr_a"
        p.add("gpsimd", lambda g, och=och, scr=scr: g.dma_start(
            out=scr[:], in_=hacc[och * 128:(och + 1) * 128, :]),
            reads=tuple(f"hacc{och}_{tg}" for tg in range(NTG)), writes=R(scrn), dma=True)
        p.add("scalar", lambda sc, scr=scr: sc.activation(scr[:], scr[:], AF.Relu),
              reads=R(scrn), writes=R(scrn))
        p.add("scalar", lambda sc, scr=scr: sc.activation(scrbf[:], scr[:], AF.Square),
              reads=R(scrn), writes=R("scrbf"))
        p.add("sync", lambda s, och=och: s.dma_start(
            out=hact[och * 128:(och + 1) * 128, :], in_=scrbf[:]),
            reads=R("scrbf"), writes=R(f"hact{och}"), dma=True)

    allhact = tuple(f"hact{o}" for o in range(32))
    for m_ in range(2):
        idxc = 2 + m_
        branch_load(idxc, bd1sb[:].rearrange("p (j w) -> p j w", w=128),
                    lambda b: pjbd1_d[b].rearrange("j p w -> p j w"), "bd1")
        branch_load(idxc, bd2sb[:, 0:8 * 128].rearrange("p (j w) -> p j w", w=128),
                    lambda b: pjbd2_d[b].rearrange("j p w -> p j w"), "bd2")
        branch_load(idxc, mixA[:].rearrange("p (c o) -> p c o", o=D),
                    lambda b: pjmix_d[b].rearrange("(c p) o -> p c o", p=128), "mixA")
        scale_bd2("bd2", bd2sb, 8 * 128, 2 + m_)
        for tg in range(NTG):
            for ch in range(32):
                pa = ps[tg % 2]
                sl2 = ch % 2
                p.add("gpsimd", lambda g, ch=ch, tg=tg, sl2=sl2: g.dma_start(
                    out=nTg_a[:, sl2 * 512:(sl2 + 1) * 512] if False else
                    mix_sb[:, sl2 * 512:(sl2 + 1) * 512],
                    in_=hact[ch * 128:(ch + 1) * 128, tg * 512:(tg + 1) * 512]),
                    reads=R(f"hact{ch}",), writes=R(f"mxsb{sl2}"), dma=True)
                p.add("tensor", lambda tt, ch=ch, pa=pa, sl2=sl2: tt.matmul(
                    pa[:, 0:512], bd1sb[:, ch * 128:(ch + 1) * 128],
                    mix_sb[:, sl2 * 512:(sl2 + 1) * 512], start=True, stop=True),
                    reads=R("bd1", f"mxsb{sl2}"), writes=R(f"ps{tg % 2}"))
                p.add("vector", lambda v, ch=ch, pa=pa: v.tensor_copy(
                    h1sb[:, ch * 512:(ch + 1) * 512], pa[:, 0:512]),
                    reads=R(f"ps{tg % 2}"), writes=R("h1"))
            for och in range(8):
                pm = ps[2 + och % 2]

                def mm_mixp(tt, och=och, pm=pm):
                    for ch in range(32):
                        inst = tt.matmul(
                            pm[:, 0:512],
                            mixA[:, ch * D + och * 128: ch * D + och * 128 + 128],
                            h1sb[:, ch * 512:(ch + 1) * 512],
                            start=(ch == 0), stop=(ch == 31))
                    return inst
                p.add("tensor", mm_mixp, reads=R("mixA", "h1"), writes=R(f"ps{2 + och % 2}"))
                msl = och % 2
                p.add("vector", lambda v, pm=pm, msl=msl: v.tensor_copy(
                    mix_sb[:, msl * 512:(msl + 1) * 512], pm[:, 0:512]),
                    reads=R(f"ps{2 + och % 2}"), writes=R(f"mxsb{msl}"))
                for ti in range(4):
                    t = tg * 4 + ti
                    pb2 = ps[4 + ti % 2]
                    p.add("tensor", lambda tt, och=och, msl=msl, ti=ti, pb2=pb2: tt.matmul(
                        pb2[:, 0:128], mix_sb[:, msl * 512 + ti * 128: msl * 512 + ti * 128 + 128],
                        bd2sb[:, och * 128:(och + 1) * 128], start=True, stop=True),
                        reads=R("bd2", f"mxsb{msl}"), writes=R(f"ps{4 + ti % 2}"))
                    sl = (och * 4 + ti) % 4
                    p.add("vector", lambda v, pb2=pb2, sl=sl: v.tensor_copy(
                        stmp[:, sl * 512:sl * 512 + 128], pb2[:, 0:128]),
                        reads=R(f"ps{4 + ti % 2}"), writes=R(f"stmp{sl}"))
                    p.add("gpsimd", lambda g, t=t, och=och, sl=sl, m_=m_: g.dma_start(
                        out=acc_d[t * 128:(t + 1) * 128, och * 128:(och + 1) * 128],
                        in_=stmp[:, sl * 512:sl * 512 + 128],
                        accum_op=(ALU.bypass if m_ == 0 else ALU.add)),
                        reads=R(f"stmp{sl}"), writes=R(f"acc{t}_{och}"), dma=True)

    for t in range(NT):
        par = t % 2
        xf, xfn = (xf_a, "xf_a") if par == 0 else (xf_b, "xf_b")
        x1f, x1fn = (x1f_a, "x1f_a") if par == 0 else (x1f_b, "x1f_b")
        yt, ytn = (yt_a, "yt_a") if par == 0 else (yt_b, "yt_b")
        _emit_final_epilogue(p, nc, t, acc_d, xin, xmid, xf, xfn, x1f, x1fn,
                             yt, ytn, sqd, small, xout,
                             acc_reads=[f"acc{t}_{o}" for o in range(8)])

    if standalone:
        p.emit(es)
        es.close()
        return nc
    p.fence()
    es.close()
    return None


def _bd_pairs(w, nb):
    out = np.zeros((nb // 2, 128, 128), np.float32)
    for j in range(nb // 2):
        out[j, 0:64, 0:64] = w[2 * j]
        out[j, 64:128, 64:128] = w[2 * j + 1]
    return out


def prep_l0_ffn_bank(inp):
    io = {}
    io["c_onesb"] = CONSTS["c_onesb"]
    io["c_onesf"] = CONSTS["c_onesf"]
    io["c_iota8"] = np.arange(8, dtype=np.float32).reshape(1, 8)
    io["fc_logits"] = np.ascontiguousarray(inp["L0_ffn_fc_logits"]).reshape(1, 8)
    io["pj_logits"] = np.ascontiguousarray(inp["L0_ffn_pj_logits"]).reshape(1, 8)
    fc_s1 = np.asarray(inp["L0_ffn_fc_s1"])
    fc_s2 = np.asarray(inp["L0_ffn_fc_s2"])
    fc_mix = np.asarray(inp["L0_ffn_fc_mix"])
    pj_s1 = np.asarray(inp["L0_ffn_pj_s1"])
    pj_s2 = np.asarray(inp["L0_ffn_pj_s2"])
    pj_mix = np.asarray(inp["L0_ffn_pj_mix"])
    io["fc_bd1"] = np.stack([_bd_pairs(fc_s1[b].reshape(16, 64, 64), 16) for b in range(8)])
    io["fc_bd2"] = np.stack([_bd_pairs(fc_s2[b].reshape(64, 64, 64), 64) for b in range(8)])
    io["pj_bd1"] = np.stack([_bd_pairs(pj_s1[b].reshape(64, 64, 64), 64) for b in range(8)])
    io["pj_bd2"] = np.stack([_bd_pairs(pj_s2[b].reshape(16, 64, 64), 16) for b in range(8)])
    g16, c64 = np.meshgrid(np.arange(16), np.arange(64), indexing="ij")
    fcm = np.empty((8, 1024, 4096), np.float32)
    for b in range(8):
        wmT = fc_mix[b].T
        idx = (c64 * 16 + (g16 + b) % 16).reshape(-1)
        fcm[b] = wmT[idx]
    io["fc_mixp"] = fcm
    g64, c64b = np.meshgrid(np.arange(64), np.arange(64), indexing="ij")
    pjm = np.empty((8, 4096, 1024), np.float32)
    for b in range(8):
        wmT = pj_mix[b].T
        idx = (c64b * 64 + (g64 + b) % 64).reshape(-1)
        pjm[b] = wmT[idx]
    io["pj_mixp"] = pjm
    return io


# version 07
# speedup vs baseline: 1.7405x; 1.5704x over previous
import contextlib
import numpy as np
import ml_dtypes

import concourse.bass as bass
import concourse.mybir as mybir
from concourse.bass_utils import run_bass_kernel_spmd

F32 = mybir.dt.float32
BF16 = mybir.dt.bfloat16
AF = mybir.ActivationFunctionType
ALU = mybir.AluOpType
AX = mybir.AxisListType

BF = ml_dtypes.bfloat16
T_FULL = 4096
D = 1024
NT = T_FULL // 128
HD = 64
EPS_TOP = 1e-5

ENGINES = ("tensor", "vector", "scalar", "gpsimd", "sync")
DEBUG = False
DEBUG_ZERO = True
DEBUG_A = True
DEBUG_B = True


NUM_EPOCH_SEMS = 6


class Epoch:
    __slots__ = ("engine", "idx", "count", "open", "cum_before", "cum_after")

    def __init__(self, engine, idx):
        self.engine = engine
        self.idx = idx
        self.count = 0
        self.open = True


class Op:
    __slots__ = ("engine", "fn", "dma", "deps", "marked", "inc_no", "tag", "epoch",
                 "raw", "fence_eps")

    def __init__(self, engine, fn, dma, tag, raw=False):
        self.engine = engine
        self.fn = fn
        self.dma = dma
        self.deps = set()
        self.marked = False
        self.inc_no = None
        self.tag = tag
        self.epoch = None
        self.raw = raw
        self.fence_eps = None


class Prog:

    def __init__(self, nc):
        self.nc = nc
        self.per_eng = {e: [] for e in ENGINES}
        self.last_w = {}
        self.readers = {}
        self.cur_epoch = {}
        self.epochs = {e: [] for e in ENGINES}

    def fence(self):
        lasts = [ops[-1] for ops in self.per_eng.values() if ops]
        fence_eps = {}
        for e in ENGINES:
            for ep in self.epochs[e]:
                ep.open = False
                fence_eps[(e, ep.idx % NUM_EPOCH_SEMS)] = ep
        self.cur_epoch = {}
        for e in ENGINES:
            op = Op(e, (lambda eng: eng.nop()), False, "fence")
            op.deps = {d for d in lasts
                       if not (d.engine == e == "tensor" and not d.dma)}
            op.fence_eps = list(fence_eps.values())
            self.per_eng[e].append(op)
        self.last_w = {}
        self.readers = {}

    def _dma_epoch(self, engine):
        ep = self.cur_epoch.get(engine)
        if ep is None or not ep.open:
            ep = Epoch(engine, len(self.epochs[engine]))
            self.epochs[engine].append(ep)
            self.cur_epoch[engine] = ep
        return ep

    def add(self, engine, fn, reads=(), writes=(), dma=False, tag="", raw=False):
        op = Op(engine, fn, dma, tag, raw)
        deps = set()
        for r in reads:
            w = self.last_w.get(r)
            if w is not None:
                deps.add(w)
        for r in writes:
            w = self.last_w.get(r)
            if w is not None:
                deps.add(w)
            for rd in self.readers.get(r, ()):
                deps.add(rd)
        for r in reads:
            self.readers.setdefault(r, []).append(op)
        for r in writes:
            self.last_w[r] = op
            self.readers[r] = []
        op.deps = {d for d in deps
                   if d is not op and not (d.engine == engine == "tensor"
                                           and not d.dma and not dma)}
        for d in op.deps:
            if d.dma and d.epoch.open:
                d.epoch.open = False
        if dma:
            ep = self._dma_epoch(engine)
            ep.count += 1
            op.epoch = ep
        self.per_eng[engine].append(op)
        return op

    def emit(self, es):
        nc = self.nc
        for ops in self.per_eng.values():
            for op in ops:
                for d in op.deps:
                    if not d.dma:
                        d.marked = True
        sem_cum = {}
        for e in ENGINES:
            for ep in self.epochs[e]:
                ep.open = False
                slot = ep.idx % NUM_EPOCH_SEMS
                prev = sem_cum.get((e, slot), 0)
                ep.cum_before = prev
                ep.cum_after = prev + ep.count
                sem_cum[(e, slot)] = ep.cum_after
        sem_c, sem_d = {}, {}
        cnt_c = {e: 0 for e in ENGINES}
        for e in ENGINES:
            for op in self.per_eng[e]:
                if not op.dma and op.marked:
                    cnt_c[e] += 1
                    op.inc_no = cnt_c[e]
        for e in ENGINES:
            if cnt_c[e]:
                sem_c[e] = es.enter_context(nc.semaphore(f"c_{e}"))
            nslots = min(NUM_EPOCH_SEMS, len(self.epochs[e]))
            if nslots:
                sem_d[e] = [es.enter_context(nc.semaphore(f"d_{e}{i}"))
                            for i in range(nslots)]

        def emit_engine(eng_obj, ename):
            waited = {}
            seen_epoch = {}
            for op in self.per_eng[ename]:
                if op.fence_eps is not None:
                    for ep in op.fence_eps:
                        slot = ep.idx % NUM_EPOCH_SEMS
                        key = ("d", ep.engine, slot)
                        val = 16 * ep.cum_after
                        if waited.get(key, 0) < val:
                            waited[key] = val
                            eng_obj.wait_ge(sem_d[ep.engine][slot], val)
                for d in sorted(op.deps, key=lambda o: (o.engine, o.inc_no or 0, id(o))):
                    if d.dma:
                        ep = d.epoch
                        slot = ep.idx % NUM_EPOCH_SEMS
                        key = ("d", d.engine, slot)
                        val = 16 * ep.cum_after
                        sem = sem_d[d.engine][slot]
                    else:
                        key, val, sem = ("c", d.engine), d.inc_no, sem_c[d.engine]
                    if waited.get(key, 0) < val:
                        waited[key] = val
                        eng_obj.wait_ge(sem, val)
                if op.dma:
                    ep = op.epoch
                    slot = ep.idx % NUM_EPOCH_SEMS
                    if seen_epoch.get(slot) != ep.idx:
                        seen_epoch[slot] = ep.idx
                        if ep.cum_before > 0:
                            key = ("d", ename, slot)
                            val = 16 * ep.cum_before
                            if waited.get(key, 0) < val:
                                waited[key] = val
                                eng_obj.wait_ge(sem_d[ename][slot], val)
                if op.raw:
                    sem = sem_d[ename][op.epoch.idx % NUM_EPOCH_SEMS]
                    op.fn(eng_obj, lambda inst, sem=sem: inst.then_inc(sem, 16))
                else:
                    inst = op.fn(eng_obj)
                    if op.dma:
                        inst.then_inc(sem_d[ename][op.epoch.idx % NUM_EPOCH_SEMS], 16)
                    elif op.marked:
                        inst.then_inc(sem_c[ename], 1)
            if ename == "sync":
                for e in ENGINES:
                    for slot in range(min(NUM_EPOCH_SEMS, len(self.epochs[e]))):
                        tot = sem_cum.get((e, slot), 0)
                        if tot:
                            eng_obj.wait_ge(sem_d[e][slot], 16 * tot)

        block = es.enter_context(nc.Block())
        block.tensor(lambda t: emit_engine(t, "tensor"))
        block.vector(lambda v: emit_engine(v, "vector"))
        block.scalar(lambda s: emit_engine(s, "scalar"))
        block.gpsimd(lambda g: emit_engine(g, "gpsimd"))
        block.sync(lambda s: emit_engine(s, "sync"))


def R(*names):
    return tuple(names)


class _FakeHandle:
    def __init__(self, ap):
        self._ap = ap

    def ap(self):
        return self._ap


class NCProxy:

    def __init__(self, nc, pfx, redirect):
        self._nc = nc
        self._pfx = pfx
        self._rd = redirect

    def dram_tensor(self, name, shape, dtype, kind="Internal", **kw):
        if name in self._rd:
            return _FakeHandle(self._rd[name])
        return self._nc.dram_tensor(self._pfx + name, shape, dtype, kind="Internal"
                                    if kind == "ExternalOutput" and name not in self._rd
                                    else ("ExternalInput" if kind == "ExternalInput" else "Internal"),
                                    **kw)

    def sbuf_tensor(self, name, shape, dtype, **kw):
        return self._nc.sbuf_tensor(self._pfx + name, shape, dtype, **kw)

    def psum_tensor(self, name, shape, dtype, **kw):
        return self._nc.psum_tensor(self._pfx + name, shape, dtype, **kw)

    def __getattr__(self, a):
        return getattr(self._nc, a)



def host_constants():
    c = {}
    c["c_ident"] = np.eye(128, dtype=np.float32)
    c["c_onesb"] = np.ones((1, 512), dtype=BF)
    c["c_onesf"] = np.ones((1, 128), dtype=np.float32)
    masks = np.zeros((4, 128, 512), dtype=np.float32)
    for j, off in enumerate((0, 128, 256, 384)):
        kl = np.arange(128)[:, None]
        ql = np.arange(512)[None, :]
        masks[j] = np.where(ql >= kl + off, 0.0, -1e30)
    c["c_masks"] = masks.astype(BF)
    return c


CONSTS = host_constants()



def emit_ln_tok(p, yt, ytn, sqd, small, eps_col, out, outname, smn="small"):
    p.add("scalar", lambda sc: sc.activation(sqd[:], yt[:], AF.Square,
                                             accum_out=small[:, 4:5]),
          reads=R(ytn), writes=R(smn))
    p.add("vector", lambda v: v.reduce_sum(small[:, 5:6], small[:, 0:2], axis=AX.X),
          reads=R(smn), writes=R(smn))
    p.add("vector", lambda v: v.tensor_scalar_mul(small[:, 6:7], small[:, 5:6], 1.0 / D),
          reads=R(smn), writes=R(smn))
    p.add("vector", lambda v: v.tensor_mul(small[:, 7:8], small[:, 6:7], small[:, 6:7]),
          reads=R(smn), writes=R(smn))
    p.add("vector", lambda v: v.scalar_tensor_tensor(small[:, 7:8], small[:, 4:5], 1.0 / D,
                                                     small[:, 7:8], ALU.mult, ALU.subtract),
          reads=R(smn), writes=R(smn))
    p.add("scalar", lambda sc: sc.activation(small[:, 8:9], small[:, 7:8], AF.Sqrt,
                                             bias=eps_col),
          reads=R(smn), writes=R(smn))
    p.add("vector", lambda v: v.reciprocal(small[:, 9:10], small[:, 8:9]),
          reads=R(smn), writes=R(smn))
    p.add("vector", lambda v: v.scalar_tensor_tensor(small[:, 10:11], small[:, 6:7], -1.0,
                                                     small[:, 9:10], ALU.mult, ALU.mult),
          reads=R(smn), writes=R(smn))
    p.add("scalar", lambda sc: sc.activation(out[:], yt[:], AF.Identity,
                                             bias=small[:, 10:11], scale=small[:, 9:10]),
          reads=R(ytn, smn), writes=R(outname))



class AttnCfg:
    def __init__(self, name, n_heads=16, kv_heads=16, dq=1024, dkv=1024,
                 has_bq=False, has_bkv=False, has_bo=False,
                 qk_prep=None, pre_norm=None, post_ln=False, epilogue="top",
                 gains=False, nf=0, prep_eps=1e-6, fold_gb=False):
        self.name = name
        self.n_heads, self.kv_heads = n_heads, kv_heads
        self.dq, self.dkv = dq, dkv
        self.has_bq, self.has_bkv, self.has_bo = has_bq, has_bkv, has_bo
        self.qk_prep = qk_prep
        self.pre_norm = pre_norm
        self.post_ln = post_ln
        self.epilogue = epilogue
        self.gains = gains
        self.nf = nf
        self.prep_eps = prep_eps
        self.fold_gb = fold_gb


def build_attn_launch(cfg: AttnCfg, FUSE=None):
    standalone = FUSE is None
    nc = (bass.Bass("TRN2", target_bir_lowering=False) if standalone
          else NCProxy(FUSE["nc"], FUSE["pfx"], FUSE["rd"]))
    es = contextlib.ExitStack()
    H, KVH, dq, dkv = cfg.n_heads, cfg.kv_heads, cfg.dq, cfg.dkv
    NPAIR = H // 2
    KVPAIR = max(KVH // 2, 1)

    xin = nc.dram_tensor("xin", [T_FULL, D], F32, kind="ExternalInput").ap()
    if cfg.epilogue == "top":
        xout = nc.dram_tensor("xout", [T_FULL, D], F32, kind="ExternalOutput").ap()

    ident_d = nc.dram_tensor("c_ident", [128, 128], F32, kind="ExternalInput").ap()
    onesb_d = nc.dram_tensor("c_onesb", [1, 512], BF16, kind="ExternalInput").ap()
    onesf_d = nc.dram_tensor("c_onesf", [1, 128], F32, kind="ExternalInput").ap()
    masks_d = nc.dram_tensor("c_masks", [4, 128, 512], BF16, kind="ExternalInput").ap()

    wq_d = nc.dram_tensor("wq", [D, dq], F32, kind="ExternalInput").ap()
    wk_d = nc.dram_tensor("wk", [D, dkv], F32, kind="ExternalInput").ap()
    wv_d = nc.dram_tensor("wv", [D, dkv], F32, kind="ExternalInput").ap()
    wo_d = nc.dram_tensor("wo", [D, D], F32, kind="ExternalInput").ap()
    bias_d = {}
    if cfg.has_bq:
        bias_d["bq"] = nc.dram_tensor("bq", [1, dq], F32, kind="ExternalInput").ap()
    if cfg.has_bkv:
        bias_d["bk"] = nc.dram_tensor("bk", [1, dkv], F32, kind="ExternalInput").ap()
        bias_d["bv"] = nc.dram_tensor("bv", [1, dkv], F32, kind="ExternalInput").ap()
    if cfg.has_bo:
        bias_d["bo"] = nc.dram_tensor("bo", [1, D], F32, kind="ExternalInput").ap()
    if cfg.post_ln:
        pg_d = nc.dram_tensor("post_gamma", [1, D], F32, kind="ExternalInput").ap()
        pb_d = nc.dram_tensor("post_beta", [1, D], F32, kind="ExternalInput").ap()
    if cfg.gains:
        qg_d = nc.dram_tensor("q_gain", [1, cfg.n_heads], F32, kind="ExternalInput").ap()
        kg_d = nc.dram_tensor("k_gain", [1, cfg.kv_heads], F32, kind="ExternalInput").ap()
    if cfg.qk_prep is not None and cfg.nf:
        ropecs_d = nc.dram_tensor("rope_cs", [T_FULL, 2 * cfg.nf], F32, kind="ExternalInput").ap()
    if cfg.fold_gb:
        gam_d = nc.dram_tensor("ln_gam_col", [128, 8], F32, kind="ExternalInput").ap()
    if cfg.epilogue == "inner":
        xmid_d = nc.dram_tensor("xmid", [T_FULL, D], F32, kind="ExternalOutput").ap()

    skind = "ExternalOutput" if (DEBUG and DEBUG_ZERO) else "Internal"
    nbf = nc.dram_tensor("nbf", [T_FULL, D], BF16, kind=skind).ap()
    qd2 = nc.dram_tensor("qd2", [H, T_FULL, 128], BF16, kind=skind).ap()
    kd2 = nc.dram_tensor("kd2", [KVH, T_FULL, 128], BF16, kind=skind).ap()
    vd = nc.dram_tensor("vd", [KVH, T_FULL, HD], BF16, kind=skind).ap()
    if DEBUG:
        dbg_ot = nc.dram_tensor("dbg_ot", [8, 128, T_FULL], BF16, kind="ExternalOutput").ap()
        dbg_mx = nc.dram_tensor("dbg_mx", [1, T_FULL], BF16, kind="ExternalOutput").ap()
        dbg_rm = nc.dram_tensor("dbg_rm", [128, 32], F32, kind="ExternalOutput").ap()
        dbg_yt = nc.dram_tensor("dbg_yt", [T_FULL, D], F32, kind="ExternalOutput").ap()
        dbg_sm = nc.dram_tensor("dbg_sm", [NT, 128, 16], F32, kind="ExternalOutput").ap()

    big = es.enter_context(nc.sbuf_tensor("big", [128, 8 * T_FULL], BF16))
    wq_sb = es.enter_context(nc.sbuf_tensor("wq_sb", [128, 8 * dq], BF16))
    wk_sb = es.enter_context(nc.sbuf_tensor("wk_sb", [128, 8 * dkv], BF16))
    wv_sb = es.enter_context(nc.sbuf_tensor("wv_sb", [128, 8 * dkv], BF16))
    xa = es.enter_context(nc.sbuf_tensor("xa", [128, D], BF16))
    xb = es.enter_context(nc.sbuf_tensor("xb", [128, D], BF16))
    qt_a = es.enter_context(nc.sbuf_tensor("qt_a", [128, dq], BF16))
    qt_b = es.enter_context(nc.sbuf_tensor("qt_b", [128, dq], BF16))
    kt_a = es.enter_context(nc.sbuf_tensor("kt_a", [128, dkv], BF16))
    kt_b = es.enter_context(nc.sbuf_tensor("kt_b", [128, dkv], BF16))
    vt_a = es.enter_context(nc.sbuf_tensor("vt_a", [128, dkv], BF16))
    vt_b = es.enter_context(nc.sbuf_tensor("vt_b", [128, dkv], BF16))
    QT = es.enter_context(nc.sbuf_tensor("QT", [128, T_FULL], BF16))
    KT = es.enter_context(nc.sbuf_tensor("KT", [128, T_FULL], BF16))
    VB = es.enter_context(nc.sbuf_tensor("VB", [128, NT * 65], BF16))
    if KVH < H:
        VBg = es.enter_context(nc.sbuf_tensor("VBg", [128, NT * 65], BF16))
    PT_a = es.enter_context(nc.sbuf_tensor("PT_a", [128, 512], BF16))
    PT_b = es.enter_context(nc.sbuf_tensor("PT_b", [128, 512], BF16))
    PT_c = es.enter_context(nc.sbuf_tensor("PT_c", [128, 512], BF16))
    cmx = es.enter_context(nc.sbuf_tensor("cmx", [128, 2 * max(T_FULL // 512, 1)], F32))
    rm = es.enter_context(nc.sbuf_tensor("rm", [128, 2 * NT], F32))
    rs_row = es.enter_context(nc.sbuf_tensor("rs_row", [1, 1024], F32))
    otmp = es.enter_context(nc.sbuf_tensor("otmp", [64, 1024], F32))
    ident = es.enter_context(nc.sbuf_tensor("ident", [128, 128], F32))
    onesb = es.enter_context(nc.sbuf_tensor("onesb", [1, 512], BF16))
    onesf = es.enter_context(nc.sbuf_tensor("onesf", [1, 128], F32))
    masks = es.enter_context(nc.sbuf_tensor("masks", [128, 4 * 512], BF16))
    bias_sb = {}
    for bn, bd in bias_d.items():
        bias_sb[bn] = es.enter_context(nc.sbuf_tensor(bn + "_sb", [1, bd.shape[1]], BF16))
    if cfg.fold_gb:
        for bn, n in (("bq", dq), ("bk", dkv), ("bv", dkv)):
            assert bn not in bias_sb
            bias_sb[bn] = es.enter_context(nc.sbuf_tensor(bn + "_sb", [1, n], BF16))
    xf_a = es.enter_context(nc.sbuf_tensor("xf_a", [128, D], F32))
    xf_b = es.enter_context(nc.sbuf_tensor("xf_b", [128, D], F32))
    yt_a = es.enter_context(nc.sbuf_tensor("yt_a", [128, D], F32))
    yt_b = es.enter_context(nc.sbuf_tensor("yt_b", [128, D], F32))
    sqd = es.enter_context(nc.sbuf_tensor("sqd", [128, D], F32))
    small = es.enter_context(nc.sbuf_tensor("small", [128, 16], F32))
    small2 = es.enter_context(nc.sbuf_tensor("small2", [128, 16], F32))
    safe_max = (cfg.qk_prep == "l0ffn")
    if not safe_max:
        qno_all = es.enter_context(nc.sbuf_tensor("qno_all", [128, NT * cfg.n_heads], F32))
        knm2 = es.enter_context(nc.sbuf_tensor("knm2", [128, NT], F32))
        negkmax = es.enter_context(nc.sbuf_tensor("negkmax", [1, 16], F32))
        qsq = es.enter_context(nc.sbuf_tensor("qsq", [128, 1024], F32))
        kno = es.enter_context(nc.sbuf_tensor("kno", [128, 16], F32))
    if cfg.post_ln:
        gb_bc = es.enter_context(nc.sbuf_tensor("gb_bc", [128, 2 * D], BF16))
        prow = es.enter_context(nc.sbuf_tensor("prow", [1, 2 * D], BF16))
    if cfg.gains:
        gains_sb = es.enter_context(nc.sbuf_tensor("gains_sb", [1, cfg.n_heads + cfg.kv_heads], F32))
        gain_bc = es.enter_context(nc.sbuf_tensor("gain_bc", [128, cfg.n_heads + cfg.kv_heads], F32))
    if cfg.qk_prep is not None:
        rope_a = es.enter_context(nc.sbuf_tensor("rope_a", [128, 2 * max(cfg.nf, 1)], F32))
        rtmp = es.enter_context(nc.sbuf_tensor("rtmp", [128, 2048], F32))
        prep_sc = es.enter_context(nc.sbuf_tensor("prep_sc", [128, 128], F32))
    if cfg.fold_gb:
        gam_sb = es.enter_context(nc.sbuf_tensor("gam_sb", [128, 8], F32))

    ps = [es.enter_context(nc.psum_tensor(f"ps{i}", [128, 512], F32)) for i in range(8)]

    p = Prog(nc) if standalone else FUSE["p"]

    p.add("vector", lambda v: v.memset(small[:, 11:12], EPS_TOP), writes=R("small"))
    p.add("vector", lambda v: v.memset(small2[:, 11:12], EPS_TOP), writes=R("small2"))

    p.add("gpsimd", lambda g: g.dma_start(out=ident[:], in_=ident_d), writes=R("ident"), dma=True)
    p.add("gpsimd", lambda g: g.dma_start(out=onesb[:], in_=onesb_d), writes=R("onesb"), dma=True)
    p.add("gpsimd", lambda g: g.dma_start(out=onesf[:], in_=onesf_d), writes=R("onesf"), dma=True)
    p.add("gpsimd", lambda g: g.dma_start(
        out=masks[:].rearrange("p (j n) -> p j n", n=512),
        in_=masks_d.rearrange("j p n -> p j n")),
        writes=R("masks"), dma=True)
    for wname, wsb, wd, dout in (("wq_sb", wq_sb, wq_d, dq), ("wk_sb", wk_sb, wk_d, dkv),
                                 ("wv_sb", wv_sb, wv_d, dkv)):
        p.add("gpsimd", lambda g, wsb=wsb, wd=wd, dout=dout: g.dma_start(
            out=wsb[:].rearrange("p (c o) -> p c o", o=dout),
            in_=wd.rearrange("(c p) o -> p c o", p=128)),
            writes=R(wname), dma=True)
    for bn in bias_d:
        p.add("gpsimd", lambda g, bn=bn: g.dma_start(out=bias_sb[bn][:], in_=bias_d[bn]),
              writes=R(bn), dma=True)
    if cfg.gains:
        p.add("gpsimd", lambda g: g.dma_start(out=gains_sb[0:1, 0:H], in_=qg_d),
              writes=R("gains"), dma=True)
        p.add("gpsimd", lambda g: g.dma_start(out=gains_sb[0:1, H:H + KVH], in_=kg_d),
              writes=R("gains"), dma=True)
        p.add("tensor", lambda t: t.matmul(ps[7][0:128, 0:H + KVH], onesf[0:1, :],
                                           gains_sb[:], start=True, stop=True),
              reads=R("onesf", "gains"), writes=R("ps7"))
        p.add("vector", lambda v: v.tensor_copy(gain_bc[:], ps[7][0:128, 0:H + KVH]),
              reads=R("ps7"), writes=R("gain_bc"))
    if cfg.fold_gb:
        p.add("gpsimd", lambda g: g.dma_start(out=gam_sb[:], in_=gam_d),
              writes=R("gam"), dma=True)
        beta_d = nc.dram_tensor("ln_beta_col", [128, 8], BF16, kind="ExternalInput").ap()
        beta_sb = es.enter_context(nc.sbuf_tensor("beta_sb", [128, 8], BF16))
        p.add("gpsimd", lambda g: g.dma_start(out=beta_sb[:], in_=beta_d),
              writes=R("beta"), dma=True)
        for wname, wsb, dout, bn in (("wq_sb", wq_sb, dq, "bq"), ("wk_sb", wk_sb, dkv, "bk"),
                                     ("wv_sb", wv_sb, dkv, "bv")):
            nchk = (dout + 511) // 512
            for half in range(nchk):
                n = min(512, dout - half * 512)

                def mm_brow(tt, wsb=wsb, dout=dout, half=half, n=n):
                    for c in range(8):
                        inst = tt.matmul(ps[6][0:1, 0:n], beta_sb[:, c:c + 1],
                                         wsb[:, c * dout + half * 512: c * dout + half * 512 + n],
                                         start=(c == 0), stop=(c == 7))
                    return inst
                p.add("tensor", mm_brow, reads=R("beta", wname), writes=R("ps6"))
                p.add("scalar", lambda sc, bn=bn, half=half, n=n: sc.activation(
                    bias_sb[bn][0:1, half * 512:half * 512 + n], ps[6][0:1, 0:n], AF.Copy,
                    bias=0.0, scale=1.0),
                    reads=R("ps6"), writes=R(bn))
        for wname, wsb, dout in (("wq_sb", wq_sb, dq), ("wk_sb", wk_sb, dkv),
                                 ("wv_sb", wv_sb, dkv)):
            for c in range(8):
                p.add("vector", lambda v, wsb=wsb, dout=dout, c=c: v.tensor_scalar(
                    wsb[:, c * dout:(c + 1) * dout], wsb[:, c * dout:(c + 1) * dout],
                    gam_sb[:, c:c + 1], None, ALU.mult),
                    reads=R(wname, "gam"), writes=R(wname))

    if cfg.pre_norm is not None:
        p.add("vector", lambda v: v.memset(small[:, 12:13], float(cfg.prep_eps)),
              writes=R("small"))
        p.add("vector", lambda v: v.memset(small2[:, 12:13], float(cfg.prep_eps)),
              writes=R("small2"))
    for t in range(NT):
        par = t % 2
        buf, bname = (xa, "xa") if par == 0 else (xb, "xb")
        if cfg.pre_norm is None:
            p.add("gpsimd", lambda g, buf=buf, t=t: g.dma_start(
                out=buf[:], in_=xin[t * 128:(t + 1) * 128, :]),
                writes=R(bname), dma=True)
        else:
            xf, xfn = (xf_a, "xf_a") if par == 0 else (xf_b, "xf_b")
            sml, smln = (small, "small") if par == 0 else (small2, "small2")
            p.add("gpsimd", lambda g, xf=xf, t=t: g.dma_start(
                out=xf[:], in_=xin[t * 128:(t + 1) * 128, :]),
                writes=R(xfn), dma=True)
            p.add("scalar", lambda sc, xf=xf, sml=sml: sc.activation(
                sqd[:], xf[:], AF.Square, accum_out=sml[:, 4:5]),
                reads=R(xfn), writes=R(smln))
            if cfg.pre_norm == "rms":
                p.add("vector", lambda v, sml=sml: v.tensor_scalar_mul(
                    sml[:, 5:6], sml[:, 4:5], 1.0 / D),
                      reads=R(smln), writes=R(smln))
                p.add("scalar", lambda sc, sml=sml: sc.activation(
                    sml[:, 6:7], sml[:, 5:6], AF.Sqrt, bias=sml[:, 12:13]),
                      reads=R(smln), writes=R(smln))
                p.add("vector", lambda v, sml=sml: v.reciprocal(sml[:, 7:8], sml[:, 6:7]),
                      reads=R(smln), writes=R(smln))
                p.add("scalar", lambda sc, xf=xf, buf=buf, sml=sml: sc.activation(
                    buf[:], xf[:], AF.Identity, bias=0.0, scale=sml[:, 7:8]),
                    reads=R(xfn, smln), writes=R(bname))
            else:
                p.add("scalar", lambda sc, xf=xf, sml=sml: sc.activation(
                    sqd[:], xf[:], AF.Identity, accum_out=sml[:, 0:1]),
                    reads=R(xfn), writes=R(smln))
                p.add("vector", lambda v, sml=sml: v.memset(sml[:, 1:2], 0.0),
                      writes=R(smln))
                emit_ln_tok(p, xf, xfn, sqd, sml, sml[:, 12:13], xf, xfn, smn=smln)
                p.add("vector", lambda v, xf=xf, buf=buf: v.tensor_copy(buf[:], xf[:]),
                      reads=R(xfn), writes=R(bname))
        p.add("sync", lambda s, buf=buf, t=t: s.dma_start(
            out=nbf[t * 128:(t + 1) * 128, :], in_=buf[:]),
            reads=R(bname), writes=R(f"dram_nbf{t}"), dma=True)

    for c in range(8):
        p.add("sync", lambda s, c=c: s.dma_start(
            out=big[:, c * T_FULL:(c + 1) * T_FULL],
            in_=nbf[:, c * 128:(c + 1) * 128], transpose=True),
            reads=tuple(f"dram_nbf{t}" for t in range(NT)), writes=R("big"), dma=True)

    def emit_proj(t, wsb, wname, dout, outtile, outname, bias, scale, evac_eng, bank_base):
        nch = (dout + 511) // 512
        for half in range(nch):
            n = min(512, dout - half * 512)
            bank = (bank_base + half) % 8
            pst = ps[bank]
            has_b = bias in bias_sb

            def mm_proj(tt, wsb=wsb, half=half, t=t, pst=pst, dout=dout, n=n, has_b=has_b):
                for c in range(8):
                    inst = tt.matmul(
                        pst[:, 0:n],
                        big[:, c * T_FULL + t * 128: c * T_FULL + (t + 1) * 128],
                        wsb[:, c * dout + half * 512: c * dout + half * 512 + n],
                        start=(c == 0), stop=(c == 7 and not has_b))
                return inst
            p.add("tensor", mm_proj, reads=R("big", wname), writes=R(f"ps{bank}"))
            if has_b:
                p.add("tensor", lambda tt, pst=pst, bias=bias, half=half, n=n: tt.matmul(
                    pst[:, 0:n], onesb[0:1, 0:128],
                    bias_sb[bias][0:1, half * 512:half * 512 + n], start=False, stop=True),
                    reads=R("onesb", bias), writes=R(f"ps{bank}"))
            if evac_eng == "scalar":
                p.add("scalar", lambda sc, outtile=outtile, half=half, pst=pst, n=n, scale=scale:
                      sc.activation(outtile[:, half * 512:half * 512 + n], pst[:, 0:n],
                                    AF.Copy, bias=0.0, scale=float(scale)),
                      reads=R(f"ps{bank}"), writes=R(outname))
            else:
                p.add("vector", lambda v, outtile=outtile, half=half, pst=pst, n=n:
                      v.tensor_copy(outtile[:, half * 512:half * 512 + n], pst[:, 0:n]),
                      reads=R(f"ps{bank}"), writes=R(outname))

    def rope_views(tile, nh, kind):
        if kind == "neox":
            base = tile[:, 0:nh * 64].rearrange("p (h w) -> p h w", w=64)
            e, o = base[:, :, 0:32], base[:, :, 32:64]
            w = 32
            cb = rope_a[:, 0:32].unsqueeze(1).broadcast_to((128, nh, 32))
            sb = rope_a[:, 32:64].unsqueeze(1).broadcast_to((128, nh, 32))
        else:
            base = tile[:, 0:nh * 64].rearrange("p (h i two) -> p h i two", h=nh, two=2)
            e, o = base[:, :, 0:16, 0:1], base[:, :, 0:16, 1:2]
            w = 16
            cb = rope_a[:, 0:16].unsqueeze(1).unsqueeze(3).broadcast_to((128, nh, 16, 1))
            sb = rope_a[:, 16:32].unsqueeze(1).unsqueeze(3).broadcast_to((128, nh, 16, 1))
        return e, o, w, cb, sb

    def tview(region, nh, w, kind):
        v = region[:, 0:nh * w].rearrange("p (h w) -> p h w", w=w)
        if kind == "gptj":
            v = v.unsqueeze(3)
        return v

    def emit_rope(tile, tname, nh, kind):
        e, o, w, cb, sb = rope_views(tile, nh, kind)
        t1 = tview(rtmp[:, 0:512], nh, w, kind)
        t2 = tview(rtmp[:, 512:1024], nh, w, kind)
        t3 = tview(rtmp[:, 1024:1536], nh, w, kind)
        t4 = tview(rtmp[:, 1536:2048], nh, w, kind)
        p.add("vector", lambda v: v.tensor_tensor(t1, e, cb, ALU.mult),
              reads=R(tname, "rope_a"), writes=R("rt1"))
        p.add("vector", lambda v: v.tensor_tensor(t2, o, sb, ALU.mult),
              reads=R(tname, "rope_a"), writes=R("rt2"))
        p.add("vector", lambda v: v.tensor_tensor(t3, e, sb, ALU.mult),
              reads=R(tname, "rope_a"), writes=R("rt3"))
        p.add("vector", lambda v: v.tensor_tensor(t4, o, cb, ALU.mult),
              reads=R(tname, "rope_a"), writes=R("rt4"))
        if kind == "neox":
            p.add("vector", lambda v: v.tensor_tensor(e, t1, t2, ALU.add),
                  reads=R("rt1", "rt2"), writes=R(tname))
            p.add("vector", lambda v: v.tensor_tensor(o, t4, t3, ALU.subtract),
                  reads=R(tname, "rt3", "rt4"), writes=R(tname))
        else:
            p.add("vector", lambda v: v.tensor_tensor(e, t1, t2, ALU.subtract),
                  reads=R("rt1", "rt2"), writes=R(tname))
            p.add("vector", lambda v: v.tensor_tensor(o, t4, t3, ALU.add),
                  reads=R(tname, "rt3", "rt4"), writes=R(tname))

    def emit_rms_gain(tile, tname, nh, gain_off, extra_scale):
        W = nh * 64
        p.add("scalar", lambda sc: sc.activation(sqd[:, 0:W], tile[:, 0:W], AF.Square),
              reads=R(tname), writes=R("sqd"))
        p.add("vector", lambda v: v.reduce_sum(
            prep_sc[:, 0:nh], sqd[:, 0:W].rearrange("p (h w) -> p h w", w=64), axis=AX.X),
            reads=R("sqd"), writes=R("prep"))
        p.add("vector", lambda v: v.tensor_scalar_mul(prep_sc[:, 16:16 + nh],
                                                      prep_sc[:, 0:nh], 1.0 / 64),
              reads=R("prep"), writes=R("prep"))
        p.add("scalar", lambda sc: sc.activation(prep_sc[:, 32:32 + nh], prep_sc[:, 16:16 + nh],
                                                 AF.Sqrt, bias=small[:, 12:13]),
              reads=R("prep", "small"), writes=R("prep"))
        p.add("vector", lambda v: v.reciprocal(prep_sc[:, 48:48 + nh], prep_sc[:, 32:32 + nh]),
              reads=R("prep"), writes=R("prep"))
        p.add("vector", lambda v: v.tensor_mul(prep_sc[:, 64:64 + nh], prep_sc[:, 48:48 + nh],
                                               gain_bc[:, gain_off:gain_off + nh]),
              reads=R("prep", "gain_bc"), writes=R("prep"))
        if extra_scale != 1.0:
            p.add("vector", lambda v: v.tensor_scalar_mul(prep_sc[:, 64:64 + nh],
                                                          prep_sc[:, 64:64 + nh], extra_scale),
                  reads=R("prep"), writes=R("prep"))
        p.add("vector", lambda v: v.tensor_tensor(
            tile[:, 0:W].rearrange("p (h w) -> p h w", w=64),
            tile[:, 0:W].rearrange("p (h w) -> p h w", w=64),
            prep_sc[:, 64:64 + nh].unsqueeze(2).broadcast_to((128, nh, 64)), ALU.mult),
            reads=R(tname, "prep"), writes=R(tname))

    q_evac_scale = 1.0 if cfg.qk_prep == "l0ffn" else 0.125
    for t in range(NT):
        par = t % 2
        qtile, qn = (qt_a, "qt_a") if par == 0 else (qt_b, "qt_b")
        ktile, kn = (kt_a, "kt_a") if par == 0 else (kt_b, "kt_b")
        vtile, vn = (vt_a, "vt_a") if par == 0 else (vt_b, "vt_b")
        emit_proj(t, wq_sb, "wq_sb", dq, qtile, qn, "bq", q_evac_scale, "scalar", par * 4)
        emit_proj(t, wk_sb, "wk_sb", dkv, ktile, kn, "bk", 1.0, "vector", par * 4 + 2)
        emit_proj(t, wv_sb, "wv_sb", dkv, vtile, vn, "bv", 1.0, "vector", par * 4 + 2)
        if cfg.qk_prep is not None:
            p.add("gpsimd", lambda g, t=t: g.dma_start(
                out=rope_a[:, 0:2 * cfg.nf], in_=ropecs_d[t * 128:(t + 1) * 128, :]),
                writes=R("rope_a"), dma=True)
            if cfg.qk_prep == "l0ffn":
                emit_rms_gain(qtile, qn, H, 0, 0.125)
                emit_rms_gain(ktile, kn, KVH, H, 1.0)
                emit_rope(qtile, qn, H, "neox")
                emit_rope(ktile, kn, KVH, "neox")
            else:
                emit_rope(qtile, qn, H, "gptj")
                emit_rope(ktile, kn, H, "gptj")
        if not safe_max:
            p.add("scalar", lambda sc, qtile=qtile, t=t: sc.activation(
                qsq[:, 0:dq], qtile[:, 0:dq], AF.Square),
                reads=R(qn), writes=R("qsq"))
            p.add("vector", lambda v, t=t: v.reduce_sum(
                qno_all[:, t * H:(t + 1) * H],
                qsq[:, 0:dq].rearrange("p (h w) -> p h w", w=64), axis=AX.X),
                reads=R("qsq"), writes=R(f"qno{t}"))
            p.add("scalar", lambda sc, t=t: sc.activation(
                qno_all[:, t * H:(t + 1) * H], qno_all[:, t * H:(t + 1) * H], AF.Sqrt),
                reads=R(f"qno{t}"), writes=R(f"qno{t}"))
            p.add("scalar", lambda sc, ktile=ktile, t=t: sc.activation(
                qsq[:, 0:dkv], ktile[:, 0:dkv], AF.Square),
                reads=R(kn, f"qno{t}"), writes=R("qsq"))
            p.add("vector", lambda v, t=t: v.reduce_sum(
                kno[:, 0:KVH],
                qsq[:, 0:dkv].rearrange("p (h w) -> p h w", w=64), axis=AX.X),
                reads=R("qsq"), writes=R("kno"))
            p.add("scalar", lambda sc: sc.activation(kno[:, 0:KVH], kno[:, 0:KVH], AF.Sqrt),
                  reads=R("kno"), writes=R("kno"))
            p.add("tensor", lambda tt: tt.matmul(
                ps[6][0:KVH, 0:128], kno[:, 0:KVH], ident[:], start=True, stop=True),
                reads=R("kno", "ident"), writes=R("ps6"))
            p.add("vector", lambda v, t=t: v.reduce_max(
                knm2[0:KVH, t:t + 1], ps[6][0:KVH, 0:128], axis=AX.X),
                reads=R("ps6"), writes=R(f"knm{t}"))
        p.add("sync", lambda s, qtile=qtile, t=t: s.dma_start(
            out=qd2.rearrange("h t w -> t h w")[t * 128:(t + 1) * 128][:, :, 0:64],
            in_=qtile[:].rearrange("p (h w) -> p h w", w=64)),
            reads=R(qn), writes=R(f"dram_q{t}"), dma=True)
        p.add("sync", lambda s, ktile=ktile, t=t: s.dma_start(
            out=kd2.rearrange("h t w -> t h w")[t * 128:(t + 1) * 128][:, :, 0:64],
            in_=ktile[:].rearrange("p (h w) -> p h w", w=64)),
            reads=R(kn), writes=R(f"dram_k{t}"), dma=True)
        p.add("sync", lambda s, vtile=vtile, t=t: s.dma_start(
            out=vd.rearrange("h t w -> t h w")[t * 128:(t + 1) * 128],
            in_=vtile[:].rearrange("p (h w) -> p h w", w=HD)),
            reads=R(vn), writes=R(f"dram_v{t}"), dma=True)

    if not safe_max:
        p.add("vector", lambda v: v.reduce_max(kno[0:KVH, 0:1],
                                               knm2[0:KVH, 0:NT], axis=AX.X),
              reads=tuple(f"knm{t}" for t in range(NT)), writes=R("kno"))
        p.add("tensor", lambda tt: tt.matmul(
            ps[6][0:1, 0:128], kno[0:128, 0:1], ident[:], start=True, stop=True),
            reads=R("kno", "ident"), writes=R("ps6"))
        p.add("scalar", lambda sc: sc.activation(
            negkmax[0:1, 0:16], ps[6][0:1, 0:16], AF.Copy, bias=0.0, scale=-1.0),
            reads=R("ps6"), writes=R("negkmax"))

    allq = tuple(f"dram_q{t}" for t in range(NT))
    allk = tuple(f"dram_k{t}" for t in range(NT))
    allv = tuple(f"dram_v{t}" for t in range(NT))

    p.add("vector", lambda v: v.memset(VB[:], 1.0), writes=R("VB"))
    NKGMAX = max(T_FULL // 512, 1)
    NQG = T_FULL // 512
    QT2 = wq_sb[:, 0:T_FULL]
    KT2 = wq_sb[:, T_FULL:2 * T_FULL]
    QTs = [(QT[:], "QT"), (QT2, "wq_sb")]
    KTs = [(KT[:], "KT"), (KT2, "wq_sb")]
    if KVH == H:
        VB2 = wk_sb[:, 0:NT * 65]
        VBs = [(VB[:], "VB"), (VB2, "wk_sb")]
        p.add("vector", lambda v: v.memset(VB2, 1.0), writes=R("wk_sb"))
    else:
        VBs = [(VB[:], "VB"), (VBg[:], "VBg")]
        p.add("vector", lambda v: v.memset(VBg[:], 1.0), writes=R("VBg"))

    def kvh_of(h):
        return h * KVH // H

    def qbuf(h):
        return QTs[h % 2]

    def kbuf(h):
        return KTs[kvh_of(h) % len(KTs)]

    def vbuf(h):
        return VBs[kvh_of(h) % len(VBs)]

    def emit_head_loads(h):
        qb, qbn = qbuf(h)
        p.add("sync", lambda s, h=h, qb=qb: s.dma_start(
            out=qb, in_=qd2[h], transpose=True),
            reads=allq, writes=R(qbn), dma=True)
        if h == 0 or kvh_of(h) != kvh_of(h - 1):
            kb, kbn = kbuf(h)
            p.add("sync", lambda s, h=h, kb=kb: s.dma_start(
                out=kb, in_=kd2[kvh_of(h)], transpose=True),
                reads=allk, writes=R(kbn), dma=True)
            p.add("gpsimd", lambda g, kb=kb: g.memset(kb[64:65, :], 1.0),
                  reads=R(kbn), writes=R(kbn))
            vb, vbn = vbuf(h)
            p.add("sync", lambda s, h=h, vb=vb: s.dma_start(
                out=vb.rearrange("p (c w) -> p c w", w=65)[:, :, 0:HD],
                in_=vd[kvh_of(h)].rearrange("(c p) w -> p c w", p=128)),
                reads=allv, writes=R(vbn), dma=True)

    def emit_passA(h, qt_lo, qt_hi):
        hp = h % 2
        qb, qbn = qbuf(h)
        kb, kbn = kbuf(h)
        for qt in range(qt_lo, qt_hi):
            nkg = (qt * 128 + 128 + 511) // 512
            for kg in range(nkg):
                n = min(512, (qt + 1) * 128 - kg * 512)
                bank = kg % 2
                p.add("tensor", lambda tt, qb=qb, kb=kb, qt=qt, kg=kg, n=n, bank=bank:
                      tt.matmul(ps[bank][:, 0:n],
                                qb[0:64, qt * 128:(qt + 1) * 128],
                                kb[0:64, kg * 512:kg * 512 + n],
                                start=True, stop=True),
                      reads=R(qbn, kbn), writes=R(f"ps{bank}"))
                p.add("vector", lambda v, kg=kg, n=n, bank=bank, hp=hp: v.reduce_max(
                    cmx[:, hp * NKGMAX + kg:hp * NKGMAX + kg + 1], ps[bank][:, 0:n], axis=AX.X),
                    reads=R(f"ps{bank}"), writes=R(f"cmx{hp}_{kg}"))
            p.add("vector", lambda v, qt=qt, nkg=nkg, hp=hp: v.reduce_max(
                rm[:, hp * NT + qt:hp * NT + qt + 1], cmx[:, hp * NKGMAX:hp * NKGMAX + nkg],
                axis=AX.X),
                reads=tuple(f"cmx{hp}_{kg}" for kg in range(nkg)), writes=R(f"rm{hp}"))

    def emit_maxrows(h):
        hp = h % 2
        qb, qbn = qbuf(h)
        if safe_max:
            for qt in range(NT):
                p.add("tensor", lambda tt, qt=qt, hp=hp: tt.matmul(
                    ps[7][0:1, 0:128], rm[:, hp * NT + qt:hp * NT + qt + 1], ident[:],
                    start=True, stop=True),
                    reads=R(f"rm{hp}", "ident"), writes=R("ps7"))
                p.add("scalar", lambda sc, qt=qt, qb=qb: sc.activation(
                    qb[64:65, qt * 128:(qt + 1) * 128], ps[7][0:1, 0:128],
                    AF.Copy, bias=0.0, scale=-1.0),
                    reads=R("ps7"), writes=R(qbn))
            return
        kvh = kvh_of(h)
        for qt in range(NT):
            p.add("tensor", lambda tt, qt=qt, h=h: tt.matmul(
                ps[7][0:1, 0:128], qno_all[:, qt * H + h:qt * H + h + 1], ident[:],
                start=True, stop=True),
                reads=R(f"qno{qt}", "ident"), writes=R("ps7"))
            p.add("scalar", lambda sc, qt=qt, qb=qb, kvh=kvh: sc.activation(
                qb[64:65, qt * 128:(qt + 1) * 128], ps[7][0:1, 0:128],
                AF.Identity, bias=0.0, scale=negkmax[0:1, kvh:kvh + 1]),
                reads=R("ps7", "negkmax"), writes=R(qbn))

    emit_head_loads(0)
    if safe_max:
        emit_passA(0, 0, NT)
    emit_maxrows(0)
    for h in range(H):
        pair = h // 2
        sub = (h % 2) * 64
        qb, qbn = qbuf(h)
        kb, kbn = kbuf(h)
        vb, vbn = vbuf(h)
        if h + 1 < H:
            emit_head_loads(h + 1)
        for qg in range(NQG):
            nkc = (qg + 1) * 4
            ob = 2 + qg % 2

            def emit_pv(kc, nkc=nkc, vb=vb, vbn=vbn, ob=ob):
                ptb, ptn = ((PT_a, "PT_a"), (PT_b, "PT_b"), (PT_c, "PT_c"))[kc % 3]
                p.add("tensor", lambda tt, kc=kc, ptb=ptb, nkc=nkc, vb=vb, ob=ob: tt.matmul(
                    ps[ob][0:65, 0:512], vb[:, kc * 65:(kc + 1) * 65], ptb[:],
                    start=(kc == 0), stop=(kc == nkc - 1)),
                    reads=R(vbn, ptn), writes=R(f"ps{ob}"))

            for kc in range(nkc):
                sbank = 4 + (kc % 3)
                pst = ps[sbank]
                ptb, ptn = ((PT_a, "PT_a"), (PT_b, "PT_b"), (PT_c, "PT_c"))[kc % 3]
                p.add("tensor", lambda tt, pst=pst, kc=kc, qg=qg, qb=qb, kb=kb: tt.matmul(
                    pst[:, 0:512],
                    kb[0:65, kc * 128:(kc + 1) * 128],
                    qb[0:65, qg * 512:(qg + 1) * 512],
                    start=True, stop=True),
                    reads=R(kbn, qbn), writes=R(f"ps{sbank}"))
                off = kc * 128 - qg * 512
                if off >= 0:
                    j = off // 128
                    p.add("vector", lambda v, pst=pst, j=j, sbank=sbank: v.tensor_tensor(
                        pst[:, 0:512], pst[:, 0:512], masks[:, j * 512:(j + 1) * 512], ALU.add),
                        reads=R(f"ps{sbank}", "masks"), writes=R(f"ps{sbank}"))
                p.add("scalar", lambda sc, pst=pst, ptb=ptb: sc.activation(
                    ptb[:], pst[:, 0:512], AF.Exp),
                    reads=R(f"ps{sbank}"), writes=R(ptn))
                if kc >= 2:
                    emit_pv(kc - 2)
            for kc in range(max(nkc - 2, 0), nkc):
                emit_pv(kc)
            if safe_max and h + 1 < H:
                emit_passA(h + 1, qg * 4, qg * 4 + 4)
            ob = 2 + qg % 2
            qp = qg % 2
            p.add("vector", lambda v, ob=ob, qp=qp: v.reciprocal(
                rs_row[0:1, qp * 512:qp * 512 + 512], ps[ob][64:65, 0:512]),
                reads=R(f"ps{ob}"), writes=R(f"rs{qp}"))
            p.add("tensor", lambda tt, qp=qp: tt.matmul(
                ps[7][0:64, 0:512], onesf[0:1, 0:64],
                rs_row[0:1, qp * 512:qp * 512 + 512], start=True, stop=True),
                reads=R("onesf", f"rs{qp}"), writes=R("ps7"))
            p.add("scalar", lambda sc, ob=ob, qp=qp: sc.copy(
                otmp[:, qp * 512:qp * 512 + 512], ps[ob][0:64, 0:512]),
                reads=R(f"ps{ob}"), writes=R(f"otmp{qp}"))
            p.add("vector", lambda v, pair=pair, sub=sub, qg=qg, qp=qp: v.tensor_tensor(
                big[sub:sub + 64, pair * T_FULL + qg * 512: pair * T_FULL + (qg + 1) * 512],
                otmp[:, qp * 512:qp * 512 + 512], ps[7][0:64, 0:512], ALU.mult),
                reads=R(f"otmp{qp}", "ps7"), writes=R("big"))
        if h + 1 < H:
            emit_maxrows(h + 1)

    p.add("gpsimd", lambda g: g.dma_start(
        out=wq_sb[:].rearrange("p (c o) -> p c o", o=D)[:, :, :],
        in_=wo_d.rearrange("(c p) o -> p c o", p=128)),
        writes=R("wq_sb"), dma=True)

    if cfg.post_ln:
        p.add("gpsimd", lambda g: g.dma_start(out=prow[0:1, 0:D], in_=pg_d),
              writes=R("prow"), dma=True)
        p.add("gpsimd", lambda g: g.dma_start(out=prow[0:1, D:2 * D], in_=pb_d),
              writes=R("prow"), dma=True)
        for c in range(4):
            p.add("tensor", lambda tt, c=c: tt.matmul(
                ps[7][:, 0:512], onesb[0:1, 0:128], prow[0:1, c * 512:(c + 1) * 512],
                start=True, stop=True), reads=R("onesb", "prow"), writes=R("ps7"))
            p.add("vector", lambda v, c=c: v.tensor_copy(
                gb_bc[:, c * 512:(c + 1) * 512], ps[7][:, 0:512]),
                reads=R("ps7"), writes=R("gb_bc"))

    for t in range(NT):
        par = t % 2
        xf, xfn = (xf_a, "xf_a") if par == 0 else (xf_b, "xf_b")
        yt, ytn = (yt_a, "yt_a") if par == 0 else (yt_b, "yt_b")
        sml, smln = (small, "small") if par == 0 else (small2, "small2")
        p.add("gpsimd", lambda g, xf=xf, t=t: g.dma_start(
            out=xf[:], in_=xin[t * 128:(t + 1) * 128, :]),
            writes=R(xfn), dma=True)
        for half in range(2):
            bank = 4 + (t % 2) * 2 + half
            pst = ps[bank]
            has_bo = "bo" in bias_sb

            def mmY(tt, t=t, half=half, pst=pst, has_bo=has_bo):
                for c in range(8):
                    inst = tt.matmul(
                        pst[:, 0:512],
                        big[:, c * T_FULL + t * 128: c * T_FULL + (t + 1) * 128],
                        wq_sb[:, c * D + half * 512: c * D + half * 512 + 512],
                        start=(c == 0), stop=(c == 7 and not has_bo))
                return inst
            p.add("tensor", mmY, reads=R("big", "wq_sb"), writes=R(f"ps{bank}"))
            if has_bo:
                p.add("tensor", lambda tt, pst=pst, half=half: tt.matmul(
                    pst[:, 0:512], onesb[0:1, 0:128],
                    bias_sb["bo"][0:1, half * 512:half * 512 + 512],
                    start=False, stop=True), reads=R("onesb", "bo"), writes=R(f"ps{bank}"))
        if cfg.post_ln:
            for half in range(2):
                bank = 4 + (t % 2) * 2 + half
                pst = ps[bank]
                p.add("vector", lambda v, yt=yt, half=half, pst=pst, sml=sml: v.tensor_scalar(
                    yt[:, half * 512:half * 512 + 512], pst[:, 0:512], 1.0, 0.0,
                    ALU.mult, ALU.add, accum_out=sml[:, half:half + 1]),
                    reads=R(f"ps{bank}"), writes=R(ytn, smln))
            emit_ln_tok(p, yt, ytn, sqd, sml, sml[:, 11:12], yt, ytn, smn=smln)
            p.add("vector", lambda v, yt=yt: v.tensor_tensor(
                yt[:], yt[:], gb_bc[:, 0:D], ALU.mult),
                reads=R(ytn, "gb_bc"), writes=R(ytn))
            p.add("vector", lambda v, yt=yt: v.tensor_tensor(
                yt[:], yt[:], gb_bc[:, D:2 * D], ALU.add),
                reads=R(ytn, "gb_bc"), writes=R(ytn))
            p.add("vector", lambda v, yt=yt, xf=xf, sml=sml: v.scalar_tensor_tensor(
                yt[:], yt[:], 1.0, xf[:], ALU.mult, ALU.add, accum_out=sml[:, 0:1]),
                reads=R(ytn, xfn), writes=R(ytn, smln))
            p.add("vector", lambda v, sml=sml: v.memset(sml[:, 1:2], 0.0), writes=R(smln))
        else:
            for half in range(2):
                bank = 4 + (t % 2) * 2 + half
                pst = ps[bank]
                p.add("vector", lambda v, yt=yt, xf=xf, half=half, pst=pst, sml=sml:
                      v.scalar_tensor_tensor(
                          yt[:, half * 512:half * 512 + 512], pst[:, 0:512], 1.0,
                          xf[:, half * 512:half * 512 + 512],
                          ALU.mult, ALU.add, accum_out=sml[:, half:half + 1]),
                      reads=R(f"ps{bank}", xfn), writes=R(ytn, smln))
        if cfg.epilogue == "inner":
            p.add("sync", lambda s, yt=yt, t=t: s.dma_start(
                out=xmid_d[t * 128:(t + 1) * 128, :], in_=yt[:]),
                reads=R(ytn), writes=R(f"dram_out{t}"), dma=True)
            continue
        emit_ln_tok(p, yt, ytn, sqd, sml, sml[:, 11:12], yt, ytn, smn=smln)
        p.add("sync", lambda s, yt=yt, t=t: s.dma_start(
            out=xout[t * 128:(t + 1) * 128, :], in_=yt[:]),
            reads=R(ytn), writes=R(f"dram_out{t}"), dma=True)

    if standalone:
        p.emit(es)
        es.close()
        return nc
    p.fence()
    es.close()
    return None



def _rope_neox_tables():
    inv = 1.0 / 10000.0 ** (np.arange(0, HD, 2, dtype=np.float32) / HD)
    fr = np.outer(np.arange(T_FULL, dtype=np.float32), inv)
    return np.concatenate([np.cos(fr), np.sin(fr)], axis=1).astype(np.float32)


def _rope_gptj_tables():
    inv = 1.0 / 10000.0 ** (np.arange(0, 32, 2, dtype=np.float32) / 32.0)
    fr = np.outer(np.arange(T_FULL, dtype=np.float32), inv)
    return np.concatenate([np.cos(fr), np.sin(fr)], axis=1).astype(np.float32)


L3_ATTN_CFG = AttnCfg("L3_attn", has_bq=True, has_bkv=True, has_bo=True)


def prep_l3_attn(inp):
    io = dict(CONSTS)
    io["wq"] = np.ascontiguousarray(inp["L3_attn_wq"])
    io["wk"] = np.ascontiguousarray(inp["L3_attn_wk"])
    io["wv"] = np.ascontiguousarray(inp["L3_attn_wv"])
    io["wo"] = np.ascontiguousarray(inp["L3_attn_wo"])
    for bn, key in (("bq", "L3_attn_bq"), ("bk", "L3_attn_bk"),
                    ("bv", "L3_attn_bv"), ("bo", "L3_attn_bo")):
        io[bn] = np.ascontiguousarray(inp[key]).reshape(1, -1)
    return io


L2_ATTN_CFG = AttnCfg("L2_attn", has_bq=True, has_bkv=True, has_bo=True, post_ln=True)


def prep_l2_attn(inp):
    io = dict(CONSTS)
    io["wq"] = np.ascontiguousarray(inp["L2_attn_Wq"])
    io["wk"] = np.ascontiguousarray(inp["L2_attn_Wk"])
    io["wv"] = np.ascontiguousarray(inp["L2_attn_Wv"])
    io["wo"] = np.ascontiguousarray(inp["L2_attn_Wo"])
    for bn, key in (("bq", "L2_attn_bq"), ("bk", "L2_attn_bk"),
                    ("bv", "L2_attn_bv"), ("bo", "L2_attn_bo")):
        io[bn] = np.ascontiguousarray(inp[key]).reshape(1, -1)
    io["post_gamma"] = np.ascontiguousarray(inp["L2_attn_gamma"]).reshape(1, D)
    io["post_beta"] = np.ascontiguousarray(inp["L2_attn_beta"]).reshape(1, D)
    return io


L1_ATTN_CFG = AttnCfg("L1_attn", qk_prep="l1attn", nf=16)


def prep_l1_attn(inp):
    io = dict(CONSTS)
    wqkvT = np.ascontiguousarray(np.asarray(inp["L1_attn_w_qkv"]).T)
    wq = np.empty((D, D), np.float32)
    wv = np.empty((D, D), np.float32)
    wk = np.empty((D, D), np.float32)
    for h in range(16):
        mp, j = h // 4, h % 4
        wq[:, h * 64:(h + 1) * 64] = wqkvT[:, mp * 768 + j * 64: mp * 768 + (j + 1) * 64]
        wv[:, h * 64:(h + 1) * 64] = wqkvT[:, mp * 768 + 256 + j * 64: mp * 768 + 256 + (j + 1) * 64]
        wk[:, h * 64:(h + 1) * 64] = wqkvT[:, mp * 768 + 512 + j * 64: mp * 768 + 512 + (j + 1) * 64]
    io["wq"], io["wk"], io["wv"] = wq, wk, wv
    io["wo"] = np.ascontiguousarray(np.asarray(inp["L1_attn_w_out"]).T)
    io["rope_cs"] = _rope_gptj_tables()
    return io


L0_FFN_ATTN_CFG = AttnCfg("L0_ffn_attn", kv_heads=4, dkv=256, qk_prep="l0ffn",
                          pre_norm="rms", epilogue="inner", gains=True, nf=32,
                          prep_eps=1e-6)


def prep_l0_ffn_attn(inp):
    io = dict(CONSTS)
    io["wq"] = np.ascontiguousarray(np.asarray(inp["L0_ffn_wq"]).T)
    io["wk"] = np.ascontiguousarray(np.asarray(inp["L0_ffn_wk"]).T)
    io["wv"] = np.ascontiguousarray(np.asarray(inp["L0_ffn_wv"]).T)
    io["wo"] = np.ascontiguousarray(np.asarray(inp["L0_ffn_wo"]).T)
    io["q_gain"] = np.ascontiguousarray(inp["L0_ffn_q_gain"]).reshape(1, 16)
    io["k_gain"] = np.ascontiguousarray(inp["L0_ffn_k_gain"]).reshape(1, 4)
    io["rope_cs"] = _rope_neox_tables()
    return io


L1_FFN_ATTN_CFG = AttnCfg("L1_ffn_attn", has_bo=True, pre_norm="ln", epilogue="inner",
                          prep_eps=1e-5, fold_gb=True)


def prep_l1_ffn_attn(inp):
    io = dict(CONSTS)
    io["wq"] = np.ascontiguousarray(inp["L1_ffn_wq"])
    io["wk"] = np.ascontiguousarray(inp["L1_ffn_wk"])
    io["wv"] = np.ascontiguousarray(inp["L1_ffn_wv"])
    io["wo"] = np.ascontiguousarray(inp["L1_ffn_wo"])
    io["bo"] = np.ascontiguousarray(inp["L1_ffn_bo"]).reshape(1, D)
    io["ln_gam_col"] = np.ascontiguousarray(
        np.asarray(inp["L1_ffn_ln1_scale"]).reshape(8, 128).T.astype(np.float32))
    io["ln_beta_col"] = np.ascontiguousarray(
        np.asarray(inp["L1_ffn_ln1_shift"]).reshape(8, 128).T).astype(BF)
    return io


def prep_l0_attn(inp):
    io = dict(CONSTS)
    io["wq"] = np.ascontiguousarray(inp["L0_attn_Wq"])
    io["wk"] = np.ascontiguousarray(inp["L0_attn_Wk"])
    io["wv"] = np.ascontiguousarray(inp["L0_attn_Wv"])
    io["wo"] = np.ascontiguousarray(inp["L0_attn_Wo"])
    io["bo"] = np.ascontiguousarray(inp["L0_attn_bo"]).reshape(1, D)
    return io


L0_ATTN_CFG = AttnCfg("L0_attn", has_bo=True)



def _run8(nc, common, per_core=None, extra=None):
    in_maps = []
    for c in range(8):
        m = dict(common)
        if per_core is not None:
            m.update(per_core[c])
        in_maps.append(m)
    res = run_bass_kernel_spmd(nc, in_maps, core_ids=list(range(8)))
    return res.results


def kernel(**inputs):
    x = np.asarray(inputs["L0_attn_x"])
    shards = [np.ascontiguousarray(x[c]) for c in range(8)]

    nc = bass.Bass("TRN2", target_bir_lowering=False)
    es = contextlib.ExitStack()
    p = Prog(nc)
    s_in = nc.dram_tensor("stream_in", [T_FULL, D], F32, kind="ExternalInput").ap()
    stream = {0: s_in}
    for i in range(1, 7):
        stream[i] = nc.dram_tensor(f"s{i}", [T_FULL, D], F32, kind="Internal").ap()
    s7 = nc.dram_tensor("s7_out", [T_FULL, D], F32, kind="ExternalOutput").ap()
    m0 = nc.dram_tensor("m0", [T_FULL, D], F32, kind="Internal").ap()
    m1 = nc.dram_tensor("m1", [T_FULL, D], F32, kind="Internal").ap()
    selw = nc.dram_tensor("selw_out", [T_FULL, 32], F32, kind="ExternalOutput").ap()
    lpos = nc.dram_tensor("lpos_out", [T_FULL, 32], F32, kind="ExternalOutput").ap()
    cnt = nc.dram_tensor("cnt_out", [1, 32], F32, kind="ExternalOutput").ap()

    def FU(pfx, rd):
        return dict(nc=nc, p=p, pfx=pfx, rd=rd)

    build_attn_launch(L0_ATTN_CFG, FU("l0a_", {"xin": s_in, "xout": stream[1]}))
    build_attn_launch(L0_FFN_ATTN_CFG, FU("l0fa_", {"xin": stream[1], "xmid": m0}))
    build_l0ffn_bank_launch(FU("l0fb_", {"xin": stream[1], "xmid": m0, "xout": stream[2]}))
    build_attn_launch(L1_ATTN_CFG, FU("l1a_", {"xin": stream[2], "xout": stream[3]}))
    build_attn_launch(L1_FFN_ATTN_CFG, FU("l1fa_", {"xin": stream[3], "xmid": m1}))
    build_l1ffn_moe_launch(FU("l1fb_", {"xin": stream[3], "xmid": m1, "xout": stream[4]}))
    build_attn_launch(L2_ATTN_CFG, FU("l2a_", {"xin": stream[4], "xout": stream[5]}))
    build_l2ffn_launch(FU("l2f_", {"xin": stream[5], "xout": stream[6]}))
    build_attn_launch(L3_ATTN_CFG, FU("l3a_", {"xin": stream[6], "xout": s7}))
    build_l3ffn_a_launch(FU("l3fa_", {"xin": s7, "selw": selw, "lpos": lpos, "cnt": cnt}))
    p.emit(es)
    es.close()

    common = {}
    for pfx, prep in (("l0a_", prep_l0_attn(inputs)),
                      ("l0fa_", prep_l0_ffn_attn(inputs)),
                      ("l0fb_", prep_l0_ffn_bank(inputs)),
                      ("l1a_", prep_l1_attn(inputs)),
                      ("l1fa_", prep_l1_ffn_attn(inputs)),
                      ("l1fb_", prep_l1_ffn_moe(inputs)),
                      ("l2a_", prep_l2_attn(inputs)),
                      ("l2f_", prep_l2_ffn(inputs)),
                      ("l3a_", prep_l3_attn(inputs)),
                      ("l3fa_", prep_l3_ffn_a(inputs))):
        for k, v in prep.items():
            common[pfx + k] = v
    r1 = _run8(nc, common, [{"stream_in": shards[c]} for c in range(8)])

    cnts = np.stack([r1[c]["cnt_out"][0] for c in range(8)], axis=0)
    pcb = []
    for c in range(8):
        pm = np.zeros((8, 1), np.float32)
        pm[:c] = 1.0
        pcb.append({"xin": np.ascontiguousarray(r1[c]["s7_out"]),
                    "selw": np.ascontiguousarray(r1[c]["selw_out"]),
                    "lpos": np.ascontiguousarray(r1[c]["lpos_out"]),
                    "gathered": cnts, "premask": pm})
    r = _run8(build_l3ffn_b_launch(), prep_l3_ffn_b(inputs), pcb)
    return np.stack([r[c]["xout"] for c in range(8)], axis=0)



def _emit_mirror_nT(p, nc, es, xin, nbf, big, xa, xb, xf_a, xf_b, sqd, small,
                    pre_norm=None, prep_eps=1e-5, src_name="xin"):
    if pre_norm is not None:
        p.add("vector", lambda v: v.memset(small[:, 12:13], float(prep_eps)),
              writes=R("small"))
    for t in range(NT):
        par = t % 2
        buf, bname = (xa, "xa") if par == 0 else (xb, "xb")
        if pre_norm is None:
            p.add("gpsimd", lambda g, buf=buf, t=t: g.dma_start(
                out=buf[:], in_=xin[t * 128:(t + 1) * 128, :]),
                writes=R(bname), dma=True)
        else:
            xf, xfn = (xf_a, "xf_a") if par == 0 else (xf_b, "xf_b")
            p.add("gpsimd", lambda g, xf=xf, t=t: g.dma_start(
                out=xf[:], in_=xin[t * 128:(t + 1) * 128, :]),
                writes=R(xfn), dma=True)
            p.add("scalar", lambda sc, xf=xf: sc.activation(
                sqd[:], xf[:], AF.Square, accum_out=small[:, 4:5]),
                reads=R(xfn), writes=R("sqd", "small"))
            if pre_norm == "rms":
                p.add("vector", lambda v: v.tensor_scalar_mul(small[:, 5:6], small[:, 4:5], 1.0 / D),
                      reads=R("small"), writes=R("small"))
                p.add("scalar", lambda sc: sc.activation(small[:, 6:7], small[:, 5:6],
                                                         AF.Sqrt, bias=small[:, 12:13]),
                      reads=R("small"), writes=R("small"))
                p.add("vector", lambda v: v.reciprocal(small[:, 7:8], small[:, 6:7]),
                      reads=R("small"), writes=R("small"))
                p.add("scalar", lambda sc, xf=xf, buf=buf: sc.activation(
                    buf[:], xf[:], AF.Identity, bias=0.0, scale=small[:, 7:8]),
                    reads=R(xfn, "small"), writes=R(bname))
            else:
                p.add("scalar", lambda sc, xf=xf: sc.activation(
                    sqd[:], xf[:], AF.Identity, accum_out=small[:, 0:1]),
                    reads=R(xfn), writes=R("sqd", "small"))
                p.add("vector", lambda v: v.memset(small[:, 1:2], 0.0),
                      writes=R("small"))
                emit_ln_tok(p, xf, xfn, sqd, small, small[:, 12:13], xf, xfn)
                p.add("vector", lambda v, xf=xf, buf=buf: v.tensor_copy(buf[:], xf[:]),
                      reads=R(xfn), writes=R(bname))
        p.add("sync", lambda s, buf=buf, t=t: s.dma_start(
            out=nbf[t * 128:(t + 1) * 128, :], in_=buf[:]),
            reads=R(bname), writes=R(f"dram_nbf{t}"), dma=True)
    if big is None:
        return
    for c in range(8):
        p.add("sync", lambda s, c=c: s.dma_start(
            out=big[:, c * T_FULL:(c + 1) * T_FULL],
            in_=nbf[:, c * 128:(c + 1) * 128], transpose=True),
            reads=tuple(f"dram_nbf{t}" for t in range(NT)), writes=R("big"), dma=True)


def _emit_top2_gate(p, gsc, wgt, t, mode, rs_bc=None):
    G = "gsc"
    p.add("vector", lambda v: v.reduce_max(gsc[:, 33:34], gsc[:, 0:8], axis=AX.X),
          reads=R(G), writes=R(G))
    p.add("vector", lambda v: v.tensor_single_scalar(gsc[:, 8:16], gsc[:, 0:8],
                                                     gsc[:, 33:34], ALU.is_ge),
          reads=R(G), writes=R(G))
    p.add("vector", lambda v: v.scalar_tensor_tensor(gsc[:, 16:24], gsc[:, 8:16], -1e30,
                                                     gsc[:, 0:8], ALU.mult, ALU.add),
          reads=R(G), writes=R(G))
    p.add("vector", lambda v: v.reduce_max(gsc[:, 34:35], gsc[:, 16:24], axis=AX.X),
          reads=R(G), writes=R(G))
    p.add("vector", lambda v: v.tensor_single_scalar(gsc[:, 24:32], gsc[:, 16:24],
                                                     gsc[:, 34:35], ALU.is_ge),
          reads=R(G), writes=R(G))
    if mode == "softmax":
        p.add("vector", lambda v: v.tensor_sub(gsc[:, 35:36], gsc[:, 33:34], gsc[:, 34:35]),
              reads=R(G), writes=R(G))
        p.add("scalar", lambda sc: sc.activation(gsc[:, 36:37], gsc[:, 35:36], AF.Sigmoid),
              reads=R(G), writes=R(G))
        p.add("vector", lambda v: v.tensor_scalar(gsc[:, 37:38], gsc[:, 36:37], -1.0, 1.0,
                                                  ALU.mult, ALU.add),
              reads=R(G), writes=R(G))
        p.add("vector", lambda v, t=t: v.tensor_scalar(wgt[:, t * 8:(t + 1) * 8],
                                                       gsc[:, 8:16], gsc[:, 36:37], None,
                                                       ALU.mult),
              reads=R(G), writes=R("wgt"))
        p.add("vector", lambda v, t=t: v.scalar_tensor_tensor(
            wgt[:, t * 8:(t + 1) * 8], gsc[:, 24:32], gsc[:, 37:38],
            wgt[:, t * 8:(t + 1) * 8], ALU.mult, ALU.add),
            reads=R(G, "wgt"), writes=R("wgt"))
    else:
        p.add("vector", lambda v: v.tensor_add(gsc[:, 35:36], gsc[:, 33:34], gsc[:, 34:35]),
              reads=R(G), writes=R(G))
        p.add("vector", lambda v: v.reciprocal(gsc[:, 36:37], gsc[:, 35:36]),
              reads=R(G), writes=R(G))
        p.add("vector", lambda v: v.tensor_add(gsc[:, 24:32], gsc[:, 8:16], gsc[:, 24:32]),
              reads=R(G), writes=R(G))
        p.add("vector", lambda v: v.tensor_mul(gsc[:, 24:32], gsc[:, 24:32], gsc[:, 0:8]),
              reads=R(G), writes=R(G))
        p.add("vector", lambda v, t=t: v.tensor_scalar(wgt[:, t * 8:(t + 1) * 8],
                                                       gsc[:, 24:32], gsc[:, 36:37], None,
                                                       ALU.mult),
              reads=R(G), writes=R("wgt"))


def _emit_final_epilogue(p, nc, t, acc_d, xin, xmid, xf, xfn, x1f, x1fn, yt, ytn,
                         sqd, small, xout, acc_reads=()):
    p.add("gpsimd", lambda g, yt=yt, t=t: g.dma_start(
        out=yt[:], in_=acc_d[t * 128:(t + 1) * 128, :]),
        reads=tuple(acc_reads), writes=R(ytn), dma=True)
    p.add("gpsimd", lambda g, xf=xf, t=t: g.dma_start(
        out=xf[:], in_=xin[t * 128:(t + 1) * 128, :]),
        writes=R(xfn), dma=True)
    if xmid is not None:
        p.add("gpsimd", lambda g, x1f=x1f, t=t: g.dma_start(
            out=x1f[:], in_=xmid[t * 128:(t + 1) * 128, :]),
            writes=R(x1fn), dma=True)
        p.add("vector", lambda v, yt=yt, x1f=x1f: v.tensor_add(yt[:], yt[:], x1f[:]),
              reads=R(ytn, x1fn), writes=R(ytn))
    p.add("vector", lambda v, yt=yt, xf=xf: v.scalar_tensor_tensor(
        yt[:], yt[:], 1.0, xf[:], ALU.mult, ALU.add, accum_out=small[:, 0:1]),
        reads=R(ytn, xfn), writes=R(ytn, "small"))
    p.add("vector", lambda v: v.memset(small[:, 1:2], 0.0), writes=R("small"))
    emit_ln_tok(p, yt, ytn, sqd, small, small[:, 11:12], yt, ytn)
    p.add("sync", lambda s, yt=yt, t=t: s.dma_start(
        out=xout[t * 128:(t + 1) * 128, :], in_=yt[:]),
        reads=R(ytn), writes=R(f"dram_out{t}"), dma=True)


def _load_nTg(p, nbf, tg, buf, bufname):
    for c in range(8):
        p.add("sync", lambda s, tg=tg, c=c, buf=buf: s.dma_start(
            out=buf[:, c * 512:(c + 1) * 512],
            in_=nbf[tg * 512:(tg + 1) * 512, c * 128:(c + 1) * 128], transpose=True),
            reads=tuple(f"dram_nbf{t}" for t in range(tg * 4, tg * 4 + 4)),
            writes=R(bufname), dma=True)


def build_l2ffn_launch(FUSE=None):
    standalone = FUSE is None
    nc = (bass.Bass("TRN2", target_bir_lowering=False) if standalone
          else NCProxy(FUSE["nc"], FUSE["pfx"], FUSE["rd"]))
    es = contextlib.ExitStack()
    E, HH = 8, 512

    xin = nc.dram_tensor("xin", [T_FULL, D], F32, kind="ExternalInput").ap()
    xout = nc.dram_tensor("xout", [T_FULL, D], F32, kind="ExternalOutput").ap()
    onesb_d = nc.dram_tensor("c_onesb", [1, 512], BF16, kind="ExternalInput").ap()
    onesf_d = nc.dram_tensor("c_onesf", [1, 128], F32, kind="ExternalInput").ap()
    gw_d = nc.dram_tensor("gate_W", [D, E], F32, kind="ExternalInput").ap()
    w1_d = nc.dram_tensor("W1", [E, D, HH], F32, kind="ExternalInput").ap()
    b1_d = nc.dram_tensor("b1", [1, E * HH], F32, kind="ExternalInput").ap()
    w2_d = nc.dram_tensor("W2", [E, HH, D], F32, kind="ExternalInput").ap()
    b2_d = nc.dram_tensor("b2", [1, E * D], F32, kind="ExternalInput").ap()
    ws1_d = nc.dram_tensor("Ws1", [D, HH], F32, kind="ExternalInput").ap()
    bs1_d = nc.dram_tensor("bs1", [1, HH], F32, kind="ExternalInput").ap()
    ws2_d = nc.dram_tensor("Ws2", [HH, D], F32, kind="ExternalInput").ap()
    bs2_d = nc.dram_tensor("bs2", [1, D], F32, kind="ExternalInput").ap()
    rsc_d = nc.dram_tensor("route_scale", [1, 1], F32, kind="ExternalInput").ap()

    nbf = nc.dram_tensor("nbf", [T_FULL, D], BF16, kind="Internal").ap()
    acc_d = nc.dram_tensor("acc", [T_FULL, D], F32, kind="Internal").ap()

    big = es.enter_context(nc.sbuf_tensor("big", [128, 8 * T_FULL], BF16))
    hsb = es.enter_context(nc.sbuf_tensor("hsb", [128, 4 * T_FULL], BF16))
    w1a = es.enter_context(nc.sbuf_tensor("w1a", [128, 8 * HH], BF16))
    w1b = es.enter_context(nc.sbuf_tensor("w1b", [128, 8 * HH], BF16))
    w2a = es.enter_context(nc.sbuf_tensor("w2a", [128, 4 * D], BF16))
    w2b = es.enter_context(nc.sbuf_tensor("w2b", [128, 4 * D], BF16))
    gw_sb = es.enter_context(nc.sbuf_tensor("gw_sb", [128, 8 * E], BF16))
    b1row = es.enter_context(nc.sbuf_tensor("b1row", [1, E * HH], BF16))
    b2row = es.enter_context(nc.sbuf_tensor("b2row", [1, E * D], BF16))
    bs1row = es.enter_context(nc.sbuf_tensor("bs1row", [1, HH], BF16))
    bs2row = es.enter_context(nc.sbuf_tensor("bs2row", [1, D], BF16))
    onesb = es.enter_context(nc.sbuf_tensor("onesb", [1, 512], BF16))
    onesf = es.enter_context(nc.sbuf_tensor("onesf", [1, 128], F32))
    rsc_sb = es.enter_context(nc.sbuf_tensor("rsc_sb", [1, 1], F32))
    rs_bc = es.enter_context(nc.sbuf_tensor("rs_bc", [128, 1], F32))
    gsc = es.enter_context(nc.sbuf_tensor("gsc", [128, 40], F32))
    wgt = es.enter_context(nc.sbuf_tensor("wgt", [128, NT * 8], F32))
    xa = es.enter_context(nc.sbuf_tensor("xa", [128, D], BF16))
    xb = es.enter_context(nc.sbuf_tensor("xb", [128, D], BF16))
    xf_a = es.enter_context(nc.sbuf_tensor("xf_a", [128, D], F32))
    xf_b = es.enter_context(nc.sbuf_tensor("xf_b", [128, D], F32))
    yt_a = es.enter_context(nc.sbuf_tensor("yt_a", [128, D], F32))
    yt_b = es.enter_context(nc.sbuf_tensor("yt_b", [128, D], F32))
    sqd = es.enter_context(nc.sbuf_tensor("sqd", [128, D], F32))
    small = es.enter_context(nc.sbuf_tensor("small", [128, 16], F32))
    stmp = es.enter_context(nc.sbuf_tensor("stmp", [128, 4 * 512], F32))

    ps = [es.enter_context(nc.psum_tensor(f"ps{i}", [128, 512], F32)) for i in range(8)]
    p = Prog(nc) if standalone else FUSE["p"]

    p.add("vector", lambda v: v.memset(small[:, 11:12], EPS_TOP), writes=R("small"))
    p.add("gpsimd", lambda g: g.dma_start(out=onesb[:], in_=onesb_d), writes=R("onesb"), dma=True)
    p.add("gpsimd", lambda g: g.dma_start(out=onesf[:], in_=onesf_d), writes=R("onesf"), dma=True)
    p.add("gpsimd", lambda g: g.dma_start(
        out=gw_sb[:].rearrange("p (c e) -> p c e", e=E),
        in_=gw_d.rearrange("(c p) e -> p c e", p=128)), writes=R("gw"), dma=True)
    for nm, sb, dd in (("b1row", b1row, b1_d), ("b2row", b2row, b2_d),
                       ("bs1row", bs1row, bs1_d), ("bs2row", bs2row, bs2_d),
                       ("rsc", rsc_sb, rsc_d)):
        p.add("gpsimd", lambda g, sb=sb, dd=dd: g.dma_start(out=sb[:], in_=dd),
              writes=R(nm), dma=True)

    _emit_mirror_nT(p, nc, es, xin, nbf, big, xa, xb, xf_a, xf_b, sqd, small)

    p.add("tensor", lambda t: t.matmul(ps[6][0:128, 0:1], onesf[0:1, :], rsc_sb[:],
                                       start=True, stop=True),
          reads=R("onesf", "rsc"), writes=R("ps6"))
    p.add("vector", lambda v: v.tensor_copy(rs_bc[:], ps[6][0:128, 0:1]),
          reads=R("ps6"), writes=R("rs_bc"))

    for t in range(NT):
        bank = 6 + (t % 2)
        pst = ps[bank]

        def mm_gate(tt, t=t, pst=pst):
            for c in range(8):
                inst = tt.matmul(pst[:, 0:E],
                                 big[:, c * T_FULL + t * 128: c * T_FULL + (t + 1) * 128],
                                 gw_sb[:, c * E:(c + 1) * E],
                                 start=(c == 0), stop=(c == 7))
            return inst
        p.add("tensor", mm_gate, reads=R("big", "gw"), writes=R(f"ps{bank}"))
        p.add("scalar", lambda sc, pst=pst: sc.activation(gsc[:, 0:8], pst[:, 0:E], AF.Sigmoid),
              reads=R(f"ps{bank}"), writes=R("gsc"))
        p.add("vector", lambda v: v.tensor_scalar(gsc[:, 0:8], gsc[:, 0:8], rs_bc[:, 0:1],
                                                  None, ALU.mult),
              reads=R("gsc", "rs_bc"), writes=R("gsc"))
        _emit_top2_gate(p, gsc, wgt, t, "norm")

    def expert_pass(tag, w1sb, w1n, w2sb, w2n, brow1, brow1_off, brow2, brow2_off,
                    act, wcol, first):
        for tg in range(T_FULL // 512):
            for mo in range(4):
                bank = mo % 2
                pst = ps[bank]

                def mm_h(tt, tg=tg, mo=mo, pst=pst, w1sb=w1sb):
                    for c in range(8):
                        tt.matmul(pst[:, 0:512],
                                  w1sb[:, c * HH + mo * 128: c * HH + mo * 128 + 128],
                                  big[:, c * T_FULL + tg * 512: c * T_FULL + (tg + 1) * 512],
                                  start=(c == 0), stop=False)
                    return tt.matmul(pst[:, 0:512],
                                     brow1[0:1, brow1_off + mo * 128: brow1_off + mo * 128 + 128],
                                     onesb[0:1, 0:512], start=False, stop=True)
                p.add("tensor", mm_h, reads=R("big", w1n, "b1row", "bs1row", "onesb"),
                      writes=R(f"ps{bank}"))
                p.add("scalar", lambda sc, tg=tg, mo=mo, pst=pst: sc.activation(
                    hsb[:, mo * T_FULL + tg * 512: mo * T_FULL + (tg + 1) * 512],
                    pst[:, 0:512], act),
                    reads=R(f"ps{bank}"), writes=R("hsb"))
        for t in range(NT):
            for half in range(2):
                bank = 2 + (t % 2) * 2 + half
                pst = ps[bank]

                def mm_y(tt, t=t, half=half, pst=pst, w2sb=w2sb):
                    for mo in range(4):
                        tt.matmul(pst[:, 0:512],
                                  hsb[:, mo * T_FULL + t * 128: mo * T_FULL + (t + 1) * 128],
                                  w2sb[:, mo * D + half * 512: mo * D + half * 512 + 512],
                                  start=(mo == 0), stop=False)
                    return tt.matmul(pst[:, 0:512], onesb[0:1, 0:128],
                                     brow2[0:1, brow2_off + half * 512: brow2_off + half * 512 + 512],
                                     start=False, stop=True)
                p.add("tensor", mm_y, reads=R("hsb", w2n, "b2row", "bs2row", "onesb"),
                      writes=R(f"ps{bank}"))
                sl = (t * 2 + half) % 4
                if wcol is None:
                    p.add("vector", lambda v, pst=pst, sl=sl: v.tensor_copy(
                        stmp[:, sl * 512:(sl + 1) * 512], pst[:, 0:512]),
                        reads=R(f"ps{bank}"), writes=R(f"stmp{sl}"))
                else:
                    p.add("vector", lambda v, pst=pst, t=t, wcol=wcol, sl=sl: v.tensor_scalar(
                        stmp[:, sl * 512:(sl + 1) * 512], pst[:, 0:512],
                        wgt[:, t * 8 + wcol: t * 8 + wcol + 1], None, ALU.mult),
                        reads=R(f"ps{bank}", "wgt"), writes=R(f"stmp{sl}"))
                p.add("gpsimd", lambda g, t=t, half=half, first=first, sl=sl: g.dma_start(
                    out=acc_d[t * 128:(t + 1) * 128, half * 512:half * 512 + 512],
                    in_=stmp[:, sl * 512:(sl + 1) * 512],
                    accum_op=(ALU.bypass if first else ALU.add)),
                    reads=R(f"stmp{sl}"), writes=R(f"acc{t}_{half}"), dma=True)

    p.add("gpsimd", lambda g: g.dma_start(
        out=w1a[:].rearrange("p (c o) -> p c o", o=HH),
        in_=ws1_d.rearrange("(c p) o -> p c o", p=128)), writes=R("w1a"), dma=True)
    p.add("gpsimd", lambda g: g.dma_start(
        out=w2a[:].rearrange("p (c o) -> p c o", o=D),
        in_=ws2_d.rearrange("(c p) o -> p c o", p=128)), writes=R("w2a"), dma=True)
    expert_pass("sh", w1a, "w1a", w2a, "w2a", bs1row, 0, bs2row, 0, AF.Gelu, None, True)
    for e in range(E):
        w1sb, w1n = (w1a, "w1a") if e % 2 == 0 else (w1b, "w1b")
        w2sb, w2n = (w2a, "w2a") if e % 2 == 0 else (w2b, "w2b")
        p.add("gpsimd", lambda g, w1sb=w1sb, e=e: g.dma_start(
            out=w1sb[:].rearrange("p (c o) -> p c o", o=HH),
            in_=w1_d[e].rearrange("(c p) o -> p c o", p=128)), writes=R(w1n), dma=True)
        p.add("gpsimd", lambda g, w2sb=w2sb, e=e: g.dma_start(
            out=w2sb[:].rearrange("p (c o) -> p c o", o=D),
            in_=w2_d[e].rearrange("(c p) o -> p c o", p=128)), writes=R(w2n), dma=True)
        expert_pass(f"e{e}", w1sb, w1n, w2sb, w2n, b1row, e * HH, b2row, e * D,
                    AF.Gelu, e, False)

    for t in range(NT):
        par = t % 2
        xf, xfn = (xf_a, "xf_a") if par == 0 else (xf_b, "xf_b")
        yt, ytn = (yt_a, "yt_a") if par == 0 else (yt_b, "yt_b")
        _emit_final_epilogue(p, nc, t, acc_d, xin, None, xf, xfn, None, None,
                             yt, ytn, sqd, small, xout,
                             acc_reads=[f"acc{t}_{h}" for h in range(2)])

    if standalone:
        p.emit(es)
        es.close()
        return nc
    p.fence()
    es.close()
    return None


def prep_l2_ffn(inp):
    io = {}
    io["c_onesb"] = CONSTS["c_onesb"]
    io["c_onesf"] = CONSTS["c_onesf"]
    io["gate_W"] = np.ascontiguousarray(inp["L2_ffn_gate_W"])
    io["W1"] = np.ascontiguousarray(inp["L2_ffn_W1"])
    io["b1"] = np.ascontiguousarray(inp["L2_ffn_b1"]).reshape(1, -1)
    io["W2"] = np.ascontiguousarray(inp["L2_ffn_W2"])
    io["b2"] = np.ascontiguousarray(inp["L2_ffn_b2"]).reshape(1, -1)
    io["Ws1"] = np.ascontiguousarray(inp["L2_ffn_Ws1"])
    io["bs1"] = np.ascontiguousarray(inp["L2_ffn_bs1"]).reshape(1, -1)
    io["Ws2"] = np.ascontiguousarray(inp["L2_ffn_Ws2"])
    io["bs2"] = np.ascontiguousarray(inp["L2_ffn_bs2"]).reshape(1, -1)
    io["route_scale"] = np.asarray(inp["L2_ffn_route_scale"], np.float32).reshape(1, 1)
    return io


def build_l1ffn_moe_launch(FUSE=None):
    standalone = FUSE is None
    nc = (bass.Bass("TRN2", target_bir_lowering=False) if standalone
          else NCProxy(FUSE["nc"], FUSE["pfx"], FUSE["rd"]))
    es = contextlib.ExitStack()
    E, HH = 8, 2048

    xin = nc.dram_tensor("xin", [T_FULL, D], F32, kind="ExternalInput").ap()
    xmid = nc.dram_tensor("xmid", [T_FULL, D], F32, kind="ExternalInput").ap()
    xout = nc.dram_tensor("xout", [T_FULL, D], F32, kind="ExternalOutput").ap()
    onesb_d = nc.dram_tensor("c_onesb", [1, 512], BF16, kind="ExternalInput").ap()
    gw_d = nc.dram_tensor("gate_w", [D, E], F32, kind="ExternalInput").ap()
    w1_d = nc.dram_tensor("w1", [E, D, HH], F32, kind="ExternalInput").ap()
    w2_d = nc.dram_tensor("w2", [E, D, HH], F32, kind="ExternalInput").ap()
    w3_d = nc.dram_tensor("w3", [E, HH, D], F32, kind="ExternalInput").ap()
    gam_d = nc.dram_tensor("ln_gam_col", [128, 8], F32, kind="ExternalInput").ap()
    beta_d = nc.dram_tensor("ln_beta_col", [128, 8], BF16, kind="ExternalInput").ap()

    nbf = nc.dram_tensor("nbf", [T_FULL, D], BF16, kind="Internal").ap()
    acc_d = nc.dram_tensor("acc", [T_FULL, D], F32, kind="Internal").ap()

    nTg_a = es.enter_context(nc.sbuf_tensor("nTg_a", [128, 8 * 512], BF16))
    nTg_b = es.enter_context(nc.sbuf_tensor("nTg_b", [128, 8 * 512], BF16))
    hsb = es.enter_context(nc.sbuf_tensor("hsb", [128, 16 * 512], BF16))
    w1sb = es.enter_context(nc.sbuf_tensor("w1sb", [128, 8 * HH], BF16))
    w2sb = es.enter_context(nc.sbuf_tensor("w2sb", [128, 8 * HH], BF16))
    w3sb = es.enter_context(nc.sbuf_tensor("w3sb", [128, 16 * D], BF16))
    gw_sb = es.enter_context(nc.sbuf_tensor("gw_sb", [128, 8 * E], BF16))
    gam_sb = es.enter_context(nc.sbuf_tensor("gam_sb", [128, 8], F32))
    beta_sb = es.enter_context(nc.sbuf_tensor("beta_sb", [128, 8], BF16))
    brow1 = es.enter_context(nc.sbuf_tensor("brow1", [1, HH], BF16))
    brow2 = es.enter_context(nc.sbuf_tensor("brow2", [1, HH], BF16))
    browg = es.enter_context(nc.sbuf_tensor("browg", [1, E], BF16))
    onesb = es.enter_context(nc.sbuf_tensor("onesb", [1, 512], BF16))
    gsc = es.enter_context(nc.sbuf_tensor("gsc", [128, 40], F32))
    wgt = es.enter_context(nc.sbuf_tensor("wgt", [128, NT * 8], F32))
    xa = es.enter_context(nc.sbuf_tensor("xa", [128, D], BF16))
    xb = es.enter_context(nc.sbuf_tensor("xb", [128, D], BF16))
    xf_a = es.enter_context(nc.sbuf_tensor("xf_a", [128, D], F32))
    xf_b = es.enter_context(nc.sbuf_tensor("xf_b", [128, D], F32))
    x1f_a = es.enter_context(nc.sbuf_tensor("x1f_a", [128, D], F32))
    x1f_b = es.enter_context(nc.sbuf_tensor("x1f_b", [128, D], F32))
    yt_a = es.enter_context(nc.sbuf_tensor("yt_a", [128, D], F32))
    yt_b = es.enter_context(nc.sbuf_tensor("yt_b", [128, D], F32))
    sqd = es.enter_context(nc.sbuf_tensor("sqd", [128, D], F32))
    small = es.enter_context(nc.sbuf_tensor("small", [128, 16], F32))
    stmp = es.enter_context(nc.sbuf_tensor("stmp", [128, 4 * 512], F32))
    sil = es.enter_context(nc.sbuf_tensor("sil", [128, 1024], F32))

    ps = [es.enter_context(nc.psum_tensor(f"ps{i}", [128, 512], F32)) for i in range(8)]
    p = Prog(nc) if standalone else FUSE["p"]

    p.add("vector", lambda v: v.memset(small[:, 11:12], EPS_TOP), writes=R("small"))
    p.add("gpsimd", lambda g: g.dma_start(out=onesb[:], in_=onesb_d), writes=R("onesb"), dma=True)
    p.add("gpsimd", lambda g: g.dma_start(out=gam_sb[:], in_=gam_d), writes=R("gam"), dma=True)
    p.add("gpsimd", lambda g: g.dma_start(out=beta_sb[:], in_=beta_d), writes=R("beta"), dma=True)
    p.add("gpsimd", lambda g: g.dma_start(
        out=gw_sb[:].rearrange("p (c e) -> p c e", e=E),
        in_=gw_d.rearrange("(c p) e -> p c e", p=128)), writes=R("gw"), dma=True)

    _emit_mirror_nT(p, nc, es, xmid, nbf, None, xa, xb, xf_a, xf_b, sqd, small,
                    pre_norm="ln", prep_eps=1e-5)

    def load_nTg(tg, buf, bufname):
        _load_nTg(p, nbf, tg, buf, bufname)

    def fold_weight(wsb, wname, dout, brow_sb, brow_name):
        nch = (dout + 511) // 512
        for half in range(nch):
            n = min(512, dout - half * 512)

            def mm_brow(tt, wsb=wsb, dout=dout, half=half, n=n):
                for c in range(8):
                    inst = tt.matmul(ps[6][0:1, 0:n], beta_sb[:, c:c + 1],
                                     wsb[:, c * dout + half * 512: c * dout + half * 512 + n],
                                     start=(c == 0), stop=(c == 7))
                return inst
            p.add("tensor", mm_brow, reads=R("beta", wname), writes=R("ps6"))
            p.add("scalar", lambda sc, half=half, n=n, brow_sb=brow_sb: sc.activation(
                brow_sb[0:1, half * 512:half * 512 + n], ps[6][0:1, 0:n], AF.Copy,
                bias=0.0, scale=1.0),
                reads=R("ps6"), writes=R(brow_name))
        for c in range(8):
            p.add("vector", lambda v, wsb=wsb, dout=dout, c=c: v.tensor_scalar(
                wsb[:, c * dout:(c + 1) * dout], wsb[:, c * dout:(c + 1) * dout],
                gam_sb[:, c:c + 1], None, ALU.mult),
                reads=R(wname, "gam"), writes=R(wname))

    fold_weight(gw_sb, "gw", E, browg, "browg")

    for tg in range(T_FULL // 512):
        buf, bufname = (nTg_a, "nTg_a") if tg % 2 == 0 else (nTg_b, "nTg_b")
        load_nTg(tg, buf, bufname)
        for ti in range(4):
            t = tg * 4 + ti
            bank = 6 + (t % 2)
            pst = ps[bank]

            def mm_gate(tt, ti=ti, pst=pst, buf=buf):
                for c in range(8):
                    tt.matmul(pst[:, 0:E],
                              buf[:, c * 512 + ti * 128: c * 512 + ti * 128 + 128],
                              gw_sb[:, c * E:(c + 1) * E],
                              start=(c == 0), stop=False)
                return tt.matmul(pst[:, 0:E], onesb[0:1, 0:128], browg[0:1, 0:E],
                                 start=False, stop=True)
            p.add("tensor", mm_gate, reads=R(bufname, "gw", "browg", "onesb"),
                  writes=R(f"ps{bank}"))
            p.add("vector", lambda v, pst=pst: v.tensor_copy(gsc[:, 0:8], pst[:, 0:E]),
                  reads=R(f"ps{bank}"), writes=R("gsc"))
            _emit_top2_gate(p, gsc, wgt, t, "softmax")

    for e in range(E):
        p.add("gpsimd", lambda g, e=e: g.dma_start(
            out=w1sb[:].rearrange("p (c o) -> p c o", o=HH),
            in_=w1_d[e].rearrange("(c p) o -> p c o", p=128)), writes=R("w1"), dma=True)
        p.add("gpsimd", lambda g, e=e: g.dma_start(
            out=w2sb[:].rearrange("p (c o) -> p c o", o=HH),
            in_=w2_d[e].rearrange("(c p) o -> p c o", p=128)), writes=R("w2"), dma=True)
        p.add("gpsimd", lambda g, e=e: g.dma_start(
            out=w3sb[:].rearrange("p (c o) -> p c o", o=D),
            in_=w3_d[e].rearrange("(c p) o -> p c o", p=128)), writes=R("w3"), dma=True)
        fold_weight(w1sb, "w1", HH, brow1, "brow1")
        fold_weight(w2sb, "w2", HH, brow2, "brow2")
        for tg in range(T_FULL // 512):
            buf, bufname = (nTg_a, "nTg_a") if tg % 2 == 0 else (nTg_b, "nTg_b")
            load_nTg(tg, buf, bufname)
            for hc in range(16):
                pa = ps[hc % 2]
                pb = ps[2 + hc % 2]

                def mm_h(tt, hc=hc, pa=pa, pb=pb, buf=buf):
                    for c in range(8):
                        tt.matmul(pa[:, 0:512],
                                  w1sb[:, c * HH + hc * 128: c * HH + hc * 128 + 128],
                                  buf[:, c * 512:(c + 1) * 512],
                                  start=(c == 0), stop=False)
                    tt.matmul(pa[:, 0:512],
                              brow1[0:1, hc * 128:hc * 128 + 128],
                              onesb[0:1, 0:512], start=False, stop=True)
                    for c in range(8):
                        tt.matmul(pb[:, 0:512],
                                  w2sb[:, c * HH + hc * 128: c * HH + hc * 128 + 128],
                                  buf[:, c * 512:(c + 1) * 512],
                                  start=(c == 0), stop=False)
                    return tt.matmul(pb[:, 0:512],
                                     brow2[0:1, hc * 128:hc * 128 + 128],
                                     onesb[0:1, 0:512], start=False, stop=True)
                p.add("tensor", mm_h, reads=R(bufname, "w1", "w2", "brow1", "brow2", "onesb"),
                      writes=R(f"ps{hc % 2}", f"ps{2 + hc % 2}"))
                p.add("scalar", lambda sc, pa=pa, hc=hc: sc.activation(
                    sil[:, (hc % 2) * 512:(hc % 2) * 512 + 512], pa[:, 0:512], AF.Silu),
                      reads=R(f"ps{hc % 2}"), writes=R(f"sil{hc % 2}"))
                p.add("vector", lambda v, pb=pb, hc=hc: v.tensor_tensor(
                    hsb[:, hc * 512:(hc + 1) * 512],
                    sil[:, (hc % 2) * 512:(hc % 2) * 512 + 512], pb[:, 0:512], ALU.mult),
                    reads=R(f"sil{hc % 2}", f"ps{2 + hc % 2}"), writes=R(f"hsb{hc}"))
            for ti in range(4):
                t = tg * 4 + ti
                for half in range(2):
                    bank = 4 + (ti % 2) * 2 + half
                    pst = ps[bank]

                    def mm_y(tt, t=t, ti=ti, half=half, pst=pst):
                        for hc in range(16):
                            inst = tt.matmul(
                                pst[:, 0:512],
                                hsb[:, hc * 512 + ti * 128: hc * 512 + ti * 128 + 128],
                                w3sb[:, hc * D + half * 512: hc * D + half * 512 + 512],
                                start=(hc == 0), stop=(hc == 15))
                        return inst
                    p.add("tensor", mm_y,
                          reads=tuple(f"hsb{hc}" for hc in range(16)) + R("w3"),
                          writes=R(f"ps{bank}"))
                    sl = (t * 2 + half) % 4
                    p.add("vector", lambda v, pst=pst, t=t, e=e, sl=sl: v.tensor_scalar(
                        stmp[:, sl * 512:(sl + 1) * 512], pst[:, 0:512],
                        wgt[:, t * 8 + e: t * 8 + e + 1], None, ALU.mult),
                        reads=R(f"ps{bank}", "wgt"), writes=R(f"stmp{sl}"))
                    p.add("gpsimd", lambda g, t=t, half=half, e=e, sl=sl: g.dma_start(
                        out=acc_d[t * 128:(t + 1) * 128, half * 512:half * 512 + 512],
                        in_=stmp[:, sl * 512:(sl + 1) * 512],
                        accum_op=(ALU.bypass if e == 0 else ALU.add)),
                        reads=R(f"stmp{sl}"), writes=R(f"acc{t}_{half}"), dma=True)

    for t in range(NT):
        par = t % 2
        xf, xfn = (xf_a, "xf_a") if par == 0 else (xf_b, "xf_b")
        x1f, x1fn = (x1f_a, "x1f_a") if par == 0 else (x1f_b, "x1f_b")
        yt, ytn = (yt_a, "yt_a") if par == 0 else (yt_b, "yt_b")
        _emit_final_epilogue(p, nc, t, acc_d, xin, xmid, xf, xfn, x1f, x1fn,
                             yt, ytn, sqd, small, xout,
                             acc_reads=[f"acc{t}_{h}" for h in range(2)])

    if standalone:
        p.emit(es)
        es.close()
        return nc
    p.fence()
    es.close()
    return None


def prep_l1_ffn_moe(inp):
    io = {}
    io["c_onesb"] = CONSTS["c_onesb"]
    io["gate_w"] = np.ascontiguousarray(inp["L1_ffn_gate_w"])
    io["w1"] = np.ascontiguousarray(inp["L1_ffn_w1"])
    io["w2"] = np.ascontiguousarray(inp["L1_ffn_w2"])
    io["w3"] = np.ascontiguousarray(inp["L1_ffn_w3"])
    io["ln_gam_col"] = np.ascontiguousarray(
        np.asarray(inp["L1_ffn_ln2_scale"]).reshape(8, 128).T.astype(np.float32))
    io["ln_beta_col"] = np.ascontiguousarray(
        np.asarray(inp["L1_ffn_ln2_shift"]).reshape(8, 128).T).astype(BF)
    return io


def build_l3ffn_a_launch(FUSE=None):
    standalone = FUSE is None
    nc = (bass.Bass("TRN2", target_bir_lowering=False) if standalone
          else NCProxy(FUSE["nc"], FUSE["pfx"], FUSE["rd"]))
    es = contextlib.ExitStack()
    EE = 32

    xin = nc.dram_tensor("xin", [T_FULL, D], F32, kind="ExternalInput").ap()
    onesb_d = nc.dram_tensor("c_onesb", [1, 512], BF16, kind="ExternalInput").ap()
    onesf_d = nc.dram_tensor("c_onesf", [1, 128], F32, kind="ExternalInput").ap()
    sut_d = nc.dram_tensor("c_strictut", [128, 128], BF16, kind="ExternalInput").ap()
    ones128_d = nc.dram_tensor("c_ones128", [128, 128], BF16, kind="ExternalInput").ap()
    rwT_d = nc.dram_tensor("rwT", [D, EE], F32, kind="ExternalInput").ap()
    ebias_d = nc.dram_tensor("ebias", [1, EE], F32, kind="ExternalInput").ap()

    selw_d = nc.dram_tensor("selw", [T_FULL, EE], F32, kind="ExternalOutput").ap()
    lpos_d = nc.dram_tensor("lpos", [T_FULL, EE], F32, kind="ExternalOutput").ap()
    cnt_d = nc.dram_tensor("cnt", [1, EE], F32, kind="ExternalOutput").ap()

    nbf = nc.dram_tensor("nbf", [T_FULL, D], BF16, kind="Internal").ap()

    big = es.enter_context(nc.sbuf_tensor("big", [128, 8 * T_FULL], BF16))
    rw_sb = es.enter_context(nc.sbuf_tensor("rw_sb", [128, 8 * EE], BF16))
    onesb = es.enter_context(nc.sbuf_tensor("onesb", [1, 512], BF16))
    onesf = es.enter_context(nc.sbuf_tensor("onesf", [1, 128], F32))
    sut = es.enter_context(nc.sbuf_tensor("sut", [128, 128], BF16))
    ones128 = es.enter_context(nc.sbuf_tensor("ones128", [128, 128], BF16))
    ebias_sb = es.enter_context(nc.sbuf_tensor("ebias_sb", [1, EE], F32))
    eb_bc = es.enter_context(nc.sbuf_tensor("eb_bc", [128, EE], F32))
    gsc = es.enter_context(nc.sbuf_tensor("gsc", [128, 5 * EE + 8], F32))
    selm = es.enter_context(nc.sbuf_tensor("selm", [128, NT * EE], BF16))
    sw_t = es.enter_context(nc.sbuf_tensor("sw_t", [128, 2 * EE], F32))
    lp_t = es.enter_context(nc.sbuf_tensor("lp_t", [128, 2 * EE], F32))
    cnt_sb = es.enter_context(nc.sbuf_tensor("cnt_sb", [1, EE], F32))
    xa = es.enter_context(nc.sbuf_tensor("xa", [128, D], BF16))
    xb = es.enter_context(nc.sbuf_tensor("xb", [128, D], BF16))
    xf_a = es.enter_context(nc.sbuf_tensor("xf_a", [128, D], F32))
    xf_b = es.enter_context(nc.sbuf_tensor("xf_b", [128, D], F32))
    sqd = es.enter_context(nc.sbuf_tensor("sqd", [128, D], F32))
    small = es.enter_context(nc.sbuf_tensor("small", [128, 16], F32))

    ps = [es.enter_context(nc.psum_tensor(f"ps{i}", [128, 512], F32)) for i in range(8)]
    p = Prog(nc) if standalone else FUSE["p"]

    p.add("gpsimd", lambda g: g.dma_start(out=onesb[:], in_=onesb_d), writes=R("onesb"), dma=True)
    p.add("gpsimd", lambda g: g.dma_start(out=onesf[:], in_=onesf_d), writes=R("onesf"), dma=True)
    p.add("gpsimd", lambda g: g.dma_start(out=sut[:], in_=sut_d), writes=R("sut"), dma=True)
    p.add("gpsimd", lambda g: g.dma_start(out=ones128[:], in_=ones128_d), writes=R("o128"), dma=True)
    p.add("gpsimd", lambda g: g.dma_start(out=ebias_sb[:], in_=ebias_d), writes=R("ebias"), dma=True)
    p.add("gpsimd", lambda g: g.dma_start(
        out=rw_sb[:].rearrange("p (c e) -> p c e", e=EE),
        in_=rwT_d.rearrange("(c p) e -> p c e", p=128)), writes=R("rw"), dma=True)

    _emit_mirror_nT(p, nc, es, xin, nbf, big, xa, xb, xf_a, xf_b, sqd, small)

    p.add("tensor", lambda t: t.matmul(ps[6][0:128, 0:EE], onesf[0:1, :], ebias_sb[:],
                                        start=True, stop=True),
          reads=R("onesf", "ebias"), writes=R("ps6"))
    p.add("vector", lambda v: v.tensor_copy(eb_bc[:], ps[6][0:128, 0:EE]),
          reads=R("ps6"), writes=R("eb_bc"))

    for t in range(NT):
        bank = 6 + (t % 2)
        pst = ps[bank]

        def mm_r(tt, t=t, pst=pst):
            for c in range(8):
                inst = tt.matmul(pst[:, 0:EE],
                                 big[:, c * T_FULL + t * 128: c * T_FULL + (t + 1) * 128],
                                 rw_sb[:, c * EE:(c + 1) * EE],
                                 start=(c == 0), stop=(c == 7))
            return inst
        p.add("tensor", mm_r, reads=R("big", "rw"), writes=R(f"ps{bank}"))
        p.add("scalar", lambda sc, pst=pst: sc.activation(gsc[:, 0:EE], pst[:, 0:EE], AF.Sigmoid),
              reads=R(f"ps{bank}"), writes=R("gsc"))
        p.add("vector", lambda v: v.tensor_add(gsc[:, EE:2 * EE], gsc[:, 0:EE], eb_bc[:]),
              reads=R("gsc", "eb_bc"), writes=R("gsc"))
        p.add("vector", lambda v: v.memset(gsc[:, 2 * EE:3 * EE], 0.0),
              writes=R("gsc"))
        for k in range(8):
            p.add("vector", lambda v: v.reduce_max(gsc[:, 5 * EE:5 * EE + 1],
                                                   gsc[:, EE:2 * EE], axis=AX.X),
                  reads=R("gsc"), writes=R("gsc"))
            p.add("vector", lambda v: v.tensor_single_scalar(
                gsc[:, 3 * EE:4 * EE], gsc[:, EE:2 * EE], gsc[:, 5 * EE:5 * EE + 1], ALU.is_ge),
                reads=R("gsc"), writes=R("gsc"))
            p.add("vector", lambda v: v.tensor_add(gsc[:, 2 * EE:3 * EE],
                                                   gsc[:, 2 * EE:3 * EE], gsc[:, 3 * EE:4 * EE]),
                  reads=R("gsc"), writes=R("gsc"))
            p.add("vector", lambda v: v.scalar_tensor_tensor(
                gsc[:, EE:2 * EE], gsc[:, 3 * EE:4 * EE], -1e30,
                gsc[:, EE:2 * EE], ALU.mult, ALU.add),
                reads=R("gsc"), writes=R("gsc"))
        p.add("vector", lambda v, t=t: v.tensor_copy(selm[:, t * EE:(t + 1) * EE],
                                                     gsc[:, 2 * EE:3 * EE]),
              reads=R("gsc"), writes=R(f"selm{t}"))
        p.add("vector", lambda v: v.tensor_mul(gsc[:, 4 * EE:5 * EE], gsc[:, 0:EE],
                                               gsc[:, 2 * EE:3 * EE]),
              reads=R("gsc"), writes=R("gsc"))
        p.add("vector", lambda v: v.reduce_sum(gsc[:, 5 * EE + 1:5 * EE + 2],
                                               gsc[:, 4 * EE:5 * EE], axis=AX.X),
              reads=R("gsc"), writes=R("gsc"))
        p.add("vector", lambda v: v.tensor_scalar_add(gsc[:, 5 * EE + 1:5 * EE + 2],
                                                      gsc[:, 5 * EE + 1:5 * EE + 2], 1e-20),
              reads=R("gsc"), writes=R("gsc"))
        p.add("vector", lambda v: v.reciprocal(gsc[:, 5 * EE + 2:5 * EE + 3],
                                               gsc[:, 5 * EE + 1:5 * EE + 2]),
              reads=R("gsc"), writes=R("gsc"))
        p.add("vector", lambda v: v.tensor_scalar_mul(gsc[:, 5 * EE + 2:5 * EE + 3],
                                                      gsc[:, 5 * EE + 2:5 * EE + 3], 2.5),
              reads=R("gsc"), writes=R("gsc"))
        sl = t % 2
        p.add("vector", lambda v, sl=sl: v.tensor_scalar(
            sw_t[:, sl * EE:(sl + 1) * EE], gsc[:, 4 * EE:5 * EE],
            gsc[:, 5 * EE + 2:5 * EE + 3], None, ALU.mult),
            reads=R("gsc"), writes=R(f"swt{sl}"))
        p.add("sync", lambda s, t=t, sl=sl: s.dma_start(
            out=selw_d[t * 128:(t + 1) * 128, :], in_=sw_t[:, sl * EE:(sl + 1) * EE]),
            reads=R(f"swt{sl}"), writes=R(f"dram_sw{t}"), dma=True)

    for t in range(NT):
        bank = 4 + (t % 2)
        pst = ps[bank]

        def mm_lp(tt, t=t, pst=pst):
            if t == 0:
                return tt.matmul(pst[:, 0:EE], sut[:], selm[:, 0:EE], start=True, stop=True)
            for i in range(t):
                tt.matmul(pst[:, 0:EE], ones128[:], selm[:, i * EE:(i + 1) * EE],
                          start=(i == 0), stop=False)
            return tt.matmul(pst[:, 0:EE], sut[:], selm[:, t * EE:(t + 1) * EE],
                             start=False, stop=True)
        p.add("tensor", mm_lp, reads=tuple(f"selm{i}" for i in range(t + 1)) + R("sut", "o128"),
              writes=R(f"ps{bank}"))
        sl = t % 2
        p.add("vector", lambda v, pst=pst, sl=sl: v.tensor_copy(
            lp_t[:, sl * EE:(sl + 1) * EE], pst[:, 0:EE]),
            reads=R(f"ps{bank}"), writes=R(f"lpt{sl}"))
        p.add("sync", lambda s, t=t, sl=sl: s.dma_start(
            out=lpos_d[t * 128:(t + 1) * 128, :], in_=lp_t[:, sl * EE:(sl + 1) * EE]),
            reads=R(f"lpt{sl}"), writes=R(f"dram_lp{t}"), dma=True)

    def mm_cnt(tt):
        for t in range(NT):
            inst = tt.matmul(ps[3][0:1, 0:EE], onesb[0:1, 0:128].bitcast(BF16),
                             selm[:, t * EE:(t + 1) * EE], start=(t == 0), stop=(t == NT - 1)) \
                if False else tt.matmul(ps[3][0:1, 0:EE], ones128[:, 0:1],
                                        selm[:, t * EE:(t + 1) * EE],
                                        start=(t == 0), stop=(t == NT - 1))
        return inst
    p.add("tensor", mm_cnt, reads=tuple(f"selm{i}" for i in range(NT)) + R("o128"),
          writes=R("ps3"))
    p.add("vector", lambda v: v.tensor_copy(cnt_sb[:], ps[3][0:1, 0:EE]),
          reads=R("ps3"), writes=R("cnt"))
    p.add("sync", lambda s: s.dma_start(out=cnt_d, in_=cnt_sb[:]),
          reads=R("cnt"), writes=R("dram_cnt"), dma=True)

    if standalone:
        p.emit(es)
        es.close()
        return nc
    p.fence()
    es.close()
    return None


def build_l3ffn_b_launch(FUSE=None):
    standalone = FUSE is None
    nc = (bass.Bass("TRN2", target_bir_lowering=False) if standalone
          else NCProxy(FUSE["nc"], FUSE["pfx"], FUSE["rd"]))
    es = contextlib.ExitStack()
    EE, II, SI = 32, 512, 2048
    NTG = T_FULL // 512

    xin = nc.dram_tensor("xin", [T_FULL, D], F32, kind="ExternalInput").ap()
    onesb_d = nc.dram_tensor("c_onesb", [1, 512], BF16, kind="ExternalInput").ap()
    onesf_d = nc.dram_tensor("c_onesf", [1, 128], F32, kind="ExternalInput").ap()
    selw_d = nc.dram_tensor("selw", [T_FULL, EE], F32, kind="ExternalInput").ap()
    lpos_d = nc.dram_tensor("lpos", [T_FULL, EE], F32, kind="ExternalInput").ap()
    gath_d = nc.dram_tensor("gathered", [8, EE], F32, kind="ExternalInput").ap()
    pmask_d = nc.dram_tensor("premask", [8, 1], F32, kind="ExternalInput").ap()
    gup_d = nc.dram_tensor("gup", [EE, D, 2 * II], F32, kind="ExternalInput").ap()
    down_d = nc.dram_tensor("down", [EE, II, D], F32, kind="ExternalInput").ap()
    sg_d = nc.dram_tensor("sgT", [D, SI], F32, kind="ExternalInput").ap()
    su_d = nc.dram_tensor("suT", [D, SI], F32, kind="ExternalInput").ap()
    sd_d = nc.dram_tensor("sdT", [SI, D], F32, kind="ExternalInput").ap()
    xout = nc.dram_tensor("xout", [T_FULL, D], F32, kind="ExternalOutput").ap()

    nbf = nc.dram_tensor("nbf", [T_FULL, D], BF16, kind="Internal").ap()
    acc_d = nc.dram_tensor("acc", [T_FULL, D], F32, kind="Internal").ap()

    nTg_a = es.enter_context(nc.sbuf_tensor("nTg_a", [128, 8 * 512], BF16))
    nTg_b = es.enter_context(nc.sbuf_tensor("nTg_b", [128, 8 * 512], BF16))
    hsb = es.enter_context(nc.sbuf_tensor("hsb", [128, 16 * 512], BF16))
    pool = es.enter_context(nc.sbuf_tensor("pool", [128, 49152], BF16))
    onesb = es.enter_context(nc.sbuf_tensor("onesb", [1, 512], BF16))
    onesf = es.enter_context(nc.sbuf_tensor("onesf", [1, 128], F32))
    gath_sb = es.enter_context(nc.sbuf_tensor("gath_sb", [8, EE], F32))
    pmask_sb = es.enter_context(nc.sbuf_tensor("pmask_sb", [8, 1], F32))
    offs_bc = es.enter_context(nc.sbuf_tensor("offs_bc", [128, EE], F32))
    wgt = es.enter_context(nc.sbuf_tensor("wgt", [128, NT * EE], F32))
    wtmp = es.enter_context(nc.sbuf_tensor("wtmp", [128, 2 * EE], F32))
    xa = es.enter_context(nc.sbuf_tensor("xa", [128, D], BF16))
    xb = es.enter_context(nc.sbuf_tensor("xb", [128, D], BF16))
    xf_a = es.enter_context(nc.sbuf_tensor("xf_a", [128, D], F32))
    xf_b = es.enter_context(nc.sbuf_tensor("xf_b", [128, D], F32))
    yt_a = es.enter_context(nc.sbuf_tensor("yt_a", [128, D], F32))
    yt_b = es.enter_context(nc.sbuf_tensor("yt_b", [128, D], F32))
    sqd = es.enter_context(nc.sbuf_tensor("sqd", [128, D], F32))
    small = es.enter_context(nc.sbuf_tensor("small", [128, 16], F32))
    stmp = es.enter_context(nc.sbuf_tensor("stmp", [128, 4 * 512], F32))
    sil = es.enter_context(nc.sbuf_tensor("sil", [128, 1024], F32))
    utmp = es.enter_context(nc.sbuf_tensor("utmp", [128, 1024], F32))
    gtmp = es.enter_context(nc.sbuf_tensor("gtmp", [128, 1024], F32))

    ps = [es.enter_context(nc.psum_tensor(f"ps{i}", [128, 512], F32)) for i in range(8)]
    p = Prog(nc) if standalone else FUSE["p"]

    p.add("vector", lambda v: v.memset(small[:, 11:12], EPS_TOP), writes=R("small"))
    p.add("gpsimd", lambda g: g.dma_start(out=onesb[:], in_=onesb_d), writes=R("onesb"), dma=True)
    p.add("gpsimd", lambda g: g.dma_start(out=onesf[:], in_=onesf_d), writes=R("onesf"), dma=True)
    p.add("gpsimd", lambda g: g.dma_start(out=gath_sb[:], in_=gath_d), writes=R("gath"), dma=True)
    p.add("gpsimd", lambda g: g.dma_start(out=pmask_sb[:], in_=pmask_d), writes=R("pmask"), dma=True)

    _emit_mirror_nT(p, nc, es, xin, nbf, None, xa, xb, xf_a, xf_b, sqd, small)

    p.add("tensor", lambda t: t.matmul(ps[6][0:1, 0:EE], pmask_sb[:], gath_sb[:],
                                       start=True, stop=True),
          reads=R("pmask", "gath"), writes=R("ps6"))
    p.add("scalar", lambda sc: sc.copy(wtmp[0:1, 0:EE], ps[6][0:1, 0:EE]),
          reads=R("ps6"), writes=R("wtmp0"))
    p.add("tensor", lambda t: t.matmul(ps[6][0:128, 0:EE], onesf[0:1, :],
                                       wtmp[0:1, 0:EE], start=True, stop=True),
          reads=R("onesf", "wtmp0"), writes=R("ps6"))
    p.add("vector", lambda v: v.tensor_copy(offs_bc[:], ps[6][0:128, 0:EE]),
          reads=R("ps6"), writes=R("offs"))

    for t in range(NT):
        sl = t % 2
        p.add("gpsimd", lambda g, t=t, sl=sl: g.dma_start(
            out=wtmp[:, sl * EE:(sl + 1) * EE], in_=lpos_d[t * 128:(t + 1) * 128, :]),
            writes=R(f"wtmp{sl}"), dma=True)
        p.add("vector", lambda v, sl=sl: v.tensor_add(
            wtmp[:, sl * EE:(sl + 1) * EE], wtmp[:, sl * EE:(sl + 1) * EE], offs_bc[:]),
            reads=R(f"wtmp{sl}", "offs"), writes=R(f"wtmp{sl}"))
        p.add("vector", lambda v, sl=sl: v.tensor_single_scalar(
            wtmp[:, sl * EE:(sl + 1) * EE], wtmp[:, sl * EE:(sl + 1) * EE], 2048.0, ALU.is_lt),
            reads=R(f"wtmp{sl}"), writes=R(f"wtmp{sl}"))
        p.add("gpsimd", lambda g, t=t: g.dma_start(
            out=wgt[:, t * EE:(t + 1) * EE], in_=selw_d[t * 128:(t + 1) * 128, :]),
            writes=R("wgt"), dma=True)
        p.add("vector", lambda v, t=t, sl=sl: v.tensor_mul(
            wgt[:, t * EE:(t + 1) * EE], wgt[:, t * EE:(t + 1) * EE],
            wtmp[:, sl * EE:(sl + 1) * EE]),
            reads=R("wgt", f"wtmp{sl}"), writes=R("wgt"))

    p.add("gpsimd", lambda g: g.dma_start(
        out=pool[:, 0:16384].rearrange("p (c o) -> p c o", o=SI),
        in_=sg_d.rearrange("(c p) o -> p c o", p=128)), writes=R("pool"), dma=True)
    p.add("gpsimd", lambda g: g.dma_start(
        out=pool[:, 16384:32768].rearrange("p (c o) -> p c o", o=SI),
        in_=su_d.rearrange("(c p) o -> p c o", p=128)), writes=R("pool"), dma=True)
    p.add("gpsimd", lambda g: g.dma_start(
        out=pool[:, 32768:49152].rearrange("p (c o) -> p c o", o=D),
        in_=sd_d.rearrange("(c p) o -> p c o", p=128)), writes=R("pool"), dma=True)

    for tg in range(NTG):
        buf, bufname = (nTg_a, "nTg_a") if tg % 2 == 0 else (nTg_b, "nTg_b")
        _load_nTg(p, nbf, tg, buf, bufname)
        for hc in range(16):
            pa = ps[hc % 2]
            pb = ps[2 + hc % 2]

            def mm_gu(tt, hc=hc, pa=pa, pb=pb, buf=buf):
                for c in range(8):
                    tt.matmul(pa[:, 0:512],
                              pool[:, c * SI + hc * 128: c * SI + hc * 128 + 128],
                              buf[:, c * 512:(c + 1) * 512],
                              start=(c == 0), stop=(c == 7))
                for c in range(8):
                    inst = tt.matmul(pb[:, 0:512],
                                     pool[:, 16384 + c * SI + hc * 128: 16384 + c * SI + hc * 128 + 128],
                                     buf[:, c * 512:(c + 1) * 512],
                                     start=(c == 0), stop=(c == 7))
                return inst
            p.add("tensor", mm_gu, reads=R(bufname, "pool"),
                  writes=R(f"ps{hc % 2}", f"ps{2 + hc % 2}"))
            p.add("scalar", lambda sc, pa=pa, hc=hc: sc.activation(
                sil[:, (hc % 2) * 512:(hc % 2) * 512 + 512], pa[:, 0:512], AF.Silu),
                  reads=R(f"ps{hc % 2}"), writes=R(f"sil{hc % 2}"))
            p.add("vector", lambda v, pb=pb, hc=hc: v.tensor_tensor(
                hsb[:, hc * 512:(hc + 1) * 512],
                sil[:, (hc % 2) * 512:(hc % 2) * 512 + 512], pb[:, 0:512], ALU.mult),
                reads=R(f"sil{hc % 2}", f"ps{2 + hc % 2}"), writes=R(f"hsb{hc}"))
        for ti in range(4):
            t = tg * 4 + ti
            for half in range(2):
                bank = 4 + (ti % 2) * 2 + half
                pst = ps[bank]

                def mm_ys(tt, ti=ti, half=half, pst=pst):
                    for hc in range(16):
                        inst = tt.matmul(
                            pst[:, 0:512],
                            hsb[:, hc * 512 + ti * 128: hc * 512 + ti * 128 + 128],
                            pool[:, 32768 + hc * D + half * 512: 32768 + hc * D + half * 512 + 512],
                            start=(hc == 0), stop=(hc == 15))
                    return inst
                p.add("tensor", mm_ys,
                      reads=tuple(f"hsb{hc}" for hc in range(16)) + R("pool"),
                      writes=R(f"ps{bank}"))
                sl = (t * 2 + half) % 4
                p.add("vector", lambda v, pst=pst, sl=sl: v.tensor_copy(
                    stmp[:, sl * 512:(sl + 1) * 512], pst[:, 0:512]),
                    reads=R(f"ps{bank}"), writes=R(f"stmp{sl}"))
                p.add("gpsimd", lambda g, t=t, half=half, sl=sl: g.dma_start(
                    out=acc_d[t * 128:(t + 1) * 128, half * 512:half * 512 + 512],
                    in_=stmp[:, sl * 512:(sl + 1) * 512]),
                    reads=R(f"stmp{sl}"), writes=R(f"acc{t}_{half}"), dma=True)

    for e in range(EE):
        p.add("gpsimd", lambda g, e=e: g.dma_start(
            out=pool[:, 0:8192].rearrange("p (c o) -> p c o", o=1024),
            in_=gup_d[e].rearrange("(c p) o -> p c o", p=128)), writes=R("pool"), dma=True)
        p.add("gpsimd", lambda g, e=e: g.dma_start(
            out=pool[:, 8192:12288].rearrange("p (c o) -> p c o", o=1024),
            in_=down_d[e].rearrange("(c p) o -> p c o", p=128)), writes=R("pool"), dma=True)
        for tg in range(NTG):
            buf, bufname = (nTg_a, "nTg_a") if tg % 2 == 0 else (nTg_b, "nTg_b")
            _load_nTg(p, nbf, tg, buf, bufname)
            for j in range(4):
                pa = ps[j % 2]
                pb = ps[2 + j % 2]

                def mm_gu_e(tt, j=j, pa=pa, pb=pb, buf=buf):
                    for c in range(8):
                        tt.matmul(pa[:, 0:512],
                                  pool[:, c * 1024 + j * 128: c * 1024 + j * 128 + 128],
                                  buf[:, c * 512:(c + 1) * 512],
                                  start=(c == 0), stop=(c == 7))
                    for c in range(8):
                        inst = tt.matmul(pb[:, 0:512],
                                         pool[:, c * 1024 + 512 + j * 128: c * 1024 + 512 + j * 128 + 128],
                                         buf[:, c * 512:(c + 1) * 512],
                                         start=(c == 0), stop=(c == 7))
                    return inst
                p.add("tensor", mm_gu_e, reads=R(bufname, "pool"),
                      writes=R(f"ps{j % 2}", f"ps{2 + j % 2}"))
                jp = j % 2
                p.add("vector", lambda v, pa=pa, jp=jp: v.tensor_scalar_min(
                    gtmp[:, jp * 512:jp * 512 + 512], pa[:, 0:512], 7.0),
                      reads=R(f"ps{jp}"), writes=R(f"gtmp{jp}"))
                p.add("scalar", lambda sc, jp=jp: sc.activation(
                    sil[:, jp * 512:jp * 512 + 512], gtmp[:, jp * 512:jp * 512 + 512], AF.Silu),
                      reads=R(f"gtmp{jp}"), writes=R(f"sil{jp}"))
                p.add("vector", lambda v, pb=pb, jp=jp: v.tensor_scalar_min(
                    utmp[:, jp * 512:jp * 512 + 512], pb[:, 0:512], 7.0),
                      reads=R(f"ps{2 + jp}"), writes=R(f"utmp{jp}"))
                p.add("vector", lambda v, jp=jp: v.tensor_scalar_max(
                    utmp[:, jp * 512:jp * 512 + 512], utmp[:, jp * 512:jp * 512 + 512], -7.0),
                      reads=R(f"utmp{jp}"), writes=R(f"utmp{jp}"))
                p.add("vector", lambda v, j=j, jp=jp: v.tensor_tensor(
                    hsb[:, j * 512:(j + 1) * 512], sil[:, jp * 512:jp * 512 + 512],
                    utmp[:, jp * 512:jp * 512 + 512], ALU.mult),
                    reads=R(f"sil{jp}", f"utmp{jp}"), writes=R(f"hsb{j}"))
            for ti in range(4):
                t = tg * 4 + ti
                for half in range(2):
                    bank = 4 + (ti % 2) * 2 + half
                    pst = ps[bank]

                    def mm_ye(tt, ti=ti, half=half, pst=pst):
                        for j in range(4):
                            inst = tt.matmul(
                                pst[:, 0:512],
                                hsb[:, j * 512 + ti * 128: j * 512 + ti * 128 + 128],
                                pool[:, 8192 + j * D + half * 512: 8192 + j * D + half * 512 + 512],
                                start=(j == 0), stop=(j == 3))
                        return inst
                    p.add("tensor", mm_ye,
                          reads=tuple(f"hsb{j}" for j in range(4)) + R("pool"),
                          writes=R(f"ps{bank}"))
                    sl = (t * 2 + half) % 4
                    p.add("vector", lambda v, pst=pst, t=t, e=e, sl=sl: v.tensor_scalar(
                        stmp[:, sl * 512:(sl + 1) * 512], pst[:, 0:512],
                        wgt[:, t * EE + e: t * EE + e + 1], None, ALU.mult),
                        reads=R(f"ps{bank}", "wgt"), writes=R(f"stmp{sl}"))
                    p.add("gpsimd", lambda g, t=t, half=half, sl=sl: g.dma_start(
                        out=acc_d[t * 128:(t + 1) * 128, half * 512:half * 512 + 512],
                        in_=stmp[:, sl * 512:(sl + 1) * 512], accum_op=ALU.add),
                        reads=R(f"stmp{sl}"), writes=R(f"acc{t}_{half}"), dma=True)

    for t in range(NT):
        par = t % 2
        xf, xfn = (xf_a, "xf_a") if par == 0 else (xf_b, "xf_b")
        yt, ytn = (yt_a, "yt_a") if par == 0 else (yt_b, "yt_b")
        _emit_final_epilogue(p, nc, t, acc_d, xin, None, xf, xfn, None, None,
                             yt, ytn, sqd, small, xout,
                             acc_reads=[f"acc{t}_{h}" for h in range(2)])

    if standalone:
        p.emit(es)
        es.close()
        return nc
    p.fence()
    es.close()
    return None


def _strictut():
    m = np.triu(np.ones((128, 128), np.float32), 1)
    return m.astype(BF)


def prep_l3_ffn_a(inp):
    io = {}
    io["c_onesb"] = CONSTS["c_onesb"]
    io["c_onesf"] = CONSTS["c_onesf"]
    io["c_strictut"] = _strictut()
    io["c_ones128"] = np.ones((128, 128), BF)
    io["rwT"] = np.ascontiguousarray(np.asarray(inp["L3_ffn_router_weight"]).T)
    io["ebias"] = np.ascontiguousarray(inp["L3_ffn_e_score_correction_bias"]).reshape(1, -1)
    return io


def prep_l3_ffn_b(inp):
    io = {}
    io["c_onesb"] = CONSTS["c_onesb"]
    io["c_onesf"] = CONSTS["c_onesf"]
    gup = np.asarray(inp["L3_ffn_gate_up_proj"])
    io["gup"] = np.ascontiguousarray(np.transpose(gup, (0, 2, 1)))
    dn = np.asarray(inp["L3_ffn_down_proj"])
    io["down"] = np.ascontiguousarray(np.transpose(dn, (0, 2, 1)))
    io["sgT"] = np.ascontiguousarray(np.asarray(inp["L3_ffn_shared_gate"]).T)
    io["suT"] = np.ascontiguousarray(np.asarray(inp["L3_ffn_shared_up"]).T)
    io["sdT"] = np.ascontiguousarray(np.asarray(inp["L3_ffn_shared_down"]).T)
    return io


def build_l0ffn_bank_launch(FUSE=None):
    standalone = FUSE is None
    nc = (bass.Bass("TRN2", target_bir_lowering=False) if standalone
          else NCProxy(FUSE["nc"], FUSE["pfx"], FUSE["rd"]))
    es = contextlib.ExitStack()

    xin = nc.dram_tensor("xin", [T_FULL, D], F32, kind="ExternalInput").ap()
    xmid = nc.dram_tensor("xmid", [T_FULL, D], F32, kind="ExternalInput").ap()
    xout = nc.dram_tensor("xout", [T_FULL, D], F32, kind="ExternalOutput").ap()
    onesb_d = nc.dram_tensor("c_onesb", [1, 512], BF16, kind="ExternalInput").ap()
    onesf_d = nc.dram_tensor("c_onesf", [1, 128], F32, kind="ExternalInput").ap()
    iota_d = nc.dram_tensor("c_iota8", [1, 8], F32, kind="ExternalInput").ap()
    fclog_d = nc.dram_tensor("fc_logits", [1, 8], F32, kind="ExternalInput").ap()
    pjlog_d = nc.dram_tensor("pj_logits", [1, 8], F32, kind="ExternalInput").ap()
    fcbd1_d = nc.dram_tensor("fc_bd1", [8, 8, 128, 128], F32, kind="ExternalInput").ap()
    fcbd2_d = nc.dram_tensor("fc_bd2", [8, 32, 128, 128], F32, kind="ExternalInput").ap()
    fcmix_d = nc.dram_tensor("fc_mixp", [8, D, 4096], F32, kind="ExternalInput").ap()
    pjbd1_d = nc.dram_tensor("pj_bd1", [8, 32, 128, 128], F32, kind="ExternalInput").ap()
    pjbd2_d = nc.dram_tensor("pj_bd2", [8, 8, 128, 128], F32, kind="ExternalInput").ap()
    pjmix_d = nc.dram_tensor("pj_mixp", [8, 4096, D], F32, kind="ExternalInput").ap()

    nbf = nc.dram_tensor("nbf", [T_FULL, D], BF16, kind="Internal").ap()
    hacc = nc.dram_tensor("hacc", [4096, T_FULL], F32, kind="Internal").ap()
    hact = nc.dram_tensor("hact", [4096, T_FULL], BF16, kind="Internal").ap()
    acc_d = nc.dram_tensor("acc", [T_FULL, D], F32, kind="Internal").ap()

    mixA = es.enter_context(nc.sbuf_tensor("mixA", [128, 32768], BF16))
    bd1sb = es.enter_context(nc.sbuf_tensor("bd1sb", [128, 32 * 128], BF16))
    bd2sb = es.enter_context(nc.sbuf_tensor("bd2sb", [128, 32 * 128], BF16))
    nTg_a = es.enter_context(nc.sbuf_tensor("nTg_a", [128, 8 * 512], BF16))
    nTg_b = es.enter_context(nc.sbuf_tensor("nTg_b", [128, 8 * 512], BF16))
    h1sb = es.enter_context(nc.sbuf_tensor("h1sb", [128, 32 * 512], BF16))
    mix_sb = es.enter_context(nc.sbuf_tensor("mix_sb", [128, 2 * 512], BF16))
    scr_a = es.enter_context(nc.sbuf_tensor("scr_a", [128, 4096], F32))
    scrbf = es.enter_context(nc.sbuf_tensor("scrbf", [128, 4096], BF16))
    onesb = es.enter_context(nc.sbuf_tensor("onesb", [1, 512], BF16))
    onesf = es.enter_context(nc.sbuf_tensor("onesf", [1, 128], F32))
    iota8 = es.enter_context(nc.sbuf_tensor("iota8", [1, 8], F32))
    lg = es.enter_context(nc.sbuf_tensor("lg", [1, 64], F32))
    idx_i = es.enter_context(nc.sbuf_tensor("idx_i", [1, 4], mybir.dt.int32))
    c_bc = es.enter_context(nc.sbuf_tensor("c_bc", [128, 4], F32))
    xa = es.enter_context(nc.sbuf_tensor("xa", [128, D], BF16))
    xb = es.enter_context(nc.sbuf_tensor("xb", [128, D], BF16))
    xf_a = es.enter_context(nc.sbuf_tensor("xf_a", [128, D], F32))
    xf_b = es.enter_context(nc.sbuf_tensor("xf_b", [128, D], F32))
    x1f_a = es.enter_context(nc.sbuf_tensor("x1f_a", [128, D], F32))
    x1f_b = es.enter_context(nc.sbuf_tensor("x1f_b", [128, D], F32))
    yt_a = es.enter_context(nc.sbuf_tensor("yt_a", [128, D], F32))
    yt_b = es.enter_context(nc.sbuf_tensor("yt_b", [128, D], F32))
    sqd = es.enter_context(nc.sbuf_tensor("sqd", [128, D], F32))
    small = es.enter_context(nc.sbuf_tensor("small", [128, 16], F32))
    stmp = es.enter_context(nc.sbuf_tensor("stmp", [128, 4 * 512], F32))

    ps = [es.enter_context(nc.psum_tensor(f"ps{i}", [128, 512], F32)) for i in range(8)]
    p = Prog(nc) if standalone else FUSE["p"]

    p.add("vector", lambda v: v.memset(small[:, 11:12], EPS_TOP), writes=R("small"))
    p.add("gpsimd", lambda g: g.dma_start(out=onesb[:], in_=onesb_d), writes=R("onesb"), dma=True)
    p.add("gpsimd", lambda g: g.dma_start(out=onesf[:], in_=onesf_d), writes=R("onesf"), dma=True)
    p.add("gpsimd", lambda g: g.dma_start(out=iota8[:], in_=iota_d), writes=R("iota"), dma=True)
    p.add("gpsimd", lambda g: g.dma_start(out=lg[0:1, 0:8], in_=fclog_d), writes=R("lg"), dma=True)
    p.add("gpsimd", lambda g: g.dma_start(out=lg[0:1, 8:16], in_=pjlog_d), writes=R("lg"), dma=True)

    for s_ in range(2):
        base = s_ * 8
        m1c, m2c = 16 + 2 * s_, 17 + 2 * s_
        k1 = 20 + 16 * s_
        k2 = 28 + 16 * s_
        dcol = 52 + 3 * s_
        p.add("vector", lambda v, base=base, m1c=m1c: v.reduce_max(
            lg[0:1, m1c:m1c + 1], lg[0:1, base:base + 8], axis=AX.X),
            reads=R("lg"), writes=R("lg"))
        p.add("vector", lambda v, base=base, m1c=m1c, k1=k1: v.tensor_single_scalar(
            lg[0:1, k1:k1 + 8], lg[0:1, base:base + 8], lg[0:1, m1c:m1c + 1], ALU.is_ge),
            reads=R("lg"), writes=R("lg"))
        p.add("vector", lambda v, base=base, k1=k1, k2=k2: v.scalar_tensor_tensor(
            lg[0:1, k2:k2 + 8], lg[0:1, k1:k1 + 8], -1e30,
            lg[0:1, base:base + 8], ALU.mult, ALU.add),
            reads=R("lg"), writes=R("lg"))
        p.add("vector", lambda v, m2c=m2c, k2=k2: v.reduce_max(
            lg[0:1, m2c:m2c + 1], lg[0:1, k2:k2 + 8], axis=AX.X),
            reads=R("lg"), writes=R("lg"))
        p.add("vector", lambda v, m2c=m2c, k2=k2: v.tensor_single_scalar(
            lg[0:1, k2:k2 + 8], lg[0:1, k2:k2 + 8], lg[0:1, m2c:m2c + 1], ALU.is_ge),
            reads=R("lg"), writes=R("lg"))
        p.add("vector", lambda v, m1c=m1c, m2c=m2c, dcol=dcol: v.tensor_sub(
            lg[0:1, dcol:dcol + 1], lg[0:1, m1c:m1c + 1], lg[0:1, m2c:m2c + 1]),
            reads=R("lg"), writes=R("lg"))
        p.add("scalar", lambda sc, dcol=dcol: sc.activation(
            lg[0:1, dcol + 1:dcol + 2], lg[0:1, dcol:dcol + 1], AF.Sigmoid),
            reads=R("lg"), writes=R("lg"))
        p.add("vector", lambda v, dcol=dcol: v.tensor_scalar(
            lg[0:1, dcol + 2:dcol + 3], lg[0:1, dcol + 1:dcol + 2],
            -1.0, 1.0, ALU.mult, ALU.add),
            reads=R("lg"), writes=R("lg"))
        for m_ in range(2):
            mk = k1 if m_ == 0 else k2
            icol = 58 + 2 * s_ + m_
            p.add("vector", lambda v, mk=mk: v.tensor_mul(
                lg[0:1, mk:mk + 8], lg[0:1, mk:mk + 8], iota8[0:1, :]),
                reads=R("lg", "iota"), writes=R("lg"))
            p.add("vector", lambda v, mk=mk, icol=icol: v.reduce_sum(
                lg[0:1, icol:icol + 1], lg[0:1, mk:mk + 8], axis=AX.X),
                reads=R("lg"), writes=R("lg"))
            p.add("vector", lambda v, s_=s_, m_=m_, icol=icol: v.tensor_copy(
                idx_i[0:1, 2 * s_ + m_:2 * s_ + m_ + 1], lg[0:1, icol:icol + 1]),
                reads=R("lg"), writes=R("idx"))
    for j, col in enumerate((53, 54, 56, 57)):
        p.add("tensor", lambda t, j=j, col=col: t.matmul(
            ps[6][0:128, j:j + 1], onesf[0:1, :], lg[0:1, col:col + 1],
            start=True, stop=True),
            reads=R("onesf", "lg"), writes=R("ps6"))
        p.add("vector", lambda v, j=j: v.tensor_copy(c_bc[:, j:j + 1], ps[6][0:128, j:j + 1]),
              reads=R("ps6"), writes=R("cbc"))

    _emit_mirror_nT(p, nc, es, xmid, nbf, None, xa, xb, xf_a, xf_b, sqd, small,
                    pre_norm="rms", prep_eps=1e-6)

    def branch_load(idx_col, out_ap, dram_by_bank, name, reads_extra=()):
        def fn(g, inc):
            reg = g.alloc_register()
            g.reg_load(reg, idx_i[0:1, idx_col:idx_col + 1])
            for b in range(8):
                with g.If_eq(reg, b):
                    inc(g.dma_start(out=out_ap, in_=dram_by_bank(b)))
            g.end_ifs()
            g.free_register(reg)
        p.add("gpsimd", fn, reads=R("idx") + tuple(reads_extra), writes=R(name),
              dma=True, raw=True)

    def scale_bd2(name, sb, width, ccol):
        p.add("vector", lambda v, sb=sb, width=width, ccol=ccol: v.tensor_scalar(
            sb[:, 0:width], sb[:, 0:width], c_bc[:, ccol:ccol + 1], None, ALU.mult),
            reads=R(name, "cbc"), writes=R(name))

    NTG = T_FULL // 512

    for m_ in range(2):
        idxc = 0 + m_
        branch_load(idxc, bd1sb[:, 0:8 * 128].rearrange("p (j w) -> p j w", w=128),
                    lambda b: fcbd1_d[b].rearrange("j p w -> p j w"), "bd1")
        branch_load(idxc, bd2sb[:].rearrange("p (j w) -> p j w", w=128),
                    lambda b: fcbd2_d[b].rearrange("j p w -> p j w"), "bd2")
        branch_load(idxc, mixA[:].rearrange("p (c o) -> p c o", o=4096),
                    lambda b: fcmix_d[b].rearrange("(c p) o -> p c o", p=128), "mixA")
        scale_bd2("bd2", bd2sb, 32 * 128, 0 + m_)
        for tg in range(NTG):
            buf, bufname = (nTg_a, "nTg_a") if tg % 2 == 0 else (nTg_b, "nTg_b")
            _load_nTg(p, nbf, tg, buf, bufname)
            for ch in range(8):
                pa = ps[tg % 2]
                p.add("tensor", lambda tt, ch=ch, pa=pa, buf=buf: tt.matmul(
                    pa[:, 0:512], bd1sb[:, ch * 128:(ch + 1) * 128],
                    buf[:, ch * 512:(ch + 1) * 512], start=True, stop=True),
                    reads=R("bd1", bufname), writes=R(f"ps{tg % 2}"))
                p.add("vector", lambda v, ch=ch, pa=pa: v.tensor_copy(
                    h1sb[:, ch * 512:(ch + 1) * 512], pa[:, 0:512]),
                    reads=R(f"ps{tg % 2}"), writes=R("h1"))
            for och in range(32):
                pm = ps[2 + och % 2]

                def mm_mix(tt, och=och, pm=pm):
                    for ch in range(8):
                        inst = tt.matmul(
                            pm[:, 0:512],
                            mixA[:, ch * 4096 + och * 128: ch * 4096 + och * 128 + 128],
                            h1sb[:, ch * 512:(ch + 1) * 512],
                            start=(ch == 0), stop=(ch == 7))
                    return inst
                p.add("tensor", mm_mix, reads=R("mixA", "h1"), writes=R(f"ps{2 + och % 2}"))
                msl = och % 2
                p.add("vector", lambda v, pm=pm, msl=msl: v.tensor_copy(
                    mix_sb[:, msl * 512:(msl + 1) * 512], pm[:, 0:512]),
                    reads=R(f"ps{2 + och % 2}"), writes=R(f"mxsb{msl}"))
                pb2 = ps[4 + och % 2]
                p.add("tensor", lambda tt, och=och, msl=msl, pb2=pb2: tt.matmul(
                    pb2[:, 0:512], bd2sb[:, och * 128:(och + 1) * 128],
                    mix_sb[:, msl * 512:(msl + 1) * 512], start=True, stop=True),
                    reads=R("bd2", f"mxsb{msl}"), writes=R(f"ps{4 + och % 2}"))
                sl = och % 4
                p.add("vector", lambda v, pb2=pb2, sl=sl: v.tensor_copy(
                    stmp[:, sl * 512:(sl + 1) * 512], pb2[:, 0:512]),
                    reads=R(f"ps{4 + och % 2}"), writes=R(f"stmp{sl}"))
                p.add("gpsimd", lambda g, och=och, tg=tg, sl=sl, m_=m_: g.dma_start(
                    out=hacc[och * 128:(och + 1) * 128, tg * 512:(tg + 1) * 512],
                    in_=stmp[:, sl * 512:(sl + 1) * 512],
                    accum_op=(ALU.bypass if m_ == 0 else ALU.add)),
                    reads=R(f"stmp{sl}"), writes=R(f"hacc{och}_{tg}"), dma=True)

    for och in range(32):
        scr, scrn = scr_a, "scr_a"
        p.add("gpsimd", lambda g, och=och, scr=scr: g.dma_start(
            out=scr[:], in_=hacc[och * 128:(och + 1) * 128, :]),
            reads=tuple(f"hacc{och}_{tg}" for tg in range(NTG)), writes=R(scrn), dma=True)
        p.add("scalar", lambda sc, scr=scr: sc.activation(scr[:], scr[:], AF.Relu),
              reads=R(scrn), writes=R(scrn))
        p.add("scalar", lambda sc, scr=scr: sc.activation(scrbf[:], scr[:], AF.Square),
              reads=R(scrn), writes=R("scrbf"))
        p.add("sync", lambda s, och=och: s.dma_start(
            out=hact[och * 128:(och + 1) * 128, :], in_=scrbf[:]),
            reads=R("scrbf"), writes=R(f"hact{och}"), dma=True)

    allhact = tuple(f"hact{o}" for o in range(32))
    for m_ in range(2):
        idxc = 2 + m_
        branch_load(idxc, bd1sb[:].rearrange("p (j w) -> p j w", w=128),
                    lambda b: pjbd1_d[b].rearrange("j p w -> p j w"), "bd1")
        branch_load(idxc, bd2sb[:, 0:8 * 128].rearrange("p (j w) -> p j w", w=128),
                    lambda b: pjbd2_d[b].rearrange("j p w -> p j w"), "bd2")
        branch_load(idxc, mixA[:].rearrange("p (c o) -> p c o", o=D),
                    lambda b: pjmix_d[b].rearrange("(c p) o -> p c o", p=128), "mixA")
        scale_bd2("bd2", bd2sb, 8 * 128, 2 + m_)
        for tg in range(NTG):
            for ch in range(32):
                pa = ps[tg % 2]
                sl2 = ch % 2
                p.add("gpsimd", lambda g, ch=ch, tg=tg, sl2=sl2: g.dma_start(
                    out=nTg_a[:, sl2 * 512:(sl2 + 1) * 512] if False else
                    mix_sb[:, sl2 * 512:(sl2 + 1) * 512],
                    in_=hact[ch * 128:(ch + 1) * 128, tg * 512:(tg + 1) * 512]),
                    reads=R(f"hact{ch}",), writes=R(f"mxsb{sl2}"), dma=True)
                p.add("tensor", lambda tt, ch=ch, pa=pa, sl2=sl2: tt.matmul(
                    pa[:, 0:512], bd1sb[:, ch * 128:(ch + 1) * 128],
                    mix_sb[:, sl2 * 512:(sl2 + 1) * 512], start=True, stop=True),
                    reads=R("bd1", f"mxsb{sl2}"), writes=R(f"ps{tg % 2}"))
                p.add("vector", lambda v, ch=ch, pa=pa: v.tensor_copy(
                    h1sb[:, ch * 512:(ch + 1) * 512], pa[:, 0:512]),
                    reads=R(f"ps{tg % 2}"), writes=R("h1"))
            for och in range(8):
                pm = ps[2 + och % 2]

                def mm_mixp(tt, och=och, pm=pm):
                    for ch in range(32):
                        inst = tt.matmul(
                            pm[:, 0:512],
                            mixA[:, ch * D + och * 128: ch * D + och * 128 + 128],
                            h1sb[:, ch * 512:(ch + 1) * 512],
                            start=(ch == 0), stop=(ch == 31))
                    return inst
                p.add("tensor", mm_mixp, reads=R("mixA", "h1"), writes=R(f"ps{2 + och % 2}"))
                msl = och % 2
                p.add("vector", lambda v, pm=pm, msl=msl: v.tensor_copy(
                    mix_sb[:, msl * 512:(msl + 1) * 512], pm[:, 0:512]),
                    reads=R(f"ps{2 + och % 2}"), writes=R(f"mxsb{msl}"))
                for ti in range(4):
                    t = tg * 4 + ti
                    pb2 = ps[4 + ti % 2]
                    p.add("tensor", lambda tt, och=och, msl=msl, ti=ti, pb2=pb2: tt.matmul(
                        pb2[:, 0:128], mix_sb[:, msl * 512 + ti * 128: msl * 512 + ti * 128 + 128],
                        bd2sb[:, och * 128:(och + 1) * 128], start=True, stop=True),
                        reads=R("bd2", f"mxsb{msl}"), writes=R(f"ps{4 + ti % 2}"))
                    sl = (och * 4 + ti) % 4
                    p.add("vector", lambda v, pb2=pb2, sl=sl: v.tensor_copy(
                        stmp[:, sl * 512:sl * 512 + 128], pb2[:, 0:128]),
                        reads=R(f"ps{4 + ti % 2}"), writes=R(f"stmp{sl}"))
                    p.add("gpsimd", lambda g, t=t, och=och, sl=sl, m_=m_: g.dma_start(
                        out=acc_d[t * 128:(t + 1) * 128, och * 128:(och + 1) * 128],
                        in_=stmp[:, sl * 512:sl * 512 + 128],
                        accum_op=(ALU.bypass if m_ == 0 else ALU.add)),
                        reads=R(f"stmp{sl}"), writes=R(f"acc{t}_{och}"), dma=True)

    for t in range(NT):
        par = t % 2
        xf, xfn = (xf_a, "xf_a") if par == 0 else (xf_b, "xf_b")
        x1f, x1fn = (x1f_a, "x1f_a") if par == 0 else (x1f_b, "x1f_b")
        yt, ytn = (yt_a, "yt_a") if par == 0 else (yt_b, "yt_b")
        _emit_final_epilogue(p, nc, t, acc_d, xin, xmid, xf, xfn, x1f, x1fn,
                             yt, ytn, sqd, small, xout,
                             acc_reads=[f"acc{t}_{o}" for o in range(8)])

    if standalone:
        p.emit(es)
        es.close()
        return nc
    p.fence()
    es.close()
    return None


def _bd_pairs(w, nb):
    out = np.zeros((nb // 2, 128, 128), np.float32)
    for j in range(nb // 2):
        out[j, 0:64, 0:64] = w[2 * j]
        out[j, 64:128, 64:128] = w[2 * j + 1]
    return out


def prep_l0_ffn_bank(inp):
    io = {}
    io["c_onesb"] = CONSTS["c_onesb"]
    io["c_onesf"] = CONSTS["c_onesf"]
    io["c_iota8"] = np.arange(8, dtype=np.float32).reshape(1, 8)
    io["fc_logits"] = np.ascontiguousarray(inp["L0_ffn_fc_logits"]).reshape(1, 8)
    io["pj_logits"] = np.ascontiguousarray(inp["L0_ffn_pj_logits"]).reshape(1, 8)
    fc_s1 = np.asarray(inp["L0_ffn_fc_s1"])
    fc_s2 = np.asarray(inp["L0_ffn_fc_s2"])
    fc_mix = np.asarray(inp["L0_ffn_fc_mix"])
    pj_s1 = np.asarray(inp["L0_ffn_pj_s1"])
    pj_s2 = np.asarray(inp["L0_ffn_pj_s2"])
    pj_mix = np.asarray(inp["L0_ffn_pj_mix"])
    io["fc_bd1"] = np.stack([_bd_pairs(fc_s1[b].reshape(16, 64, 64), 16) for b in range(8)])
    io["fc_bd2"] = np.stack([_bd_pairs(fc_s2[b].reshape(64, 64, 64), 64) for b in range(8)])
    io["pj_bd1"] = np.stack([_bd_pairs(pj_s1[b].reshape(64, 64, 64), 64) for b in range(8)])
    io["pj_bd2"] = np.stack([_bd_pairs(pj_s2[b].reshape(16, 64, 64), 16) for b in range(8)])
    g16, c64 = np.meshgrid(np.arange(16), np.arange(64), indexing="ij")
    fcm = np.empty((8, 1024, 4096), np.float32)
    for b in range(8):
        wmT = fc_mix[b].T
        idx = (c64 * 16 + (g16 + b) % 16).reshape(-1)
        fcm[b] = wmT[idx]
    io["fc_mixp"] = fcm
    g64, c64b = np.meshgrid(np.arange(64), np.arange(64), indexing="ij")
    pjm = np.empty((8, 4096, 1024), np.float32)
    for b in range(8):
        wmT = pj_mix[b].T
        idx = (c64b * 64 + (g64 + b) % 64).reshape(-1)
        pjm[b] = wmT[idx]
    io["pj_mixp"] = pjm
    return io
